# version 08
# baseline (speedup 1.0000x reference)
import numpy as np
import ml_dtypes
from contextlib import ExitStack

import concourse.bass as bass
import concourse.mybir as mybir
from concourse.bass_utils import run_bass_kernel_spmd

F32 = mybir.dt.float32
BF16 = mybir.dt.bfloat16
AF = mybir.ActivationFunctionType
ALU = mybir.AluOpType
bfdt = ml_dtypes.bfloat16


def mk_nc():
    return bass.Bass("TRN2", target_bir_lowering=False, dynamic_dma_scratch_size=16384)

B, S, D = 4, 4096, 1024
NCORES = 8
SH = S // 2
NSC = SH // 512

L0_SCALE = 1.0 / 8.0
L0_C = 16.0
N_HEADS, HD, DL = 16, 64, 256
L1_SCALE = float(1.0 / (np.sqrt(np.float32(1024.0)) + 1e-8))
L1_C = 4.0
LAM = 1e-4



class Sem:
    def __init__(self, handle, step=1):
        self.h = handle
        self.step = step
        self.n = 0

    def bump(self):
        self.n += self.step
        return self.n


class Prog:
    ENGINES = ("gpsimd", "sync", "scalar", "vector", "tensor")

    def __init__(self, nc):
        self.nc = nc
        self.steps = {e: [] for e in self.ENGINES}

    def on(self, eng, fn):
        self.steps[eng].append(fn)

    def inc(self, eng, fn, sem):
        t = sem.bump()
        self.steps[eng].append(lambda E: fn(E).then_inc(sem.h, sem.step))
        return t

    def wait(self, eng, sem, val):
        self.steps[eng].append(lambda E: E.wait_ge(sem.h, val))

    def flush(self):
        if not any(self.steps[e] for e in self.ENGINES):
            return
        with self.nc.Block() as blk:
            for e in self.ENGINES:
                fns = self.steps[e]
                if fns:
                    getattr(blk, e)(lambda eng, fns=fns: [f(eng) for f in fns])
        self.steps = {e: [] for e in self.ENGINES}


class Sems:

    def __init__(self, nc, stack, tag=""):
        self.nc = nc
        self.stack = stack
        self._n16 = 0
        self.tag = tag
        self.WL = Sem(stack.enter_context(nc.semaphore(f"{tag}wl")), 16)
        self.MM = Sem(stack.enter_context(nc.semaphore(f"{tag}mm")), 1)
        self.SE = Sem(stack.enter_context(nc.semaphore(f"{tag}se")), 1)
        self.VE = Sem(stack.enter_context(nc.semaphore(f"{tag}ve")), 1)
        self.ST = Sem(stack.enter_context(nc.semaphore(f"{tag}st")), 16)
        self.XL = Sem(stack.enter_context(nc.semaphore(f"{tag}xl")), 16)
        self.GP = Sem(stack.enter_context(nc.semaphore(f"{tag}gp")), 1)

    def new16(self, name=""):
        self._n16 += 1
        return Sem(self.stack.enter_context(self.nc.semaphore(f"{self.tag}L{self._n16}{name}")), 16)


def mm_group(P, sems, bank_ap, pairs, bump=True, accum_from=None, accum_to=None):
    n = len(pairs)
    tick = None
    for i, (st, mv) in enumerate(pairs):
        first = (i == 0) if accum_from is None else accum_from
        last = (i == n - 1) if accum_to is None else False
        if i == n - 1 and bump:
            tick = P.inc("tensor",
                         lambda t, st=st, mv=mv, first=first, last=last:
                         t.matmul(bank_ap, st, mv, start=first, stop=last), sems.MM)
        else:
            P.on("tensor",
                 lambda t, st=st, mv=mv, first=first, last=last:
                 t.matmul(bank_ap, st, mv, start=first, stop=last))
    return tick


def emit_proj(P, sems, *, nO, nC, nSc, st_ap, mv_ap, banks, bank_ticks, evac):
    nb = len(banks)
    ti = 0
    for o in range(nO):
        for s in range(nSc):
            bi = ti % nb
            bank = banks[bi]
            prev = bank_ticks.get(id(banks[bi]))
            if prev is not None:
                P.wait("tensor", prev[0], prev[1])
            pairs = [(st_ap(c, o), mv_ap(c, s)) for c in range(nC)]
            mmt = mm_group(P, sems, bank[:, :], pairs)
            bank_ticks[id(banks[bi])] = evac(o, s, bank, mmt)
            ti += 1


def inc_chain(P, eng, fn, sem):
    t = P.inc(eng, fn, sem)
    P.wait(eng, sem, t)
    return t


def scalar_copy_evac(P, sems, dst_ap, bank, mmt):
    P.wait("scalar", sems.MM, mmt)
    t = P.inc("scalar", lambda sc: sc.copy(dst_ap, bank[:, :]), sems.SE)
    return (sems.SE, t)



def build_p1(nc, P, sems, io):
    ph0 = ExitStack()
    with ph0:
        pb = [ph0.enter_context(nc.psum_tensor(f"p1_pb{i}", [128, 512], F32)) for i in range(8)]
        OT2 = ph0.enter_context(nc.sbuf_tensor("p1_ot2", [128, 8 * SH], BF16))
        _build_p1_attn(nc, P, sems, io, pb, OT2)
        _build_p1_proj(nc, P, sems, io, OT2, pb)


def _build_p1_attn(nc, P, sems, io, pb, OT2):
  pha = ExitStack()
  with pha:
    KT = [pha.enter_context(nc.sbuf_tensor(f"p1_kt{i}", [128, S], BF16)) for i in range(8)]
    VK = pha.enter_context(nc.sbuf_tensor("p1_vk", [128, 32 * 512], BF16))

    ph = ExitStack()
    with ph:
        WK = ph.enter_context(nc.sbuf_tensor("p1_wk", [128, 8 * 1024], BF16))
        WV = ph.enter_context(nc.sbuf_tensor("p1_wv", [128, 8 * 512], BF16))
        XS = [ph.enter_context(nc.sbuf_tensor(f"p1_xs{k}", [128, 8 * 512], BF16)) for k in range(2)]

        wA = sems.new16("wa")
        sXS = [sems.new16(f"xs{k}") for k in range(2)]
        for i in range(8):
            P.inc("gpsimd", lambda g, i=i: g.dma_start(out=WK[:, i * 1024:(i + 1) * 1024],
                                                       in_=io["wqkT"][i * 128:(i + 1) * 128, 1024:2048]), wA)
            P.inc("gpsimd", lambda g, i=i: g.dma_start(out=WV[:, i * 512:(i + 1) * 512],
                                                       in_=io["wv_own"][i * 128:(i + 1) * 128, :]), wA)
        P.wait("tensor", wA, wA.n)

        xs_used = {}
        xs_load = {}

        def load_xs(si):
            buf = XS[si % 2]
            sem = sXS[si % 2]
            prev = xs_used.get(si - 2)
            if prev is not None:
                P.wait("gpsimd", sems.MM, prev)
            for e in range(8):
                P.inc("gpsimd",
                      lambda g, e=e, buf=buf, si=si:
                      g.dma_start(out=buf[:, e * 512:(e + 1) * 512],
                                  in_=io["xT"][e * 128:(e + 1) * 128, si * 512:(si + 1) * 512]),
                      sem)
            xs_load[si] = (sem, sem.n)

        load_xs(0)
        bank_ticks = {}
        for si in range(8):
            xbuf = XS[si % 2]
            P.wait("tensor", xs_load[si][0], xs_load[si][1])
            emit_proj(P, sems, nO=8, nC=8, nSc=1,
                      st_ap=lambda c, o: WK[:, c * 1024 + o * 128: c * 1024 + (o + 1) * 128],
                      mv_ap=lambda c, _s, xbuf=xbuf: xbuf[:, c * 512:(c + 1) * 512],
                      banks=pb[0:2], bank_ticks=bank_ticks,
                      evac=lambda o, _s, bank, mmt, si=si: scalar_copy_evac(
                          P, sems, KT[o][:, si * 512:(si + 1) * 512], bank, mmt))
            emit_proj(P, sems, nO=4, nC=8, nSc=1,
                      st_ap=lambda c, j, xbuf=xbuf: xbuf[:, c * 512 + j * 128: c * 512 + (j + 1) * 128],
                      mv_ap=lambda c, _s: WV[:, c * 512:(c + 1) * 512],
                      banks=pb[0:2], bank_ticks=bank_ticks,
                      evac=lambda j, _s, bank, mmt, si=si: scalar_copy_evac(
                          P, sems, VK[:, (4 * si + j) * 512:(4 * si + j + 1) * 512], bank, mmt))
            xs_used[si] = sems.MM.n
            if si < 7:
                load_xs(si + 1)
        P.flush()

    ph = ExitStack()
    with ph:
        WQ = ph.enter_context(nc.sbuf_tensor("p1_wq", [128, 8 * 1024], BF16))
        XQ = [ph.enter_context(nc.sbuf_tensor(f"p1_xq{k}", [128, 8 * 512], BF16)) for k in range(2)]
        QT = [ph.enter_context(nc.sbuf_tensor(f"p1_qt{k}", [128, 8 * 512], BF16)) for k in range(2)]
        Ebuf = [ph.enter_context(nc.sbuf_tensor(f"p1_e{k}", [128, 512], BF16)) for k in range(3)]
        ones_bf = ph.enter_context(nc.sbuf_tensor("p1_ones", [128, 1], BF16))
        cbias = ph.enter_context(nc.sbuf_tensor("p1_cb", [128, 1], F32))
        rcol = [ph.enter_context(nc.sbuf_tensor(f"p1_rc{k}", [128, 1], F32)) for k in range(4)]

        wB = sems.new16("wb")
        sXQ = [sems.new16(f"xq{k}") for k in range(2)]
        for i in range(8):
            P.inc("gpsimd", lambda g, i=i: g.dma_start(out=WQ[:, i * 1024:(i + 1) * 1024],
                                                       in_=io["wqkT"][i * 128:(i + 1) * 128, 0:1024]), wB)
        P.inc("gpsimd", lambda g: g.memset(ones_bf[:, :], 1.0), sems.GP)
        P.inc("gpsimd", lambda g: g.memset(cbias[:, :], -L0_C), sems.GP)
        P.wait("tensor", sems.GP, sems.GP.n)
        P.wait("tensor", wB, wB.n)
        P.wait("scalar", sems.GP, sems.GP.n)

        xq_used = {}
        xq_load = {}

        def load_xq(qci):
            buf = XQ[qci % 2]
            sem = sXQ[qci % 2]
            prev = xq_used.get(qci - 2)
            if prev is not None:
                P.wait("gpsimd", sems.MM, prev)
            for e in range(8):
                P.inc("gpsimd",
                      lambda g, e=e, buf=buf, qci=qci:
                      g.dma_start(out=buf[:, e * 512:(e + 1) * 512],
                                  in_=io["xT"][e * 128:(e + 1) * 128, qci * 512:(qci + 1) * 512]),
                      sem)
            xq_load[qci] = (sem, sem.n)

        load_xq(0)
        prev_chunk_evac = None
        bank_ticks = {}
        for qc in range(8):
            qbuf = QT[qc % 2]
            xbuf = XQ[qc % 2]
            P.wait("tensor", xq_load[qc][0], xq_load[qc][1])
            emit_proj(P, sems, nO=8, nC=8, nSc=1,
                      st_ap=lambda c, o: WQ[:, c * 1024 + o * 128: c * 1024 + (o + 1) * 128],
                      mv_ap=lambda c, _s, xbuf=xbuf: xbuf[:, c * 512:(c + 1) * 512],
                      banks=pb[0:2], bank_ticks=bank_ticks,
                      evac=lambda o, _s, bank, mmt, qbuf=qbuf: scalar_copy_evac(
                          P, sems, qbuf[:, o * 512:(o + 1) * 512], bank, mmt))
            xq_used[qc] = sems.MM.n
            if qc < 7:
                load_xq(qc + 1)
            P.wait("tensor", sems.SE, sems.SE.n)
            if prev_chunk_evac is not None:
                P.wait("tensor", sems.VE, prev_chunk_evac)

            for kt in range(32):
                bi = kt % 2
                bank = pb[bi]
                prev = bank_ticks.get(id(pb[bi]))
                if prev is not None:
                    P.wait("tensor", prev[0], prev[1])
                pairs = [(KT[dd][:, kt * 128:(kt + 1) * 128], qbuf[:, dd * 512:(dd + 1) * 512])
                         for dd in range(8)]
                mmt = mm_group(P, sems, bank[:, :], pairs)
                eb = Ebuf[kt % 3]
                P.wait("scalar", sems.MM, mmt)
                et = P.inc("scalar",
                           lambda sc, eb=eb, bank=bank:
                           sc.activation(eb[:, :], bank[:, :], AF.Exp, bias=cbias[:, :], scale=L0_SCALE),
                           sems.SE)
                bank_ticks[id(pb[bi])] = (sems.SE, et)
                P.wait("tensor", sems.SE, et)
                first, last = kt == 0, kt == 31
                for qs in range(4):
                    stE = eb[:, qs * 128:(qs + 1) * 128]
                    mvV = VK[:, kt * 512:(kt + 1) * 512]
                    avb = pb[2 + qs]
                    P.on("tensor",
                         lambda t, stE=stE, mvV=mvV, avb=avb, first=first, last=last:
                         t.matmul(avb[:, :], stE, mvV, start=first, stop=last))
                    dcol = pb[6][:, qs:qs + 1]
                    dfirst = first and qs == 0
                    dlast = last and qs == 3
                    if dlast:
                        mmt2 = P.inc("tensor",
                                     lambda t, stE=stE, dcol=dcol:
                                     t.matmul(dcol, stE, ones_bf[:, :], start=False, stop=True), sems.MM)
                    else:
                        P.on("tensor",
                             lambda t, stE=stE, dcol=dcol, dfirst=dfirst:
                             t.matmul(dcol, stE, ones_bf[:, :], start=dfirst, stop=False))
            P.wait("vector", sems.MM, mmt2)
            for qs in range(4):
                qt = qc * 4 + qs
                hc, r = qt % 8, qt // 8
                rc = rcol[qs]
                inc_chain(P, "vector", lambda v, rc=rc, qs=qs: v.reciprocal(rc[:, :], pb[6][:, qs:qs + 1]), sems.VE)
                dst = OT2[:, hc * SH + r: hc * SH + r + 4 * 511 + 1: 4]
                t2 = inc_chain(P, "vector",
                               lambda v, dst=dst, qs=qs, rc=rc:
                               v.tensor_scalar_mul(dst, pb[2 + qs][:, :], rc[:, :]), sems.VE)
            prev_chunk_evac = t2
        P.flush()


def _build_p1_proj(nc, P, sems, io, OT2, pb):
    ph2 = ExitStack()
    with ph2:
        RES = [ph2.enter_context(nc.sbuf_tensor(f"p1_res{i}", [128, SH], F32)) for i in range(8)]
        _p1_proj_mm(nc, P, sems, io, OT2, pb, RES)
        emit_ln(nc, P, sems, "p1ln", RES, io["x1f"], pb)


def _p1_proj_mm(nc, P, sems, io, OT2, pb, RES):
    ph2 = ExitStack()
    with ph2:
        WP = ph2.enter_context(nc.sbuf_tensor("p1_wp", [128, 8 * 1024], BF16))
        XR = [ph2.enter_context(nc.sbuf_tensor(f"p1_xr{i}", [128, SH], F32)) for i in range(8)]
        bp = ph2.enter_context(nc.sbuf_tensor("p1_bp", [128, 8], F32))

        wP = sems.new16("wp")
        xR = sems.new16("xr")
        for i in range(8):
            P.inc("gpsimd", lambda g, i=i: g.dma_start(out=WP[:, i * 1024:(i + 1) * 1024],
                                                       in_=io["wpT"][i * 128:(i + 1) * 128, :]), wP)
            P.inc("sync", lambda sy, i=i: sy.dma_start(out=XR[i][:, :],
                                                       in_=io["x_res"][i * 128:(i + 1) * 128, :]), xR)
        P.inc("sync", lambda sy: sy.dma_start(out=bp[:, :], in_=io["bproj8"][:, :]), xR)
        P.wait("vector", wP, wP.n)
        P.wait("vector", xR, xR.n)
        for o in range(8):
            P.inc("vector",
                  lambda v, o=o: v.tensor_scalar_add(XR[o][:, :], XR[o][:, :], bp[:, o:o + 1]), sems.VE)
        P.flush()

        bank_ticks = {}

        def proj_evac(o, s, bank, mmt):
            P.wait("vector", sems.MM, mmt)
            t = P.inc("vector",
                      lambda v, o=o, s=s, bank=bank:
                      v.tensor_add(RES[o][:, s * 512:(s + 1) * 512], bank[:, :],
                                   XR[o][:, s * 512:(s + 1) * 512]), sems.VE)
            return (sems.VE, t)

        emit_proj(P, sems, nO=8, nC=8, nSc=4,
                  st_ap=lambda c, o: WP[:, c * 1024 + o * 128: c * 1024 + (o + 1) * 128],
                  mv_ap=lambda c, s: OT2[:, c * SH + s * 512: c * SH + (s + 1) * 512],
                  banks=pb[0:4], bank_ticks=bank_ticks, evac=proj_evac)
        P.flush()



def emit_ln(nc, P, sems, tag, RES, out_f32_dram, pbs):
    ph = ExitStack()
    with ph:
        R16 = [ph.enter_context(nc.sbuf_tensor(f"{tag}_r16{k}", [128, SH], BF16)) for k in range(2)]
        SQ = [ph.enter_context(nc.sbuf_tensor(f"{tag}_sq{k}", [128, SH], BF16)) for k in range(2)]
        onesc = ph.enter_context(nc.sbuf_tensor(f"{tag}_oc", [128, 1], BF16))
        onesr = ph.enter_context(nc.sbuf_tensor(f"{tag}_or", [1, 128], F32))
        zcol = ph.enter_context(nc.sbuf_tensor(f"{tag}_zc", [1, 1], F32))
        ecol = ph.enter_context(nc.sbuf_tensor(f"{tag}_ec", [1, 1], F32))
        rows = ph.enter_context(nc.sbuf_tensor(f"{tag}_rows", [1, 4 * SH], F32))
        AB = ph.enter_context(nc.sbuf_tensor(f"{tag}_abb", [128, SH], F32))
        CB = ph.enter_context(nc.sbuf_tensor(f"{tag}_cbb", [128, SH], F32))
        XF = [ph.enter_context(nc.sbuf_tensor(f"{tag}_xf{k}", [128, SH], F32)) for k in range(2)]

        P.on("gpsimd", lambda g: g.memset(onesc[:, :], 1.0))
        P.on("gpsimd", lambda g: g.memset(onesr[:, :], 1.0))
        P.on("gpsimd", lambda g: g.memset(zcol[:, :], 0.0))
        P.on("gpsimd", lambda g: g.memset(ecol[:, :], 1e-5))
        P.flush()

        pe_tick = {}
        for o in range(8):
            k = o % 2
            if o >= 2:
                P.wait("vector", sems.MM, pe_tick[o - 2])
                P.wait("scalar", sems.MM, pe_tick[o - 2])
            cv = P.inc("vector", lambda v, o=o, k=k: v.tensor_copy(R16[k][:, :], RES[o][:, :]), sems.VE)
            cs = P.inc("scalar", lambda sc, o=o, k=k: sc.activation(SQ[k][:, :], RES[o][:, :], AF.Square), sems.SE)
            P.wait("tensor", sems.VE, cv)
            P.wait("tensor", sems.SE, cs)
            for s in range(NSC):
                first, last = o == 0, o == 7
                P.on("tensor", lambda t, s=s, k=k, first=first, last=last:
                     t.matmul(pbs[s][0:1, :], onesc[:, :], R16[k][:, s * 512:(s + 1) * 512],
                              start=first, stop=last))
                if s < NSC - 1:
                    P.on("tensor", lambda t, s=s, k=k, first=first, last=last:
                         t.matmul(pbs[4 + s][0:1, :], onesc[:, :], SQ[k][:, s * 512:(s + 1) * 512],
                                  start=first, stop=last))
                else:
                    pe_tick[o] = P.inc("tensor", lambda t, s=s, k=k, first=first, last=last:
                                       t.matmul(pbs[4 + s][0:1, :], onesc[:, :], SQ[k][:, s * 512:(s + 1) * 512],
                                                start=first, stop=last), sems.MM)
        P.wait("vector", sems.MM, pe_tick[7])
        for s in range(NSC):
            P.inc("vector", lambda v, s=s: v.tensor_copy(rows[0:1, s * 512:(s + 1) * 512], pbs[s][0:1, :]), sems.VE)
            P.inc("vector", lambda v, s=s: v.tensor_copy(rows[0:1, SH + s * 512: SH + (s + 1) * 512],
                                                         pbs[4 + s][0:1, :]), sems.VE)
        P.flush()

        s1 = rows[0:1, 0:SH]
        s2 = rows[0:1, SH:2 * SH]
        a = rows[0:1, 2 * SH:3 * SH]
        c = rows[0:1, 3 * SH:4 * SH]
        inc_chain(P, "vector", lambda v: v.tensor_scalar_mul(c, s1, 1.0 / 1024.0), sems.VE)
        inc_chain(P, "vector", lambda v: v.tensor_scalar_mul(s2, s2, 1.0 / 1024.0), sems.VE)
        inc_chain(P, "vector", lambda v: v.tensor_tensor(a, c, c, ALU.mult), sems.VE)
        inc_chain(P, "vector", lambda v: v.tensor_tensor(a, s2, a, ALU.subtract), sems.VE)
        P.flush()
        P.inc("scalar", lambda sc: sc.activation(a, a, AF.Ln, bias=ecol[:, :], scale=1.0), sems.SE)
        P.flush()
        P.inc("scalar", lambda sc: sc.activation(a, a, AF.Exp, bias=zcol[:, :], scale=-0.5), sems.SE)
        P.flush()
        for s in range(NSC):
            P.on("tensor", lambda t, s=s: t.matmul(pbs[s][:, :], onesr[:, :],
                                                   a[0:1, s * 512:(s + 1) * 512], start=True, stop=True))
            if s < NSC - 1:
                P.on("tensor", lambda t, s=s: t.matmul(pbs[4 + s][:, :], onesr[:, :],
                                                       c[0:1, s * 512:(s + 1) * 512], start=True, stop=True))
            else:
                bct = P.inc("tensor", lambda t, s=s: t.matmul(pbs[4 + s][:, :], onesr[:, :],
                                                              c[0:1, s * 512:(s + 1) * 512], start=True, stop=True), sems.MM)
        P.wait("vector", sems.MM, bct)
        for s in range(NSC):
            P.inc("vector", lambda v, s=s: v.tensor_copy(AB[:, s * 512:(s + 1) * 512], pbs[s][:, :]), sems.VE)
            P.inc("vector", lambda v, s=s: v.tensor_copy(CB[:, s * 512:(s + 1) * 512], pbs[4 + s][:, :]), sems.VE)
        P.flush()

        sstore = [sems.new16(f"{tag}st{k}") for k in range(2)]
        store_tick = {}
        for o in range(8):
            k = o % 2
            if o >= 2:
                P.wait("vector", store_tick[o - 2][0], store_tick[o - 2][1])
            inc_chain(P, "vector", lambda v, o=o, k=k: v.tensor_tensor(XF[k][:, :], RES[o][:, :], CB[:, :], ALU.subtract), sems.VE)
            vt = inc_chain(P, "vector", lambda v, k=k: v.tensor_tensor(XF[k][:, :], XF[k][:, :], AB[:, :], ALU.mult), sems.VE)
            P.wait("sync", sems.VE, vt)
            t = P.inc("sync", lambda sy, o=o, k=k: sy.dma_start(out=out_f32_dram[o * 128:(o + 1) * 128, :],
                                                                in_=XF[k][:, :]), sstore[k])
            store_tick[o] = (sstore[k], t)
        P.wait("sync", sstore[0], sstore[0].n)
        P.wait("sync", sstore[1], sstore[1].n)
        P.flush()




def build_p2(nc, P, sems, io):
    ph0 = ExitStack()
    with ph0:
        pb = [ph0.enter_context(nc.psum_tensor(f"p2_pb{i}", [128, 512], F32)) for i in range(8)]
        X1bT = ph0.enter_context(nc.sbuf_tensor("p2_x1b", [128, 8 * SH], BF16))
        X1b = [X1bT[:, i * SH:(i + 1) * SH] for i in range(8)]
        B4 = ph0.enter_context(nc.sbuf_tensor("p2_b4", [128, 4 * SH], BF16))

        lx = sems.new16("p2x")
        for i in range(8):
            P.inc("gpsimd", lambda g, i=i: g.dma_start(out=X1b[i],
                                                       in_=io["x1f"][i * 128:(i + 1) * 128, :]), lx)
        _p2_gate(nc, P, sems, io, pb, X1b, B4, lx)

        REST = ph0.enter_context(nc.sbuf_tensor("p2_res", [128, 8 * SH], F32))
        RES = [REST[:, i * SH:(i + 1) * SH] for i in range(8)]
        _p2_units(nc, P, sems, io, pb, X1b, B4, RES)
        emit_ln(nc, P, sems, "p2ln", RES, io["x2f"], pb)


def _p2_gate(nc, P, sems, io, pb, X1b, B4, lx):
    ph = ExitStack()
    with ph:
        GW = ph.enter_context(nc.sbuf_tensor("p2_gw", [128, 8 * 4], F32))
        GB = ph.enter_context(nc.sbuf_tensor("p2_gb", [4, 1], F32))
        G4 = ph.enter_context(nc.sbuf_tensor("p2_g4", [4, SH], F32))
        G1 = ph.enter_context(nc.sbuf_tensor("p2_g1", [1, 4 * SH], F32))
        MW = ph.enter_context(nc.sbuf_tensor("p2_mw", [1, 4 * SH], F32))
        TR = ph.enter_context(nc.sbuf_tensor("p2_tr", [1, 3 * SH], F32))
        onesr = ph.enter_context(nc.sbuf_tensor("p2_or", [1, 128], F32))

        lw = sems.new16("p2gw")
        XG = [ph.enter_context(nc.sbuf_tensor(f"p2_xg{k}", [128, 8 * 512], F32)) for k in range(2)]
        for c in range(8):
            P.inc("gpsimd", lambda g, c=c: g.dma_start(out=GW[:, c * 4:(c + 1) * 4],
                                                       in_=io["gate_w"][c * 128:(c + 1) * 128, :]), lw)
        P.inc("gpsimd", lambda g: g.dma_start(out=GB[:, :], in_=io["gate_b"][:, :]), lw)
        sXG = [sems.new16(f"p2xg{k}") for k in range(2)]
        xg_used = {}
        xg_ld = {}

        def load_xg(si):
            k = si % 2
            prev = xg_used.get(si - 2)
            if prev is not None:
                P.wait("sync", sems.MM, prev)
            for c in range(8):
                P.inc("sync", lambda sy, c=c, k=k, si=si:
                      sy.dma_start(out=XG[k][:, c * 512:(c + 1) * 512],
                                   in_=io["x1f"][c * 128:(c + 1) * 128, si * 512:(si + 1) * 512]), sXG[k])
            xg_ld[si] = (sXG[k], sXG[k].n)

        load_xg(0)
        P.inc("gpsimd", lambda g: g.memset(onesr[:, :], 1.0), sems.GP)
        gp_t = sems.GP.n
        P.wait("tensor", lw, lw.n)
        P.wait("tensor", lx, lx.n)
        for s4 in range(NSC):
            bank = pb[s4 % 2]
            P.wait("tensor", xg_ld[s4][0], xg_ld[s4][1])
            if s4 < NSC - 1:
                load_xg(s4 + 1)
            pairs = [(GW[:, c * 4:(c + 1) * 4], XG[s4 % 2][:, c * 512:(c + 1) * 512]) for c in range(8)]
            mmt = mm_group(P, sems, bank[0:4, :], pairs)
            xg_used[s4] = sems.MM.n
            P.wait("vector", sems.MM, mmt)
            t = inc_chain(P, "vector", lambda v, s4=s4, bank=bank:
                          v.tensor_scalar_add(G4[:, s4 * 512:(s4 + 1) * 512], bank[0:4, :], GB[:, :]), sems.VE)
            P.wait("tensor", sems.VE, t)
        st0 = sems.new16("p2g1")
        P.wait("sync", sems.VE, sems.VE.n)
        for i in range(4):
            P.inc("sync", lambda sy, i=i: sy.dma_start(out=G1[0:1, i * SH:(i + 1) * SH],
                                                       in_=G4[i:i + 1, :]), st0)
        P.wait("vector", st0, st0.n)
        P.wait("scalar", st0, st0.n)

        def row(t_, i):
            return t_[0:1, i * SH:(i + 1) * SH]

        e_r = [row(G1, i) for i in range(4)]
        w_r = [row(MW, i) for i in range(4)]
        m_r, t1, t2 = row(TR, 0), row(TR, 1), row(TR, 2)
        V, E_ = "vector", sems.VE
        inc_chain(P, V, lambda v: v.tensor_tensor(m_r, e_r[0], e_r[1], ALU.max), E_)
        inc_chain(P, V, lambda v: v.tensor_tensor(t1, e_r[2], e_r[3], ALU.max), E_)
        inc_chain(P, V, lambda v: v.tensor_tensor(m_r, m_r, t1, ALU.max), E_)
        for i in range(4):
            inc_chain(P, V, lambda v, i=i: v.tensor_tensor(e_r[i], e_r[i], m_r, ALU.subtract), E_)
            P.wait("scalar", sems.VE, sems.VE.n)
            t = P.inc("scalar", lambda sc, i=i: sc.activation(e_r[i], e_r[i], AF.Exp), sems.SE)
            P.wait(V, sems.SE, t)
        inc_chain(P, V, lambda v: v.tensor_tensor(t1, e_r[0], e_r[1], ALU.add), E_)
        inc_chain(P, V, lambda v: v.tensor_tensor(t2, e_r[2], e_r[3], ALU.add), E_)
        inc_chain(P, V, lambda v: v.tensor_tensor(t1, t1, t2, ALU.add), E_)
        inc_chain(P, V, lambda v: v.reciprocal(t1, t1), E_)
        for i in range(4):
            js = [j for j in range(4) if j != i]
            inc_chain(P, V, lambda v, i=i, j=js[0]: v.tensor_tensor(w_r[i], e_r[j], e_r[i], ALU.is_gt), E_)
            for j in js[1:]:
                inc_chain(P, V, lambda v, i=i, j=j: v.tensor_tensor(t2, e_r[j], e_r[i], ALU.is_gt), E_)
                inc_chain(P, V, lambda v, i=i: v.tensor_tensor(w_r[i], w_r[i], t2, ALU.add), E_)
            inc_chain(P, V, lambda v, i=i: v.tensor_scalar(w_r[i], w_r[i], 2.0, None, ALU.is_lt), E_)
            inc_chain(P, V, lambda v, i=i: v.tensor_tensor(w_r[i], w_r[i], e_r[i], ALU.mult), E_)
            inc_chain(P, V, lambda v, i=i: v.tensor_tensor(w_r[i], w_r[i], t1, ALU.mult), E_)
        P.wait("tensor", sems.VE, sems.VE.n)
        P.wait("tensor", sems.GP, gp_t)
        for i in range(4):
            for s4 in range(NSC):
                bank = pb[2 + (s4 % 2)]
                mmt = P.inc("tensor", lambda t, i=i, s4=s4, bank=bank:
                            t.matmul(bank[:, :], onesr[:, :], w_r[i][0:1, s4 * 512:(s4 + 1) * 512],
                                     start=True, stop=True), sems.MM)
                P.wait("vector", sems.MM, mmt)
                t = inc_chain(P, V, lambda v, i=i, s4=s4, bank=bank:
                              v.tensor_copy(B4[:, i * SH + s4 * 512: i * SH + (s4 + 1) * 512], bank[:, :]), sems.VE)
                P.wait("tensor", sems.VE, t)
        P.flush()


def _p2_units(nc, P, sems, io, pb, X1b, B4, RES, which=None):
    ph = ExitStack()
    with ph:
        HT = ph.enter_context(nc.sbuf_tensor("p2_ht", [128, 32 * 1024], BF16))
        W1S = [ph.enter_context(nc.sbuf_tensor(f"p2_w1s{k}", [128, 8 * 256], BF16)) for k in range(2)]
        W2S = [ph.enter_context(nc.sbuf_tensor(f"p2_w2s{k}", [128, 32 * 128], BF16)) for k in range(2)]
        X1R = [ph.enter_context(nc.sbuf_tensor("p2_x1r0", [128, 512], F32))]
        B1C = ph.enter_context(nc.sbuf_tensor("p2_b1c", [128, 6 * 32], F32))
        B2C = ph.enter_context(nc.sbuf_tensor("p2_b2c", [128, 6 * 8], F32))
        TMP = ph.enter_context(nc.sbuf_tensor("p2_tmp", [128, 512], F32))
        TMPB = ph.enter_context(nc.sbuf_tensor("p2_tmpb", [128, 512], BF16))

        lb = sems.new16("p2b")
        P.inc("sync", lambda sy: sy.dma_start(out=B1C[:, 0:64], in_=io["sb1c"][:, :]), lb)
        P.inc("sync", lambda sy: sy.dma_start(out=B1C[:, 64:192], in_=io["rb1c"][:, :]), lb)
        P.inc("sync", lambda sy: sy.dma_start(out=B2C[:, 0:16], in_=io["sb2c"][:, :]), lb)
        P.inc("sync", lambda sy: sy.dma_start(out=B2C[:, 16:48], in_=io["rb2c"][:, :]), lb)
        P.wait("scalar", lb, lb.n)
        P.wait("vector", lb, lb.n)

        units = []
        for n in range(2):
            units.append((io["sw1f"], n * 1024, io["sw2b"], n * 1024, n * 32, n * 8, "s", n))
        for e in range(4):
            units.append((io["rw1f"], e * 1024, io["rw2b"], e * 1024, 64 + e * 32, 16 + e * 8, "r", e))

        sW1 = [sems.new16(f"p2w1{k}") for k in range(2)]
        sW2 = [sems.new16(f"p2w2{k}") for k in range(2)]
        sXR = [sems.new16("p2xr0")]
        w1_used = {}
        w2_used = {}
        xr_used = {}
        w1_ld = {}
        w2_ld = {}
        xr_ld = {}

        def load_w1(idx, wdram, wrow, fs):
            k = idx % 2
            prev = w1_used.get(idx - 2)
            if prev is not None:
                P.wait("gpsimd", sems.MM, prev)
            for c in range(8):
                P.inc("gpsimd", lambda g, c=c, k=k, wdram=wdram, wrow=wrow, fs=fs:
                      g.dma_start(out=W1S[k][:, c * 256:(c + 1) * 256],
                                  in_=wdram[wrow + c * 128: wrow + (c + 1) * 128, fs * 256:(fs + 1) * 256]), sW1[k])
            w1_ld[idx] = (sW1[k], sW1[k].n)

        def load_w2(idx, wdram, wrow, o):
            k = idx % 2
            prev = w2_used.get(idx - 2)
            if prev is not None:
                P.wait("gpsimd", sems.MM, prev)
            P.inc("gpsimd", lambda g, k=k, wdram=wdram, wrow=wrow, o=o:
                  g.dma_start(out=W2S[k][:, :],
                              in_=wdram[(wrow + o * 128):(wrow + (o + 1) * 128), :]), sW2[k])
            w2_ld[idx] = (sW2[k], sW2[k].n)

        def load_xr(idx, p2, o, s4):
            prev = xr_used.get(idx - 1)
            if prev is not None:
                P.wait("sync", sems.VE, prev)
            P.inc("sync", lambda sy, o=o, p2=p2, s4=s4:
                  sy.dma_start(out=X1R[0][:, :],
                               in_=io["x1f"][o * 128:(o + 1) * 128,
                                             p2 * 1024 + s4 * 512: p2 * 1024 + (s4 + 1) * 512]), sXR[0])
            xr_ld[idx] = (sXR[0], sXR[0].n)

        bank_ticks = {}
        for p2 in range(2):
            for ui, (w1d, w1r, w2d, w2r, b1b, b2b, kind, eidx) in enumerate(units):
                if which is not None and ui not in which:
                    continue
                gi = p2 * 6 + ui
                w1idx0 = gi * 16
                for fs in range(16):
                    load_w1(w1idx0 + fs, w1d, w1r, fs)
                    P.wait("tensor", w1_ld[w1idx0 + fs][0], w1_ld[w1idx0 + fs][1])
                    for fc2 in range(2):
                        fc = fs * 2 + fc2
                        for s4 in range(2):
                            bi = (fc * 2 + s4) % 2
                            bank = pb[bi]
                            prev = bank_ticks.get(id(bank))
                            if prev is not None:
                                P.wait("tensor", prev[0], prev[1])
                            pairs = [(W1S[(w1idx0 + fs) % 2][:, c * 256 + fc2 * 128: c * 256 + (fc2 + 1) * 128],
                                      X1b[c][:, p2 * 1024 + s4 * 512: p2 * 1024 + (s4 + 1) * 512])
                                     for c in range(8)]
                            mmt = mm_group(P, sems, bank[:, :], pairs)
                            P.wait("scalar", sems.MM, mmt)
                            et = P.inc("scalar",
                                       lambda sc, fc=fc, s4=s4, bank=bank, b1b=b1b:
                                       sc.activation(HT[:, fc * 1024 + s4 * 512: fc * 1024 + (s4 + 1) * 512],
                                                     bank[:, :], AF.Relu, bias=B1C[:, b1b + fc: b1b + fc + 1],
                                                     scale=1.0), sems.SE)
                            bank_ticks[id(bank)] = (sems.SE, et)
                    w1_used[w1idx0 + fs] = sems.MM.n
                P.wait("tensor", sems.SE, sems.SE.n)
                if gi == 0 and "dbg_ht" in io:
                    dbg_s = sems.new16("dbght")
                    P.wait("gpsimd", sems.SE, sems.SE.n)
                    P.inc("gpsimd", lambda g: g.dma_start(out=io["dbg_ht"], in_=HT[:, :]), dbg_s)
                    P.wait("gpsimd", dbg_s, dbg_s.n)
                w2idx0 = gi * 8
                load_w2(w2idx0, w2d, w2r, 0)
                for o in range(8):
                    if o < 7:
                        load_w2(w2idx0 + o + 1, w2d, w2r, o + 1)
                    P.wait("tensor", w2_ld[w2idx0 + o][0], w2_ld[w2idx0 + o][1])
                    for s4 in range(2):
                        if kind == "s" and eidx == 0:
                            load_xr((p2 * 8 + o) * 2 + s4, p2, o, s4)
                        bi = 2 + s4 % 2
                        bank = pb[bi]
                        prev = bank_ticks.get(id(bank))
                        if prev is not None:
                            P.wait("tensor", prev[0], prev[1])
                        pairs = [(W2S[(w2idx0 + o) % 2][:, fc * 128:(fc + 1) * 128],
                                  HT[:, fc * 1024 + s4 * 512: fc * 1024 + (s4 + 1) * 512])
                                 for fc in range(32)]
                        mmt = mm_group(P, sems, bank[:, :], pairs)
                        P.wait("vector", sems.MM, mmt)
                        dres = RES[o][:, p2 * 1024 + s4 * 512: p2 * 1024 + (s4 + 1) * 512]
                        b2col = B2C[:, b2b + o: b2b + o + 1]
                        if kind == "s":
                            inc_chain(P, "vector",
                                      lambda v, bank=bank, b2col=b2col:
                                      v.tensor_scalar(TMP[:, :], bank[:, :], b2col, 0.5, ALU.add, ALU.mult),
                                      sems.VE)
                            if eidx == 0:
                                xi = (p2 * 8 + o) * 2 + s4
                                P.wait("vector", xr_ld[xi][0], xr_ld[xi][1])
                                t = inc_chain(P, "vector",
                                              lambda v, dres=dres:
                                              v.tensor_tensor(dres, TMP[:, :], X1R[0][:, :], ALU.add),
                                              sems.VE)
                                xr_used[xi] = sems.VE.n
                            else:
                                t = inc_chain(P, "vector",
                                              lambda v, dres=dres: v.tensor_tensor(dres, TMP[:, :], dres, ALU.add),
                                              sems.VE)
                        else:
                            inc_chain(P, "vector",
                                      lambda v, bank=bank, b2col=b2col:
                                      v.tensor_scalar_add(TMPB[:, :], bank[:, :], b2col), sems.VE)
                            inc_chain(P, "vector",
                                      lambda v, eidx=eidx, p2=p2, s4=s4:
                                      v.tensor_tensor(TMP[:, :], TMPB[:, :],
                                                      B4[:, eidx * SH + p2 * 1024 + s4 * 512: eidx * SH + p2 * 1024 + (s4 + 1) * 512],
                                                      ALU.mult), sems.VE)
                            t = inc_chain(P, "vector",
                                          lambda v, dres=dres: v.tensor_tensor(dres, TMP[:, :], dres, ALU.add),
                                          sems.VE)
                        bank_ticks[id(bank)] = (sems.VE, t)
                    w2_used[w2idx0 + o] = sems.MM.n
        P.flush()



def build_p4a(nc, P, sems, io):
    ph0 = ExitStack()
    with ph0:
        pb = [ph0.enter_context(nc.psum_tensor(f"p4_pb{i}", [128, 512], F32)) for i in range(8)]
        KR = ph0.enter_context(nc.sbuf_tensor("p4_kr", [128, 4 * S], BF16))
        VA = ph0.enter_context(nc.sbuf_tensor("p4_va", [128, 32 * 520], BF16))
        QT = ph0.enter_context(nc.sbuf_tensor("p4_qt", [128, 4 * S], BF16))

        ph = ExitStack()
        with ph:
            LT = ph.enter_context(nc.sbuf_tensor("p4_lt", [128, 2 * S], BF16))
            ph2 = ExitStack()
            with ph2:
                WKV = ph2.enter_context(nc.sbuf_tensor("p4_wkv", [128, 16 * 1024], BF16))
                WQH = ph2.enter_context(nc.sbuf_tensor("p4_wqh", [128, 8 * 512], BF16))
                WD = ph2.enter_context(nc.sbuf_tensor("p4_wd", [128, 8 * 256], BF16))
                XS = [ph2.enter_context(nc.sbuf_tensor(f"p4_xs{k}", [128, 8 * 512], BF16)) for k in range(2)]
                KVC = [ph2.enter_context(nc.sbuf_tensor(f"p4_kvc{k}", [128, 8 * 512], BF16)) for k in range(2)]

                lw = sems.new16("p4w")
                for i in range(8):
                    P.inc("gpsimd", lambda g, i=i: g.dma_start(out=WKV[:, i * 1024:(i + 1) * 1024],
                                                               in_=io["wk"][i * 128:(i + 1) * 128, :]), lw)
                    P.inc("gpsimd", lambda g, i=i: g.dma_start(out=WKV[:, (8 + i) * 1024:(9 + i) * 1024],
                                                               in_=io["wv"][i * 128:(i + 1) * 128, :]), lw)
                    P.inc("gpsimd", lambda g, i=i: g.dma_start(out=WQH[:, i * 512:(i + 1) * 512],
                                                               in_=io["wqh"][i * 128:(i + 1) * 128, :]), lw)
                    P.inc("gpsimd", lambda g, i=i: g.dma_start(out=WD[:, i * 256:(i + 1) * 256],
                                                               in_=io["wdown"][i * 128:(i + 1) * 128, :]), lw)
                P.wait("tensor", lw, lw.n)

                sXS = [sems.new16(f"p4xs{k}") for k in range(2)]
                xs_used, xs_ld = {}, {}

                def load_xs(si):
                    k = si % 2
                    prev = xs_used.get(si - 2)
                    if prev is not None:
                        P.wait("gpsimd", sems.MM, prev)
                    half, sl = divmod(si, 4)
                    for c in range(8):
                        P.inc("gpsimd", lambda g, c=c, k=k, half=half, sl=sl:
                              g.dma_start(out=XS[k][:, c * 512:(c + 1) * 512],
                                          in_=io["x2full"][half * 1024 + c * 128: half * 1024 + (c + 1) * 128,
                                                           sl * 512:(sl + 1) * 512]), sXS[k])
                    xs_ld[si] = (sXS[k], sXS[k].n)

                load_xs(0)
                bank_ticks = {}
                kvc_t = {}
                for si in range(8):
                    k = si % 2
                    P.wait("tensor", xs_ld[si][0], xs_ld[si][1])
                    if si < 7:
                        load_xs(si + 1)
                    for o in range(8):
                        bi = o % 2
                        bank = pb[bi]
                        prev = bank_ticks.get(id(bank))
                        if prev is not None:
                            P.wait("tensor", prev[0], prev[1])
                        pairs = [(WKV[:, c * 1024 + o * 128: c * 1024 + (o + 1) * 128],
                                  XS[k][:, c * 512:(c + 1) * 512]) for c in range(8)]
                        pairs += [(WKV[:, (8 + c) * 1024 + o * 128: (8 + c) * 1024 + (o + 1) * 128],
                                   XS[k][:, c * 512:(c + 1) * 512]) for c in range(8)]
                        mmt = mm_group(P, sems, bank[:, :], pairs)
                        bank_ticks[id(bank)] = scalar_copy_evac(
                            P, sems, KVC[k][:, o * 512:(o + 1) * 512], bank, mmt)
                    kvc_done = sems.SE.n
                    for o in range(4):
                        bi = o % 2
                        bank = pb[bi]
                        prev = bank_ticks.get(id(bank))
                        if prev is not None:
                            P.wait("tensor", prev[0], prev[1])
                        pairs = [(WQH[:, c * 512 + o * 128: c * 512 + (o + 1) * 128],
                                  XS[k][:, c * 512:(c + 1) * 512]) for c in range(8)]
                        mmt = mm_group(P, sems, bank[:, :], pairs)
                        bank_ticks[id(bank)] = scalar_copy_evac(
                            P, sems, QT[:, o * S + si * 512: o * S + (si + 1) * 512], bank, mmt)
                    xs_used[si] = sems.MM.n
                    P.wait("tensor", sems.SE, kvc_done)
                    prevk = kvc_t.get(k)
                    for o in range(2):
                        bi = o % 2
                        bank = pb[bi]
                        prev = bank_ticks.get(id(bank))
                        if prev is not None:
                            P.wait("tensor", prev[0], prev[1])
                        pairs = [(WD[:, c * 256 + o * 128: c * 256 + (o + 1) * 128],
                                  KVC[k][:, c * 512:(c + 1) * 512]) for c in range(8)]
                        mmt = mm_group(P, sems, bank[:, :], pairs)
                        bank_ticks[id(bank)] = scalar_copy_evac(
                            P, sems, LT[:, o * S + si * 512: o * S + (si + 1) * 512], bank, mmt)
                    kvc_t[k] = sems.MM.n
                P.flush()

            ph3 = ExitStack()
            with ph3:
                WUK = ph3.enter_context(nc.sbuf_tensor("p4_wuk", [128, 2 * 512], BF16))
                WUV = ph3.enter_context(nc.sbuf_tensor("p4_wuv", [128, 2 * 512], BF16))
                lw2 = sems.new16("p4u")
                for i in range(2):
                    P.inc("gpsimd", lambda g, i=i: g.dma_start(out=WUK[:, i * 512:(i + 1) * 512],
                                                               in_=io["wupkh"][i * 128:(i + 1) * 128, :]), lw2)
                    P.inc("gpsimd", lambda g, i=i: g.dma_start(out=WUV[:, i * 512:(i + 1) * 512],
                                                               in_=io["wupvh"][i * 128:(i + 1) * 128, :]), lw2)
                P.wait("tensor", lw2, lw2.n)
                bank_ticks = {}
                for o in range(4):
                    for sc_ in range(8):
                        bi = (o * 8 + sc_) % 2
                        bank = pb[bi]
                        prev = bank_ticks.get(id(bank))
                        if prev is not None:
                            P.wait("tensor", prev[0], prev[1])
                        pairs = [(WUK[:, lc * 512 + o * 128: lc * 512 + (o + 1) * 128],
                                  LT[:, lc * S + sc_ * 512: lc * S + (sc_ + 1) * 512]) for lc in range(2)]
                        mmt = mm_group(P, sems, bank[:, :], pairs)
                        bank_ticks[id(bank)] = scalar_copy_evac(
                            P, sems, KR[:, o * S + sc_ * 512: o * S + (sc_ + 1) * 512], bank, mmt)
                P.inc("gpsimd", lambda g: g.memset(VA[:, 64:32 * 520:65], 1.0), sems.GP)
                P.wait("vector", sems.GP, sems.GP.n)
                for kt in range(32):
                    bi = kt % 2
                    bank = pb[bi]
                    prev = bank_ticks.get(id(bank))
                    if prev is not None:
                        P.wait("tensor", prev[0], prev[1])
                    pairs = [(LT[:, lc * S + kt * 128: lc * S + (kt + 1) * 128],
                              WUV[:, lc * 512:(lc + 1) * 512]) for lc in range(2)]
                    mmt = mm_group(P, sems, bank[:, :], pairs)
                    P.wait("vector", sems.MM, mmt)
                    t = inc_chain(P, "vector",
                                  lambda v, kt=kt, bank=bank:
                                  v.tensor_copy(VA[:, kt * 520: kt * 520 + 520].rearrange("p (h c) -> p h c", h=8)[:, :, 0:64],
                                                bank[:, :].rearrange("p (h c) -> p h c", h=8)), sems.VE)
                    bank_ticks[id(bank)] = (sems.VE, t)
                P.wait("tensor", sems.VE, sems.VE.n)
                P.flush()

        _p4_attn(nc, P, sems, io, pb, KR, VA, QT)


def _p4_attn(nc, P, sems, io, pb, KR, VA, QT):
    ph = ExitStack()
    with ph:
        OT = ph.enter_context(nc.sbuf_tensor("p4_ot", [128, 4 * S], BF16))
        QPM = ph.enter_context(nc.sbuf_tensor("p4_qpm", [128, 2 * 4 * 512], BF16))
        AKQ = ph.enter_context(nc.sbuf_tensor("p4_akq", [128, 256], F32))
        DEC = ph.enter_context(nc.sbuf_tensor("p4_dec", [128, 4 * 512], F32))
        DQL = ph.enter_context(nc.sbuf_tensor("p4_dql", [1, 1024], F32))
        CB = ph.enter_context(nc.sbuf_tensor("p4_cb", [128, 1], F32))
        Ebuf = [ph.enter_context(nc.sbuf_tensor(f"p4_e{k}", [128, 512], BF16)) for k in range(3)]
        TMX = ph.enter_context(nc.sbuf_tensor("p4_tmx", [128, 512], F32))
        RC = ph.enter_context(nc.sbuf_tensor("p4_rc", [1, 512], F32))
        BC = ph.enter_context(nc.sbuf_tensor("p4_bc", [64, 512], F32))
        DQB = ph.enter_context(nc.sbuf_tensor("p4_dqb", [128, 2 * 512], BF16))
        onesr = ph.enter_context(nc.sbuf_tensor("p4_or", [1, 128], F32))

        lw = sems.new16("p4c")
        P.inc("sync", lambda sy: sy.dma_start(out=AKQ[:, :], in_=io["akq"][:, :]), lw)
        P.inc("sync", lambda sy: sy.dma_start(out=DEC[:, :], in_=io["dec4"][:, :]), lw)
        P.inc("sync", lambda sy: sy.dma_start(out=DQL[:, :], in_=io["dql"][:, :]), lw)
        P.inc("gpsimd", lambda g: g.memset(CB[:, :], -L1_C), sems.GP)
        P.inc("gpsimd", lambda g: g.memset(onesr[:, :], 1.0), sems.GP)
        P.wait("tensor", lw, lw.n)
        P.wait("tensor", sems.GP, sems.GP.n)
        P.wait("scalar", sems.GP, sems.GP.n)
        P.wait("vector", lw, lw.n)
        P.flush()

        sc_ticks = {}
        for qc in range(8):
            for sgn in range(2):
                bmt = P.inc("tensor", lambda t, sgn=sgn:
                            t.matmul(pb[6][:, :], onesr[:, :], DQL[0:1, sgn * 512:(sgn + 1) * 512],
                                     start=True, stop=True), sems.MM)
                P.wait("vector", sems.MM, bmt)
                ct = inc_chain(P, "vector", lambda v, sgn=sgn:
                               v.tensor_copy(DQB[:, sgn * 512:(sgn + 1) * 512], pb[6][:, :]), sems.VE)
                P.wait("tensor", sems.VE, ct)
            P.wait("vector", sems.SE, sems.SE.n)
            for ht in range(4):
                for sgn in range(2):
                    inc_chain(P, "vector",
                              lambda v, ht=ht, sgn=sgn, qc=qc:
                              v.tensor_tensor(QPM[:, (sgn * 4 + ht) * 512:(sgn * 4 + ht + 1) * 512],
                                              QT[:, ht * S + qc * 512: ht * S + (qc + 1) * 512],
                                              DQB[:, sgn * 512:(sgn + 1) * 512], ALU.mult), sems.VE)
            qpm_t = sems.VE.n
            P.wait("tensor", sems.VE, qpm_t)

            for grp in range(2):
                for kt in range(32):
                    cls = 0 if kt < 4 * qc else (2 if kt > 4 * qc + 3 else 1)
                    for hh in range(4):
                        h = grp * 4 + hh
                        ht, m = h // 2, h % 2
                        si = (kt * 4 + hh) % 2
                        sbank = pb[si]
                        prev = sc_ticks.get(si)
                        if prev is not None:
                            P.wait("tensor", prev[0], prev[1])
                        st = KR[64 * m:64 * m + 64, ht * S + kt * 128: ht * S + (kt + 1) * 128]
                        if cls == 1:
                            mv = QT[64 * m:64 * m + 64, ht * S + qc * 512: ht * S + (qc + 1) * 512]
                        else:
                            sgn = 0 if cls == 0 else 1
                            mv = QPM[64 * m:64 * m + 64, (sgn * 4 + ht) * 512:(sgn * 4 + ht + 1) * 512]
                        mmt = P.inc("tensor",
                                    lambda t, st=st, mv=mv, sbank=sbank, m=m:
                                    t.matmul(sbank[:, :], st, mv, start=True, stop=True,
                                             tile_position=(64 * m, 0)), sems.MM)
                        eb = Ebuf[(kt * 4 + hh) % 3]
                        acol = AKQ[:, qc * 32 + kt: qc * 32 + kt + 1]
                        if cls == 1:
                            P.wait("vector", sems.MM, mmt)
                            dct = inc_chain(P, "vector",
                                            lambda v, sbank=sbank, kt=kt, qc=qc:
                                            v.tensor_tensor(TMX[:, :], sbank[:, :],
                                                            DEC[:, (kt - 4 * qc) * 512:(kt - 4 * qc + 1) * 512],
                                                            ALU.mult), sems.VE)
                            P.wait("scalar", sems.VE, dct)
                            et = P.inc("scalar",
                                       lambda sc, eb=eb, acol=acol:
                                       sc.activation(eb[:, :], TMX[:, :], AF.Exp, bias=CB[:, :], scale=acol), sems.SE)
                            sc_ticks[si] = (sems.VE, dct)
                        else:
                            P.wait("scalar", sems.MM, mmt)
                            et = P.inc("scalar",
                                       lambda sc, eb=eb, sbank=sbank, acol=acol:
                                       sc.activation(eb[:, :], sbank[:, :], AF.Exp, bias=CB[:, :], scale=acol), sems.SE)
                            sc_ticks[si] = (sems.SE, et)
                        P.wait("tensor", sems.SE, et)
                        avb = pb[2 + hh]
                        stV = VA[:, kt * 520 + 65 * h: kt * 520 + 65 * h + 65]
                        first, last = kt == 0, kt == 31
                        if last:
                            avt = P.inc("tensor",
                                        lambda t, stV=stV, eb=eb, avb=avb, first=first:
                                        t.matmul(avb[0:65, :], stV, eb[:, :], start=first, stop=True), sems.MM)
                        else:
                            P.on("tensor",
                                 lambda t, stV=stV, eb=eb, avb=avb, first=first:
                                 t.matmul(avb[0:65, :], stV, eb[:, :], start=first, stop=False))
                P.wait("vector", sems.MM, avt)
                for hh in range(4):
                    h = grp * 4 + hh
                    avb = pb[2 + hh]
                    rct = inc_chain(P, "vector", lambda v, avb=avb: v.reciprocal(RC[:, :], avb[64:65, :]), sems.VE)
                    P.wait("tensor", sems.VE, rct)
                    bmt = P.inc("tensor", lambda t: t.matmul(pb[6][0:64, :], onesr[0:1, 0:64], RC[0:1, :],
                                                             start=True, stop=True), sems.MM)
                    P.wait("vector", sems.MM, bmt)
                    inc_chain(P, "vector", lambda v: v.tensor_copy(BC[:, :], pb[6][0:64, :]), sems.VE)
                    ht, m = h // 2, h % 2
                    t = inc_chain(P, "vector",
                                  lambda v, avb=avb, ht=ht, m=m, qc=qc:
                                  v.tensor_tensor(OT[64 * m:64 * m + 64, ht * S + qc * 512: ht * S + (qc + 1) * 512],
                                                  avb[0:64, :], BC[:, :], ALU.mult), sems.VE)
                P.wait("tensor", sems.VE, sems.VE.n)
        st0 = sems.new16("p4st")
        P.wait("gpsimd", sems.VE, sems.VE.n)
        for i in range(4):
            for hf in range(2):
                P.inc("gpsimd", lambda g, i=i, hf=hf:
                      g.dma_start(out=io["otf"][hf * 512 + i * 128: hf * 512 + (i + 1) * 128, :],
                                  in_=OT[:, i * S + hf * SH: i * S + (hf + 1) * SH]), st0)
        P.wait("gpsimd", st0, st0.n)
        P.flush()



def build_p42(nc, P, sems, io):
    ph0 = ExitStack()
    with ph0:
        pb = [ph0.enter_context(nc.psum_tensor(f"p42_pb{i}", [128, 512], F32)) for i in range(8)]
        OT = ph0.enter_context(nc.sbuf_tensor("p42_ot", [128, 8 * SH], BF16))
        OT0 = ph0.enter_context(nc.sbuf_tensor("p42_ot0", [128, 8], F32))
        Q0 = ph0.enter_context(nc.sbuf_tensor("p42_q0", [128, 8], BF16))
        phm = ExitStack()
        with phm:
            QT = phm.enter_context(nc.sbuf_tensor("p42_qt", [128, 8 * SH], BF16))
            LT = phm.enter_context(nc.sbuf_tensor("p42_lt", [128, 2 * S], BF16))
            _p42_main(nc, P, sems, io, pb, QT, OT, LT, OT0, Q0)
        _p42_proj(nc, P, sems, io, pb, OT, OT0)


def _p42_main(nc, P, sems, io, pb, QT, OT, LT, OT0, Q0):
        ph = ExitStack()
        with ph:
            WKV = ph.enter_context(nc.sbuf_tensor("p42_wkv", [128, 16 * 1024], BF16))
            WQ = ph.enter_context(nc.sbuf_tensor("p42_wq", [128, 8 * 1024], BF16))
            WD = ph.enter_context(nc.sbuf_tensor("p42_wd", [128, 8 * 256], BF16))
            XS = [ph.enter_context(nc.sbuf_tensor(f"p42_xs{k}", [128, 8 * 512], BF16)) for k in range(2)]
            KVC = [ph.enter_context(nc.sbuf_tensor(f"p42_kvc{k}", [128, 8 * 512], BF16)) for k in range(2)]
            X20B = ph.enter_context(nc.sbuf_tensor("p42_x20", [128, 8], BF16))

            lw = sems.new16("p42w")
            for i in range(8):
                P.inc("gpsimd", lambda g, i=i: g.dma_start(out=WKV[:, i * 1024:(i + 1) * 1024],
                                                           in_=io["wk"][i * 128:(i + 1) * 128, :]), lw)
                P.inc("gpsimd", lambda g, i=i: g.dma_start(out=WKV[:, (8 + i) * 1024:(9 + i) * 1024],
                                                           in_=io["wv"][i * 128:(i + 1) * 128, :]), lw)
                P.inc("gpsimd", lambda g, i=i: g.dma_start(out=WQ[:, i * 1024:(i + 1) * 1024],
                                                           in_=io["wq"][i * 128:(i + 1) * 128, :]), lw)
                P.inc("gpsimd", lambda g, i=i: g.dma_start(out=WD[:, i * 256:(i + 1) * 256],
                                                           in_=io["wdown"][i * 128:(i + 1) * 128, :]), lw)
                P.inc("gpsimd", lambda g, i=i: g.dma_start(out=X20B[:, i:i + 1],
                                                           in_=io["x20"][i * 128:(i + 1) * 128, :]), lw)
            P.wait("tensor", lw, lw.n)

            sXS = [sems.new16(f"p42x{k}") for k in range(2)]
            xs_used, xs_ld = {}, {}

            def load_xs(si):
                k = si % 2
                prev = xs_used.get(si - 2)
                if prev is not None:
                    P.wait("gpsimd", sems.MM, prev)
                for c in range(8):
                    P.inc("gpsimd", lambda g, c=c, k=k, si=si:
                          g.dma_start(out=XS[k][:, c * 512:(c + 1) * 512],
                                      in_=io["x2full"][c * 128:(c + 1) * 128, si * 512:(si + 1) * 512]), sXS[k])
                xs_ld[si] = (sXS[k], sXS[k].n)

            load_xs(0)
            bank_ticks = {}
            for si in range(8):
                k = si % 2
                P.wait("tensor", xs_ld[si][0], xs_ld[si][1])
                if si < 7:
                    load_xs(si + 1)
                for o in range(8):
                    bi = o % 2
                    bank = pb[bi]
                    prev = bank_ticks.get(id(bank))
                    if prev is not None:
                        P.wait("tensor", prev[0], prev[1])
                    pairs = [(WKV[:, c * 1024 + o * 128: c * 1024 + (o + 1) * 128],
                              XS[k][:, c * 512:(c + 1) * 512]) for c in range(8)]
                    pairs += [(WKV[:, (8 + c) * 1024 + o * 128: (8 + c) * 1024 + (o + 1) * 128],
                               XS[k][:, c * 512:(c + 1) * 512]) for c in range(8)]
                    mmt = mm_group(P, sems, bank[:, :], pairs)
                    bank_ticks[id(bank)] = scalar_copy_evac(
                        P, sems, KVC[k][:, o * 512:(o + 1) * 512], bank, mmt)
                kvc_done = sems.SE.n
                if si < 4:
                    for o in range(8):
                        bi = o % 2
                        bank = pb[bi]
                        prev = bank_ticks.get(id(bank))
                        if prev is not None:
                            P.wait("tensor", prev[0], prev[1])
                        pairs = [(WQ[:, c * 1024 + o * 128: c * 1024 + (o + 1) * 128],
                                  XS[k][:, c * 512:(c + 1) * 512]) for c in range(8)]
                        mmt = mm_group(P, sems, bank[:, :], pairs)
                        bank_ticks[id(bank)] = scalar_copy_evac(
                            P, sems, QT[:, o * SH + si * 512: o * SH + (si + 1) * 512], bank, mmt)
                xs_used[si] = sems.MM.n
                P.wait("tensor", sems.SE, kvc_done)
                for o in range(2):
                    bi = o % 2
                    bank = pb[bi]
                    prev = bank_ticks.get(id(bank))
                    if prev is not None:
                        P.wait("tensor", prev[0], prev[1])
                    pairs = [(WD[:, c * 256 + o * 128: c * 256 + (o + 1) * 128],
                              KVC[k][:, c * 512:(c + 1) * 512]) for c in range(8)]
                    mmt = mm_group(P, sems, bank[:, :], pairs)
                    bank_ticks[id(bank)] = scalar_copy_evac(
                        P, sems, LT[:, o * S + si * 512: o * S + (si + 1) * 512], bank, mmt)
            for o in range(8):
                bi = o % 2
                bank = pb[bi]
                prev = bank_ticks.get(id(bank))
                if prev is not None:
                    P.wait("tensor", prev[0], prev[1])
                pairs = [(WQ[:, c * 1024 + o * 128: c * 1024 + (o + 1) * 128],
                          X20B[:, c:c + 1]) for c in range(8)]
                mmt = mm_group(P, sems, bank[:, 0:1], pairs)
                P.wait("scalar", sems.MM, mmt)
                t = P.inc("scalar", lambda sc, o=o, bank=bank: sc.copy(Q0[:, o:o + 1], bank[:, 0:1]), sems.SE)
                bank_ticks[id(bank)] = (sems.SE, t)
            P.flush()

        for hp in range(2):
            _p42_headhalf(nc, P, sems, io, pb, QT, OT, LT, OT0, Q0, hp)


def _p42_headhalf(nc, P, sems, io, pb, QT, OT, LT, OT0, Q0, hp):
    ph = ExitStack()
    with ph:
        KR = ph.enter_context(nc.sbuf_tensor(f"p42_kr{hp}", [128, 4 * S], BF16))
        VA = ph.enter_context(nc.sbuf_tensor(f"p42_va{hp}", [128, 32 * 520], BF16))
        WUK = ph.enter_context(nc.sbuf_tensor(f"p42_wuk{hp}", [128, 2 * 512], BF16))
        WUV = ph.enter_context(nc.sbuf_tensor(f"p42_wuv{hp}", [128, 2 * 512], BF16))
        QPM = ph.enter_context(nc.sbuf_tensor(f"p42_qpm{hp}", [128, 2 * 4 * 512], BF16))
        AKQ = ph.enter_context(nc.sbuf_tensor(f"p42_akq{hp}", [128, 128], F32))
        DEC = ph.enter_context(nc.sbuf_tensor(f"p42_dec{hp}", [128, 4 * 512], F32))
        AK0 = ph.enter_context(nc.sbuf_tensor(f"p42_ak0{hp}", [128, 32], F32))
        DQL = ph.enter_context(nc.sbuf_tensor(f"p42_dql{hp}", [1, 1024], F32))
        CB = ph.enter_context(nc.sbuf_tensor(f"p42_cb{hp}", [128, 1], F32))
        Ebuf = [ph.enter_context(nc.sbuf_tensor(f"p42_e{hp}_{k}", [128, 512], BF16)) for k in range(3)]
        E0 = ph.enter_context(nc.sbuf_tensor(f"p42_e0{hp}", [128, 1], BF16))
        TMX2 = [ph.enter_context(nc.sbuf_tensor(f"p42_tmx{hp}_{k}", [128, 512], F32)) for k in range(2)]
        RC = ph.enter_context(nc.sbuf_tensor(f"p42_rc{hp}", [1, 512], F32))
        BC = ph.enter_context(nc.sbuf_tensor(f"p42_bc{hp}", [64, 512], F32))
        DQB = ph.enter_context(nc.sbuf_tensor(f"p42_dqb{hp}", [128, 2 * 512], BF16))
        onesr = ph.enter_context(nc.sbuf_tensor(f"p42_or{hp}", [1, 128], F32))

        lw = sems.new16(f"p42h{hp}")
        for i in range(2):
            P.inc("gpsimd", lambda g, i=i, hp=hp: g.dma_start(out=WUK[:, i * 512:(i + 1) * 512],
                                                              in_=io["wupk"][i * 128:(i + 1) * 128, hp * 512:(hp + 1) * 512]), lw)
            P.inc("gpsimd", lambda g, i=i, hp=hp: g.dma_start(out=WUV[:, i * 512:(i + 1) * 512],
                                                              in_=io["wupv"][i * 128:(i + 1) * 128, hp * 512:(hp + 1) * 512]), lw)
        lc = sems.new16(f"p42c{hp}")
        P.inc("sync", lambda sy: sy.dma_start(out=AKQ[:, :], in_=io["akq4"][:, :]), lc)
        P.inc("sync", lambda sy: sy.dma_start(out=DEC[:, :], in_=io["dec4"][:, :]), lc)
        P.inc("sync", lambda sy: sy.dma_start(out=DQL[:, :], in_=io["dql"][:, :]), lc)
        P.inc("sync", lambda sy: sy.dma_start(out=AK0[:, :], in_=io["ak0"][:, :]), lc)
        P.inc("gpsimd", lambda g: g.memset(CB[:, :], -L1_C), sems.GP)
        P.inc("gpsimd", lambda g: g.memset(onesr[:, :], 1.0), sems.GP)
        P.wait("tensor", lw, lw.n)
        P.wait("tensor", lc, lc.n)
        P.wait("tensor", sems.GP, sems.GP.n)
        P.wait("scalar", sems.GP, sems.GP.n)
        P.wait("vector", lc, lc.n)

        bank_ticks = {}
        for o in range(4):
            for sc_ in range(8):
                bi = (o * 8 + sc_) % 2
                bank = pb[bi]
                prev = bank_ticks.get(id(bank))
                if prev is not None:
                    P.wait("tensor", prev[0], prev[1])
                pairs = [(WUK[:, lc2 * 512 + o * 128: lc2 * 512 + (o + 1) * 128],
                          LT[:, lc2 * S + sc_ * 512: lc2 * S + (sc_ + 1) * 512]) for lc2 in range(2)]
                mmt = mm_group(P, sems, bank[:, :], pairs)
                bank_ticks[id(bank)] = scalar_copy_evac(
                    P, sems, KR[:, o * S + sc_ * 512: o * S + (sc_ + 1) * 512], bank, mmt)
        P.inc("gpsimd", lambda g: g.memset(VA[:, 64:32 * 520:65], 1.0), sems.GP)
        P.wait("vector", sems.GP, sems.GP.n)
        for kt in range(32):
            bi = kt % 2
            bank = pb[bi]
            prev = bank_ticks.get(id(bank))
            if prev is not None:
                P.wait("tensor", prev[0], prev[1])
            pairs = [(LT[:, lc2 * S + kt * 128: lc2 * S + (kt + 1) * 128],
                      WUV[:, lc2 * 512:(lc2 + 1) * 512]) for lc2 in range(2)]
            mmt = mm_group(P, sems, bank[:, :], pairs)
            P.wait("vector", sems.MM, mmt)
            t = inc_chain(P, "vector",
                          lambda v, kt=kt, bank=bank:
                          v.tensor_copy(VA[:, kt * 520: kt * 520 + 520].rearrange("p (h c) -> p h c", h=8)[:, :, 0:64],
                                        bank[:, :].rearrange("p (h c) -> p h c", h=8)), sems.VE)
            bank_ticks[id(bank)] = (sems.VE, t)
        P.flush()

        sc_ticks = {}
        prev_group_evac = None
        for qc in range(4):
            if prev_group_evac is not None:
                P.wait("tensor", sems.VE, prev_group_evac)
            for sgn in range(2):
                bmt = P.inc("tensor", lambda t, sgn=sgn:
                            t.matmul(pb[6][:, :], onesr[:, :], DQL[0:1, sgn * 512:(sgn + 1) * 512],
                                     start=True, stop=True), sems.MM)
                P.wait("vector", sems.MM, bmt)
                ct = inc_chain(P, "vector", lambda v, sgn=sgn:
                               v.tensor_copy(DQB[:, sgn * 512:(sgn + 1) * 512], pb[6][:, :]), sems.VE)
                P.wait("tensor", sems.VE, ct)
            for ht in range(4):
                for sgn in range(2):
                    inc_chain(P, "vector",
                              lambda v, ht=ht, sgn=sgn, qc=qc:
                              v.tensor_tensor(QPM[:, (sgn * 4 + ht) * 512:(sgn * 4 + ht + 1) * 512],
                                              QT[:, (4 * hp + ht) * SH + qc * 512: (4 * hp + ht) * SH + (qc + 1) * 512],
                                              DQB[:, sgn * 512:(sgn + 1) * 512], ALU.mult), sems.VE)
            qpm_t = sems.VE.n
            P.wait("tensor", sems.VE, qpm_t)

            for grp in range(2):
                if prev_group_evac is not None:
                    P.wait("tensor", sems.VE, prev_group_evac)
                tiles = [(kt, hh) for kt in range(32) for hh in range(4)]
                ND = 3
                sbanks = [pb[0], pb[1], pb[7]]
                et_q = {}
                mixed_prev = None

                def emit_av(t):
                    kt, hh = tiles[t]
                    eb, et = et_q.pop(t)
                    h = grp * 4 + hh
                    avb = pb[2 + hh]
                    stV = VA[:, kt * 520 + 65 * h: kt * 520 + 65 * h + 65]
                    first, last = kt == 0, kt == 31
                    P.wait("tensor", sems.SE, et)
                    if last and hh == 3:
                        return P.inc("tensor",
                                     lambda tn, stV=stV, eb=eb, avb=avb, first=first:
                                     tn.matmul(avb[0:65, :], stV, eb[:, :], start=first, stop=True), sems.MM)
                    P.on("tensor",
                         lambda tn, stV=stV, eb=eb, avb=avb, first=first, last=last:
                         tn.matmul(avb[0:65, :], stV, eb[:, :], start=first, stop=last))
                    return None

                avt = None
                for t, (kt, hh) in enumerate(tiles):
                    cls = 0 if kt < 4 * qc else (2 if kt > 4 * qc + 3 else 1)
                    h = grp * 4 + hh
                    ht, m = h // 2, h % 2
                    si_ = t % ND
                    sbank = sbanks[si_]
                    prev = sc_ticks.get(si_)
                    if prev is not None:
                        P.wait("tensor", prev[0], prev[1])
                    st = KR[64 * m:64 * m + 64, ht * S + kt * 128: ht * S + (kt + 1) * 128]
                    if cls == 1:
                        mv = QT[64 * m:64 * m + 64, (4 * hp + ht) * SH + qc * 512: (4 * hp + ht) * SH + (qc + 1) * 512]
                    else:
                        sgn = 0 if cls == 0 else 1
                        mv = QPM[64 * m:64 * m + 64, (sgn * 4 + ht) * 512:(sgn * 4 + ht + 1) * 512]
                    mmt = P.inc("tensor",
                                lambda tn, st=st, mv=mv, sbank=sbank, m=m:
                                tn.matmul(sbank[:, :], st, mv, start=True, stop=True,
                                          tile_position=(64 * m, 0)), sems.MM)
                    eb = Ebuf[t % 3]
                    acol = AKQ[:, qc * 32 + kt: qc * 32 + kt + 1]
                    if cls == 1:
                        tmx = TMX2[t % 2]
                        P.wait("vector", sems.MM, mmt)
                        if mixed_prev is not None:
                            P.wait("vector", sems.SE, mixed_prev)
                        dct = P.inc("vector",
                                    lambda v, sbank=sbank, kt=kt, qc=qc, tmx=tmx:
                                    v.tensor_tensor(tmx[:, :], sbank[:, :],
                                                    DEC[:, (kt - 4 * qc) * 512:(kt - 4 * qc + 1) * 512],
                                                    ALU.mult), sems.VE)
                        P.wait("scalar", sems.VE, dct)
                        et = P.inc("scalar",
                                   lambda sc, eb=eb, acol=acol, tmx=tmx:
                                   sc.activation(eb[:, :], tmx[:, :], AF.Exp, bias=CB[:, :], scale=acol), sems.SE)
                        mixed_prev = et
                        sc_ticks[si_] = (sems.VE, dct)
                    else:
                        P.wait("scalar", sems.MM, mmt)
                        et = P.inc("scalar",
                                   lambda sc, eb=eb, sbank=sbank, acol=acol:
                                   sc.activation(eb[:, :], sbank[:, :], AF.Exp, bias=CB[:, :], scale=acol), sems.SE)
                        sc_ticks[si_] = (sems.SE, et)
                    et_q[t] = (eb, et)
                    if t >= 1:
                        r = emit_av(t - 1)
                        if r is not None:
                            avt = r
                r = emit_av(len(tiles) - 1)
                if r is not None:
                    avt = r
                P.wait("vector", sems.MM, avt)
                for hh in range(4):
                    h = grp * 4 + hh
                    hg = 8 * hp + h
                    avb = pb[2 + hh]
                    rct = inc_chain(P, "vector", lambda v, avb=avb: v.reciprocal(RC[:, :], avb[64:65, :]), sems.VE)
                    P.wait("tensor", sems.VE, rct)
                    bmt = P.inc("tensor", lambda t: t.matmul(pb[6][0:64, :], onesr[0:1, 0:64], RC[0:1, :],
                                                             start=True, stop=True), sems.MM)
                    P.wait("vector", sems.MM, bmt)
                    inc_chain(P, "vector", lambda v: v.tensor_copy(BC[:, :], pb[6][0:64, :]), sems.VE)
                    ht2, m2 = hg // 2, hg % 2
                    t = inc_chain(P, "vector",
                                  lambda v, avb=avb, ht2=ht2, m2=m2, qc=qc:
                                  v.tensor_tensor(OT[64 * m2:64 * m2 + 64, ht2 * SH + qc * 512: ht2 * SH + (qc + 1) * 512],
                                                  avb[0:64, :], BC[:, :], ALU.mult), sems.VE)
                prev_group_evac = sems.VE.n
        P.wait("tensor", sems.VE, sems.VE.n)
        for h in range(8):
            hg = 8 * hp + h
            ht, m = h // 2, h % 2
            av0 = pb[4]
            for kt in range(32):
                st = KR[64 * m:64 * m + 64, ht * S + kt * 128: ht * S + (kt + 1) * 128]
                mv = Q0[64 * m:64 * m + 64, (4 * hp + ht):(4 * hp + ht) + 1]
                sb0 = pb[kt % 2]
                mmt = P.inc("tensor", lambda t, st=st, mv=mv, m=m, sb0=sb0:
                            t.matmul(sb0[:, 0:1], st, mv, start=True, stop=True,
                                     tile_position=(64 * m, 0)), sems.MM)
                P.wait("scalar", sems.MM, mmt)
                a0 = AK0[:, kt:kt + 1]
                et = P.inc("scalar", lambda sc, a0=a0, sb0=sb0:
                           sc.activation(E0[:, :], sb0[:, 0:1], AF.Exp, bias=CB[:, :], scale=a0), sems.SE)
                P.wait("tensor", sems.SE, et)
                stV = VA[:, kt * 520 + 65 * h: kt * 520 + 65 * h + 65]
                first, last = kt == 0, kt == 31
                if last:
                    avt = P.inc("tensor", lambda t, stV=stV, first=first:
                                t.matmul(av0[0:65, 0:1], stV, E0[:, :], start=first, stop=True), sems.MM)
                else:
                    P.on("tensor", lambda t, stV=stV, first=first:
                         t.matmul(av0[0:65, 0:1], stV, E0[:, :], start=first, stop=False))
            P.wait("vector", sems.MM, avt)
            rct = inc_chain(P, "vector", lambda v: v.reciprocal(RC[0:1, 0:1], av0[64:65, 0:1]), sems.VE)
            P.wait("tensor", sems.VE, rct)
            bmt = P.inc("tensor", lambda t: t.matmul(pb[6][0:64, 0:1], onesr[0:1, 0:64], RC[0:1, 0:1],
                                                     start=True, stop=True), sems.MM)
            P.wait("vector", sems.MM, bmt)
            inc_chain(P, "vector", lambda v: v.tensor_copy(BC[:, 0:1], pb[6][0:64, 0:1]), sems.VE)
            t = inc_chain(P, "vector",
                          lambda v, m=m, hg=hg:
                          v.tensor_tensor(OT0[64 * m:64 * m + 64, (hg // 2):(hg // 2) + 1],
                                          av0[0:64, 0:1], BC[:, 0:1], ALU.mult), sems.VE)
            P.wait("tensor", sems.VE, t)
        P.flush()


def _p42_proj(nc, P, sems, io, pb, OT, OT0):
    ph2 = ExitStack()
    with ph2:
        RES = ph2.enter_context(nc.sbuf_tensor("p42_res", [128, 8 * SH], F32))
        RESL = [RES[:, i * SH:(i + 1) * SH] for i in range(8)]
        ph = ExitStack()
        with ph:
            WO = ph.enter_context(nc.sbuf_tensor("p42_wo", [128, 8 * 1024], BF16))
            XR = ph.enter_context(nc.sbuf_tensor("p42_xr2", [128, 8 * SH], F32))
            lw = sems.new16("p42o")
            lx = sems.new16("p42r")
            for i in range(8):
                P.inc("gpsimd", lambda g, i=i: g.dma_start(out=WO[:, i * 1024:(i + 1) * 1024],
                                                           in_=io["wo"][i * 128:(i + 1) * 128, :]), lw)
                P.inc("sync", lambda sy, i=i: sy.dma_start(out=XR[:, i * SH:(i + 1) * SH],
                                                           in_=io["x2full"][i * 128:(i + 1) * 128, 0:SH]), lx)
            P.wait("tensor", lw, lw.n)
            P.wait("vector", lx, lx.n)
            bank_ticks = {}

            def evac(o, s4, bank, mmt):
                P.wait("vector", sems.MM, mmt)
                t = P.inc("vector",
                          lambda v, o=o, s4=s4, bank=bank:
                          v.tensor_add(RESL[o][:, s4 * 512:(s4 + 1) * 512], bank[:, :],
                                       XR[:, o * SH + s4 * 512: o * SH + (s4 + 1) * 512]), sems.VE)
                return (sems.VE, t)

            emit_proj(P, sems, nO=8, nC=8, nSc=4,
                      st_ap=lambda c, o: WO[:, c * 1024 + o * 128: c * 1024 + (o + 1) * 128],
                      mv_ap=lambda c, s4: OT[:, c * SH + s4 * 512: c * SH + (s4 + 1) * 512],
                      banks=pb[0:4], bank_ticks=bank_ticks, evac=evac)
            _p42_tok0proj(nc, P, sems, io, pb, WO, OT0, ph)
            P.flush()
        emit_ln(nc, P, sems, "p42ln", RESL, io["x3f"], pb)


def _p42_tok0proj(nc, P, sems, io, pb, WO, OT0, ph):
    X20 = ph.enter_context(nc.sbuf_tensor("t2_x2", [128, 8], F32))
    OT0B = ph.enter_context(nc.sbuf_tensor("t2_ob", [128, 8], BF16))
    R0 = ph.enter_context(nc.sbuf_tensor("t2_r", [128, 8], F32))
    RW0 = ph.enter_context(nc.sbuf_tensor("t2_rw", [1, 16], F32))
    OC0 = ph.enter_context(nc.sbuf_tensor("t2_oc", [128, 1], BF16))
    MC0 = ph.enter_context(nc.sbuf_tensor("t2_mc", [128, 2], F32))
    EPS0 = ph.enter_context(nc.sbuf_tensor("t2_ep", [1, 1], F32))
    Z0 = ph.enter_context(nc.sbuf_tensor("t2_z", [1, 1], F32))
    onesr0 = ph.enter_context(nc.sbuf_tensor("t2_or", [1, 128], F32))
    lz2 = sems.new16("t2m")
    for c in range(8):
        P.inc("sync", lambda sy, c=c: sy.dma_start(out=X20[:, c:c + 1], in_=io["x20"][c * 128:(c + 1) * 128, :]), lz2)
    P.inc("gpsimd", lambda g: g.memset(OC0[:, :], 1.0), sems.GP)
    P.inc("gpsimd", lambda g: g.memset(EPS0[:, :], 1e-5), sems.GP)
    P.inc("gpsimd", lambda g: g.memset(Z0[:, :], 0.0), sems.GP)
    P.inc("gpsimd", lambda g: g.memset(onesr0[:, :], 1.0), sems.GP)
    P.wait("vector", lz2, lz2.n)
    P.wait("vector", sems.GP, sems.GP.n)
    cb_t = inc_chain(P, "vector", lambda v: v.tensor_copy(OT0B[:, :], OT0[:, :]), sems.VE)
    P.wait("tensor", sems.VE, cb_t)
    P.wait("tensor", sems.GP, sems.GP.n)
    P.wait("scalar", sems.GP, sems.GP.n)
    for o in range(8):
        for c in range(8):
            first = (o == 0 and c == 0)
            last = (o == 7 and c == 7)
            stp = WO[:, c * 1024 + o * 128: c * 1024 + (o + 1) * 128]
            if last:
                mmt = P.inc("tensor", lambda t, stp=stp, o=o, c=c, first=first:
                            t.matmul(pb[4][:, o:o + 1], stp, OT0B[:, c:c + 1], start=first, stop=True), sems.MM)
            else:
                P.on("tensor", lambda t, stp=stp, o=o, c=c, first=first:
                     t.matmul(pb[4][:, o:o + 1], stp, OT0B[:, c:c + 1], start=first, stop=False))
    P.wait("vector", sems.MM, mmt)
    t = inc_chain(P, "vector", lambda v: v.tensor_add(R0[:, :], pb[4][:, 0:8], X20[:, :]), sems.VE)
    P.wait("tensor", sems.VE, t)
    R0H = ph.enter_context(nc.sbuf_tensor("t2_rh", [128, 8], BF16))
    R0S = ph.enter_context(nc.sbuf_tensor("t2_rs", [128, 8], BF16))
    ct = inc_chain(P, "vector", lambda v: v.tensor_copy(R0H[:, :], R0[:, :]), sems.VE)
    P.wait("scalar", sems.VE, t)
    st_ = P.inc("scalar", lambda sc: sc.activation(R0S[:, :], R0[:, :], AF.Square), sems.SE)
    P.wait("tensor", sems.VE, ct)
    P.wait("tensor", sems.SE, st_)
    P.on("tensor", lambda t: t.matmul(pb[5][0:1, 0:8], OC0[:, :], R0H[:, :], start=True, stop=True))
    mmt2 = P.inc("tensor", lambda t: t.matmul(pb[5][0:1, 8:16], OC0[:, :], R0S[:, :], start=True, stop=True), sems.MM)
    P.wait("vector", sems.MM, mmt2)
    inc_chain(P, "vector", lambda v: v.tensor_copy(RW0[0:1, 0:16], pb[5][0:1, 0:16]), sems.VE)

    def rr(a, b_):
        return RW0[0:1, a:b_]
    inc_chain(P, "vector", lambda v: v.tensor_tensor(rr(0, 4), rr(0, 4), rr(4, 8), ALU.add), sems.VE)
    inc_chain(P, "vector", lambda v: v.tensor_tensor(rr(0, 2), rr(0, 2), rr(2, 4), ALU.add), sems.VE)
    inc_chain(P, "vector", lambda v: v.tensor_tensor(rr(0, 1), rr(0, 1), rr(1, 2), ALU.add), sems.VE)
    inc_chain(P, "vector", lambda v: v.tensor_tensor(rr(8, 12), rr(8, 12), rr(12, 16), ALU.add), sems.VE)
    inc_chain(P, "vector", lambda v: v.tensor_tensor(rr(8, 10), rr(8, 10), rr(10, 12), ALU.add), sems.VE)
    inc_chain(P, "vector", lambda v: v.tensor_tensor(rr(8, 9), rr(8, 9), rr(9, 10), ALU.add), sems.VE)
    inc_chain(P, "vector", lambda v: v.tensor_scalar_mul(rr(1, 2), rr(0, 1), 1.0 / 1024.0), sems.VE)
    inc_chain(P, "vector", lambda v: v.tensor_scalar_mul(rr(8, 9), rr(8, 9), 1.0 / 1024.0), sems.VE)
    inc_chain(P, "vector", lambda v: v.tensor_tensor(rr(2, 3), rr(1, 2), rr(1, 2), ALU.mult), sems.VE)
    ve_t = inc_chain(P, "vector", lambda v: v.tensor_tensor(rr(2, 3), rr(8, 9), rr(2, 3), ALU.subtract), sems.VE)
    P.wait("scalar", sems.VE, ve_t)
    t1_ = P.inc("scalar", lambda sc: sc.activation(rr(3, 4), rr(2, 3), AF.Ln, bias=EPS0[:, :], scale=1.0), sems.SE)
    P.wait("scalar", sems.SE, t1_)
    t2_ = P.inc("scalar", lambda sc: sc.activation(rr(3, 4), rr(3, 4), AF.Exp, bias=Z0[:, :], scale=-0.5), sems.SE)
    P.wait("tensor", sems.SE, t2_)
    P.on("tensor", lambda t: t.matmul(pb[6][:, 0:1], onesr0[:, :], rr(1, 2), start=True, stop=True))
    mmt3 = P.inc("tensor", lambda t: t.matmul(pb[6][:, 1:2], onesr0[:, :], rr(3, 4), start=True, stop=True), sems.MM)
    P.wait("vector", sems.MM, mmt3)
    inc_chain(P, "vector", lambda v: v.tensor_copy(MC0[:, :], pb[6][:, 0:2]), sems.VE)
    inc_chain(P, "vector", lambda v: v.tensor_scalar(R0[:, :], R0[:, :], MC0[:, 0:1], None, ALU.subtract), sems.VE)
    xt_ = inc_chain(P, "vector", lambda v: v.tensor_scalar(R0[:, :], R0[:, :], MC0[:, 1:2], None, ALU.mult), sems.VE)
    stt = sems.new16("t2s")
    P.wait("sync", sems.VE, xt_)
    for c in range(8):
        P.inc("sync", lambda sy, c=c: sy.dma_start(out=io["x30d"][c * 128:(c + 1) * 128, :], in_=R0[:, c:c + 1]), stt)
    P.wait("sync", stt, stt.n)



def build_p4b(nc, P, sems, io):
    ph0 = ExitStack()
    with ph0:
        pb = [ph0.enter_context(nc.psum_tensor(f"p4b_pb{i}", [128, 512], F32)) for i in range(8)]
        RES = ph0.enter_context(nc.sbuf_tensor("p4b_res", [128, 8 * SH], F32))
        RESL = [RES[:, i * SH:(i + 1) * SH] for i in range(8)]
        ph = ExitStack()
        with ph:
            OTX = ph.enter_context(nc.sbuf_tensor("p4b_otx", [128, 8 * SH], BF16))
            WO = ph.enter_context(nc.sbuf_tensor("p4b_wo", [128, 8 * 1024], BF16))
            XR = ph.enter_context(nc.sbuf_tensor("p4b_xr", [128, 8 * SH], F32))
            lw = sems.new16("p4bw")
            lx = sems.new16("p4bx")
            if "otx" in io:
                for i in range(8):
                    P.inc("gpsimd", lambda g, i=i: g.dma_start(out=OTX[:, i * SH:(i + 1) * SH],
                                                               in_=io["otx"][i * 128:(i + 1) * 128, :]), lw)
            else:
                OFS = ph.enter_context(nc.sbuf_tensor("p4b_ofs", [1, 1], mybir.dt.int32))
                lo = sems.new16("p4bo")
                P.inc("gpsimd", lambda g: g.dma_start(out=OFS[:, :], in_=io["ofs"]), lo)
                P.wait("gpsimd", lo, lo.n)
                otg_t = io["otg_t"]
                for _ in range(8):
                    lw.bump()

                def emit_otx_loads(g):
                    with g.register("p4b_r0") as r0, g.register("p4b_r1") as r1:
                        g.reg_load(r0, OFS[0:1, 0:1])
                        for c in range(8):
                            base = ((c // 4) * 1024 + (c % 4) * 128) * SH
                            g.reg_mov(r1, base)
                            g.reg_add(r1, r1, r0)
                            g.dma_start(out=OTX[:, c * SH:(c + 1) * SH],
                                        in_=bass.AP(otg_t, r1, [[SH, 128], [1, SH]])).then_inc(lw.h, 16)
                P.on("gpsimd", emit_otx_loads)
            for i in range(8):
                P.inc("gpsimd", lambda g, i=i: g.dma_start(out=WO[:, i * 1024:(i + 1) * 1024],
                                                           in_=io["wo"][i * 128:(i + 1) * 128, :]), lw)
                P.inc("sync", lambda sy, i=i: sy.dma_start(out=XR[:, i * SH:(i + 1) * SH],
                                                           in_=io["x2own"][i * 128:(i + 1) * 128, :]), lx)
            P.wait("tensor", lw, lw.n)
            P.wait("vector", lx, lx.n)
            bank_ticks = {}

            def evac(o, s4, bank, mmt):
                P.wait("vector", sems.MM, mmt)
                t = P.inc("vector",
                          lambda v, o=o, s4=s4, bank=bank:
                          v.tensor_add(RESL[o][:, s4 * 512:(s4 + 1) * 512], bank[:, :],
                                       XR[:, o * SH + s4 * 512: o * SH + (s4 + 1) * 512]), sems.VE)
                return (sems.VE, t)

            emit_proj(P, sems, nO=8, nC=8, nSc=4,
                      st_ap=lambda c, o: WO[:, c * 1024 + o * 128: c * 1024 + (o + 1) * 128],
                      mv_ap=lambda c, s4: OTX[:, c * SH + s4 * 512: c * SH + (s4 + 1) * 512],
                      banks=pb[0:4], bank_ticks=bank_ticks, evac=evac)
            if "ot0" in io:
                _p4b_tok0(nc, P, sems, io, pb, WO, ph)
            P.flush()
        emit_ln(nc, P, sems, "p4ln", RESL, io["x3f"], pb)


def _p4b_tok0(nc, P, sems, io, pb, WO, ph):
    OT0 = ph.enter_context(nc.sbuf_tensor("t0_ot", [128, 8], BF16))
    X20 = ph.enter_context(nc.sbuf_tensor("t0_x2", [128, 8], F32))
    R0 = ph.enter_context(nc.sbuf_tensor("t0_r", [128, 8], F32))
    RW0 = ph.enter_context(nc.sbuf_tensor("t0_rw", [1, 16], F32))
    OC0 = ph.enter_context(nc.sbuf_tensor("t0_oc", [128, 1], BF16))
    lz = sems.new16("t0l")
    lz2 = sems.new16("t0m")
    for c in range(8):
        P.inc("gpsimd", lambda g, c=c: g.dma_start(out=OT0[:, c:c + 1], in_=io["ot0"][c * 128:(c + 1) * 128, :]), lz)
        P.inc("sync", lambda sy, c=c: sy.dma_start(out=X20[:, c:c + 1], in_=io["x20"][c * 128:(c + 1) * 128, :]), lz2)
    P.inc("gpsimd", lambda g: g.memset(OC0[:, :], 1.0), sems.GP)
    P.wait("tensor", lz, lz.n)
    P.wait("vector", lz2, lz2.n)
    P.wait("tensor", sems.GP, sems.GP.n)
    for o in range(8):
        for c in range(8):
            first = (o == 0 and c == 0)
            last = (o == 7 and c == 7)
            stp = WO[:, c * 1024 + o * 128: c * 1024 + (o + 1) * 128]
            if last:
                mmt = P.inc("tensor", lambda t, stp=stp, o=o, c=c, first=first:
                            t.matmul(pb[4][:, o:o + 1], stp, OT0[:, c:c + 1], start=first, stop=True), sems.MM)
            else:
                P.on("tensor", lambda t, stp=stp, o=o, c=c, first=first:
                     t.matmul(pb[4][:, o:o + 1], stp, OT0[:, c:c + 1], start=first, stop=False))
    P.wait("vector", sems.MM, mmt)
    t = inc_chain(P, "vector", lambda v: v.tensor_add(R0[:, :], pb[4][:, 0:8], X20[:, :]), sems.VE)
    P.wait("tensor", sems.VE, t)
    R0H = ph.enter_context(nc.sbuf_tensor("t0_rh", [128, 8], BF16))
    R0S = ph.enter_context(nc.sbuf_tensor("t0_rs", [128, 8], BF16))
    ct = inc_chain(P, "vector", lambda v: v.tensor_copy(R0H[:, :], R0[:, :]), sems.VE)
    P.wait("scalar", sems.VE, ct)
    st_ = P.inc("scalar", lambda sc: sc.activation(R0S[:, :], R0[:, :], AF.Square), sems.SE)
    P.wait("tensor", sems.VE, ct)
    P.wait("tensor", sems.SE, st_)
    P.on("tensor", lambda t: t.matmul(pb[5][0:1, 0:8], OC0[:, :], R0H[:, :], start=True, stop=True))
    mmt2 = P.inc("tensor", lambda t: t.matmul(pb[5][0:1, 8:16], OC0[:, :], R0S[:, :], start=True, stop=True), sems.MM)
    P.wait("vector", sems.MM, mmt2)
    inc_chain(P, "vector", lambda v: v.tensor_copy(RW0[0:1, 0:16], pb[5][0:1, 0:16]), sems.VE)
    def rr(a, b_):
        return RW0[0:1, a:b_]
    inc_chain(P, "vector", lambda v: v.tensor_tensor(rr(0, 4), rr(0, 4), rr(4, 8), ALU.add), sems.VE)
    inc_chain(P, "vector", lambda v: v.tensor_tensor(rr(0, 2), rr(0, 2), rr(2, 4), ALU.add), sems.VE)
    inc_chain(P, "vector", lambda v: v.tensor_tensor(rr(0, 1), rr(0, 1), rr(1, 2), ALU.add), sems.VE)
    inc_chain(P, "vector", lambda v: v.tensor_tensor(rr(8, 12), rr(8, 12), rr(12, 16), ALU.add), sems.VE)
    inc_chain(P, "vector", lambda v: v.tensor_tensor(rr(8, 10), rr(8, 10), rr(10, 12), ALU.add), sems.VE)
    inc_chain(P, "vector", lambda v: v.tensor_tensor(rr(8, 9), rr(8, 9), rr(9, 10), ALU.add), sems.VE)
    inc_chain(P, "vector", lambda v: v.tensor_scalar_mul(rr(1, 2), rr(0, 1), 1.0 / 1024.0), sems.VE)
    inc_chain(P, "vector", lambda v: v.tensor_scalar_mul(rr(8, 9), rr(8, 9), 1.0 / 1024.0), sems.VE)
    inc_chain(P, "vector", lambda v: v.tensor_tensor(rr(2, 3), rr(1, 2), rr(1, 2), ALU.mult), sems.VE)
    inc_chain(P, "vector", lambda v: v.tensor_tensor(rr(2, 3), rr(8, 9), rr(2, 3), ALU.subtract), sems.VE)
    ve_t = sems.VE.n
    P.wait("scalar", sems.VE, ve_t)
    EPS0 = ph.enter_context(nc.sbuf_tensor("t0_ep", [1, 1], F32))
    Z0 = ph.enter_context(nc.sbuf_tensor("t0_z", [1, 1], F32))
    P.inc("gpsimd", lambda g: g.memset(EPS0[:, :], 1e-5), sems.GP)
    P.inc("gpsimd", lambda g: g.memset(Z0[:, :], 0.0), sems.GP)
    P.wait("scalar", sems.GP, sems.GP.n)
    t1_ = P.inc("scalar", lambda sc: sc.activation(rr(3, 4), rr(2, 3), AF.Ln, bias=EPS0[:, :], scale=1.0), sems.SE)
    P.wait("scalar", sems.SE, t1_)
    t2_ = P.inc("scalar", lambda sc: sc.activation(rr(3, 4), rr(3, 4), AF.Exp, bias=Z0[:, :], scale=-0.5), sems.SE)
    P.wait("vector", sems.SE, t2_)
    MC0 = ph.enter_context(nc.sbuf_tensor("t0_mc", [128, 2], F32))
    onesr0 = ph.enter_context(nc.sbuf_tensor("t0_or", [1, 128], F32))
    P.inc("gpsimd", lambda g: g.memset(onesr0[:, :], 1.0), sems.GP)
    P.wait("tensor", sems.GP, sems.GP.n)
    P.wait("tensor", sems.SE, t2_)
    P.on("tensor", lambda t: t.matmul(pb[6][:, 0:1], onesr0[:, :], rr(1, 2), start=True, stop=True))
    mmt3 = P.inc("tensor", lambda t: t.matmul(pb[6][:, 1:2], onesr0[:, :], rr(3, 4), start=True, stop=True), sems.MM)
    P.wait("vector", sems.MM, mmt3)
    inc_chain(P, "vector", lambda v: v.tensor_copy(MC0[:, :], pb[6][:, 0:2]), sems.VE)
    inc_chain(P, "vector", lambda v: v.tensor_scalar(R0[:, :], R0[:, :], MC0[:, 0:1], None, ALU.subtract), sems.VE)
    xt_ = inc_chain(P, "vector", lambda v: v.tensor_scalar(R0[:, :], R0[:, :], MC0[:, 1:2], None, ALU.mult), sems.VE)
    stt = sems.new16("t0s")
    P.wait("sync", sems.VE, xt_)
    for c in range(8):
        P.inc("sync", lambda sy, c=c: sy.dma_start(out=io["x30d"][c * 128:(c + 1) * 128, :], in_=R0[:, c:c + 1]), stt)
    P.wait("sync", stt, stt.n)



def build_p5(nc, P, sems, io):
    ph0 = ExitStack()
    with ph0:
        pb = [ph0.enter_context(nc.psum_tensor(f"p5_pb{i}", [128, 512], F32)) for i in range(8)]
        X3b = ph0.enter_context(nc.sbuf_tensor("p5_x3b", [128, 8 * SH], BF16))
        ACC = ph0.enter_context(nc.sbuf_tensor("p5_acc", [128, 8 * SH], F32))
        MCF = ph0.enter_context(nc.sbuf_tensor("p5_mcf", [128, 8], F32))
        BSUM = ph0.enter_context(nc.sbuf_tensor("p5_bsum", [128, 8], F32))
        TMXP = ph0.enter_context(nc.sbuf_tensor("p5_tmxp", [128, 512], F32))

        lx = sems.new16("p5x")
        for i in range(8):
            P.inc("gpsimd", lambda g, i=i: g.dma_start(out=X3b[:, i * SH:(i + 1) * SH],
                                                       in_=io["x3f"][i * 128:(i + 1) * 128, :]), lx)
        _p5_route(nc, P, sems, io, pb, MCF, BSUM)
        _p5_units(nc, P, sems, io, pb, X3b, ACC, MCF, BSUM, TMXP, lx)
        ACCL = [ACC[:, i * SH:(i + 1) * SH] for i in range(8)]
        emit_ln(nc, P, sems, "p5ln", ACCL, io["yf"], pb)


def _p5_route(nc, P, sems, io, pb, MCF, BSUM):
    ph = ExitStack()
    with ph:
        WR = ph.enter_context(nc.sbuf_tensor("p5_wr", [128, 8 * 8], F32))
        X30 = ph.enter_context(nc.sbuf_tensor("p5_x30", [128, 8], F32))
        BR = ph.enter_context(nc.sbuf_tensor("p5_br", [8, 1], F32))
        SR = ph.enter_context(nc.sbuf_tensor("p5_sr", [8, 1], F32))
        RW = ph.enter_context(nc.sbuf_tensor("p5_rw", [1, 32], F32))
        B2 = ph.enter_context(nc.sbuf_tensor("p5_b2", [128, 64], F32))
        onesr = ph.enter_context(nc.sbuf_tensor("p5_or", [1, 128], F32))
        lw = sems.new16("p5r")
        for c in range(8):
            P.inc("sync", lambda sy, c=c: sy.dma_start(out=WR[:, c * 8:(c + 1) * 8],
                                                       in_=io["wr"][c * 128:(c + 1) * 128, :]), lw)
            P.inc("sync", lambda sy, c=c: sy.dma_start(out=X30[:, c:c + 1],
                                                       in_=io["x30"][c * 128:(c + 1) * 128, :]), lw)
        P.inc("sync", lambda sy: sy.dma_start(out=BR[:, :], in_=io["br"][:, :]), lw)
        P.inc("sync", lambda sy: sy.dma_start(out=B2[:, :], in_=io["b2c"][:, :]), lw)
        P.inc("gpsimd", lambda g: g.memset(onesr[:, :], 1.0), sems.GP)
        P.wait("tensor", lw, lw.n)
        P.wait("tensor", sems.GP, sems.GP.n)
        pairs = [(WR[:, c * 8:(c + 1) * 8], X30[:, c:c + 1]) for c in range(8)]
        mmt = mm_group(P, sems, pb[0][0:8, 0:1], pairs)
        P.wait("vector", sems.MM, mmt)
        P.wait("vector", lw, lw.n)
        inc_chain(P, "vector", lambda v: v.tensor_tensor(SR[:, :], pb[0][0:8, 0:1], BR[:, :], ALU.add), sems.VE)
        st0 = sems.new16("p5g")
        P.wait("sync", sems.VE, sems.VE.n)
        for e in range(8):
            P.inc("sync", lambda sy, e=e: sy.dma_start(out=RW[0:1, e:e + 1], in_=SR[e:e + 1, :]), st0)
        P.wait("vector", st0, st0.n)
        for e in range(8):
            js = [j for j in range(8) if j != e]
            inc_chain(P, "vector", lambda v, e=e, j=js[0]:
                      v.tensor_tensor(RW[0:1, 8 + e:9 + e], RW[0:1, j:j + 1], RW[0:1, e:e + 1], ALU.is_gt), sems.VE)
            for j in js[1:]:
                inc_chain(P, "vector", lambda v, e=e, j=j:
                          v.tensor_tensor(RW[0:1, 24:25], RW[0:1, j:j + 1], RW[0:1, e:e + 1], ALU.is_gt), sems.VE)
                inc_chain(P, "vector", lambda v, e=e:
                          v.tensor_tensor(RW[0:1, 8 + e:9 + e], RW[0:1, 8 + e:9 + e], RW[0:1, 24:25], ALU.add), sems.VE)
            inc_chain(P, "vector", lambda v, e=e:
                      v.tensor_scalar(RW[0:1, 16 + e:17 + e], RW[0:1, 8 + e:9 + e], 2.0, 0.5, ALU.is_lt, ALU.mult), sems.VE)
        P.wait("tensor", sems.VE, sems.VE.n)
        bmt = P.inc("tensor", lambda t: t.matmul(pb[1][:, 0:8], onesr[:, :], RW[0:1, 16:24],
                                                 start=True, stop=True), sems.MM)
        P.inc("gpsimd", lambda g: g.memset(BSUM[:, :], 0.0), sems.GP)
        P.wait("vector", sems.MM, bmt)
        P.wait("vector", sems.GP, sems.GP.n)
        inc_chain(P, "vector", lambda v: v.tensor_copy(MCF[:, :], pb[1][:, 0:8]), sems.VE)
        TM8 = ph.enter_context(nc.sbuf_tensor("p5_tm8", [128, 8], F32))
        for e in range(8):
            inc_chain(P, "vector", lambda v, e=e:
                      v.tensor_scalar_mul(TM8[:, :], B2[:, e * 8:(e + 1) * 8], MCF[:, e:e + 1]), sems.VE)
            inc_chain(P, "vector", lambda v: v.tensor_tensor(BSUM[:, :], BSUM[:, :], TM8[:, :], ALU.add), sems.VE)
        P.flush()


def _p5_units(nc, P, sems, io, pb, X3b, ACC, MCF, BSUM, TMXP, lx):
    ph = ExitStack()
    with ph:
        HT = ph.enter_context(nc.sbuf_tensor("p5_ht", [128, 8 * SH], BF16))
        W1S = [ph.enter_context(nc.sbuf_tensor(f"p5_w1s{k}", [128, 8 * 256], BF16)) for k in range(2)]
        W2S = [ph.enter_context(nc.sbuf_tensor(f"p5_w2s{k}", [128, 8 * 128], BF16)) for k in range(2)]
        B1C = ph.enter_context(nc.sbuf_tensor("p5_b1c", [128, 64], F32))
        XRS = ph.enter_context(nc.sbuf_tensor("p5_xr", [128, 512], F32))

        lb = sems.new16("p5b")
        P.inc("sync", lambda sy: sy.dma_start(out=B1C[:, :], in_=io["b1c"][:, :]), lb)
        P.wait("scalar", lb, lb.n)
        P.wait("tensor", lx, lx.n)

        sW1 = [sems.new16(f"p5w1{k}") for k in range(2)]
        sW2 = [sems.new16(f"p5w2{k}") for k in range(2)]
        sXR = sems.new16("p5xr")
        w1_used, w2_used, xr_used = {}, {}, {}
        w1_ld, w2_ld, xr_ld = {}, {}, {}

        def load_w1(idx, e, fs):
            k = idx % 2
            prev = w1_used.get(idx - 2)
            if prev is not None:
                P.wait("gpsimd", sems.MM, prev)
            for c in range(8):
                P.inc("gpsimd", lambda g, c=c, k=k, e=e, fs=fs:
                      g.dma_start(out=W1S[k][:, c * 256:(c + 1) * 256],
                                  in_=io["w1f"][e * 1024 + c * 128: e * 1024 + (c + 1) * 128,
                                                fs * 256:(fs + 1) * 256]), sW1[k])
            w1_ld[idx] = (sW1[k], sW1[k].n)

        def load_w2(idx, e, o):
            k = idx % 2
            prev = w2_used.get(idx - 2)
            if prev is not None:
                P.wait("gpsimd", sems.MM, prev)
            P.inc("gpsimd", lambda g, k=k, e=e, o=o:
                  g.dma_start(out=W2S[k][:, :],
                              in_=io["w2b"][(e * 8 + o) * 128:(e * 8 + o + 1) * 128, :]), sW2[k])
            w2_ld[idx] = (sW2[k], sW2[k].n)

        def load_xr(idx, o, s4):
            prev = xr_used.get(idx - 1)
            if prev is not None:
                P.wait("sync", sems.VE, prev)
            P.inc("sync", lambda sy, o=o, s4=s4:
                  sy.dma_start(out=XRS[:, :], in_=io["x3f"][o * 128:(o + 1) * 128, s4 * 512:(s4 + 1) * 512]), sXR)
            xr_ld[idx] = (sXR, sXR.n)

        bank_ticks = {}
        mc_t = sems.VE.n
        P.wait("vector", sems.VE, mc_t)
        for e in range(8):
            w1i0 = e * 4
            load_w1(w1i0, e, 0)
            for fs in range(4):
                if fs < 3:
                    load_w1(w1i0 + fs + 1, e, fs + 1)
                P.wait("tensor", w1_ld[w1i0 + fs][0], w1_ld[w1i0 + fs][1])
                for fc2 in range(2):
                    fc = fs * 2 + fc2
                    for s4 in range(4):
                        bi = (fc * 4 + s4) % 2
                        bank = pb[bi]
                        prev = bank_ticks.get(id(bank))
                        if prev is not None:
                            P.wait("tensor", prev[0], prev[1])
                        pairs = [(W1S[(w1i0 + fs) % 2][:, c * 256 + fc2 * 128: c * 256 + (fc2 + 1) * 128],
                                  X3b[:, c * SH + s4 * 512: c * SH + (s4 + 1) * 512]) for c in range(8)]
                        mmt = mm_group(P, sems, bank[:, :], pairs)
                        P.wait("scalar", sems.MM, mmt)
                        et = P.inc("scalar",
                                   lambda sc, fc=fc, s4=s4, bank=bank, e=e:
                                   sc.activation(HT[:, fc * SH + s4 * 512: fc * SH + (s4 + 1) * 512],
                                                 bank[:, :], AF.Gelu, bias=B1C[:, e * 8 + fc: e * 8 + fc + 1],
                                                 scale=1.0), sems.SE)
                        bank_ticks[id(bank)] = (sems.SE, et)
                w1_used[w1i0 + fs] = sems.MM.n
            P.wait("tensor", sems.SE, sems.SE.n)
            w2i0 = e * 8
            load_w2(w2i0, e, 0)
            for o in range(8):
                if o < 7:
                    load_w2(w2i0 + o + 1, e, o + 1)
                P.wait("tensor", w2_ld[w2i0 + o][0], w2_ld[w2i0 + o][1])
                for s4 in range(4):
                    bi = 2 + s4 % 2
                    bank = pb[bi]
                    prev = bank_ticks.get(id(bank))
                    if prev is not None:
                        P.wait("tensor", prev[0], prev[1])
                    pairs = [(W2S[(w2i0 + o) % 2][:, fc * 128:(fc + 1) * 128],
                              HT[:, fc * SH + s4 * 512: fc * SH + (s4 + 1) * 512]) for fc in range(8)]
                    mmt = mm_group(P, sems, bank[:, :], pairs)
                    P.wait("vector", sems.MM, mmt)
                    dacc = ACC[:, o * SH + s4 * 512: o * SH + (s4 + 1) * 512]
                    if e == 0:
                        load_xr(o * 4 + s4, o, s4)
                        P.wait("vector", xr_ld[o * 4 + s4][0], xr_ld[o * 4 + s4][1])
                        inc_chain(P, "vector", lambda v, bank=bank:
                                  v.tensor_scalar_mul(TMXP[:, :], bank[:, :], MCF[:, 0:1]), sems.VE)
                        t = inc_chain(P, "vector", lambda v, dacc=dacc:
                                      v.tensor_tensor(dacc, TMXP[:, :], XRS[:, :], ALU.add), sems.VE)
                        xr_used[o * 4 + s4] = sems.VE.n
                    else:
                        inc_chain(P, "vector", lambda v, bank=bank, e=e:
                                  v.tensor_scalar_mul(TMXP[:, :], bank[:, :], MCF[:, e:e + 1]), sems.VE)
                        t = inc_chain(P, "vector", lambda v, dacc=dacc:
                                      v.tensor_tensor(dacc, TMXP[:, :], dacc, ALU.add), sems.VE)
                    bank_ticks[id(bank)] = (sems.VE, t)
                w2_used[w2i0 + o] = sems.MM.n
        for o in range(8):
            inc_chain(P, "vector", lambda v, o=o:
                      v.tensor_scalar_add(ACC[:, o * SH:(o + 1) * SH], ACC[:, o * SH:(o + 1) * SH],
                                          BSUM[:, o:o + 1]), sems.VE)
        P.flush()



def prep_p1(ins):
    x = np.ascontiguousarray(np.asarray(ins["L0_attn_x"], dtype=np.float32))
    xT = np.ascontiguousarray(x.transpose(0, 2, 1))
    wqkT = np.ascontiguousarray(np.asarray(ins["L0_attn_w_qkv"]).T)
    wpT = np.ascontiguousarray(np.asarray(ins["L0_attn_w_proj"]).T)
    bproj8 = np.ascontiguousarray(np.asarray(ins["L0_attn_b_proj"]).reshape(8, 128).T)
    per_core = []
    for c in range(8):
        b, p = c // 2, c % 2
        per_core.append({
            "xT": xT[b],
            "wqkT": wqkT,
            "wv_own": np.ascontiguousarray(wqkT[:, 2048 + 512 * p: 2048 + 512 * (p + 1)]),
            "wpT": wpT,
            "bproj8": bproj8,
            "x_res": np.ascontiguousarray(xT[b][:, SH * p: SH * (p + 1)]),
        })
    return per_core


def build_p1_standalone():
    nc = mk_nc()
    io = {}
    io["xT"] = nc.dram_tensor("xT", [1024, S], F32, kind="ExternalInput").ap()
    io["wqkT"] = nc.dram_tensor("wqkT", [1024, 3072], F32, kind="ExternalInput").ap()
    io["wv_own"] = nc.dram_tensor("wv_own", [1024, 512], F32, kind="ExternalInput").ap()
    io["wpT"] = nc.dram_tensor("wpT", [1024, 1024], F32, kind="ExternalInput").ap()
    io["bproj8"] = nc.dram_tensor("bproj8", [128, 8], F32, kind="ExternalInput").ap()
    io["x_res"] = nc.dram_tensor("x_res", [1024, SH], F32, kind="ExternalInput").ap()
    io["x1f"] = nc.dram_tensor("x1f", [1024, SH], F32, kind="ExternalOutput").ap()
    with ExitStack() as stack:
        sems = Sems(nc, stack, "p1")
        P = Prog(nc)
        build_p1(nc, P, sems, io)
        P.flush()
    return nc


def prep_p2(ins):
    def col128(b):
        b = np.asarray(b, dtype=np.float32)
        return np.ascontiguousarray(b.reshape(-1, 128).T)

    def w2block(w2):
        w2 = np.asarray(w2, dtype=np.float32)
        E = w2.shape[0]
        blk = w2.reshape(E, 32, 128, 8, 128).transpose(0, 3, 2, 1, 4)
        return np.ascontiguousarray(blk.reshape(E * 8 * 128, 4096))

    sw1 = np.asarray(ins["L0_ffn_sw1"], dtype=np.float32)
    rw1 = np.asarray(ins["L0_ffn_rw1"], dtype=np.float32)
    shared = {
        "gate_w": np.ascontiguousarray(np.asarray(ins["L0_ffn_gate_w"], dtype=np.float32)),
        "gate_b": np.ascontiguousarray(np.asarray(ins["L0_ffn_gate_b"], dtype=np.float32).reshape(4, 1)),
        "sw1f": np.ascontiguousarray(sw1.reshape(2 * 1024, 4096)),
        "rw1f": np.ascontiguousarray(rw1.reshape(4 * 1024, 4096)),
        "sw2b": w2block(ins["L0_ffn_sw2"]),
        "rw2b": w2block(ins["L0_ffn_rw2"]),
        "sb1c": np.ascontiguousarray(np.asarray(ins["L0_ffn_sb1"], dtype=np.float32).reshape(2 * 32, 128).T.reshape(128, 64)),
        "rb1c": np.ascontiguousarray(np.asarray(ins["L0_ffn_rb1"], dtype=np.float32).reshape(4 * 32, 128).T.reshape(128, 128)),
        "sb2c": np.ascontiguousarray(np.asarray(ins["L0_ffn_sb2"], dtype=np.float32).reshape(2 * 8, 128).T.reshape(128, 16)),
        "rb2c": np.ascontiguousarray(np.asarray(ins["L0_ffn_rb2"], dtype=np.float32).reshape(4 * 8, 128).T.reshape(128, 32)),
    }
    return shared


def build_p2_standalone():
    nc = mk_nc()
    io = {}
    io["x1f"] = nc.dram_tensor("x1f", [1024, SH], F32, kind="ExternalInput").ap()
    io["gate_w"] = nc.dram_tensor("gate_w", [1024, 4], F32, kind="ExternalInput").ap()
    io["gate_b"] = nc.dram_tensor("gate_b", [4, 1], F32, kind="ExternalInput").ap()
    io["sw1f"] = nc.dram_tensor("sw1f", [2048, 4096], F32, kind="ExternalInput").ap()
    io["rw1f"] = nc.dram_tensor("rw1f", [4096, 4096], F32, kind="ExternalInput").ap()
    io["sw2b"] = nc.dram_tensor("sw2b", [2048, 4096], F32, kind="ExternalInput").ap()
    io["rw2b"] = nc.dram_tensor("rw2b", [4096, 4096], F32, kind="ExternalInput").ap()
    io["sb1c"] = nc.dram_tensor("sb1c", [128, 64], F32, kind="ExternalInput").ap()
    io["rb1c"] = nc.dram_tensor("rb1c", [128, 128], F32, kind="ExternalInput").ap()
    io["sb2c"] = nc.dram_tensor("sb2c", [128, 16], F32, kind="ExternalInput").ap()
    io["rb2c"] = nc.dram_tensor("rb2c", [128, 32], F32, kind="ExternalInput").ap()
    io["x2f"] = nc.dram_tensor("x2f", [1024, SH], F32, kind="ExternalOutput").ap()
    with ExitStack() as stack:
        sems = Sems(nc, stack, "p2")
        P = Prog(nc)
        build_p2(nc, P, sems, io)
        P.flush()
    return nc


def prep_p4a(ins, x2):
    lam = LAM
    akq = np.zeros((128, 256), np.float32)
    p_ = np.arange(128)
    for qc in range(8):
        for kt in range(32):
            if kt < 4 * qc:
                col = np.exp(-lam * (512 * qc - 128 * kt - p_)) / 32.0
            elif kt > 4 * qc + 3:
                col = np.exp(-lam * (128 * kt + p_ - 512 * qc)) / 32.0
            else:
                col = np.full(128, 1.0 / 32.0)
            akq[:, qc * 32 + kt] = col
    f_ = np.arange(512)
    dql = np.concatenate([np.exp(-lam * f_), np.exp(lam * f_)]).astype(np.float32).reshape(1, 1024)
    dec4 = np.zeros((128, 4 * 512), np.float32)
    for d in range(4):
        dec4[:, d * 512:(d + 1) * 512] = np.exp(-lam * np.abs((128 * d + p_)[:, None] - f_[None, :]))
    wk = np.ascontiguousarray(np.asarray(ins["L1_attn_Wk"], np.float32))
    wv = np.ascontiguousarray(np.asarray(ins["L1_attn_Wv"], np.float32))
    wq = np.asarray(ins["L1_attn_Wq"], np.float32)
    wd = np.ascontiguousarray(np.asarray(ins["L1_attn_W_down_kv"], np.float32))
    wupk = np.asarray(ins["L1_attn_W_up_k"], np.float32)
    wupv = np.asarray(ins["L1_attn_W_up_v"], np.float32)
    per_core = []
    for c in range(8):
        b, p = c // 2, c % 2
        x2T = x2[b].T
        per_core.append({
            "x2full": np.ascontiguousarray(np.concatenate([x2T[:, :SH], x2T[:, SH:]], axis=0)),
            "wk": wk, "wv": wv, "wdown": wd,
            "wqh": np.ascontiguousarray(wq[:, 512 * p:512 * (p + 1)]),
            "wupkh": np.ascontiguousarray(wupk[:, 512 * p:512 * (p + 1)]),
            "wupvh": np.ascontiguousarray(wupv[:, 512 * p:512 * (p + 1)]),
            "akq": akq, "dql": dql, "dec4": dec4,
        })
    return per_core


def prep_p4b(ins, x2, ot_by_core):
    wo = np.ascontiguousarray(np.asarray(ins["L1_attn_Wo"], np.float32))
    per_core = []
    for c in range(8):
        b, p = c // 2, c % 2
        otx = np.concatenate([ot_by_core[2 * b][:, SH * p:SH * (p + 1)],
                              ot_by_core[2 * b + 1][:, SH * p:SH * (p + 1)]], axis=0)
        per_core.append({
            "otx": np.ascontiguousarray(otx),
            "wo": wo,
            "x2own": np.ascontiguousarray(x2[b].T[:, SH * p:SH * (p + 1)]),
        })
    return per_core


def prep_p5(ins, x3):
    w1 = np.asarray(ins["L1_ffn_W1"], np.float32)
    w2 = np.asarray(ins["L1_ffn_W2"], np.float32)
    w2b = np.ascontiguousarray(w2.reshape(8, 8, 128, 8, 128).transpose(0, 3, 2, 1, 4).reshape(8192, 1024))
    shared = {
        "wr": np.ascontiguousarray(np.asarray(ins["L1_ffn_Wr"], np.float32)),
        "br": np.ascontiguousarray(np.asarray(ins["L1_ffn_br"], np.float32).reshape(8, 1)),
        "w1f": np.ascontiguousarray(w1.reshape(8 * 1024, 1024)),
        "w2b": w2b,
        "b1c": np.ascontiguousarray(np.asarray(ins["L1_ffn_b1"], np.float32).reshape(8 * 8, 128).T.reshape(128, 64)),
        "b2c": np.ascontiguousarray(np.asarray(ins["L1_ffn_b2"], np.float32).reshape(8 * 8, 128).T.reshape(128, 64)),
    }
    per_core = []
    for c in range(8):
        b, p = c // 2, c % 2
        d = dict(shared)
        d["x3f"] = np.ascontiguousarray(x3[b].T[:, SH * p:SH * (p + 1)])
        d["x30"] = np.ascontiguousarray(x3[b].T[:, 0:1])
        per_core.append(d)
    return per_core


def build_p4a_standalone():
    nc = mk_nc()
    io = {}
    for nm, shp in [("x2full", [2048, 2048]), ("wk", [1024, 1024]), ("wv", [1024, 1024]),
                    ("wqh", [1024, 512]), ("wdown", [1024, 256]),
                    ("wupkh", [256, 512]), ("wupvh", [256, 512]),
                    ("akq", [128, 256]), ("dql", [1, 1024]), ("dec4", [128, 4 * 512])]:
        io[nm] = nc.dram_tensor(nm, shp, F32, kind="ExternalInput").ap()
    io["otf"] = nc.dram_tensor("otf", [1024, SH], F32, kind="ExternalOutput").ap()
    with ExitStack() as stack:
        sems = Sems(nc, stack, "p4")
        P = Prog(nc)
        build_p4a(nc, P, sems, io)
        P.flush()
    return nc


def build_p4b_standalone():
    nc = mk_nc()
    io = {}
    for nm, shp in [("otx", [1024, SH]), ("wo", [1024, 1024]), ("x2own", [1024, SH])]:
        io[nm] = nc.dram_tensor(nm, shp, F32, kind="ExternalInput").ap()
    io["x3f"] = nc.dram_tensor("x3f", [1024, SH], F32, kind="ExternalOutput").ap()
    with ExitStack() as stack:
        sems = Sems(nc, stack, "pb")
        P = Prog(nc)
        build_p4b(nc, P, sems, io)
        P.flush()
    return nc


def build_p5_standalone():
    nc = mk_nc()
    io = {}
    for nm, shp in [("x3f", [1024, SH]), ("x30", [1024, 1]), ("wr", [1024, 8]), ("br", [8, 1]),
                    ("w1f", [8192, 1024]), ("w2b", [8192, 1024]), ("b1c", [128, 64]), ("b2c", [128, 64])]:
        io[nm] = nc.dram_tensor(nm, shp, F32, kind="ExternalInput").ap()
    io["yf"] = nc.dram_tensor("yf", [1024, SH], F32, kind="ExternalOutput").ap()
    with ExitStack() as stack:
        sems = Sems(nc, stack, "p5")
        P = Prog(nc)
        build_p5(nc, P, sems, io)
        P.flush()
    return nc


def build_launchA():
    nc = mk_nc()
    io = {}
    ext = [("xT", [1024, S]), ("wqkT", [1024, 3072]), ("wv_own", [1024, 512]),
           ("wpT", [1024, 1024]), ("bproj8", [128, 8]), ("x_res", [1024, SH]),
           ("gate_w", [1024, 4]), ("gate_b", [4, 1]),
           ("sw1f", [2048, 4096]), ("rw1f", [4096, 4096]), ("sw2b", [2048, 4096]), ("rw2b", [4096, 4096]),
           ("sb1c", [128, 64]), ("rb1c", [128, 128]), ("sb2c", [128, 16]), ("rb2c", [128, 32])]
    for nm, shp in ext:
        io[nm] = nc.dram_tensor(nm, shp, F32, kind="ExternalInput").ap()
    x1f = nc.dram_tensor("x1f", [1024, SH], F32)
    io["x1f"] = x1f.ap()
    io["x2f"] = nc.dram_tensor("x2f", [1024, SH], F32, kind="ExternalOutput").ap()
    with ExitStack() as stack:
        sems = Sems(nc, stack, "a")
        P = Prog(nc)
        build_p1(nc, P, sems, io)
        P.flush()
        build_p2(nc, P, sems, io)
        P.flush()
    return nc


def build_launchC():
    nc = mk_nc()
    io = {}
    for nm, shp in [("otx", [1024, SH]), ("wo", [1024, 1024]), ("x2own", [1024, SH]),
                    ("ot0", [1024, 1]), ("x20", [1024, 1]), ("wr", [1024, 8]), ("br", [8, 1]),
                    ("w1f", [8192, 1024]), ("w2b", [8192, 1024]), ("b1c", [128, 64]), ("b2c", [128, 64])]:
        io[nm] = nc.dram_tensor(nm, shp, F32, kind="ExternalInput").ap()
    x3f = nc.dram_tensor("x3f", [1024, SH], F32)
    io["x3f"] = x3f.ap()
    io["x30d"] = nc.dram_tensor("x30d", [1024, 1], F32).ap()
    io["x30"] = io["x30d"]
    io["yf"] = nc.dram_tensor("yf", [1024, SH], F32, kind="ExternalOutput").ap()
    with ExitStack() as stack:
        sems = Sems(nc, stack, "c")
        P = Prog(nc)
        build_p4b(nc, P, sems, io)
        P.flush()
        build_p5(nc, P, sems, io)
        P.flush()
    return nc


def build_launchB2():
    nc = mk_nc()
    io = {}
    for nm, shp in [("x2full", [1024, S]), ("x20", [1024, 1]),
                    ("wk", [1024, 1024]), ("wv", [1024, 1024]), ("wq", [1024, 1024]),
                    ("wdown", [1024, 256]), ("wupk", [256, 1024]), ("wupv", [256, 1024]),
                    ("wo", [1024, 1024]),
                    ("akq4", [128, 128]), ("dql", [1, 1024]), ("dec4", [128, 2048]), ("ak0", [128, 32]),
                    ("wr", [1024, 8]), ("br", [8, 1]), ("w1f", [8192, 1024]), ("w2b", [8192, 1024]),
                    ("b1c", [128, 64]), ("b2c", [128, 64])]:
        io[nm] = nc.dram_tensor(nm, shp, F32, kind="ExternalInput").ap()
    x3f = nc.dram_tensor("x3f", [1024, SH], F32)
    io["x3f"] = x3f.ap()
    io["x30d"] = nc.dram_tensor("x30d", [1024, 1], F32).ap()
    io["x30"] = io["x30d"]
    io["yf"] = nc.dram_tensor("yf", [1024, SH], F32, kind="ExternalOutput").ap()
    with ExitStack() as stack:
        sems = Sems(nc, stack, "b2")
        P = Prog(nc)
        build_p42(nc, P, sems, io)
        P.flush()
        build_p5(nc, P, sems, io)
        P.flush()
    return nc


def prep_l2_consts(ins):
    lam = LAM
    p_ = np.arange(128)
    akq4 = np.zeros((128, 128), np.float32)
    for qc in range(4):
        for kt in range(32):
            if kt < 4 * qc:
                col = np.exp(-lam * (512 * qc - 128 * kt - p_)) / 32.0
            elif kt > 4 * qc + 3:
                col = np.exp(-lam * (128 * kt + p_ - 512 * qc)) / 32.0
            else:
                col = np.full(128, 1.0 / 32.0)
            akq4[:, qc * 32 + kt] = col
    f_ = np.arange(512)
    dql = np.concatenate([np.exp(-lam * f_), np.exp(lam * f_)]).astype(np.float32).reshape(1, 1024)
    dec4 = np.zeros((128, 4 * 512), np.float32)
    for d in range(4):
        dec4[:, d * 512:(d + 1) * 512] = np.exp(-lam * np.abs((128 * d + p_)[:, None] - f_[None, :]))
    w1 = ins["L1_ffn_W1"]
    w2 = ins["L1_ffn_W2"]
    w2b = np.ascontiguousarray(w2.reshape(8, 8, 128, 8, 128).transpose(0, 3, 2, 1, 4).reshape(8192, 1024))
    shared = {
        "wk": np.ascontiguousarray(ins["L1_attn_Wk"]), "wv": np.ascontiguousarray(ins["L1_attn_Wv"]),
        "wq": np.ascontiguousarray(ins["L1_attn_Wq"]), "wdown": np.ascontiguousarray(ins["L1_attn_W_down_kv"]),
        "wupk": np.ascontiguousarray(ins["L1_attn_W_up_k"]), "wupv": np.ascontiguousarray(ins["L1_attn_W_up_v"]),
        "wo": np.ascontiguousarray(ins["L1_attn_Wo"]),
        "akq4": akq4, "dql": dql, "dec4": dec4,
        "wr": np.ascontiguousarray(ins["L1_ffn_Wr"]),
        "br": np.ascontiguousarray(ins["L1_ffn_br"].reshape(8, 1)),
        "w1f": np.ascontiguousarray(w1.reshape(8 * 1024, 1024)), "w2b": w2b,
        "b1c": np.ascontiguousarray(ins["L1_ffn_b1"].reshape(8 * 8, 128).T.reshape(128, 64)),
        "b2c": np.ascontiguousarray(ins["L1_ffn_b2"].reshape(8 * 8, 128).T.reshape(128, 64)),
    }
    ak0 = np.zeros((2, 128, 32), np.float32)
    kk = (128 * np.arange(32)[None, :] + p_[:, None]).astype(np.float64)
    kk = (p_[:, None] + 128 * np.arange(32)[None, :])
    ak0[0] = np.exp(-lam * kk) / 32.0
    ak0[1] = np.exp(-lam * (4095.0 - kk)) / 32.0
    return shared, ak0


def kernel2(ins):
    ins = _to_np(ins)
    cores = list(range(NCORES))
    pc1 = prep_p1(ins)
    shared2 = prep_p2(ins)
    in_a = []
    for c in cores:
        d = {}
        d.update(pc1[c])
        d.update(shared2)
        in_a.append(d)
    ra = run_bass_kernel_spmd(build_launchA(), in_a, core_ids=cores)
    x2f = [ra.results[c]["x2f"] for c in cores]

    shared, ak0 = prep_l2_consts(ins)
    in_b = []
    for c in cores:
        b, p = c // 2, c % 2
        x2T = np.concatenate([x2f[2 * b], x2f[2 * b + 1]], axis=1)
        if p:
            x2v = np.ascontiguousarray(x2T[:, ::-1])
        else:
            x2v = np.ascontiguousarray(x2T)
        d = dict(shared)
        d["x2full"] = x2v
        d["x20"] = np.ascontiguousarray(x2T[:, 0:1])
        d["ak0"] = ak0[p]
        in_b.append(d)
    rb = run_bass_kernel_spmd(build_launchB2(), in_b, core_ids=cores)
    out = np.empty((4, 4096, 1024), np.float32)
    for c in cores:
        b, p = c // 2, c % 2
        y = rb.results[c]["yf"]
        if p:
            out[b, SH:, :] = y[:, ::-1].T
        else:
            out[b, :SH, :] = y.T
    return out


def kernel3(ins):
    ins = _to_np(ins)
    cores = list(range(NCORES))
    pc1 = prep_p1(ins)
    shared2 = prep_p2(ins)
    in_a = []
    for c in cores:
        d = {}
        d.update(pc1[c])
        d.update(shared2)
        in_a.append(d)
    ra = run_bass_kernel_spmd(build_launchA(), in_a, core_ids=cores)
    x2f = [ra.results[c]["x2f"] for c in cores]

    x2full = [np.ascontiguousarray(np.concatenate([x2f[2 * b], x2f[2 * b + 1]], axis=1)) for b in range(4)]
    pc4a = prep_p4a_from_cols(ins, x2full)
    rb = run_bass_kernel_spmd(build_p4a_standalone(), pc4a, core_ids=cores)
    ot = [rb.results[c]["otf"] for c in cores]

    wo = np.ascontiguousarray(ins["L1_attn_Wo"])
    shared5 = prep_p5_shared(ins)
    in_c = []
    for c in cores:
        b, p = c // 2, c % 2
        otx = np.concatenate([ot[2 * b][p * 512:(p + 1) * 512, :],
                              ot[2 * b + 1][p * 512:(p + 1) * 512, :]], axis=0)
        ot0 = np.concatenate([ot[2 * b][0:512, 0:1], ot[2 * b + 1][0:512, 0:1]], axis=0)
        d = {"otx": np.ascontiguousarray(otx), "wo": wo, "x2own": x2f[c],
             "ot0": np.ascontiguousarray(ot0), "x20": np.ascontiguousarray(x2f[2 * b][:, 0:1])}
        d.update(shared5)
        in_c.append(d)
    rc_ = run_bass_kernel_spmd(build_launchC(), in_c, core_ids=cores)
    out = np.empty((4, 4096, 1024), np.float32)
    for c in cores:
        b, p = c // 2, c % 2
        out[b, SH * p:SH * (p + 1), :] = rc_.results[c]["yf"].T
    return out


RG_PAIRS = [[0, 1], [2, 3], [4, 5], [6, 7]]


def build_fused():
    from concourse.aot_env import allow_ici_all_to_all
    import contextlib
    nc = mk_nc()
    io = {}
    ext = [("xT", [1024, S]), ("wqkT", [1024, 3072]), ("wv_own", [1024, 512]),
           ("wpT", [1024, 1024]), ("bproj8", [128, 8]), ("x_res", [1024, SH]),
           ("gate_w", [1024, 4]), ("gate_b", [4, 1]),
           ("sw1f", [2048, 4096]), ("rw1f", [4096, 4096]), ("sw2b", [2048, 4096]), ("rw2b", [4096, 4096]),
           ("sb1c", [128, 64]), ("rb1c", [128, 128]), ("sb2c", [128, 16]), ("rb2c", [128, 32]),
           ("wk", [1024, 1024]), ("wv", [1024, 1024]), ("wqh", [1024, 512]), ("wdown", [1024, 256]),
           ("wupkh", [256, 512]), ("wupvh", [256, 512]),
           ("akq", [128, 256]), ("dql", [1, 1024]), ("dec4", [128, 4 * 512]),
           ("wo", [1024, 1024]),
           ("wr", [1024, 8]), ("br", [8, 1]), ("w1f", [8192, 1024]), ("w2b", [8192, 1024]),
           ("b1c", [128, 64]), ("b2c", [128, 64])]
    for nm, shp in ext:
        io[nm] = nc.dram_tensor(nm, shp, F32, kind="ExternalInput").ap()
    io["ofs"] = nc.dram_tensor("ofs", [1, 1], mybir.dt.int32, kind="ExternalInput").ap()
    x1f = nc.dram_tensor("x1f", [1024, SH], F32)
    x2f = nc.dram_tensor("x2f", [1024, SH], F32)
    x2full = nc.dram_tensor("x2full_i", [2048, SH], F32, addr_space="Local")
    otf = nc.dram_tensor("otf_i", [1024, SH], F32)
    otg = nc.dram_tensor("otg_i", [2048, SH], F32, addr_space="Local")
    x3f = nc.dram_tensor("x3f_i", [1024, SH], F32)
    x30b = nc.dram_tensor("x30b", [1024, 1], F32)
    x30g = nc.dram_tensor("x30g", [2048, 1], F32, addr_space="Local")
    io["x1f"] = x1f.ap()
    io["x2f"] = x2f.ap()
    io["x2full"] = x2full.ap()
    io["otf"] = otf.ap()
    io["yf"] = nc.dram_tensor("yf", [1024, SH], F32, kind="ExternalOutput").ap()

    with ExitStack() as stack:
        stack.enter_context(allow_ici_all_to_all())
        sems = Sems(nc, stack, "f")
        P = Prog(nc)
        build_p1(nc, P, sems, io)
        P.flush()
        build_p2(nc, P, sems, io)
        P.flush()
        cc1 = Sem(stack.enter_context(nc.semaphore("fcc1")), 16)
        P.inc("gpsimd", lambda g: g.collective_compute(
            "AllGather", ALU.bypass, replica_groups=RG_PAIRS,
            ins=[x2f.ap().opt()], outs=[x2full.ap().opt()]), cc1)
        P.wait("gpsimd", cc1, 16)
        P.flush()
        build_p4a(nc, P, sems, io)
        P.flush()
        cc2 = Sem(stack.enter_context(nc.semaphore("fcc2")), 16)
        P.inc("gpsimd", lambda g: g.collective_compute(
            "AllGather", ALU.bypass, replica_groups=RG_PAIRS,
            ins=[otf.ap().opt()], outs=[otg.ap().opt()]), cc2)
        P.wait("gpsimd", cc2, 16)
        P.flush()
        io4b = {"otg_t": otg, "ofs": io["ofs"], "wo": io["wo"], "x2own": x2f.ap(), "x3f": x3f.ap()}
        build_p4b(nc, P, sems, io4b)
        P.flush()
        cc3 = Sem(stack.enter_context(nc.semaphore("fcc3")), 16)
        cps = sems.new16("fx30")
        P.inc("gpsimd", lambda g: g.dma_start(out=x30b.ap(), in_=x3f.ap()[:, 0:1], allow_slow_non_contiguous=True), cps)
        P.wait("gpsimd", cps, cps.n)
        P.inc("gpsimd", lambda g: g.collective_compute(
            "AllGather", ALU.bypass, replica_groups=RG_PAIRS,
            ins=[x30b.ap().opt()], outs=[x30g.ap().opt()]), cc3)
        P.wait("gpsimd", cc3, 16)
        P.flush()
        io5 = {"x3f": x3f.ap(), "x30": x30g.ap()[0:1024, :], "wr": io["wr"], "br": io["br"],
               "w1f": io["w1f"], "w2b": io["w2b"], "b1c": io["b1c"], "b2c": io["b2c"],
               "yf": io["yf"]}
        build_p5(nc, P, sems, io5)
        P.flush()
    return nc


def kernel(**ins):
    return kernel2(ins)


def kernel_fused(**ins):
    ins = _to_np(ins)
    cores = list(range(NCORES))
    pc1 = prep_p1(ins)
    shared2 = prep_p2(ins)
    pc4a = prep_p4a_consts(ins)
    wo = np.ascontiguousarray(ins["L1_attn_Wo"])
    shared5 = prep_p5_shared(ins)
    in_maps = []
    for c in cores:
        d = {}
        d.update(pc1[c])
        d.update(shared2)
        d.update(pc4a[c])
        d["wo"] = wo
        d.update(shared5)
        d["ofs"] = np.array([[(c % 2) * 512 * SH]], dtype=np.int32)
        in_maps.append(d)
    nc = build_fused()
    res = run_bass_kernel_spmd(nc, in_maps, core_ids=cores)
    out = np.empty((4, 4096, 1024), np.float32)
    for c in cores:
        b, p = c // 2, c % 2
        out[b, SH * p:SH * (p + 1), :] = res.results[c]["yf"].T
    return out


def prep_p4a_consts(ins):
    lam = LAM
    akq = np.zeros((128, 256), np.float32)
    p_ = np.arange(128)
    for qc in range(8):
        for kt in range(32):
            if kt < 4 * qc:
                col = np.exp(-lam * (512 * qc - 128 * kt - p_)) / 32.0
            elif kt > 4 * qc + 3:
                col = np.exp(-lam * (128 * kt + p_ - 512 * qc)) / 32.0
            else:
                col = np.full(128, 1.0 / 32.0)
            akq[:, qc * 32 + kt] = col
    f_ = np.arange(512)
    dql = np.concatenate([np.exp(-lam * f_), np.exp(lam * f_)]).astype(np.float32).reshape(1, 1024)
    dec4 = np.zeros((128, 4 * 512), np.float32)
    for d in range(4):
        dec4[:, d * 512:(d + 1) * 512] = np.exp(-lam * np.abs((128 * d + p_)[:, None] - f_[None, :]))
    wk = np.ascontiguousarray(ins["L1_attn_Wk"])
    wv = np.ascontiguousarray(ins["L1_attn_Wv"])
    wq = ins["L1_attn_Wq"]
    wd = np.ascontiguousarray(ins["L1_attn_W_down_kv"])
    wupk = ins["L1_attn_W_up_k"]
    wupv = ins["L1_attn_W_up_v"]
    per_core = []
    for c in range(8):
        b, p = c // 2, c % 2
        per_core.append({
            "wk": wk, "wv": wv, "wdown": wd,
            "wqh": np.ascontiguousarray(wq[:, 512 * p:512 * (p + 1)]),
            "wupkh": np.ascontiguousarray(wupk[:, 512 * p:512 * (p + 1)]),
            "wupvh": np.ascontiguousarray(wupv[:, 512 * p:512 * (p + 1)]),
            "akq": akq, "dql": dql, "dec4": dec4,
        })
    return per_core


def _to_np(ins):
    out = {}
    for k, v in ins.items():
        a = np.asarray(v)
        if a.dtype != np.float32:
            a = np.asarray(a, dtype=np.float32)
        out[k] = a
    return out


def prep_p4a_from_cols(ins, x2fullT):
    lam = LAM
    akq = np.zeros((128, 256), np.float32)
    p_ = np.arange(128)
    for qc in range(8):
        for kt in range(32):
            if kt < 4 * qc:
                col = np.exp(-lam * (512 * qc - 128 * kt - p_)) / 32.0
            elif kt > 4 * qc + 3:
                col = np.exp(-lam * (128 * kt + p_ - 512 * qc)) / 32.0
            else:
                col = np.full(128, 1.0 / 32.0)
            akq[:, qc * 32 + kt] = col
    f_ = np.arange(512)
    dql = np.concatenate([np.exp(-lam * f_), np.exp(lam * f_)]).astype(np.float32).reshape(1, 1024)
    dec4 = np.zeros((128, 4 * 512), np.float32)
    for d in range(4):
        dec4[:, d * 512:(d + 1) * 512] = np.exp(-lam * np.abs((128 * d + p_)[:, None] - f_[None, :]))
    wk = np.ascontiguousarray(ins["L1_attn_Wk"])
    wv = np.ascontiguousarray(ins["L1_attn_Wv"])
    wq = ins["L1_attn_Wq"]
    wd = np.ascontiguousarray(ins["L1_attn_W_down_kv"])
    wupk = ins["L1_attn_W_up_k"]
    wupv = ins["L1_attn_W_up_v"]
    per_core = []
    for c in range(8):
        b, p = c // 2, c % 2
        per_core.append({
            "x2full": np.ascontiguousarray(np.concatenate([x2fullT[b][:, :SH], x2fullT[b][:, SH:]], axis=0)),
            "wk": wk, "wv": wv, "wdown": wd,
            "wqh": np.ascontiguousarray(wq[:, 512 * p:512 * (p + 1)]),
            "wupkh": np.ascontiguousarray(wupk[:, 512 * p:512 * (p + 1)]),
            "wupvh": np.ascontiguousarray(wupv[:, 512 * p:512 * (p + 1)]),
            "akq": akq, "dql": dql, "dec4": dec4,
        })
    return per_core


def prep_p5_shared(ins):
    w1 = ins["L1_ffn_W1"]
    w2 = ins["L1_ffn_W2"]
    w2b = np.ascontiguousarray(w2.reshape(8, 8, 128, 8, 128).transpose(0, 3, 2, 1, 4).reshape(8192, 1024))
    return {
        "wr": np.ascontiguousarray(ins["L1_ffn_Wr"]),
        "br": np.ascontiguousarray(ins["L1_ffn_br"].reshape(8, 1)),
        "w1f": np.ascontiguousarray(w1.reshape(8 * 1024, 1024)),
        "w2b": w2b,
        "b1c": np.ascontiguousarray(ins["L1_ffn_b1"].reshape(8 * 8, 128).T.reshape(128, 64)),
        "b2c": np.ascontiguousarray(ins["L1_ffn_b2"].reshape(8 * 8, 128).T.reshape(128, 64)),
    }


# version 09
# speedup vs baseline: 1.2784x; 1.2784x over previous
import numpy as np
import ml_dtypes
from contextlib import ExitStack

import concourse.bass as bass
import concourse.mybir as mybir
from concourse.bass_utils import run_bass_kernel_spmd

F32 = mybir.dt.float32
BF16 = mybir.dt.bfloat16
AF = mybir.ActivationFunctionType
ALU = mybir.AluOpType
bfdt = ml_dtypes.bfloat16


def mk_nc():
    return bass.Bass("TRN2", target_bir_lowering=False, dynamic_dma_scratch_size=16384)

B, S, D = 4, 4096, 1024
NCORES = 8
SH = S // 2
NSC = SH // 512

L0_SCALE = 1.0 / 8.0
L0_C = 16.0
N_HEADS, HD, DL = 16, 64, 256
L1_SCALE = float(1.0 / (np.sqrt(np.float32(1024.0)) + 1e-8))
L1_C = 4.0
LAM = 1e-4



class Sem:
    def __init__(self, handle, step=1):
        self.h = handle
        self.step = step
        self.n = 0

    def bump(self):
        self.n += self.step
        return self.n


class Prog:
    ENGINES = ("gpsimd", "sync", "scalar", "vector", "tensor")

    def __init__(self, nc):
        self.nc = nc
        self.steps = {e: [] for e in self.ENGINES}

    def on(self, eng, fn):
        self.steps[eng].append(fn)

    def inc(self, eng, fn, sem):
        t = sem.bump()
        self.steps[eng].append(lambda E: fn(E).then_inc(sem.h, sem.step))
        return t

    def wait(self, eng, sem, val):
        self.steps[eng].append(lambda E: E.wait_ge(sem.h, val))

    def flush(self):
        if not any(self.steps[e] for e in self.ENGINES):
            return
        with self.nc.Block() as blk:
            for e in self.ENGINES:
                fns = self.steps[e]
                if fns:
                    getattr(blk, e)(lambda eng, fns=fns: [f(eng) for f in fns])
        self.steps = {e: [] for e in self.ENGINES}


class Sems:

    def __init__(self, nc, stack, tag=""):
        self.nc = nc
        self.stack = stack
        self._n16 = 0
        self.tag = tag
        self.WL = Sem(stack.enter_context(nc.semaphore(f"{tag}wl")), 16)
        self.MM = Sem(stack.enter_context(nc.semaphore(f"{tag}mm")), 1)
        self.SE = Sem(stack.enter_context(nc.semaphore(f"{tag}se")), 1)
        self.VE = Sem(stack.enter_context(nc.semaphore(f"{tag}ve")), 1)
        self.ST = Sem(stack.enter_context(nc.semaphore(f"{tag}st")), 16)
        self.XL = Sem(stack.enter_context(nc.semaphore(f"{tag}xl")), 16)
        self.GP = Sem(stack.enter_context(nc.semaphore(f"{tag}gp")), 1)

    def new16(self, name=""):
        self._n16 += 1
        return Sem(self.stack.enter_context(self.nc.semaphore(f"{self.tag}L{self._n16}{name}")), 16)


def mm_group(P, sems, bank_ap, pairs, bump=True, accum_from=None, accum_to=None):
    n = len(pairs)
    tick = None
    for i, (st, mv) in enumerate(pairs):
        first = (i == 0) if accum_from is None else accum_from
        last = (i == n - 1) if accum_to is None else False
        if i == n - 1 and bump:
            tick = P.inc("tensor",
                         lambda t, st=st, mv=mv, first=first, last=last:
                         t.matmul(bank_ap, st, mv, start=first, stop=last), sems.MM)
        else:
            P.on("tensor",
                 lambda t, st=st, mv=mv, first=first, last=last:
                 t.matmul(bank_ap, st, mv, start=first, stop=last))
    return tick


def emit_proj(P, sems, *, nO, nC, nSc, st_ap, mv_ap, banks, bank_ticks, evac):
    nb = len(banks)
    ti = 0
    for o in range(nO):
        for s in range(nSc):
            bi = ti % nb
            bank = banks[bi]
            prev = bank_ticks.get(id(banks[bi]))
            if prev is not None:
                P.wait("tensor", prev[0], prev[1])
            pairs = [(st_ap(c, o), mv_ap(c, s)) for c in range(nC)]
            mmt = mm_group(P, sems, bank[:, :], pairs)
            bank_ticks[id(banks[bi])] = evac(o, s, bank, mmt)
            ti += 1


def inc_chain(P, eng, fn, sem):
    t = P.inc(eng, fn, sem)
    P.wait(eng, sem, t)
    return t


def scalar_copy_evac(P, sems, dst_ap, bank, mmt):
    P.wait("scalar", sems.MM, mmt)
    t = P.inc("scalar", lambda sc: sc.copy(dst_ap, bank[:, :]), sems.SE)
    return (sems.SE, t)



def build_p1(nc, P, sems, io):
    ph0 = ExitStack()
    with ph0:
        pb = [ph0.enter_context(nc.psum_tensor(f"p1_pb{i}", [128, 512], F32)) for i in range(8)]
        OT2 = ph0.enter_context(nc.sbuf_tensor("p1_ot2", [128, 8 * SH], BF16))
        _build_p1_attn(nc, P, sems, io, pb, OT2)
        _build_p1_proj(nc, P, sems, io, OT2, pb)


def _build_p1_attn(nc, P, sems, io, pb, OT2):
  pha = ExitStack()
  with pha:
    KT = [pha.enter_context(nc.sbuf_tensor(f"p1_kt{i}", [128, S], BF16)) for i in range(8)]
    VK = pha.enter_context(nc.sbuf_tensor("p1_vk", [128, 32 * 512], BF16))

    ph = ExitStack()
    with ph:
        WK = ph.enter_context(nc.sbuf_tensor("p1_wk", [128, 8 * 1024], BF16))
        WV = ph.enter_context(nc.sbuf_tensor("p1_wv", [128, 8 * 512], BF16))
        XS = [ph.enter_context(nc.sbuf_tensor(f"p1_xs{k}", [128, 8 * 512], BF16)) for k in range(2)]

        wA = sems.new16("wa")
        sXS = [sems.new16(f"xs{k}") for k in range(2)]
        for i in range(8):
            P.inc("gpsimd", lambda g, i=i: g.dma_start(out=WK[:, i * 1024:(i + 1) * 1024],
                                                       in_=io["wqkT"][i * 128:(i + 1) * 128, 1024:2048]), wA)
            P.inc("gpsimd", lambda g, i=i: g.dma_start(out=WV[:, i * 512:(i + 1) * 512],
                                                       in_=io["wv_own"][i * 128:(i + 1) * 128, :]), wA)
        P.wait("tensor", wA, wA.n)

        xs_used = {}
        xs_load = {}

        def load_xs(si):
            buf = XS[si % 2]
            sem = sXS[si % 2]
            prev = xs_used.get(si - 2)
            if prev is not None:
                P.wait("gpsimd", sems.MM, prev)
            for e in range(8):
                P.inc("gpsimd",
                      lambda g, e=e, buf=buf, si=si:
                      g.dma_start(out=buf[:, e * 512:(e + 1) * 512],
                                  in_=io["xT"][e * 128:(e + 1) * 128, si * 512:(si + 1) * 512]),
                      sem)
            xs_load[si] = (sem, sem.n)

        load_xs(0)
        bank_ticks = {}
        for si in range(8):
            xbuf = XS[si % 2]
            P.wait("tensor", xs_load[si][0], xs_load[si][1])
            emit_proj(P, sems, nO=8, nC=8, nSc=1,
                      st_ap=lambda c, o: WK[:, c * 1024 + o * 128: c * 1024 + (o + 1) * 128],
                      mv_ap=lambda c, _s, xbuf=xbuf: xbuf[:, c * 512:(c + 1) * 512],
                      banks=pb[0:2], bank_ticks=bank_ticks,
                      evac=lambda o, _s, bank, mmt, si=si: scalar_copy_evac(
                          P, sems, KT[o][:, si * 512:(si + 1) * 512], bank, mmt))
            emit_proj(P, sems, nO=4, nC=8, nSc=1,
                      st_ap=lambda c, j, xbuf=xbuf: xbuf[:, c * 512 + j * 128: c * 512 + (j + 1) * 128],
                      mv_ap=lambda c, _s: WV[:, c * 512:(c + 1) * 512],
                      banks=pb[0:2], bank_ticks=bank_ticks,
                      evac=lambda j, _s, bank, mmt, si=si: scalar_copy_evac(
                          P, sems, VK[:, (4 * si + j) * 512:(4 * si + j + 1) * 512], bank, mmt))
            xs_used[si] = sems.MM.n
            if si < 7:
                load_xs(si + 1)
        P.flush()

    ph = ExitStack()
    with ph:
        WQ = ph.enter_context(nc.sbuf_tensor("p1_wq", [128, 8 * 1024], BF16))
        XQ = [ph.enter_context(nc.sbuf_tensor(f"p1_xq{k}", [128, 8 * 512], BF16)) for k in range(2)]
        QT = [ph.enter_context(nc.sbuf_tensor(f"p1_qt{k}", [128, 8 * 512], BF16)) for k in range(2)]
        Ebuf = [ph.enter_context(nc.sbuf_tensor(f"p1_e{k}", [128, 512], BF16)) for k in range(3)]
        ones_bf = ph.enter_context(nc.sbuf_tensor("p1_ones", [128, 1], BF16))
        cbias = ph.enter_context(nc.sbuf_tensor("p1_cb", [128, 1], F32))
        rcol = [ph.enter_context(nc.sbuf_tensor(f"p1_rc{k}", [128, 1], F32)) for k in range(4)]

        wB = sems.new16("wb")
        sXQ = [sems.new16(f"xq{k}") for k in range(2)]
        for i in range(8):
            P.inc("gpsimd", lambda g, i=i: g.dma_start(out=WQ[:, i * 1024:(i + 1) * 1024],
                                                       in_=io["wqkT"][i * 128:(i + 1) * 128, 0:1024]), wB)
        P.inc("gpsimd", lambda g: g.memset(ones_bf[:, :], 1.0), sems.GP)
        P.inc("gpsimd", lambda g: g.memset(cbias[:, :], -L0_C), sems.GP)
        P.wait("tensor", sems.GP, sems.GP.n)
        P.wait("tensor", wB, wB.n)
        P.wait("scalar", sems.GP, sems.GP.n)

        xq_used = {}
        xq_load = {}

        def load_xq(qci):
            buf = XQ[qci % 2]
            sem = sXQ[qci % 2]
            prev = xq_used.get(qci - 2)
            if prev is not None:
                P.wait("gpsimd", sems.MM, prev)
            for e in range(8):
                P.inc("gpsimd",
                      lambda g, e=e, buf=buf, qci=qci:
                      g.dma_start(out=buf[:, e * 512:(e + 1) * 512],
                                  in_=io["xT"][e * 128:(e + 1) * 128, qci * 512:(qci + 1) * 512]),
                      sem)
            xq_load[qci] = (sem, sem.n)

        load_xq(0)
        prev_chunk_evac = None
        bank_ticks = {}
        for qc in range(8):
            qbuf = QT[qc % 2]
            xbuf = XQ[qc % 2]
            P.wait("tensor", xq_load[qc][0], xq_load[qc][1])
            emit_proj(P, sems, nO=8, nC=8, nSc=1,
                      st_ap=lambda c, o: WQ[:, c * 1024 + o * 128: c * 1024 + (o + 1) * 128],
                      mv_ap=lambda c, _s, xbuf=xbuf: xbuf[:, c * 512:(c + 1) * 512],
                      banks=pb[0:2], bank_ticks=bank_ticks,
                      evac=lambda o, _s, bank, mmt, qbuf=qbuf: scalar_copy_evac(
                          P, sems, qbuf[:, o * 512:(o + 1) * 512], bank, mmt))
            xq_used[qc] = sems.MM.n
            if qc < 7:
                load_xq(qc + 1)
            P.wait("tensor", sems.SE, sems.SE.n)
            if prev_chunk_evac is not None:
                P.wait("tensor", sems.VE, prev_chunk_evac)

            for kt in range(32):
                bi = kt % 2
                bank = pb[bi]
                prev = bank_ticks.get(id(pb[bi]))
                if prev is not None:
                    P.wait("tensor", prev[0], prev[1])
                pairs = [(KT[dd][:, kt * 128:(kt + 1) * 128], qbuf[:, dd * 512:(dd + 1) * 512])
                         for dd in range(8)]
                mmt = mm_group(P, sems, bank[:, :], pairs)
                eb = Ebuf[kt % 3]
                P.wait("scalar", sems.MM, mmt)
                et = P.inc("scalar",
                           lambda sc, eb=eb, bank=bank:
                           sc.activation(eb[:, :], bank[:, :], AF.Exp, bias=cbias[:, :], scale=L0_SCALE),
                           sems.SE)
                bank_ticks[id(pb[bi])] = (sems.SE, et)
                P.wait("tensor", sems.SE, et)
                first, last = kt == 0, kt == 31
                for qs in range(4):
                    stE = eb[:, qs * 128:(qs + 1) * 128]
                    mvV = VK[:, kt * 512:(kt + 1) * 512]
                    avb = pb[2 + qs]
                    P.on("tensor",
                         lambda t, stE=stE, mvV=mvV, avb=avb, first=first, last=last:
                         t.matmul(avb[:, :], stE, mvV, start=first, stop=last))
                    dcol = pb[6][:, qs:qs + 1]
                    dfirst = first and qs == 0
                    dlast = last and qs == 3
                    if dlast:
                        mmt2 = P.inc("tensor",
                                     lambda t, stE=stE, dcol=dcol:
                                     t.matmul(dcol, stE, ones_bf[:, :], start=False, stop=True), sems.MM)
                    else:
                        P.on("tensor",
                             lambda t, stE=stE, dcol=dcol, dfirst=dfirst:
                             t.matmul(dcol, stE, ones_bf[:, :], start=dfirst, stop=False))
            P.wait("vector", sems.MM, mmt2)
            for qs in range(4):
                qt = qc * 4 + qs
                hc, r = qt % 8, qt // 8
                rc = rcol[qs]
                inc_chain(P, "vector", lambda v, rc=rc, qs=qs: v.reciprocal(rc[:, :], pb[6][:, qs:qs + 1]), sems.VE)
                dst = OT2[:, hc * SH + r: hc * SH + r + 4 * 511 + 1: 4]
                t2 = inc_chain(P, "vector",
                               lambda v, dst=dst, qs=qs, rc=rc:
                               v.tensor_scalar_mul(dst, pb[2 + qs][:, :], rc[:, :]), sems.VE)
            prev_chunk_evac = t2
        P.flush()


def _build_p1_proj(nc, P, sems, io, OT2, pb):
    ph2 = ExitStack()
    with ph2:
        RES = [ph2.enter_context(nc.sbuf_tensor(f"p1_res{i}", [128, SH], F32)) for i in range(8)]
        _p1_proj_mm(nc, P, sems, io, OT2, pb, RES)
        emit_ln(nc, P, sems, "p1ln", RES, io["x1f"], pb)


def _p1_proj_mm(nc, P, sems, io, OT2, pb, RES):
    ph2 = ExitStack()
    with ph2:
        WP = ph2.enter_context(nc.sbuf_tensor("p1_wp", [128, 8 * 1024], BF16))
        XR = [ph2.enter_context(nc.sbuf_tensor(f"p1_xr{i}", [128, SH], F32)) for i in range(8)]
        bp = ph2.enter_context(nc.sbuf_tensor("p1_bp", [128, 8], F32))

        wP = sems.new16("wp")
        xR = sems.new16("xr")
        for i in range(8):
            P.inc("gpsimd", lambda g, i=i: g.dma_start(out=WP[:, i * 1024:(i + 1) * 1024],
                                                       in_=io["wpT"][i * 128:(i + 1) * 128, :]), wP)
            P.inc("sync", lambda sy, i=i: sy.dma_start(out=XR[i][:, :],
                                                       in_=io["x_res"][i * 128:(i + 1) * 128, :]), xR)
        P.inc("sync", lambda sy: sy.dma_start(out=bp[:, :], in_=io["bproj8"][:, :]), xR)
        P.wait("vector", wP, wP.n)
        P.wait("vector", xR, xR.n)
        for o in range(8):
            P.inc("vector",
                  lambda v, o=o: v.tensor_scalar_add(XR[o][:, :], XR[o][:, :], bp[:, o:o + 1]), sems.VE)
        P.flush()

        bank_ticks = {}

        def proj_evac(o, s, bank, mmt):
            P.wait("vector", sems.MM, mmt)
            t = P.inc("vector",
                      lambda v, o=o, s=s, bank=bank:
                      v.tensor_add(RES[o][:, s * 512:(s + 1) * 512], bank[:, :],
                                   XR[o][:, s * 512:(s + 1) * 512]), sems.VE)
            return (sems.VE, t)

        emit_proj(P, sems, nO=8, nC=8, nSc=4,
                  st_ap=lambda c, o: WP[:, c * 1024 + o * 128: c * 1024 + (o + 1) * 128],
                  mv_ap=lambda c, s: OT2[:, c * SH + s * 512: c * SH + (s + 1) * 512],
                  banks=pb[0:4], bank_ticks=bank_ticks, evac=proj_evac)
        P.flush()



def emit_ln(nc, P, sems, tag, RES, out_f32_dram, pbs):
    ph = ExitStack()
    with ph:
        R16 = [ph.enter_context(nc.sbuf_tensor(f"{tag}_r16{k}", [128, SH], BF16)) for k in range(2)]
        SQ = [ph.enter_context(nc.sbuf_tensor(f"{tag}_sq{k}", [128, SH], BF16)) for k in range(2)]
        onesc = ph.enter_context(nc.sbuf_tensor(f"{tag}_oc", [128, 1], BF16))
        onesr = ph.enter_context(nc.sbuf_tensor(f"{tag}_or", [1, 128], F32))
        zcol = ph.enter_context(nc.sbuf_tensor(f"{tag}_zc", [1, 1], F32))
        ecol = ph.enter_context(nc.sbuf_tensor(f"{tag}_ec", [1, 1], F32))
        rows = ph.enter_context(nc.sbuf_tensor(f"{tag}_rows", [1, 4 * SH], F32))
        AB = ph.enter_context(nc.sbuf_tensor(f"{tag}_abb", [128, SH], F32))
        CB = ph.enter_context(nc.sbuf_tensor(f"{tag}_cbb", [128, SH], F32))
        XF = [ph.enter_context(nc.sbuf_tensor(f"{tag}_xf{k}", [128, SH], F32)) for k in range(2)]

        P.on("gpsimd", lambda g: g.memset(onesc[:, :], 1.0))
        P.on("gpsimd", lambda g: g.memset(onesr[:, :], 1.0))
        P.on("gpsimd", lambda g: g.memset(zcol[:, :], 0.0))
        P.on("gpsimd", lambda g: g.memset(ecol[:, :], 1e-5))
        P.flush()

        pe_tick = {}
        for o in range(8):
            k = o % 2
            if o >= 2:
                P.wait("vector", sems.MM, pe_tick[o - 2])
                P.wait("scalar", sems.MM, pe_tick[o - 2])
            cv = P.inc("vector", lambda v, o=o, k=k: v.tensor_copy(R16[k][:, :], RES[o][:, :]), sems.VE)
            cs = P.inc("scalar", lambda sc, o=o, k=k: sc.activation(SQ[k][:, :], RES[o][:, :], AF.Square), sems.SE)
            P.wait("tensor", sems.VE, cv)
            P.wait("tensor", sems.SE, cs)
            for s in range(NSC):
                first, last = o == 0, o == 7
                P.on("tensor", lambda t, s=s, k=k, first=first, last=last:
                     t.matmul(pbs[s][0:1, :], onesc[:, :], R16[k][:, s * 512:(s + 1) * 512],
                              start=first, stop=last))
                if s < NSC - 1:
                    P.on("tensor", lambda t, s=s, k=k, first=first, last=last:
                         t.matmul(pbs[4 + s][0:1, :], onesc[:, :], SQ[k][:, s * 512:(s + 1) * 512],
                                  start=first, stop=last))
                else:
                    pe_tick[o] = P.inc("tensor", lambda t, s=s, k=k, first=first, last=last:
                                       t.matmul(pbs[4 + s][0:1, :], onesc[:, :], SQ[k][:, s * 512:(s + 1) * 512],
                                                start=first, stop=last), sems.MM)
        P.wait("vector", sems.MM, pe_tick[7])
        for s in range(NSC):
            P.inc("vector", lambda v, s=s: v.tensor_copy(rows[0:1, s * 512:(s + 1) * 512], pbs[s][0:1, :]), sems.VE)
            P.inc("vector", lambda v, s=s: v.tensor_copy(rows[0:1, SH + s * 512: SH + (s + 1) * 512],
                                                         pbs[4 + s][0:1, :]), sems.VE)
        P.flush()

        s1 = rows[0:1, 0:SH]
        s2 = rows[0:1, SH:2 * SH]
        a = rows[0:1, 2 * SH:3 * SH]
        c = rows[0:1, 3 * SH:4 * SH]
        inc_chain(P, "vector", lambda v: v.tensor_scalar_mul(c, s1, 1.0 / 1024.0), sems.VE)
        inc_chain(P, "vector", lambda v: v.tensor_scalar_mul(s2, s2, 1.0 / 1024.0), sems.VE)
        inc_chain(P, "vector", lambda v: v.tensor_tensor(a, c, c, ALU.mult), sems.VE)
        inc_chain(P, "vector", lambda v: v.tensor_tensor(a, s2, a, ALU.subtract), sems.VE)
        P.flush()
        P.inc("scalar", lambda sc: sc.activation(a, a, AF.Ln, bias=ecol[:, :], scale=1.0), sems.SE)
        P.flush()
        P.inc("scalar", lambda sc: sc.activation(a, a, AF.Exp, bias=zcol[:, :], scale=-0.5), sems.SE)
        P.flush()
        for s in range(NSC):
            P.on("tensor", lambda t, s=s: t.matmul(pbs[s][:, :], onesr[:, :],
                                                   a[0:1, s * 512:(s + 1) * 512], start=True, stop=True))
            if s < NSC - 1:
                P.on("tensor", lambda t, s=s: t.matmul(pbs[4 + s][:, :], onesr[:, :],
                                                       c[0:1, s * 512:(s + 1) * 512], start=True, stop=True))
            else:
                bct = P.inc("tensor", lambda t, s=s: t.matmul(pbs[4 + s][:, :], onesr[:, :],
                                                              c[0:1, s * 512:(s + 1) * 512], start=True, stop=True), sems.MM)
        P.wait("vector", sems.MM, bct)
        for s in range(NSC):
            P.inc("vector", lambda v, s=s: v.tensor_copy(AB[:, s * 512:(s + 1) * 512], pbs[s][:, :]), sems.VE)
            P.inc("vector", lambda v, s=s: v.tensor_copy(CB[:, s * 512:(s + 1) * 512], pbs[4 + s][:, :]), sems.VE)
        P.flush()

        sstore = [sems.new16(f"{tag}st{k}") for k in range(2)]
        store_tick = {}
        for o in range(8):
            k = o % 2
            if o >= 2:
                P.wait("vector", store_tick[o - 2][0], store_tick[o - 2][1])
            inc_chain(P, "vector", lambda v, o=o, k=k: v.tensor_tensor(XF[k][:, :], RES[o][:, :], CB[:, :], ALU.subtract), sems.VE)
            vt = inc_chain(P, "vector", lambda v, k=k: v.tensor_tensor(XF[k][:, :], XF[k][:, :], AB[:, :], ALU.mult), sems.VE)
            P.wait("sync", sems.VE, vt)
            t = P.inc("sync", lambda sy, o=o, k=k: sy.dma_start(out=out_f32_dram[o * 128:(o + 1) * 128, :],
                                                                in_=XF[k][:, :]), sstore[k])
            store_tick[o] = (sstore[k], t)
        P.wait("sync", sstore[0], sstore[0].n)
        P.wait("sync", sstore[1], sstore[1].n)
        P.flush()




def build_p2(nc, P, sems, io):
    ph0 = ExitStack()
    with ph0:
        pb = [ph0.enter_context(nc.psum_tensor(f"p2_pb{i}", [128, 512], F32)) for i in range(8)]
        X1bT = ph0.enter_context(nc.sbuf_tensor("p2_x1b", [128, 8 * SH], BF16))
        X1b = [X1bT[:, i * SH:(i + 1) * SH] for i in range(8)]
        B4 = ph0.enter_context(nc.sbuf_tensor("p2_b4", [128, 4 * SH], BF16))

        lx = sems.new16("p2x")
        for i in range(8):
            P.inc("gpsimd", lambda g, i=i: g.dma_start(out=X1b[i],
                                                       in_=io["x1f"][i * 128:(i + 1) * 128, :]), lx)
        _p2_gate(nc, P, sems, io, pb, X1b, B4, lx)

        REST = ph0.enter_context(nc.sbuf_tensor("p2_res", [128, 8 * SH], F32))
        RES = [REST[:, i * SH:(i + 1) * SH] for i in range(8)]
        _p2_units(nc, P, sems, io, pb, X1b, B4, RES)
        emit_ln(nc, P, sems, "p2ln", RES, io["x2f"], pb)


def _p2_gate(nc, P, sems, io, pb, X1b, B4, lx):
    ph = ExitStack()
    with ph:
        GW = ph.enter_context(nc.sbuf_tensor("p2_gw", [128, 8 * 4], F32))
        GB = ph.enter_context(nc.sbuf_tensor("p2_gb", [4, 1], F32))
        G4 = ph.enter_context(nc.sbuf_tensor("p2_g4", [4, SH], F32))
        G1 = ph.enter_context(nc.sbuf_tensor("p2_g1", [1, 4 * SH], F32))
        MW = ph.enter_context(nc.sbuf_tensor("p2_mw", [1, 4 * SH], F32))
        TR = ph.enter_context(nc.sbuf_tensor("p2_tr", [1, 3 * SH], F32))
        onesr = ph.enter_context(nc.sbuf_tensor("p2_or", [1, 128], F32))

        lw = sems.new16("p2gw")
        XG = [ph.enter_context(nc.sbuf_tensor(f"p2_xg{k}", [128, 8 * 512], F32)) for k in range(2)]
        for c in range(8):
            P.inc("gpsimd", lambda g, c=c: g.dma_start(out=GW[:, c * 4:(c + 1) * 4],
                                                       in_=io["gate_w"][c * 128:(c + 1) * 128, :]), lw)
        P.inc("gpsimd", lambda g: g.dma_start(out=GB[:, :], in_=io["gate_b"][:, :]), lw)
        sXG = [sems.new16(f"p2xg{k}") for k in range(2)]
        xg_used = {}
        xg_ld = {}

        def load_xg(si):
            k = si % 2
            prev = xg_used.get(si - 2)
            if prev is not None:
                P.wait("sync", sems.MM, prev)
            for c in range(8):
                P.inc("sync", lambda sy, c=c, k=k, si=si:
                      sy.dma_start(out=XG[k][:, c * 512:(c + 1) * 512],
                                   in_=io["x1f"][c * 128:(c + 1) * 128, si * 512:(si + 1) * 512]), sXG[k])
            xg_ld[si] = (sXG[k], sXG[k].n)

        load_xg(0)
        P.inc("gpsimd", lambda g: g.memset(onesr[:, :], 1.0), sems.GP)
        gp_t = sems.GP.n
        P.wait("tensor", lw, lw.n)
        P.wait("tensor", lx, lx.n)
        for s4 in range(NSC):
            bank = pb[s4 % 2]
            P.wait("tensor", xg_ld[s4][0], xg_ld[s4][1])
            if s4 < NSC - 1:
                load_xg(s4 + 1)
            pairs = [(GW[:, c * 4:(c + 1) * 4], XG[s4 % 2][:, c * 512:(c + 1) * 512]) for c in range(8)]
            mmt = mm_group(P, sems, bank[0:4, :], pairs)
            xg_used[s4] = sems.MM.n
            P.wait("vector", sems.MM, mmt)
            t = inc_chain(P, "vector", lambda v, s4=s4, bank=bank:
                          v.tensor_scalar_add(G4[:, s4 * 512:(s4 + 1) * 512], bank[0:4, :], GB[:, :]), sems.VE)
            P.wait("tensor", sems.VE, t)
        st0 = sems.new16("p2g1")
        P.wait("sync", sems.VE, sems.VE.n)
        for i in range(4):
            P.inc("sync", lambda sy, i=i: sy.dma_start(out=G1[0:1, i * SH:(i + 1) * SH],
                                                       in_=G4[i:i + 1, :]), st0)
        P.wait("vector", st0, st0.n)
        P.wait("scalar", st0, st0.n)

        def row(t_, i):
            return t_[0:1, i * SH:(i + 1) * SH]

        e_r = [row(G1, i) for i in range(4)]
        w_r = [row(MW, i) for i in range(4)]
        m_r, t1, t2 = row(TR, 0), row(TR, 1), row(TR, 2)
        V, E_ = "vector", sems.VE
        inc_chain(P, V, lambda v: v.tensor_tensor(m_r, e_r[0], e_r[1], ALU.max), E_)
        inc_chain(P, V, lambda v: v.tensor_tensor(t1, e_r[2], e_r[3], ALU.max), E_)
        inc_chain(P, V, lambda v: v.tensor_tensor(m_r, m_r, t1, ALU.max), E_)
        for i in range(4):
            inc_chain(P, V, lambda v, i=i: v.tensor_tensor(e_r[i], e_r[i], m_r, ALU.subtract), E_)
            P.wait("scalar", sems.VE, sems.VE.n)
            t = P.inc("scalar", lambda sc, i=i: sc.activation(e_r[i], e_r[i], AF.Exp), sems.SE)
            P.wait(V, sems.SE, t)
        inc_chain(P, V, lambda v: v.tensor_tensor(t1, e_r[0], e_r[1], ALU.add), E_)
        inc_chain(P, V, lambda v: v.tensor_tensor(t2, e_r[2], e_r[3], ALU.add), E_)
        inc_chain(P, V, lambda v: v.tensor_tensor(t1, t1, t2, ALU.add), E_)
        inc_chain(P, V, lambda v: v.reciprocal(t1, t1), E_)
        for i in range(4):
            js = [j for j in range(4) if j != i]
            inc_chain(P, V, lambda v, i=i, j=js[0]: v.tensor_tensor(w_r[i], e_r[j], e_r[i], ALU.is_gt), E_)
            for j in js[1:]:
                inc_chain(P, V, lambda v, i=i, j=j: v.tensor_tensor(t2, e_r[j], e_r[i], ALU.is_gt), E_)
                inc_chain(P, V, lambda v, i=i: v.tensor_tensor(w_r[i], w_r[i], t2, ALU.add), E_)
            inc_chain(P, V, lambda v, i=i: v.tensor_scalar(w_r[i], w_r[i], 2.0, None, ALU.is_lt), E_)
            inc_chain(P, V, lambda v, i=i: v.tensor_tensor(w_r[i], w_r[i], e_r[i], ALU.mult), E_)
            inc_chain(P, V, lambda v, i=i: v.tensor_tensor(w_r[i], w_r[i], t1, ALU.mult), E_)
        P.wait("tensor", sems.VE, sems.VE.n)
        P.wait("tensor", sems.GP, gp_t)
        for i in range(4):
            for s4 in range(NSC):
                bank = pb[2 + (s4 % 2)]
                mmt = P.inc("tensor", lambda t, i=i, s4=s4, bank=bank:
                            t.matmul(bank[:, :], onesr[:, :], w_r[i][0:1, s4 * 512:(s4 + 1) * 512],
                                     start=True, stop=True), sems.MM)
                P.wait("vector", sems.MM, mmt)
                t = inc_chain(P, V, lambda v, i=i, s4=s4, bank=bank:
                              v.tensor_copy(B4[:, i * SH + s4 * 512: i * SH + (s4 + 1) * 512], bank[:, :]), sems.VE)
                P.wait("tensor", sems.VE, t)
        P.flush()


def _p2_units(nc, P, sems, io, pb, X1b, B4, RES, which=None):
    ph = ExitStack()
    with ph:
        HT = ph.enter_context(nc.sbuf_tensor("p2_ht", [128, 32 * 1024], BF16))
        W1S = [ph.enter_context(nc.sbuf_tensor(f"p2_w1s{k}", [128, 8 * 256], BF16)) for k in range(2)]
        W2S = [ph.enter_context(nc.sbuf_tensor(f"p2_w2s{k}", [128, 32 * 128], BF16)) for k in range(2)]
        X1R = [ph.enter_context(nc.sbuf_tensor("p2_x1r0", [128, 512], F32))]
        B1C = ph.enter_context(nc.sbuf_tensor("p2_b1c", [128, 6 * 32], F32))
        B2C = ph.enter_context(nc.sbuf_tensor("p2_b2c", [128, 6 * 8], F32))
        TMP = ph.enter_context(nc.sbuf_tensor("p2_tmp", [128, 512], F32))
        TMPB = ph.enter_context(nc.sbuf_tensor("p2_tmpb", [128, 512], BF16))

        lb = sems.new16("p2b")
        P.inc("sync", lambda sy: sy.dma_start(out=B1C[:, 0:64], in_=io["sb1c"][:, :]), lb)
        P.inc("sync", lambda sy: sy.dma_start(out=B1C[:, 64:192], in_=io["rb1c"][:, :]), lb)
        P.inc("sync", lambda sy: sy.dma_start(out=B2C[:, 0:16], in_=io["sb2c"][:, :]), lb)
        P.inc("sync", lambda sy: sy.dma_start(out=B2C[:, 16:48], in_=io["rb2c"][:, :]), lb)
        P.wait("scalar", lb, lb.n)
        P.wait("vector", lb, lb.n)

        units = []
        for n in range(2):
            units.append((io["sw1f"], n * 1024, io["sw2b"], n * 1024, n * 32, n * 8, "s", n))
        for e in range(4):
            units.append((io["rw1f"], e * 1024, io["rw2b"], e * 1024, 64 + e * 32, 16 + e * 8, "r", e))

        sW1 = [sems.new16(f"p2w1{k}") for k in range(2)]
        sW2 = [sems.new16(f"p2w2{k}") for k in range(2)]
        sXR = [sems.new16("p2xr0")]
        w1_used = {}
        w2_used = {}
        xr_used = {}
        w1_ld = {}
        w2_ld = {}
        xr_ld = {}

        def load_w1(idx, wdram, wrow, fs):
            k = idx % 2
            prev = w1_used.get(idx - 2)
            if prev is not None:
                P.wait("gpsimd", sems.MM, prev)
            for c in range(8):
                P.inc("gpsimd", lambda g, c=c, k=k, wdram=wdram, wrow=wrow, fs=fs:
                      g.dma_start(out=W1S[k][:, c * 256:(c + 1) * 256],
                                  in_=wdram[wrow + c * 128: wrow + (c + 1) * 128, fs * 256:(fs + 1) * 256]), sW1[k])
            w1_ld[idx] = (sW1[k], sW1[k].n)

        def load_w2(idx, wdram, wrow, o):
            k = idx % 2
            prev = w2_used.get(idx - 2)
            if prev is not None:
                P.wait("gpsimd", sems.MM, prev)
            P.inc("gpsimd", lambda g, k=k, wdram=wdram, wrow=wrow, o=o:
                  g.dma_start(out=W2S[k][:, :],
                              in_=wdram[(wrow + o * 128):(wrow + (o + 1) * 128), :]), sW2[k])
            w2_ld[idx] = (sW2[k], sW2[k].n)

        def load_xr(idx, p2, o, s4):
            prev = xr_used.get(idx - 1)
            if prev is not None:
                P.wait("sync", sems.VE, prev)
            P.inc("sync", lambda sy, o=o, p2=p2, s4=s4:
                  sy.dma_start(out=X1R[0][:, :],
                               in_=io["x1f"][o * 128:(o + 1) * 128,
                                             p2 * 1024 + s4 * 512: p2 * 1024 + (s4 + 1) * 512]), sXR[0])
            xr_ld[idx] = (sXR[0], sXR[0].n)

        bank_ticks = {}
        for p2 in range(2):
            for ui, (w1d, w1r, w2d, w2r, b1b, b2b, kind, eidx) in enumerate(units):
                if which is not None and ui not in which:
                    continue
                gi = p2 * 6 + ui
                w1idx0 = gi * 16
                for fs in range(16):
                    load_w1(w1idx0 + fs, w1d, w1r, fs)
                    P.wait("tensor", w1_ld[w1idx0 + fs][0], w1_ld[w1idx0 + fs][1])
                    for fc2 in range(2):
                        fc = fs * 2 + fc2
                        for s4 in range(2):
                            bi = (fc * 2 + s4) % 2
                            bank = pb[bi]
                            prev = bank_ticks.get(id(bank))
                            if prev is not None:
                                P.wait("tensor", prev[0], prev[1])
                            pairs = [(W1S[(w1idx0 + fs) % 2][:, c * 256 + fc2 * 128: c * 256 + (fc2 + 1) * 128],
                                      X1b[c][:, p2 * 1024 + s4 * 512: p2 * 1024 + (s4 + 1) * 512])
                                     for c in range(8)]
                            mmt = mm_group(P, sems, bank[:, :], pairs)
                            P.wait("scalar", sems.MM, mmt)
                            et = P.inc("scalar",
                                       lambda sc, fc=fc, s4=s4, bank=bank, b1b=b1b:
                                       sc.activation(HT[:, fc * 1024 + s4 * 512: fc * 1024 + (s4 + 1) * 512],
                                                     bank[:, :], AF.Relu, bias=B1C[:, b1b + fc: b1b + fc + 1],
                                                     scale=1.0), sems.SE)
                            bank_ticks[id(bank)] = (sems.SE, et)
                    w1_used[w1idx0 + fs] = sems.MM.n
                P.wait("tensor", sems.SE, sems.SE.n)
                if gi == 0 and "dbg_ht" in io:
                    dbg_s = sems.new16("dbght")
                    P.wait("gpsimd", sems.SE, sems.SE.n)
                    P.inc("gpsimd", lambda g: g.dma_start(out=io["dbg_ht"], in_=HT[:, :]), dbg_s)
                    P.wait("gpsimd", dbg_s, dbg_s.n)
                w2idx0 = gi * 8
                load_w2(w2idx0, w2d, w2r, 0)
                for o in range(8):
                    if o < 7:
                        load_w2(w2idx0 + o + 1, w2d, w2r, o + 1)
                    P.wait("tensor", w2_ld[w2idx0 + o][0], w2_ld[w2idx0 + o][1])
                    for s4 in range(2):
                        if kind == "s" and eidx == 0:
                            load_xr((p2 * 8 + o) * 2 + s4, p2, o, s4)
                        bi = 2 + s4 % 2
                        bank = pb[bi]
                        prev = bank_ticks.get(id(bank))
                        if prev is not None:
                            P.wait("tensor", prev[0], prev[1])
                        pairs = [(W2S[(w2idx0 + o) % 2][:, fc * 128:(fc + 1) * 128],
                                  HT[:, fc * 1024 + s4 * 512: fc * 1024 + (s4 + 1) * 512])
                                 for fc in range(32)]
                        mmt = mm_group(P, sems, bank[:, :], pairs)
                        P.wait("vector", sems.MM, mmt)
                        dres = RES[o][:, p2 * 1024 + s4 * 512: p2 * 1024 + (s4 + 1) * 512]
                        b2col = B2C[:, b2b + o: b2b + o + 1]
                        if kind == "s":
                            inc_chain(P, "vector",
                                      lambda v, bank=bank, b2col=b2col:
                                      v.tensor_scalar(TMP[:, :], bank[:, :], b2col, 0.5, ALU.add, ALU.mult),
                                      sems.VE)
                            if eidx == 0:
                                xi = (p2 * 8 + o) * 2 + s4
                                P.wait("vector", xr_ld[xi][0], xr_ld[xi][1])
                                t = inc_chain(P, "vector",
                                              lambda v, dres=dres:
                                              v.tensor_tensor(dres, TMP[:, :], X1R[0][:, :], ALU.add),
                                              sems.VE)
                                xr_used[xi] = sems.VE.n
                            else:
                                t = inc_chain(P, "vector",
                                              lambda v, dres=dres: v.tensor_tensor(dres, TMP[:, :], dres, ALU.add),
                                              sems.VE)
                        else:
                            inc_chain(P, "vector",
                                      lambda v, bank=bank, b2col=b2col:
                                      v.tensor_scalar_add(TMPB[:, :], bank[:, :], b2col), sems.VE)
                            inc_chain(P, "vector",
                                      lambda v, eidx=eidx, p2=p2, s4=s4:
                                      v.tensor_tensor(TMP[:, :], TMPB[:, :],
                                                      B4[:, eidx * SH + p2 * 1024 + s4 * 512: eidx * SH + p2 * 1024 + (s4 + 1) * 512],
                                                      ALU.mult), sems.VE)
                            t = inc_chain(P, "vector",
                                          lambda v, dres=dres: v.tensor_tensor(dres, TMP[:, :], dres, ALU.add),
                                          sems.VE)
                        bank_ticks[id(bank)] = (sems.VE, t)
                    w2_used[w2idx0 + o] = sems.MM.n
        P.flush()



def build_p4a(nc, P, sems, io):
    ph0 = ExitStack()
    with ph0:
        pb = [ph0.enter_context(nc.psum_tensor(f"p4_pb{i}", [128, 512], F32)) for i in range(8)]
        KR = ph0.enter_context(nc.sbuf_tensor("p4_kr", [128, 4 * S], BF16))
        VA = ph0.enter_context(nc.sbuf_tensor("p4_va", [128, 32 * 520], BF16))
        QT = ph0.enter_context(nc.sbuf_tensor("p4_qt", [128, 4 * S], BF16))

        ph = ExitStack()
        with ph:
            LT = ph.enter_context(nc.sbuf_tensor("p4_lt", [128, 2 * S], BF16))
            ph2 = ExitStack()
            with ph2:
                WKV = ph2.enter_context(nc.sbuf_tensor("p4_wkv", [128, 16 * 1024], BF16))
                WQH = ph2.enter_context(nc.sbuf_tensor("p4_wqh", [128, 8 * 512], BF16))
                WD = ph2.enter_context(nc.sbuf_tensor("p4_wd", [128, 8 * 256], BF16))
                XS = [ph2.enter_context(nc.sbuf_tensor(f"p4_xs{k}", [128, 8 * 512], BF16)) for k in range(2)]
                KVC = [ph2.enter_context(nc.sbuf_tensor(f"p4_kvc{k}", [128, 8 * 512], BF16)) for k in range(2)]

                lw = sems.new16("p4w")
                for i in range(8):
                    P.inc("gpsimd", lambda g, i=i: g.dma_start(out=WKV[:, i * 1024:(i + 1) * 1024],
                                                               in_=io["wk"][i * 128:(i + 1) * 128, :]), lw)
                    P.inc("gpsimd", lambda g, i=i: g.dma_start(out=WKV[:, (8 + i) * 1024:(9 + i) * 1024],
                                                               in_=io["wv"][i * 128:(i + 1) * 128, :]), lw)
                    P.inc("gpsimd", lambda g, i=i: g.dma_start(out=WQH[:, i * 512:(i + 1) * 512],
                                                               in_=io["wqh"][i * 128:(i + 1) * 128, :]), lw)
                    P.inc("gpsimd", lambda g, i=i: g.dma_start(out=WD[:, i * 256:(i + 1) * 256],
                                                               in_=io["wdown"][i * 128:(i + 1) * 128, :]), lw)
                P.wait("tensor", lw, lw.n)

                sXS = [sems.new16(f"p4xs{k}") for k in range(2)]
                xs_used, xs_ld = {}, {}

                def load_xs(si):
                    k = si % 2
                    prev = xs_used.get(si - 2)
                    if prev is not None:
                        P.wait("gpsimd", sems.MM, prev)
                    half, sl = divmod(si, 4)
                    for c in range(8):
                        P.inc("gpsimd", lambda g, c=c, k=k, half=half, sl=sl:
                              g.dma_start(out=XS[k][:, c * 512:(c + 1) * 512],
                                          in_=io["x2full"][half * 1024 + c * 128: half * 1024 + (c + 1) * 128,
                                                           sl * 512:(sl + 1) * 512]), sXS[k])
                    xs_ld[si] = (sXS[k], sXS[k].n)

                load_xs(0)
                bank_ticks = {}
                kvc_t = {}
                for si in range(8):
                    k = si % 2
                    P.wait("tensor", xs_ld[si][0], xs_ld[si][1])
                    if si < 7:
                        load_xs(si + 1)
                    for o in range(8):
                        bi = o % 2
                        bank = pb[bi]
                        prev = bank_ticks.get(id(bank))
                        if prev is not None:
                            P.wait("tensor", prev[0], prev[1])
                        pairs = [(WKV[:, c * 1024 + o * 128: c * 1024 + (o + 1) * 128],
                                  XS[k][:, c * 512:(c + 1) * 512]) for c in range(8)]
                        pairs += [(WKV[:, (8 + c) * 1024 + o * 128: (8 + c) * 1024 + (o + 1) * 128],
                                   XS[k][:, c * 512:(c + 1) * 512]) for c in range(8)]
                        mmt = mm_group(P, sems, bank[:, :], pairs)
                        bank_ticks[id(bank)] = scalar_copy_evac(
                            P, sems, KVC[k][:, o * 512:(o + 1) * 512], bank, mmt)
                    kvc_done = sems.SE.n
                    for o in range(4):
                        bi = o % 2
                        bank = pb[bi]
                        prev = bank_ticks.get(id(bank))
                        if prev is not None:
                            P.wait("tensor", prev[0], prev[1])
                        pairs = [(WQH[:, c * 512 + o * 128: c * 512 + (o + 1) * 128],
                                  XS[k][:, c * 512:(c + 1) * 512]) for c in range(8)]
                        mmt = mm_group(P, sems, bank[:, :], pairs)
                        bank_ticks[id(bank)] = scalar_copy_evac(
                            P, sems, QT[:, o * S + si * 512: o * S + (si + 1) * 512], bank, mmt)
                    xs_used[si] = sems.MM.n
                    P.wait("tensor", sems.SE, kvc_done)
                    prevk = kvc_t.get(k)
                    for o in range(2):
                        bi = o % 2
                        bank = pb[bi]
                        prev = bank_ticks.get(id(bank))
                        if prev is not None:
                            P.wait("tensor", prev[0], prev[1])
                        pairs = [(WD[:, c * 256 + o * 128: c * 256 + (o + 1) * 128],
                                  KVC[k][:, c * 512:(c + 1) * 512]) for c in range(8)]
                        mmt = mm_group(P, sems, bank[:, :], pairs)
                        bank_ticks[id(bank)] = scalar_copy_evac(
                            P, sems, LT[:, o * S + si * 512: o * S + (si + 1) * 512], bank, mmt)
                    kvc_t[k] = sems.MM.n
                P.flush()

            ph3 = ExitStack()
            with ph3:
                WUK = ph3.enter_context(nc.sbuf_tensor("p4_wuk", [128, 2 * 512], BF16))
                WUV = ph3.enter_context(nc.sbuf_tensor("p4_wuv", [128, 2 * 512], BF16))
                lw2 = sems.new16("p4u")
                for i in range(2):
                    P.inc("gpsimd", lambda g, i=i: g.dma_start(out=WUK[:, i * 512:(i + 1) * 512],
                                                               in_=io["wupkh"][i * 128:(i + 1) * 128, :]), lw2)
                    P.inc("gpsimd", lambda g, i=i: g.dma_start(out=WUV[:, i * 512:(i + 1) * 512],
                                                               in_=io["wupvh"][i * 128:(i + 1) * 128, :]), lw2)
                P.wait("tensor", lw2, lw2.n)
                bank_ticks = {}
                for o in range(4):
                    for sc_ in range(8):
                        bi = (o * 8 + sc_) % 2
                        bank = pb[bi]
                        prev = bank_ticks.get(id(bank))
                        if prev is not None:
                            P.wait("tensor", prev[0], prev[1])
                        pairs = [(WUK[:, lc * 512 + o * 128: lc * 512 + (o + 1) * 128],
                                  LT[:, lc * S + sc_ * 512: lc * S + (sc_ + 1) * 512]) for lc in range(2)]
                        mmt = mm_group(P, sems, bank[:, :], pairs)
                        bank_ticks[id(bank)] = scalar_copy_evac(
                            P, sems, KR[:, o * S + sc_ * 512: o * S + (sc_ + 1) * 512], bank, mmt)
                P.inc("gpsimd", lambda g: g.memset(VA[:, 64:32 * 520:65], 1.0), sems.GP)
                P.wait("vector", sems.GP, sems.GP.n)
                for kt in range(32):
                    bi = kt % 2
                    bank = pb[bi]
                    prev = bank_ticks.get(id(bank))
                    if prev is not None:
                        P.wait("tensor", prev[0], prev[1])
                    pairs = [(LT[:, lc * S + kt * 128: lc * S + (kt + 1) * 128],
                              WUV[:, lc * 512:(lc + 1) * 512]) for lc in range(2)]
                    mmt = mm_group(P, sems, bank[:, :], pairs)
                    P.wait("vector", sems.MM, mmt)
                    t = inc_chain(P, "vector",
                                  lambda v, kt=kt, bank=bank:
                                  v.tensor_copy(VA[:, kt * 520: kt * 520 + 520].rearrange("p (h c) -> p h c", h=8)[:, :, 0:64],
                                                bank[:, :].rearrange("p (h c) -> p h c", h=8)), sems.VE)
                    bank_ticks[id(bank)] = (sems.VE, t)
                P.wait("tensor", sems.VE, sems.VE.n)
                P.flush()

        _p4_attn(nc, P, sems, io, pb, KR, VA, QT)


def _p4_attn(nc, P, sems, io, pb, KR, VA, QT):
    ph = ExitStack()
    with ph:
        OT = ph.enter_context(nc.sbuf_tensor("p4_ot", [128, 4 * S], BF16))
        QPM = ph.enter_context(nc.sbuf_tensor("p4_qpm", [128, 2 * 4 * 512], BF16))
        AKQ = ph.enter_context(nc.sbuf_tensor("p4_akq", [128, 256], F32))
        DEC = ph.enter_context(nc.sbuf_tensor("p4_dec", [128, 4 * 512], F32))
        DQL = ph.enter_context(nc.sbuf_tensor("p4_dql", [1, 1024], F32))
        CB = ph.enter_context(nc.sbuf_tensor("p4_cb", [128, 1], F32))
        Ebuf = [ph.enter_context(nc.sbuf_tensor(f"p4_e{k}", [128, 512], BF16)) for k in range(3)]
        TMX = ph.enter_context(nc.sbuf_tensor("p4_tmx", [128, 512], F32))
        RC = ph.enter_context(nc.sbuf_tensor("p4_rc", [1, 512], F32))
        BC = ph.enter_context(nc.sbuf_tensor("p4_bc", [64, 512], F32))
        DQB = ph.enter_context(nc.sbuf_tensor("p4_dqb", [128, 2 * 512], BF16))
        onesr = ph.enter_context(nc.sbuf_tensor("p4_or", [1, 128], F32))

        lw = sems.new16("p4c")
        P.inc("sync", lambda sy: sy.dma_start(out=AKQ[:, :], in_=io["akq"][:, :]), lw)
        P.inc("sync", lambda sy: sy.dma_start(out=DEC[:, :], in_=io["dec4"][:, :]), lw)
        P.inc("sync", lambda sy: sy.dma_start(out=DQL[:, :], in_=io["dql"][:, :]), lw)
        P.inc("gpsimd", lambda g: g.memset(CB[:, :], -L1_C), sems.GP)
        P.inc("gpsimd", lambda g: g.memset(onesr[:, :], 1.0), sems.GP)
        P.wait("tensor", lw, lw.n)
        P.wait("tensor", sems.GP, sems.GP.n)
        P.wait("scalar", sems.GP, sems.GP.n)
        P.wait("vector", lw, lw.n)
        P.flush()

        sc_ticks = {}
        for qc in range(8):
            for sgn in range(2):
                bmt = P.inc("tensor", lambda t, sgn=sgn:
                            t.matmul(pb[6][:, :], onesr[:, :], DQL[0:1, sgn * 512:(sgn + 1) * 512],
                                     start=True, stop=True), sems.MM)
                P.wait("vector", sems.MM, bmt)
                ct = inc_chain(P, "vector", lambda v, sgn=sgn:
                               v.tensor_copy(DQB[:, sgn * 512:(sgn + 1) * 512], pb[6][:, :]), sems.VE)
                P.wait("tensor", sems.VE, ct)
            P.wait("vector", sems.SE, sems.SE.n)
            for ht in range(4):
                for sgn in range(2):
                    inc_chain(P, "vector",
                              lambda v, ht=ht, sgn=sgn, qc=qc:
                              v.tensor_tensor(QPM[:, (sgn * 4 + ht) * 512:(sgn * 4 + ht + 1) * 512],
                                              QT[:, ht * S + qc * 512: ht * S + (qc + 1) * 512],
                                              DQB[:, sgn * 512:(sgn + 1) * 512], ALU.mult), sems.VE)
            qpm_t = sems.VE.n
            P.wait("tensor", sems.VE, qpm_t)

            for grp in range(2):
                for kt in range(32):
                    cls = 0 if kt < 4 * qc else (2 if kt > 4 * qc + 3 else 1)
                    for hh in range(4):
                        h = grp * 4 + hh
                        ht, m = h // 2, h % 2
                        si = (kt * 4 + hh) % 2
                        sbank = pb[si]
                        prev = sc_ticks.get(si)
                        if prev is not None:
                            P.wait("tensor", prev[0], prev[1])
                        st = KR[64 * m:64 * m + 64, ht * S + kt * 128: ht * S + (kt + 1) * 128]
                        if cls == 1:
                            mv = QT[64 * m:64 * m + 64, ht * S + qc * 512: ht * S + (qc + 1) * 512]
                        else:
                            sgn = 0 if cls == 0 else 1
                            mv = QPM[64 * m:64 * m + 64, (sgn * 4 + ht) * 512:(sgn * 4 + ht + 1) * 512]
                        mmt = P.inc("tensor",
                                    lambda t, st=st, mv=mv, sbank=sbank, m=m:
                                    t.matmul(sbank[:, :], st, mv, start=True, stop=True,
                                             tile_position=(64 * m, 0)), sems.MM)
                        eb = Ebuf[(kt * 4 + hh) % 3]
                        acol = AKQ[:, qc * 32 + kt: qc * 32 + kt + 1]
                        if cls == 1:
                            P.wait("vector", sems.MM, mmt)
                            dct = inc_chain(P, "vector",
                                            lambda v, sbank=sbank, kt=kt, qc=qc:
                                            v.tensor_tensor(TMX[:, :], sbank[:, :],
                                                            DEC[:, (kt - 4 * qc) * 512:(kt - 4 * qc + 1) * 512],
                                                            ALU.mult), sems.VE)
                            P.wait("scalar", sems.VE, dct)
                            et = P.inc("scalar",
                                       lambda sc, eb=eb, acol=acol:
                                       sc.activation(eb[:, :], TMX[:, :], AF.Exp, bias=CB[:, :], scale=acol), sems.SE)
                            sc_ticks[si] = (sems.VE, dct)
                        else:
                            P.wait("scalar", sems.MM, mmt)
                            et = P.inc("scalar",
                                       lambda sc, eb=eb, sbank=sbank, acol=acol:
                                       sc.activation(eb[:, :], sbank[:, :], AF.Exp, bias=CB[:, :], scale=acol), sems.SE)
                            sc_ticks[si] = (sems.SE, et)
                        P.wait("tensor", sems.SE, et)
                        avb = pb[2 + hh]
                        stV = VA[:, kt * 520 + 65 * h: kt * 520 + 65 * h + 65]
                        first, last = kt == 0, kt == 31
                        if last:
                            avt = P.inc("tensor",
                                        lambda t, stV=stV, eb=eb, avb=avb, first=first:
                                        t.matmul(avb[0:65, :], stV, eb[:, :], start=first, stop=True), sems.MM)
                        else:
                            P.on("tensor",
                                 lambda t, stV=stV, eb=eb, avb=avb, first=first:
                                 t.matmul(avb[0:65, :], stV, eb[:, :], start=first, stop=False))
                P.wait("vector", sems.MM, avt)
                for hh in range(4):
                    h = grp * 4 + hh
                    avb = pb[2 + hh]
                    rct = inc_chain(P, "vector", lambda v, avb=avb: v.reciprocal(RC[:, :], avb[64:65, :]), sems.VE)
                    P.wait("tensor", sems.VE, rct)
                    bmt = P.inc("tensor", lambda t: t.matmul(pb[6][0:64, :], onesr[0:1, 0:64], RC[0:1, :],
                                                             start=True, stop=True), sems.MM)
                    P.wait("vector", sems.MM, bmt)
                    inc_chain(P, "vector", lambda v: v.tensor_copy(BC[:, :], pb[6][0:64, :]), sems.VE)
                    ht, m = h // 2, h % 2
                    t = inc_chain(P, "vector",
                                  lambda v, avb=avb, ht=ht, m=m, qc=qc:
                                  v.tensor_tensor(OT[64 * m:64 * m + 64, ht * S + qc * 512: ht * S + (qc + 1) * 512],
                                                  avb[0:64, :], BC[:, :], ALU.mult), sems.VE)
                P.wait("tensor", sems.VE, sems.VE.n)
        st0 = sems.new16("p4st")
        P.wait("gpsimd", sems.VE, sems.VE.n)
        for i in range(4):
            for hf in range(2):
                P.inc("gpsimd", lambda g, i=i, hf=hf:
                      g.dma_start(out=io["otf"][hf * 512 + i * 128: hf * 512 + (i + 1) * 128, :],
                                  in_=OT[:, i * S + hf * SH: i * S + (hf + 1) * SH]), st0)
        P.wait("gpsimd", st0, st0.n)
        P.flush()



def build_p42(nc, P, sems, io):
    ph0 = ExitStack()
    with ph0:
        pb = [ph0.enter_context(nc.psum_tensor(f"p42_pb{i}", [128, 512], F32)) for i in range(8)]
        OT = ph0.enter_context(nc.sbuf_tensor("p42_ot", [128, 8 * SH], BF16))
        OT0 = ph0.enter_context(nc.sbuf_tensor("p42_ot0", [128, 8], F32))
        Q0 = ph0.enter_context(nc.sbuf_tensor("p42_q0", [128, 8], BF16))
        phm = ExitStack()
        with phm:
            QT = phm.enter_context(nc.sbuf_tensor("p42_qt", [128, 8 * SH], BF16))
            LT = phm.enter_context(nc.sbuf_tensor("p42_lt", [128, 2 * S], BF16))
            _p42_main(nc, P, sems, io, pb, QT, OT, LT, OT0, Q0)
        _p42_proj(nc, P, sems, io, pb, OT, OT0)


def _p42_main(nc, P, sems, io, pb, QT, OT, LT, OT0, Q0):
        ph = ExitStack()
        with ph:
            WKV = ph.enter_context(nc.sbuf_tensor("p42_wkv", [128, 16 * 1024], BF16))
            WQ = ph.enter_context(nc.sbuf_tensor("p42_wq", [128, 8 * 1024], BF16))
            WD = ph.enter_context(nc.sbuf_tensor("p42_wd", [128, 8 * 256], BF16))
            XS = [ph.enter_context(nc.sbuf_tensor(f"p42_xs{k}", [128, 8 * 512], BF16)) for k in range(2)]
            KVC = [ph.enter_context(nc.sbuf_tensor(f"p42_kvc{k}", [128, 8 * 512], BF16)) for k in range(2)]
            X20B = ph.enter_context(nc.sbuf_tensor("p42_x20", [128, 8], BF16))

            lw = sems.new16("p42w")
            for i in range(8):
                P.inc("gpsimd", lambda g, i=i: g.dma_start(out=WKV[:, i * 1024:(i + 1) * 1024],
                                                           in_=io["wk"][i * 128:(i + 1) * 128, :]), lw)
                P.inc("gpsimd", lambda g, i=i: g.dma_start(out=WKV[:, (8 + i) * 1024:(9 + i) * 1024],
                                                           in_=io["wv"][i * 128:(i + 1) * 128, :]), lw)
                P.inc("gpsimd", lambda g, i=i: g.dma_start(out=WQ[:, i * 1024:(i + 1) * 1024],
                                                           in_=io["wq"][i * 128:(i + 1) * 128, :]), lw)
                P.inc("gpsimd", lambda g, i=i: g.dma_start(out=WD[:, i * 256:(i + 1) * 256],
                                                           in_=io["wdown"][i * 128:(i + 1) * 128, :]), lw)
                P.inc("gpsimd", lambda g, i=i: g.dma_start(out=X20B[:, i:i + 1],
                                                           in_=io["x20"][i * 128:(i + 1) * 128, :]), lw)
            P.wait("tensor", lw, lw.n)

            sXS = [sems.new16(f"p42x{k}") for k in range(2)]
            xs_used, xs_ld = {}, {}

            def load_xs(si):
                k = si % 2
                prev = xs_used.get(si - 2)
                if prev is not None:
                    P.wait("gpsimd", sems.MM, prev)
                for c in range(8):
                    P.inc("gpsimd", lambda g, c=c, k=k, si=si:
                          g.dma_start(out=XS[k][:, c * 512:(c + 1) * 512],
                                      in_=io["x2full"][c * 128:(c + 1) * 128, si * 512:(si + 1) * 512]), sXS[k])
                xs_ld[si] = (sXS[k], sXS[k].n)

            load_xs(0)
            bank_ticks = {}
            for si in range(8):
                k = si % 2
                P.wait("tensor", xs_ld[si][0], xs_ld[si][1])
                if si < 7:
                    load_xs(si + 1)
                for o in range(8):
                    bi = o % 2
                    bank = pb[bi]
                    prev = bank_ticks.get(id(bank))
                    if prev is not None:
                        P.wait("tensor", prev[0], prev[1])
                    pairs = [(WKV[:, c * 1024 + o * 128: c * 1024 + (o + 1) * 128],
                              XS[k][:, c * 512:(c + 1) * 512]) for c in range(8)]
                    pairs += [(WKV[:, (8 + c) * 1024 + o * 128: (8 + c) * 1024 + (o + 1) * 128],
                               XS[k][:, c * 512:(c + 1) * 512]) for c in range(8)]
                    mmt = mm_group(P, sems, bank[:, :], pairs)
                    bank_ticks[id(bank)] = scalar_copy_evac(
                        P, sems, KVC[k][:, o * 512:(o + 1) * 512], bank, mmt)
                kvc_done = sems.SE.n
                if si < 4:
                    for o in range(8):
                        bi = o % 2
                        bank = pb[bi]
                        prev = bank_ticks.get(id(bank))
                        if prev is not None:
                            P.wait("tensor", prev[0], prev[1])
                        pairs = [(WQ[:, c * 1024 + o * 128: c * 1024 + (o + 1) * 128],
                                  XS[k][:, c * 512:(c + 1) * 512]) for c in range(8)]
                        mmt = mm_group(P, sems, bank[:, :], pairs)
                        bank_ticks[id(bank)] = scalar_copy_evac(
                            P, sems, QT[:, o * SH + si * 512: o * SH + (si + 1) * 512], bank, mmt)
                xs_used[si] = sems.MM.n
                P.wait("tensor", sems.SE, kvc_done)
                for o in range(2):
                    bi = o % 2
                    bank = pb[bi]
                    prev = bank_ticks.get(id(bank))
                    if prev is not None:
                        P.wait("tensor", prev[0], prev[1])
                    pairs = [(WD[:, c * 256 + o * 128: c * 256 + (o + 1) * 128],
                              KVC[k][:, c * 512:(c + 1) * 512]) for c in range(8)]
                    mmt = mm_group(P, sems, bank[:, :], pairs)
                    bank_ticks[id(bank)] = scalar_copy_evac(
                        P, sems, LT[:, o * S + si * 512: o * S + (si + 1) * 512], bank, mmt)
            for o in range(8):
                bi = o % 2
                bank = pb[bi]
                prev = bank_ticks.get(id(bank))
                if prev is not None:
                    P.wait("tensor", prev[0], prev[1])
                pairs = [(WQ[:, c * 1024 + o * 128: c * 1024 + (o + 1) * 128],
                          X20B[:, c:c + 1]) for c in range(8)]
                mmt = mm_group(P, sems, bank[:, 0:1], pairs)
                P.wait("scalar", sems.MM, mmt)
                t = P.inc("scalar", lambda sc, o=o, bank=bank: sc.copy(Q0[:, o:o + 1], bank[:, 0:1]), sems.SE)
                bank_ticks[id(bank)] = (sems.SE, t)
            P.flush()

        for hp in range(2):
            _p42_headhalf(nc, P, sems, io, pb, QT, OT, LT, OT0, Q0, hp)


def _p42_headhalf(nc, P, sems, io, pb, QT, OT, LT, OT0, Q0, hp):
    ph = ExitStack()
    with ph:
        KR = ph.enter_context(nc.sbuf_tensor(f"p42_kr{hp}", [128, 4 * S], BF16))
        VA = ph.enter_context(nc.sbuf_tensor(f"p42_va{hp}", [128, 32 * 520], BF16))
        WUK = ph.enter_context(nc.sbuf_tensor(f"p42_wuk{hp}", [128, 2 * 512], BF16))
        WUV = ph.enter_context(nc.sbuf_tensor(f"p42_wuv{hp}", [128, 2 * 512], BF16))
        QPM = ph.enter_context(nc.sbuf_tensor(f"p42_qpm{hp}", [128, 2 * 4 * 512], BF16))
        AKQ = ph.enter_context(nc.sbuf_tensor(f"p42_akq{hp}", [128, 128], F32))
        DEC = ph.enter_context(nc.sbuf_tensor(f"p42_dec{hp}", [128, 4 * 512], F32))
        AK0 = ph.enter_context(nc.sbuf_tensor(f"p42_ak0{hp}", [128, 32], F32))
        DQL = ph.enter_context(nc.sbuf_tensor(f"p42_dql{hp}", [1, 1024], F32))
        CB = ph.enter_context(nc.sbuf_tensor(f"p42_cb{hp}", [128, 1], F32))
        Ebuf = [ph.enter_context(nc.sbuf_tensor(f"p42_e{hp}_{k}", [128, 512], BF16)) for k in range(3)]
        E0 = ph.enter_context(nc.sbuf_tensor(f"p42_e0{hp}", [128, 1], BF16))
        TMX2 = [ph.enter_context(nc.sbuf_tensor(f"p42_tmx{hp}_{k}", [128, 512], F32)) for k in range(2)]
        RC = ph.enter_context(nc.sbuf_tensor(f"p42_rc{hp}", [1, 512], F32))
        BC = ph.enter_context(nc.sbuf_tensor(f"p42_bc{hp}", [64, 512], F32))
        DQB = ph.enter_context(nc.sbuf_tensor(f"p42_dqb{hp}", [128, 2 * 512], BF16))
        onesr = ph.enter_context(nc.sbuf_tensor(f"p42_or{hp}", [1, 128], F32))

        lw = sems.new16(f"p42h{hp}")
        for i in range(2):
            P.inc("gpsimd", lambda g, i=i, hp=hp: g.dma_start(out=WUK[:, i * 512:(i + 1) * 512],
                                                              in_=io["wupk"][i * 128:(i + 1) * 128, hp * 512:(hp + 1) * 512]), lw)
            P.inc("gpsimd", lambda g, i=i, hp=hp: g.dma_start(out=WUV[:, i * 512:(i + 1) * 512],
                                                              in_=io["wupv"][i * 128:(i + 1) * 128, hp * 512:(hp + 1) * 512]), lw)
        lc = sems.new16(f"p42c{hp}")
        P.inc("sync", lambda sy: sy.dma_start(out=AKQ[:, :], in_=io["akq4"][:, :]), lc)
        P.inc("sync", lambda sy: sy.dma_start(out=DEC[:, :], in_=io["dec4"][:, :]), lc)
        P.inc("sync", lambda sy: sy.dma_start(out=DQL[:, :], in_=io["dql"][:, :]), lc)
        P.inc("sync", lambda sy: sy.dma_start(out=AK0[:, :], in_=io["ak0"][:, :]), lc)
        P.inc("gpsimd", lambda g: g.memset(CB[:, :], -L1_C), sems.GP)
        P.inc("gpsimd", lambda g: g.memset(onesr[:, :], 1.0), sems.GP)
        P.wait("tensor", lw, lw.n)
        P.wait("tensor", lc, lc.n)
        P.wait("tensor", sems.GP, sems.GP.n)
        P.wait("scalar", sems.GP, sems.GP.n)
        P.wait("vector", lc, lc.n)

        bank_ticks = {}
        for o in range(4):
            for sc_ in range(8):
                bi = (o * 8 + sc_) % 2
                bank = pb[bi]
                prev = bank_ticks.get(id(bank))
                if prev is not None:
                    P.wait("tensor", prev[0], prev[1])
                pairs = [(WUK[:, lc2 * 512 + o * 128: lc2 * 512 + (o + 1) * 128],
                          LT[:, lc2 * S + sc_ * 512: lc2 * S + (sc_ + 1) * 512]) for lc2 in range(2)]
                mmt = mm_group(P, sems, bank[:, :], pairs)
                bank_ticks[id(bank)] = scalar_copy_evac(
                    P, sems, KR[:, o * S + sc_ * 512: o * S + (sc_ + 1) * 512], bank, mmt)
        P.inc("gpsimd", lambda g: g.memset(VA[:, 64:32 * 520:65], 1.0), sems.GP)
        P.wait("vector", sems.GP, sems.GP.n)
        for kt in range(32):
            bi = kt % 2
            bank = pb[bi]
            prev = bank_ticks.get(id(bank))
            if prev is not None:
                P.wait("tensor", prev[0], prev[1])
            pairs = [(LT[:, lc2 * S + kt * 128: lc2 * S + (kt + 1) * 128],
                      WUV[:, lc2 * 512:(lc2 + 1) * 512]) for lc2 in range(2)]
            mmt = mm_group(P, sems, bank[:, :], pairs)
            P.wait("vector", sems.MM, mmt)
            t = inc_chain(P, "vector",
                          lambda v, kt=kt, bank=bank:
                          v.tensor_copy(VA[:, kt * 520: kt * 520 + 520].rearrange("p (h c) -> p h c", h=8)[:, :, 0:64],
                                        bank[:, :].rearrange("p (h c) -> p h c", h=8)), sems.VE)
            bank_ticks[id(bank)] = (sems.VE, t)
        P.flush()

        sc_ticks = {}
        prev_group_evac = None
        for qc in range(4):
            if prev_group_evac is not None:
                P.wait("tensor", sems.VE, prev_group_evac)
            for sgn in range(2):
                bmt = P.inc("tensor", lambda t, sgn=sgn:
                            t.matmul(pb[6][:, :], onesr[:, :], DQL[0:1, sgn * 512:(sgn + 1) * 512],
                                     start=True, stop=True), sems.MM)
                P.wait("vector", sems.MM, bmt)
                ct = inc_chain(P, "vector", lambda v, sgn=sgn:
                               v.tensor_copy(DQB[:, sgn * 512:(sgn + 1) * 512], pb[6][:, :]), sems.VE)
                P.wait("tensor", sems.VE, ct)
            for ht in range(4):
                for sgn in range(2):
                    inc_chain(P, "vector",
                              lambda v, ht=ht, sgn=sgn, qc=qc:
                              v.tensor_tensor(QPM[:, (sgn * 4 + ht) * 512:(sgn * 4 + ht + 1) * 512],
                                              QT[:, (4 * hp + ht) * SH + qc * 512: (4 * hp + ht) * SH + (qc + 1) * 512],
                                              DQB[:, sgn * 512:(sgn + 1) * 512], ALU.mult), sems.VE)
            qpm_t = sems.VE.n
            P.wait("tensor", sems.VE, qpm_t)

            for grp in range(2):
                if prev_group_evac is not None:
                    P.wait("tensor", sems.VE, prev_group_evac)
                tiles = [(kt, hh) for kt in range(32) for hh in range(4)]
                ND = 3
                sbanks = [pb[0], pb[1], pb[7]]
                et_q = {}
                mixed_prev = None

                def emit_av(t):
                    kt, hh = tiles[t]
                    eb, et = et_q.pop(t)
                    h = grp * 4 + hh
                    avb = pb[2 + hh]
                    stV = VA[:, kt * 520 + 65 * h: kt * 520 + 65 * h + 65]
                    first, last = kt == 0, kt == 31
                    P.wait("tensor", sems.SE, et)
                    if last and hh == 3:
                        return P.inc("tensor",
                                     lambda tn, stV=stV, eb=eb, avb=avb, first=first:
                                     tn.matmul(avb[0:65, :], stV, eb[:, :], start=first, stop=True), sems.MM)
                    P.on("tensor",
                         lambda tn, stV=stV, eb=eb, avb=avb, first=first, last=last:
                         tn.matmul(avb[0:65, :], stV, eb[:, :], start=first, stop=last))
                    return None

                avt = None
                for t, (kt, hh) in enumerate(tiles):
                    cls = 0 if kt < 4 * qc else (2 if kt > 4 * qc + 3 else 1)
                    h = grp * 4 + hh
                    ht, m = h // 2, h % 2
                    si_ = t % ND
                    sbank = sbanks[si_]
                    prev = sc_ticks.get(si_)
                    if prev is not None:
                        P.wait("tensor", prev[0], prev[1])
                    st = KR[64 * m:64 * m + 64, ht * S + kt * 128: ht * S + (kt + 1) * 128]
                    if cls == 1:
                        mv = QT[64 * m:64 * m + 64, (4 * hp + ht) * SH + qc * 512: (4 * hp + ht) * SH + (qc + 1) * 512]
                    else:
                        sgn = 0 if cls == 0 else 1
                        mv = QPM[64 * m:64 * m + 64, (sgn * 4 + ht) * 512:(sgn * 4 + ht + 1) * 512]
                    mmt = P.inc("tensor",
                                lambda tn, st=st, mv=mv, sbank=sbank, m=m:
                                tn.matmul(sbank[:, :], st, mv, start=True, stop=True,
                                          tile_position=(64 * m, 0)), sems.MM)
                    eb = Ebuf[t % 3]
                    acol = AKQ[:, qc * 32 + kt: qc * 32 + kt + 1]
                    if cls == 1:
                        tmx = TMX2[t % 2]
                        P.wait("vector", sems.MM, mmt)
                        if mixed_prev is not None:
                            P.wait("vector", sems.SE, mixed_prev)
                        dct = P.inc("vector",
                                    lambda v, sbank=sbank, kt=kt, qc=qc, tmx=tmx:
                                    v.tensor_tensor(tmx[:, :], sbank[:, :],
                                                    DEC[:, (kt - 4 * qc) * 512:(kt - 4 * qc + 1) * 512],
                                                    ALU.mult), sems.VE)
                        P.wait("scalar", sems.VE, dct)
                        et = P.inc("scalar",
                                   lambda sc, eb=eb, acol=acol, tmx=tmx:
                                   sc.activation(eb[:, :], tmx[:, :], AF.Exp, bias=CB[:, :], scale=acol), sems.SE)
                        mixed_prev = et
                        sc_ticks[si_] = (sems.VE, dct)
                    else:
                        P.wait("scalar", sems.MM, mmt)
                        et = P.inc("scalar",
                                   lambda sc, eb=eb, sbank=sbank, acol=acol:
                                   sc.activation(eb[:, :], sbank[:, :], AF.Exp, bias=CB[:, :], scale=acol), sems.SE)
                        sc_ticks[si_] = (sems.SE, et)
                    et_q[t] = (eb, et)
                    if t >= 1:
                        r = emit_av(t - 1)
                        if r is not None:
                            avt = r
                r = emit_av(len(tiles) - 1)
                if r is not None:
                    avt = r
                P.wait("vector", sems.MM, avt)
                for hh in range(4):
                    h = grp * 4 + hh
                    hg = 8 * hp + h
                    avb = pb[2 + hh]
                    rct = inc_chain(P, "vector", lambda v, avb=avb: v.reciprocal(RC[:, :], avb[64:65, :]), sems.VE)
                    P.wait("tensor", sems.VE, rct)
                    bmt = P.inc("tensor", lambda t: t.matmul(pb[6][0:64, :], onesr[0:1, 0:64], RC[0:1, :],
                                                             start=True, stop=True), sems.MM)
                    P.wait("vector", sems.MM, bmt)
                    inc_chain(P, "vector", lambda v: v.tensor_copy(BC[:, :], pb[6][0:64, :]), sems.VE)
                    ht2, m2 = hg // 2, hg % 2
                    t = inc_chain(P, "vector",
                                  lambda v, avb=avb, ht2=ht2, m2=m2, qc=qc:
                                  v.tensor_tensor(OT[64 * m2:64 * m2 + 64, ht2 * SH + qc * 512: ht2 * SH + (qc + 1) * 512],
                                                  avb[0:64, :], BC[:, :], ALU.mult), sems.VE)
                prev_group_evac = sems.VE.n
        P.wait("tensor", sems.VE, sems.VE.n)
        for h in range(8):
            hg = 8 * hp + h
            ht, m = h // 2, h % 2
            av0 = pb[4]
            for kt in range(32):
                st = KR[64 * m:64 * m + 64, ht * S + kt * 128: ht * S + (kt + 1) * 128]
                mv = Q0[64 * m:64 * m + 64, (4 * hp + ht):(4 * hp + ht) + 1]
                sb0 = pb[kt % 2]
                mmt = P.inc("tensor", lambda t, st=st, mv=mv, m=m, sb0=sb0:
                            t.matmul(sb0[:, 0:1], st, mv, start=True, stop=True,
                                     tile_position=(64 * m, 0)), sems.MM)
                P.wait("scalar", sems.MM, mmt)
                a0 = AK0[:, kt:kt + 1]
                et = P.inc("scalar", lambda sc, a0=a0, sb0=sb0:
                           sc.activation(E0[:, :], sb0[:, 0:1], AF.Exp, bias=CB[:, :], scale=a0), sems.SE)
                P.wait("tensor", sems.SE, et)
                stV = VA[:, kt * 520 + 65 * h: kt * 520 + 65 * h + 65]
                first, last = kt == 0, kt == 31
                if last:
                    avt = P.inc("tensor", lambda t, stV=stV, first=first:
                                t.matmul(av0[0:65, 0:1], stV, E0[:, :], start=first, stop=True), sems.MM)
                else:
                    P.on("tensor", lambda t, stV=stV, first=first:
                         t.matmul(av0[0:65, 0:1], stV, E0[:, :], start=first, stop=False))
            P.wait("vector", sems.MM, avt)
            rct = inc_chain(P, "vector", lambda v: v.reciprocal(RC[0:1, 0:1], av0[64:65, 0:1]), sems.VE)
            P.wait("tensor", sems.VE, rct)
            bmt = P.inc("tensor", lambda t: t.matmul(pb[6][0:64, 0:1], onesr[0:1, 0:64], RC[0:1, 0:1],
                                                     start=True, stop=True), sems.MM)
            P.wait("vector", sems.MM, bmt)
            inc_chain(P, "vector", lambda v: v.tensor_copy(BC[:, 0:1], pb[6][0:64, 0:1]), sems.VE)
            t = inc_chain(P, "vector",
                          lambda v, m=m, hg=hg:
                          v.tensor_tensor(OT0[64 * m:64 * m + 64, (hg // 2):(hg // 2) + 1],
                                          av0[0:64, 0:1], BC[:, 0:1], ALU.mult), sems.VE)
            P.wait("tensor", sems.VE, t)
        P.flush()


def _p42_proj(nc, P, sems, io, pb, OT, OT0):
    ph2 = ExitStack()
    with ph2:
        RES = ph2.enter_context(nc.sbuf_tensor("p42_res", [128, 8 * SH], F32))
        RESL = [RES[:, i * SH:(i + 1) * SH] for i in range(8)]
        ph = ExitStack()
        with ph:
            WO = ph.enter_context(nc.sbuf_tensor("p42_wo", [128, 8 * 1024], BF16))
            XR = ph.enter_context(nc.sbuf_tensor("p42_xr2", [128, 8 * SH], F32))
            lw = sems.new16("p42o")
            lx = sems.new16("p42r")
            for i in range(8):
                P.inc("gpsimd", lambda g, i=i: g.dma_start(out=WO[:, i * 1024:(i + 1) * 1024],
                                                           in_=io["wo"][i * 128:(i + 1) * 128, :]), lw)
                P.inc("sync", lambda sy, i=i: sy.dma_start(out=XR[:, i * SH:(i + 1) * SH],
                                                           in_=io["x2full"][i * 128:(i + 1) * 128, 0:SH]), lx)
            P.wait("tensor", lw, lw.n)
            P.wait("vector", lx, lx.n)
            bank_ticks = {}

            def evac(o, s4, bank, mmt):
                P.wait("vector", sems.MM, mmt)
                t = P.inc("vector",
                          lambda v, o=o, s4=s4, bank=bank:
                          v.tensor_add(RESL[o][:, s4 * 512:(s4 + 1) * 512], bank[:, :],
                                       XR[:, o * SH + s4 * 512: o * SH + (s4 + 1) * 512]), sems.VE)
                return (sems.VE, t)

            emit_proj(P, sems, nO=8, nC=8, nSc=4,
                      st_ap=lambda c, o: WO[:, c * 1024 + o * 128: c * 1024 + (o + 1) * 128],
                      mv_ap=lambda c, s4: OT[:, c * SH + s4 * 512: c * SH + (s4 + 1) * 512],
                      banks=pb[0:4], bank_ticks=bank_ticks, evac=evac)
            _p42_tok0proj(nc, P, sems, io, pb, WO, OT0, ph)
            P.flush()
        emit_ln(nc, P, sems, "p42ln", RESL, io["x3f"], pb)


def _p42_tok0proj(nc, P, sems, io, pb, WO, OT0, ph):
    X20 = ph.enter_context(nc.sbuf_tensor("t2_x2", [128, 8], F32))
    OT0B = ph.enter_context(nc.sbuf_tensor("t2_ob", [128, 8], BF16))
    R0 = ph.enter_context(nc.sbuf_tensor("t2_r", [128, 8], F32))
    RW0 = ph.enter_context(nc.sbuf_tensor("t2_rw", [1, 16], F32))
    OC0 = ph.enter_context(nc.sbuf_tensor("t2_oc", [128, 1], BF16))
    MC0 = ph.enter_context(nc.sbuf_tensor("t2_mc", [128, 2], F32))
    EPS0 = ph.enter_context(nc.sbuf_tensor("t2_ep", [1, 1], F32))
    Z0 = ph.enter_context(nc.sbuf_tensor("t2_z", [1, 1], F32))
    onesr0 = ph.enter_context(nc.sbuf_tensor("t2_or", [1, 128], F32))
    lz2 = sems.new16("t2m")
    for c in range(8):
        P.inc("sync", lambda sy, c=c: sy.dma_start(out=X20[:, c:c + 1], in_=io["x20"][c * 128:(c + 1) * 128, :]), lz2)
    P.inc("gpsimd", lambda g: g.memset(OC0[:, :], 1.0), sems.GP)
    P.inc("gpsimd", lambda g: g.memset(EPS0[:, :], 1e-5), sems.GP)
    P.inc("gpsimd", lambda g: g.memset(Z0[:, :], 0.0), sems.GP)
    P.inc("gpsimd", lambda g: g.memset(onesr0[:, :], 1.0), sems.GP)
    P.wait("vector", lz2, lz2.n)
    P.wait("vector", sems.GP, sems.GP.n)
    cb_t = inc_chain(P, "vector", lambda v: v.tensor_copy(OT0B[:, :], OT0[:, :]), sems.VE)
    P.wait("tensor", sems.VE, cb_t)
    P.wait("tensor", sems.GP, sems.GP.n)
    P.wait("scalar", sems.GP, sems.GP.n)
    for o in range(8):
        for c in range(8):
            first = (o == 0 and c == 0)
            last = (o == 7 and c == 7)
            stp = WO[:, c * 1024 + o * 128: c * 1024 + (o + 1) * 128]
            if last:
                mmt = P.inc("tensor", lambda t, stp=stp, o=o, c=c, first=first:
                            t.matmul(pb[4][:, o:o + 1], stp, OT0B[:, c:c + 1], start=first, stop=True), sems.MM)
            else:
                P.on("tensor", lambda t, stp=stp, o=o, c=c, first=first:
                     t.matmul(pb[4][:, o:o + 1], stp, OT0B[:, c:c + 1], start=first, stop=False))
    P.wait("vector", sems.MM, mmt)
    t = inc_chain(P, "vector", lambda v: v.tensor_add(R0[:, :], pb[4][:, 0:8], X20[:, :]), sems.VE)
    P.wait("tensor", sems.VE, t)
    R0H = ph.enter_context(nc.sbuf_tensor("t2_rh", [128, 8], BF16))
    R0S = ph.enter_context(nc.sbuf_tensor("t2_rs", [128, 8], BF16))
    ct = inc_chain(P, "vector", lambda v: v.tensor_copy(R0H[:, :], R0[:, :]), sems.VE)
    P.wait("scalar", sems.VE, t)
    st_ = P.inc("scalar", lambda sc: sc.activation(R0S[:, :], R0[:, :], AF.Square), sems.SE)
    P.wait("tensor", sems.VE, ct)
    P.wait("tensor", sems.SE, st_)
    P.on("tensor", lambda t: t.matmul(pb[5][0:1, 0:8], OC0[:, :], R0H[:, :], start=True, stop=True))
    mmt2 = P.inc("tensor", lambda t: t.matmul(pb[5][0:1, 8:16], OC0[:, :], R0S[:, :], start=True, stop=True), sems.MM)
    P.wait("vector", sems.MM, mmt2)
    inc_chain(P, "vector", lambda v: v.tensor_copy(RW0[0:1, 0:16], pb[5][0:1, 0:16]), sems.VE)

    def rr(a, b_):
        return RW0[0:1, a:b_]
    inc_chain(P, "vector", lambda v: v.tensor_tensor(rr(0, 4), rr(0, 4), rr(4, 8), ALU.add), sems.VE)
    inc_chain(P, "vector", lambda v: v.tensor_tensor(rr(0, 2), rr(0, 2), rr(2, 4), ALU.add), sems.VE)
    inc_chain(P, "vector", lambda v: v.tensor_tensor(rr(0, 1), rr(0, 1), rr(1, 2), ALU.add), sems.VE)
    inc_chain(P, "vector", lambda v: v.tensor_tensor(rr(8, 12), rr(8, 12), rr(12, 16), ALU.add), sems.VE)
    inc_chain(P, "vector", lambda v: v.tensor_tensor(rr(8, 10), rr(8, 10), rr(10, 12), ALU.add), sems.VE)
    inc_chain(P, "vector", lambda v: v.tensor_tensor(rr(8, 9), rr(8, 9), rr(9, 10), ALU.add), sems.VE)
    inc_chain(P, "vector", lambda v: v.tensor_scalar_mul(rr(1, 2), rr(0, 1), 1.0 / 1024.0), sems.VE)
    inc_chain(P, "vector", lambda v: v.tensor_scalar_mul(rr(8, 9), rr(8, 9), 1.0 / 1024.0), sems.VE)
    inc_chain(P, "vector", lambda v: v.tensor_tensor(rr(2, 3), rr(1, 2), rr(1, 2), ALU.mult), sems.VE)
    ve_t = inc_chain(P, "vector", lambda v: v.tensor_tensor(rr(2, 3), rr(8, 9), rr(2, 3), ALU.subtract), sems.VE)
    P.wait("scalar", sems.VE, ve_t)
    t1_ = P.inc("scalar", lambda sc: sc.activation(rr(3, 4), rr(2, 3), AF.Ln, bias=EPS0[:, :], scale=1.0), sems.SE)
    P.wait("scalar", sems.SE, t1_)
    t2_ = P.inc("scalar", lambda sc: sc.activation(rr(3, 4), rr(3, 4), AF.Exp, bias=Z0[:, :], scale=-0.5), sems.SE)
    P.wait("tensor", sems.SE, t2_)
    P.on("tensor", lambda t: t.matmul(pb[6][:, 0:1], onesr0[:, :], rr(1, 2), start=True, stop=True))
    mmt3 = P.inc("tensor", lambda t: t.matmul(pb[6][:, 1:2], onesr0[:, :], rr(3, 4), start=True, stop=True), sems.MM)
    P.wait("vector", sems.MM, mmt3)
    inc_chain(P, "vector", lambda v: v.tensor_copy(MC0[:, :], pb[6][:, 0:2]), sems.VE)
    inc_chain(P, "vector", lambda v: v.tensor_scalar(R0[:, :], R0[:, :], MC0[:, 0:1], None, ALU.subtract), sems.VE)
    xt_ = inc_chain(P, "vector", lambda v: v.tensor_scalar(R0[:, :], R0[:, :], MC0[:, 1:2], None, ALU.mult), sems.VE)
    stt = sems.new16("t2s")
    P.wait("sync", sems.VE, xt_)
    for c in range(8):
        P.inc("sync", lambda sy, c=c: sy.dma_start(out=io["x30d"][c * 128:(c + 1) * 128, :], in_=R0[:, c:c + 1]), stt)
    P.wait("sync", stt, stt.n)



def build_p4b(nc, P, sems, io):
    ph0 = ExitStack()
    with ph0:
        pb = [ph0.enter_context(nc.psum_tensor(f"p4b_pb{i}", [128, 512], F32)) for i in range(8)]
        RES = ph0.enter_context(nc.sbuf_tensor("p4b_res", [128, 8 * SH], F32))
        RESL = [RES[:, i * SH:(i + 1) * SH] for i in range(8)]
        ph = ExitStack()
        with ph:
            OTX = ph.enter_context(nc.sbuf_tensor("p4b_otx", [128, 8 * SH], BF16))
            WO = ph.enter_context(nc.sbuf_tensor("p4b_wo", [128, 8 * 1024], BF16))
            XR = ph.enter_context(nc.sbuf_tensor("p4b_xr", [128, 8 * SH], F32))
            lw = sems.new16("p4bw")
            lx = sems.new16("p4bx")
            if "otx" in io:
                for i in range(8):
                    P.inc("gpsimd", lambda g, i=i: g.dma_start(out=OTX[:, i * SH:(i + 1) * SH],
                                                               in_=io["otx"][i * 128:(i + 1) * 128, :]), lw)
            else:
                OFS = ph.enter_context(nc.sbuf_tensor("p4b_ofs", [1, 1], mybir.dt.int32))
                lo = sems.new16("p4bo")
                P.inc("gpsimd", lambda g: g.dma_start(out=OFS[:, :], in_=io["ofs"]), lo)
                P.wait("gpsimd", lo, lo.n)
                otg_t = io["otg_t"]
                for _ in range(8):
                    lw.bump()

                def emit_otx_loads(g):
                    with g.register("p4b_r0") as r0, g.register("p4b_r1") as r1:
                        g.reg_load(r0, OFS[0:1, 0:1])
                        for c in range(8):
                            base = ((c // 4) * 1024 + (c % 4) * 128) * SH
                            g.reg_mov(r1, base)
                            g.reg_add(r1, r1, r0)
                            g.dma_start(out=OTX[:, c * SH:(c + 1) * SH],
                                        in_=bass.AP(otg_t, r1, [[SH, 128], [1, SH]])).then_inc(lw.h, 16)
                P.on("gpsimd", emit_otx_loads)
            for i in range(8):
                P.inc("gpsimd", lambda g, i=i: g.dma_start(out=WO[:, i * 1024:(i + 1) * 1024],
                                                           in_=io["wo"][i * 128:(i + 1) * 128, :]), lw)
                P.inc("sync", lambda sy, i=i: sy.dma_start(out=XR[:, i * SH:(i + 1) * SH],
                                                           in_=io["x2own"][i * 128:(i + 1) * 128, :]), lx)
            P.wait("tensor", lw, lw.n)
            P.wait("vector", lx, lx.n)
            bank_ticks = {}

            def evac(o, s4, bank, mmt):
                P.wait("vector", sems.MM, mmt)
                t = P.inc("vector",
                          lambda v, o=o, s4=s4, bank=bank:
                          v.tensor_add(RESL[o][:, s4 * 512:(s4 + 1) * 512], bank[:, :],
                                       XR[:, o * SH + s4 * 512: o * SH + (s4 + 1) * 512]), sems.VE)
                return (sems.VE, t)

            emit_proj(P, sems, nO=8, nC=8, nSc=4,
                      st_ap=lambda c, o: WO[:, c * 1024 + o * 128: c * 1024 + (o + 1) * 128],
                      mv_ap=lambda c, s4: OTX[:, c * SH + s4 * 512: c * SH + (s4 + 1) * 512],
                      banks=pb[0:4], bank_ticks=bank_ticks, evac=evac)
            if "ot0" in io:
                _p4b_tok0(nc, P, sems, io, pb, WO, ph)
            P.flush()
        emit_ln(nc, P, sems, "p4ln", RESL, io["x3f"], pb)


def _p4b_tok0(nc, P, sems, io, pb, WO, ph):
    OT0 = ph.enter_context(nc.sbuf_tensor("t0_ot", [128, 8], BF16))
    X20 = ph.enter_context(nc.sbuf_tensor("t0_x2", [128, 8], F32))
    R0 = ph.enter_context(nc.sbuf_tensor("t0_r", [128, 8], F32))
    RW0 = ph.enter_context(nc.sbuf_tensor("t0_rw", [1, 16], F32))
    OC0 = ph.enter_context(nc.sbuf_tensor("t0_oc", [128, 1], BF16))
    lz = sems.new16("t0l")
    lz2 = sems.new16("t0m")
    for c in range(8):
        P.inc("gpsimd", lambda g, c=c: g.dma_start(out=OT0[:, c:c + 1], in_=io["ot0"][c * 128:(c + 1) * 128, :]), lz)
        P.inc("sync", lambda sy, c=c: sy.dma_start(out=X20[:, c:c + 1], in_=io["x20"][c * 128:(c + 1) * 128, :]), lz2)
    P.inc("gpsimd", lambda g: g.memset(OC0[:, :], 1.0), sems.GP)
    P.wait("tensor", lz, lz.n)
    P.wait("vector", lz2, lz2.n)
    P.wait("tensor", sems.GP, sems.GP.n)
    for o in range(8):
        for c in range(8):
            first = (o == 0 and c == 0)
            last = (o == 7 and c == 7)
            stp = WO[:, c * 1024 + o * 128: c * 1024 + (o + 1) * 128]
            if last:
                mmt = P.inc("tensor", lambda t, stp=stp, o=o, c=c, first=first:
                            t.matmul(pb[4][:, o:o + 1], stp, OT0[:, c:c + 1], start=first, stop=True), sems.MM)
            else:
                P.on("tensor", lambda t, stp=stp, o=o, c=c, first=first:
                     t.matmul(pb[4][:, o:o + 1], stp, OT0[:, c:c + 1], start=first, stop=False))
    P.wait("vector", sems.MM, mmt)
    t = inc_chain(P, "vector", lambda v: v.tensor_add(R0[:, :], pb[4][:, 0:8], X20[:, :]), sems.VE)
    P.wait("tensor", sems.VE, t)
    R0H = ph.enter_context(nc.sbuf_tensor("t0_rh", [128, 8], BF16))
    R0S = ph.enter_context(nc.sbuf_tensor("t0_rs", [128, 8], BF16))
    ct = inc_chain(P, "vector", lambda v: v.tensor_copy(R0H[:, :], R0[:, :]), sems.VE)
    P.wait("scalar", sems.VE, ct)
    st_ = P.inc("scalar", lambda sc: sc.activation(R0S[:, :], R0[:, :], AF.Square), sems.SE)
    P.wait("tensor", sems.VE, ct)
    P.wait("tensor", sems.SE, st_)
    P.on("tensor", lambda t: t.matmul(pb[5][0:1, 0:8], OC0[:, :], R0H[:, :], start=True, stop=True))
    mmt2 = P.inc("tensor", lambda t: t.matmul(pb[5][0:1, 8:16], OC0[:, :], R0S[:, :], start=True, stop=True), sems.MM)
    P.wait("vector", sems.MM, mmt2)
    inc_chain(P, "vector", lambda v: v.tensor_copy(RW0[0:1, 0:16], pb[5][0:1, 0:16]), sems.VE)
    def rr(a, b_):
        return RW0[0:1, a:b_]
    inc_chain(P, "vector", lambda v: v.tensor_tensor(rr(0, 4), rr(0, 4), rr(4, 8), ALU.add), sems.VE)
    inc_chain(P, "vector", lambda v: v.tensor_tensor(rr(0, 2), rr(0, 2), rr(2, 4), ALU.add), sems.VE)
    inc_chain(P, "vector", lambda v: v.tensor_tensor(rr(0, 1), rr(0, 1), rr(1, 2), ALU.add), sems.VE)
    inc_chain(P, "vector", lambda v: v.tensor_tensor(rr(8, 12), rr(8, 12), rr(12, 16), ALU.add), sems.VE)
    inc_chain(P, "vector", lambda v: v.tensor_tensor(rr(8, 10), rr(8, 10), rr(10, 12), ALU.add), sems.VE)
    inc_chain(P, "vector", lambda v: v.tensor_tensor(rr(8, 9), rr(8, 9), rr(9, 10), ALU.add), sems.VE)
    inc_chain(P, "vector", lambda v: v.tensor_scalar_mul(rr(1, 2), rr(0, 1), 1.0 / 1024.0), sems.VE)
    inc_chain(P, "vector", lambda v: v.tensor_scalar_mul(rr(8, 9), rr(8, 9), 1.0 / 1024.0), sems.VE)
    inc_chain(P, "vector", lambda v: v.tensor_tensor(rr(2, 3), rr(1, 2), rr(1, 2), ALU.mult), sems.VE)
    inc_chain(P, "vector", lambda v: v.tensor_tensor(rr(2, 3), rr(8, 9), rr(2, 3), ALU.subtract), sems.VE)
    ve_t = sems.VE.n
    P.wait("scalar", sems.VE, ve_t)
    EPS0 = ph.enter_context(nc.sbuf_tensor("t0_ep", [1, 1], F32))
    Z0 = ph.enter_context(nc.sbuf_tensor("t0_z", [1, 1], F32))
    P.inc("gpsimd", lambda g: g.memset(EPS0[:, :], 1e-5), sems.GP)
    P.inc("gpsimd", lambda g: g.memset(Z0[:, :], 0.0), sems.GP)
    P.wait("scalar", sems.GP, sems.GP.n)
    t1_ = P.inc("scalar", lambda sc: sc.activation(rr(3, 4), rr(2, 3), AF.Ln, bias=EPS0[:, :], scale=1.0), sems.SE)
    P.wait("scalar", sems.SE, t1_)
    t2_ = P.inc("scalar", lambda sc: sc.activation(rr(3, 4), rr(3, 4), AF.Exp, bias=Z0[:, :], scale=-0.5), sems.SE)
    P.wait("vector", sems.SE, t2_)
    MC0 = ph.enter_context(nc.sbuf_tensor("t0_mc", [128, 2], F32))
    onesr0 = ph.enter_context(nc.sbuf_tensor("t0_or", [1, 128], F32))
    P.inc("gpsimd", lambda g: g.memset(onesr0[:, :], 1.0), sems.GP)
    P.wait("tensor", sems.GP, sems.GP.n)
    P.wait("tensor", sems.SE, t2_)
    P.on("tensor", lambda t: t.matmul(pb[6][:, 0:1], onesr0[:, :], rr(1, 2), start=True, stop=True))
    mmt3 = P.inc("tensor", lambda t: t.matmul(pb[6][:, 1:2], onesr0[:, :], rr(3, 4), start=True, stop=True), sems.MM)
    P.wait("vector", sems.MM, mmt3)
    inc_chain(P, "vector", lambda v: v.tensor_copy(MC0[:, :], pb[6][:, 0:2]), sems.VE)
    inc_chain(P, "vector", lambda v: v.tensor_scalar(R0[:, :], R0[:, :], MC0[:, 0:1], None, ALU.subtract), sems.VE)
    xt_ = inc_chain(P, "vector", lambda v: v.tensor_scalar(R0[:, :], R0[:, :], MC0[:, 1:2], None, ALU.mult), sems.VE)
    stt = sems.new16("t0s")
    P.wait("sync", sems.VE, xt_)
    for c in range(8):
        P.inc("sync", lambda sy, c=c: sy.dma_start(out=io["x30d"][c * 128:(c + 1) * 128, :], in_=R0[:, c:c + 1]), stt)
    P.wait("sync", stt, stt.n)



def build_p5(nc, P, sems, io):
    ph0 = ExitStack()
    with ph0:
        pb = [ph0.enter_context(nc.psum_tensor(f"p5_pb{i}", [128, 512], F32)) for i in range(8)]
        X3b = ph0.enter_context(nc.sbuf_tensor("p5_x3b", [128, 8 * SH], BF16))
        ACC = ph0.enter_context(nc.sbuf_tensor("p5_acc", [128, 8 * SH], F32))
        EIX = ph0.enter_context(nc.sbuf_tensor("p5_eix", [1, 2], mybir.dt.int32))
        BSUM = ph0.enter_context(nc.sbuf_tensor("p5_bsum", [128, 8], F32))

        lx = sems.new16("p5x")
        for i in range(8):
            P.inc("gpsimd", lambda g, i=i: g.dma_start(out=X3b[:, i * SH:(i + 1) * SH],
                                                       in_=io["x3f"][i * 128:(i + 1) * 128, :]), lx)
        _p5_route(nc, P, sems, io, pb, EIX, BSUM)
        _p5_units(nc, P, sems, io, pb, X3b, ACC, EIX, BSUM, lx)
        ACCL = [ACC[:, i * SH:(i + 1) * SH] for i in range(8)]
        emit_ln(nc, P, sems, "p5ln", ACCL, io["yf"], pb)


def _p5_route(nc, P, sems, io, pb, EIX, BSUM):
    ph = ExitStack()
    with ph:
        WR = ph.enter_context(nc.sbuf_tensor("p5_wr", [128, 8 * 8], F32))
        X30 = ph.enter_context(nc.sbuf_tensor("p5_x30", [128, 8], F32))
        BR = ph.enter_context(nc.sbuf_tensor("p5_br", [8, 1], F32))
        SR = ph.enter_context(nc.sbuf_tensor("p5_sr", [8, 1], F32))
        RW = ph.enter_context(nc.sbuf_tensor("p5_rw", [1, 64], F32))
        B2 = ph.enter_context(nc.sbuf_tensor("p5_b2", [128, 64], F32))
        onesr = ph.enter_context(nc.sbuf_tensor("p5_or", [1, 128], F32))
        lw = sems.new16("p5r")
        for c in range(8):
            P.inc("sync", lambda sy, c=c: sy.dma_start(out=WR[:, c * 8:(c + 1) * 8],
                                                       in_=io["wr"][c * 128:(c + 1) * 128, :]), lw)
            P.inc("sync", lambda sy, c=c: sy.dma_start(out=X30[:, c:c + 1],
                                                       in_=io["x30"][c * 128:(c + 1) * 128, :]), lw)
        P.inc("sync", lambda sy: sy.dma_start(out=BR[:, :], in_=io["br"][:, :]), lw)
        P.inc("sync", lambda sy: sy.dma_start(out=B2[:, :], in_=io["b2c"][:, :]), lw)
        P.inc("gpsimd", lambda g: g.memset(onesr[:, :], 1.0), sems.GP)
        P.wait("tensor", lw, lw.n)
        P.wait("tensor", sems.GP, sems.GP.n)
        pairs = [(WR[:, c * 8:(c + 1) * 8], X30[:, c:c + 1]) for c in range(8)]
        mmt = mm_group(P, sems, pb[0][0:8, 0:1], pairs)
        P.wait("vector", sems.MM, mmt)
        P.wait("vector", lw, lw.n)
        inc_chain(P, "vector", lambda v: v.tensor_tensor(SR[:, :], pb[0][0:8, 0:1], BR[:, :], ALU.add), sems.VE)
        st0 = sems.new16("p5g")
        P.wait("sync", sems.VE, sems.VE.n)
        for e in range(8):
            P.inc("sync", lambda sy, e=e: sy.dma_start(out=RW[0:1, e:e + 1], in_=SR[e:e + 1, :]), st0)
        P.wait("vector", st0, st0.n)
        for e in range(8):
            js = [j for j in range(8) if j != e]
            inc_chain(P, "vector", lambda v, e=e, j=js[0]:
                      v.tensor_tensor(RW[0:1, 8 + e:9 + e], RW[0:1, j:j + 1], RW[0:1, e:e + 1], ALU.is_gt), sems.VE)
            for j in js[1:]:
                inc_chain(P, "vector", lambda v, e=e, j=j:
                          v.tensor_tensor(RW[0:1, 24:25], RW[0:1, j:j + 1], RW[0:1, e:e + 1], ALU.is_gt), sems.VE)
                inc_chain(P, "vector", lambda v, e=e:
                          v.tensor_tensor(RW[0:1, 8 + e:9 + e], RW[0:1, 8 + e:9 + e], RW[0:1, 24:25], ALU.add), sems.VE)
            inc_chain(P, "vector", lambda v, e=e:
                      v.tensor_scalar(RW[0:1, 16 + e:17 + e], RW[0:1, 8 + e:9 + e], 2.0, 0.5, ALU.is_lt, ALU.mult), sems.VE)
        P.wait("tensor", sems.VE, sems.VE.n)
        for e in range(8):
            inc_chain(P, "vector", lambda v, e=e:
                      v.tensor_scalar(RW[0:1, 25 + e:26 + e], RW[0:1, 8 + e:9 + e], 0.5, float(e),
                                      ALU.is_lt, ALU.mult), sems.VE)
            inc_chain(P, "vector", lambda v, e=e:
                      v.tensor_scalar(RW[0:1, 33 + e:34 + e], RW[0:1, 8 + e:9 + e], 1.0, 0.0,
                                      ALU.is_gt, ALU.bypass), sems.VE)
            inc_chain(P, "vector", lambda v, e=e:
                      v.tensor_scalar(RW[0:1, 41 + e:42 + e], RW[0:1, 33 + e:34 + e], 1.0, float(e),
                                      ALU.is_lt, ALU.mult), sems.VE)
        def tsum(base):
            inc_chain(P, "vector", lambda v: v.tensor_tensor(RW[0:1, base:base + 4], RW[0:1, base:base + 4], RW[0:1, base + 4:base + 8], ALU.add), sems.VE)
            inc_chain(P, "vector", lambda v: v.tensor_tensor(RW[0:1, base:base + 2], RW[0:1, base:base + 2], RW[0:1, base + 2:base + 4], ALU.add), sems.VE)
            inc_chain(P, "vector", lambda v: v.tensor_tensor(RW[0:1, base:base + 1], RW[0:1, base:base + 1], RW[0:1, base + 1:base + 2], ALU.add), sems.VE)
        tsum(25)
        tsum(41)
        inc_chain(P, "vector", lambda v: v.tensor_tensor(RW[0:1, 41:42], RW[0:1, 41:42], RW[0:1, 25:26], ALU.subtract), sems.VE)
        ei_t = inc_chain(P, "vector", lambda v: v.tensor_copy(EIX[0:1, 0:1], RW[0:1, 25:26]), sems.VE)
        ei_t = inc_chain(P, "vector", lambda v: v.tensor_copy(EIX[0:1, 1:2], RW[0:1, 41:42]), sems.VE)
        P.wait("gpsimd", sems.VE, ei_t)
        P.flush()


def _p5_units(nc, P, sems, io, pb, X3b, ACC, EIX, BSUM, lx):
    ph = ExitStack()
    with ph:
        HT = ph.enter_context(nc.sbuf_tensor("p5_ht", [128, 8 * SH], BF16))
        W1S = [ph.enter_context(nc.sbuf_tensor(f"p5_w1s{k}", [128, 8 * 256], BF16)) for k in range(2)]
        W2S = [ph.enter_context(nc.sbuf_tensor(f"p5_w2s{k}", [128, 2 * 1024], BF16)) for k in range(2)]
        B1S = [ph.enter_context(nc.sbuf_tensor(f"p5_b1s{k}", [128, 8], F32)) for k in range(2)]
        B2S = [ph.enter_context(nc.sbuf_tensor(f"p5_b2s{k}", [128, 8], F32)) for k in range(2)]
        XRS = ph.enter_context(nc.sbuf_tensor("p5_xr", [128, 512], F32))

        w1h = io["w1f_t"]
        w2h = io["w2b_t"]
        b1h = io["b1c_t"]
        b2h = io["b2c_t"]
        P.wait("tensor", lx, lx.n)

        sW1 = [sems.new16(f"p5w1{k}") for k in range(2)]
        sW2 = [sems.new16(f"p5w2{k}") for k in range(2)]
        sB12 = sems.new16("p5b12")
        sXR = sems.new16("p5xr")
        w1_used, w2_used, xr_used = {}, {}, {}
        w1_ld, w2_ld, xr_ld = {}, {}, {}

        def load_bias(slot):
            for _ in range(2):
                sB12.bump()

            def emitfn(g, slot=slot):
                with g.register(f"p5r0_{slot}") as r0, g.register(f"p5r1_{slot}") as r1:
                    g.reg_load(r0, EIX[0:1, slot:slot + 1])
                    g.reg_mul(r1, r0, 8)
                    g.dma_start(out=B1S[slot][:, :], in_=bass.AP(b1h, r1, [[64, 128], [1, 8]])).then_inc(sB12.h, 16)
                    g.dma_start(out=B2S[slot][:, :], in_=bass.AP(b2h, r1, [[64, 128], [1, 8]])).then_inc(sB12.h, 16)
            P.on("gpsimd", emitfn)

        def load_w1(idx, slot, fs):
            k = idx % 2
            prev = w1_used.get(idx - 2)
            if prev is not None:
                P.wait("gpsimd", sems.MM, prev)
            sW1[k].bump()

            def emitfn(g, slot=slot, fs=fs, k=k):
                with g.register(f"p5w1r1_{slot}_{fs}") as r1:
                    g.reg_load(r1, EIX[0:1, slot:slot + 1])
                    g.reg_mul(r1, r1, 1048576)
                    g.reg_add(r1, r1, fs * 256)
                    g.dma_start(out=W1S[k][:, :],
                                in_=bass.AP(w1h, r1, [[1024, 128], [131072, 8], [1, 256]])).then_inc(sW1[k].h, 16)
            P.on("gpsimd", emitfn)
            w1_ld[idx] = (sW1[k], sW1[k].n)

        def load_w2(idx, slot, og):
            k = idx % 2
            prev = w2_used.get(idx - 2)
            if prev is not None:
                P.wait("gpsimd", sems.MM, prev)
            sW2[k].bump()

            def emitfn(g, slot=slot, og=og, k=k):
                with g.register(f"p5w2r0_{slot}_{og}") as r0:
                    g.reg_load(r0, EIX[0:1, slot:slot + 1])
                    g.reg_mul(r0, r0, 1048576)
                    g.reg_add(r0, r0, og * 262144)
                    g.dma_start(out=W2S[k][:, :],
                                in_=bass.AP(w2h, r0, [[1024, 128], [131072, 2], [1, 1024]])).then_inc(sW2[k].h, 16)
            P.on("gpsimd", emitfn)
            w2_ld[idx] = (sW2[k], sW2[k].n)

        def load_xr(idx, o, s4):
            prev = xr_used.get(idx - 1)
            if prev is not None:
                P.wait("sync", sems.VE, prev)
            P.inc("sync", lambda sy, o=o, s4=s4:
                  sy.dma_start(out=XRS[:, :], in_=io["x3f"][o * 128:(o + 1) * 128, s4 * 512:(s4 + 1) * 512]), sXR)
            xr_ld[idx] = (sXR, sXR.n)

        load_bias(0)
        load_bias(1)
        bank_ticks = {}
        for slot in range(2):
            w1i0 = slot * 4
            load_w1(w1i0, slot, 0)
            for fs in range(4):
                if fs < 3:
                    load_w1(w1i0 + fs + 1, slot, fs + 1)
                P.wait("tensor", w1_ld[w1i0 + fs][0], w1_ld[w1i0 + fs][1])
                if fs == 0 and slot == 0:
                    P.wait("scalar", sB12, sB12.n)
                for fc2 in range(2):
                    fc = fs * 2 + fc2
                    for s4 in range(4):
                        bi = (fc * 4 + s4) % 2
                        bank = pb[bi]
                        prev = bank_ticks.get(id(bank))
                        if prev is not None:
                            P.wait("tensor", prev[0], prev[1])
                        pairs = [(W1S[(w1i0 + fs) % 2][:, c * 256 + fc2 * 128: c * 256 + (fc2 + 1) * 128],
                                  X3b[:, c * SH + s4 * 512: c * SH + (s4 + 1) * 512]) for c in range(8)]
                        mmt = mm_group(P, sems, bank[:, :], pairs)
                        P.wait("scalar", sems.MM, mmt)
                        et = P.inc("scalar",
                                   lambda sc, fc=fc, s4=s4, bank=bank, slot=slot:
                                   sc.activation(HT[:, fc * SH + s4 * 512: fc * SH + (s4 + 1) * 512],
                                                 bank[:, :], AF.Gelu, bias=B1S[slot][:, fc:fc + 1],
                                                 scale=1.0), sems.SE)
                        bank_ticks[id(bank)] = (sems.SE, et)
                w1_used[w1i0 + fs] = sems.MM.n
            P.wait("tensor", sems.SE, sems.SE.n)
            w2i0 = slot * 4
            load_w2(w2i0, slot, 0)
            for og in range(4):
                if og < 3:
                    load_w2(w2i0 + og + 1, slot, og + 1)
                P.wait("tensor", w2_ld[w2i0 + og][0], w2_ld[w2i0 + og][1])
                for oo in range(2):
                    o = og * 2 + oo
                    for s4 in range(4):
                        bi = 2 + s4 % 2
                        bank = pb[bi]
                        prev = bank_ticks.get(id(bank))
                        if prev is not None:
                            P.wait("tensor", prev[0], prev[1])
                        pairs = [(W2S[(w2i0 + og) % 2][:, oo * 1024 + fc * 128: oo * 1024 + (fc + 1) * 128],
                                  HT[:, fc * SH + s4 * 512: fc * SH + (s4 + 1) * 512]) for fc in range(8)]
                        mmt = mm_group(P, sems, bank[:, :], pairs)
                        P.wait("vector", sems.MM, mmt)
                        dacc = ACC[:, o * SH + s4 * 512: o * SH + (s4 + 1) * 512]
                        if slot == 0:
                            load_xr(o * 4 + s4, o, s4)
                            P.wait("vector", xr_ld[o * 4 + s4][0], xr_ld[o * 4 + s4][1])
                            t = inc_chain(P, "vector", lambda v, bank=bank, dacc=dacc:
                                          v.scalar_tensor_tensor(dacc, bank[:, :], 0.5, XRS[:, :],
                                                                 ALU.mult, ALU.add), sems.VE)
                            xr_used[o * 4 + s4] = sems.VE.n
                        else:
                            t = inc_chain(P, "vector", lambda v, bank=bank, dacc=dacc:
                                          v.scalar_tensor_tensor(dacc, bank[:, :], 0.5, dacc,
                                                                 ALU.mult, ALU.add), sems.VE)
                        bank_ticks[id(bank)] = (sems.VE, t)
                w2_used[w2i0 + og] = sems.MM.n
        P.wait("vector", sB12, sB12.n)
        inc_chain(P, "vector", lambda v: v.tensor_tensor(BSUM[:, 0:8], B2S[0][:, :], B2S[1][:, :], ALU.add), sems.VE)
        inc_chain(P, "vector", lambda v: v.tensor_scalar_mul(BSUM[:, 0:8], BSUM[:, 0:8], 0.5), sems.VE)
        for o in range(8):
            inc_chain(P, "vector", lambda v, o=o:
                      v.tensor_scalar_add(ACC[:, o * SH:(o + 1) * SH], ACC[:, o * SH:(o + 1) * SH],
                                          BSUM[:, o:o + 1]), sems.VE)
        P.flush()

def prep_p1(ins):
    x = np.ascontiguousarray(np.asarray(ins["L0_attn_x"], dtype=np.float32))
    xT = np.ascontiguousarray(x.transpose(0, 2, 1))
    wqkT = np.ascontiguousarray(np.asarray(ins["L0_attn_w_qkv"]).T)
    wpT = np.ascontiguousarray(np.asarray(ins["L0_attn_w_proj"]).T)
    bproj8 = np.ascontiguousarray(np.asarray(ins["L0_attn_b_proj"]).reshape(8, 128).T)
    per_core = []
    for c in range(8):
        b, p = c // 2, c % 2
        per_core.append({
            "xT": xT[b],
            "wqkT": wqkT,
            "wv_own": np.ascontiguousarray(wqkT[:, 2048 + 512 * p: 2048 + 512 * (p + 1)]),
            "wpT": wpT,
            "bproj8": bproj8,
            "x_res": np.ascontiguousarray(xT[b][:, SH * p: SH * (p + 1)]),
        })
    return per_core


def build_p1_standalone():
    nc = mk_nc()
    io = {}
    io["xT"] = nc.dram_tensor("xT", [1024, S], F32, kind="ExternalInput").ap()
    io["wqkT"] = nc.dram_tensor("wqkT", [1024, 3072], F32, kind="ExternalInput").ap()
    io["wv_own"] = nc.dram_tensor("wv_own", [1024, 512], F32, kind="ExternalInput").ap()
    io["wpT"] = nc.dram_tensor("wpT", [1024, 1024], F32, kind="ExternalInput").ap()
    io["bproj8"] = nc.dram_tensor("bproj8", [128, 8], F32, kind="ExternalInput").ap()
    io["x_res"] = nc.dram_tensor("x_res", [1024, SH], F32, kind="ExternalInput").ap()
    io["x1f"] = nc.dram_tensor("x1f", [1024, SH], F32, kind="ExternalOutput").ap()
    with ExitStack() as stack:
        sems = Sems(nc, stack, "p1")
        P = Prog(nc)
        build_p1(nc, P, sems, io)
        P.flush()
    return nc


def prep_p2(ins):
    def col128(b):
        b = np.asarray(b, dtype=np.float32)
        return np.ascontiguousarray(b.reshape(-1, 128).T)

    def w2block(w2):
        w2 = np.asarray(w2, dtype=np.float32)
        E = w2.shape[0]
        blk = w2.reshape(E, 32, 128, 8, 128).transpose(0, 3, 2, 1, 4)
        return np.ascontiguousarray(blk.reshape(E * 8 * 128, 4096))

    sw1 = np.asarray(ins["L0_ffn_sw1"], dtype=np.float32)
    rw1 = np.asarray(ins["L0_ffn_rw1"], dtype=np.float32)
    shared = {
        "gate_w": np.ascontiguousarray(np.asarray(ins["L0_ffn_gate_w"], dtype=np.float32)),
        "gate_b": np.ascontiguousarray(np.asarray(ins["L0_ffn_gate_b"], dtype=np.float32).reshape(4, 1)),
        "sw1f": np.ascontiguousarray(sw1.reshape(2 * 1024, 4096)),
        "rw1f": np.ascontiguousarray(rw1.reshape(4 * 1024, 4096)),
        "sw2b": w2block(ins["L0_ffn_sw2"]),
        "rw2b": w2block(ins["L0_ffn_rw2"]),
        "sb1c": np.ascontiguousarray(np.asarray(ins["L0_ffn_sb1"], dtype=np.float32).reshape(2 * 32, 128).T.reshape(128, 64)),
        "rb1c": np.ascontiguousarray(np.asarray(ins["L0_ffn_rb1"], dtype=np.float32).reshape(4 * 32, 128).T.reshape(128, 128)),
        "sb2c": np.ascontiguousarray(np.asarray(ins["L0_ffn_sb2"], dtype=np.float32).reshape(2 * 8, 128).T.reshape(128, 16)),
        "rb2c": np.ascontiguousarray(np.asarray(ins["L0_ffn_rb2"], dtype=np.float32).reshape(4 * 8, 128).T.reshape(128, 32)),
    }
    return shared


def build_p2_standalone():
    nc = mk_nc()
    io = {}
    io["x1f"] = nc.dram_tensor("x1f", [1024, SH], F32, kind="ExternalInput").ap()
    io["gate_w"] = nc.dram_tensor("gate_w", [1024, 4], F32, kind="ExternalInput").ap()
    io["gate_b"] = nc.dram_tensor("gate_b", [4, 1], F32, kind="ExternalInput").ap()
    io["sw1f"] = nc.dram_tensor("sw1f", [2048, 4096], F32, kind="ExternalInput").ap()
    io["rw1f"] = nc.dram_tensor("rw1f", [4096, 4096], F32, kind="ExternalInput").ap()
    io["sw2b"] = nc.dram_tensor("sw2b", [2048, 4096], F32, kind="ExternalInput").ap()
    io["rw2b"] = nc.dram_tensor("rw2b", [4096, 4096], F32, kind="ExternalInput").ap()
    io["sb1c"] = nc.dram_tensor("sb1c", [128, 64], F32, kind="ExternalInput").ap()
    io["rb1c"] = nc.dram_tensor("rb1c", [128, 128], F32, kind="ExternalInput").ap()
    io["sb2c"] = nc.dram_tensor("sb2c", [128, 16], F32, kind="ExternalInput").ap()
    io["rb2c"] = nc.dram_tensor("rb2c", [128, 32], F32, kind="ExternalInput").ap()
    io["x2f"] = nc.dram_tensor("x2f", [1024, SH], F32, kind="ExternalOutput").ap()
    with ExitStack() as stack:
        sems = Sems(nc, stack, "p2")
        P = Prog(nc)
        build_p2(nc, P, sems, io)
        P.flush()
    return nc


def prep_p4a(ins, x2):
    lam = LAM
    akq = np.zeros((128, 256), np.float32)
    p_ = np.arange(128)
    for qc in range(8):
        for kt in range(32):
            if kt < 4 * qc:
                col = np.exp(-lam * (512 * qc - 128 * kt - p_)) / 32.0
            elif kt > 4 * qc + 3:
                col = np.exp(-lam * (128 * kt + p_ - 512 * qc)) / 32.0
            else:
                col = np.full(128, 1.0 / 32.0)
            akq[:, qc * 32 + kt] = col
    f_ = np.arange(512)
    dql = np.concatenate([np.exp(-lam * f_), np.exp(lam * f_)]).astype(np.float32).reshape(1, 1024)
    dec4 = np.zeros((128, 4 * 512), np.float32)
    for d in range(4):
        dec4[:, d * 512:(d + 1) * 512] = np.exp(-lam * np.abs((128 * d + p_)[:, None] - f_[None, :]))
    wk = np.ascontiguousarray(np.asarray(ins["L1_attn_Wk"], np.float32))
    wv = np.ascontiguousarray(np.asarray(ins["L1_attn_Wv"], np.float32))
    wq = np.asarray(ins["L1_attn_Wq"], np.float32)
    wd = np.ascontiguousarray(np.asarray(ins["L1_attn_W_down_kv"], np.float32))
    wupk = np.asarray(ins["L1_attn_W_up_k"], np.float32)
    wupv = np.asarray(ins["L1_attn_W_up_v"], np.float32)
    per_core = []
    for c in range(8):
        b, p = c // 2, c % 2
        x2T = x2[b].T
        per_core.append({
            "x2full": np.ascontiguousarray(np.concatenate([x2T[:, :SH], x2T[:, SH:]], axis=0)),
            "wk": wk, "wv": wv, "wdown": wd,
            "wqh": np.ascontiguousarray(wq[:, 512 * p:512 * (p + 1)]),
            "wupkh": np.ascontiguousarray(wupk[:, 512 * p:512 * (p + 1)]),
            "wupvh": np.ascontiguousarray(wupv[:, 512 * p:512 * (p + 1)]),
            "akq": akq, "dql": dql, "dec4": dec4,
        })
    return per_core


def prep_p4b(ins, x2, ot_by_core):
    wo = np.ascontiguousarray(np.asarray(ins["L1_attn_Wo"], np.float32))
    per_core = []
    for c in range(8):
        b, p = c // 2, c % 2
        otx = np.concatenate([ot_by_core[2 * b][:, SH * p:SH * (p + 1)],
                              ot_by_core[2 * b + 1][:, SH * p:SH * (p + 1)]], axis=0)
        per_core.append({
            "otx": np.ascontiguousarray(otx),
            "wo": wo,
            "x2own": np.ascontiguousarray(x2[b].T[:, SH * p:SH * (p + 1)]),
        })
    return per_core


def prep_p5(ins, x3):
    w1 = np.asarray(ins["L1_ffn_W1"], np.float32)
    w2 = np.asarray(ins["L1_ffn_W2"], np.float32)
    w2b = np.ascontiguousarray(w2.reshape(8, 8, 128, 8, 128).transpose(0, 3, 2, 1, 4).reshape(8192, 1024))
    shared = {
        "wr": np.ascontiguousarray(np.asarray(ins["L1_ffn_Wr"], np.float32)),
        "br": np.ascontiguousarray(np.asarray(ins["L1_ffn_br"], np.float32).reshape(8, 1)),
        "w1f": np.ascontiguousarray(w1.reshape(8 * 1024, 1024)),
        "w2b": w2b,
        "b1c": np.ascontiguousarray(np.asarray(ins["L1_ffn_b1"], np.float32).reshape(8 * 8, 128).T.reshape(128, 64)),
        "b2c": np.ascontiguousarray(np.asarray(ins["L1_ffn_b2"], np.float32).reshape(8 * 8, 128).T.reshape(128, 64)),
    }
    per_core = []
    for c in range(8):
        b, p = c // 2, c % 2
        d = dict(shared)
        d["x3f"] = np.ascontiguousarray(x3[b].T[:, SH * p:SH * (p + 1)])
        d["x30"] = np.ascontiguousarray(x3[b].T[:, 0:1])
        per_core.append(d)
    return per_core


def build_p4a_standalone():
    nc = mk_nc()
    io = {}
    for nm, shp in [("x2full", [2048, 2048]), ("wk", [1024, 1024]), ("wv", [1024, 1024]),
                    ("wqh", [1024, 512]), ("wdown", [1024, 256]),
                    ("wupkh", [256, 512]), ("wupvh", [256, 512]),
                    ("akq", [128, 256]), ("dql", [1, 1024]), ("dec4", [128, 4 * 512])]:
        io[nm] = nc.dram_tensor(nm, shp, F32, kind="ExternalInput").ap()
    io["otf"] = nc.dram_tensor("otf", [1024, SH], F32, kind="ExternalOutput").ap()
    with ExitStack() as stack:
        sems = Sems(nc, stack, "p4")
        P = Prog(nc)
        build_p4a(nc, P, sems, io)
        P.flush()
    return nc


def build_p4b_standalone():
    nc = mk_nc()
    io = {}
    for nm, shp in [("otx", [1024, SH]), ("wo", [1024, 1024]), ("x2own", [1024, SH])]:
        io[nm] = nc.dram_tensor(nm, shp, F32, kind="ExternalInput").ap()
    io["x3f"] = nc.dram_tensor("x3f", [1024, SH], F32, kind="ExternalOutput").ap()
    with ExitStack() as stack:
        sems = Sems(nc, stack, "pb")
        P = Prog(nc)
        build_p4b(nc, P, sems, io)
        P.flush()
    return nc


def build_p5_standalone():
    nc = mk_nc()
    io = {}
    for nm, shp in [("x3f", [1024, SH]), ("x30", [1024, 1]), ("wr", [1024, 8]), ("br", [8, 1]),
                    ("w1f", [8192, 1024]), ("w2b", [8192, 1024]), ("b1c", [128, 64]), ("b2c", [128, 64])]:
        io[nm] = nc.dram_tensor(nm, shp, F32, kind="ExternalInput").ap()
    io["yf"] = nc.dram_tensor("yf", [1024, SH], F32, kind="ExternalOutput").ap()
    with ExitStack() as stack:
        sems = Sems(nc, stack, "p5")
        P = Prog(nc)
        build_p5(nc, P, sems, io)
        P.flush()
    return nc


def build_launchA():
    nc = mk_nc()
    io = {}
    ext = [("xT", [1024, S]), ("wqkT", [1024, 3072]), ("wv_own", [1024, 512]),
           ("wpT", [1024, 1024]), ("bproj8", [128, 8]), ("x_res", [1024, SH]),
           ("gate_w", [1024, 4]), ("gate_b", [4, 1]),
           ("sw1f", [2048, 4096]), ("rw1f", [4096, 4096]), ("sw2b", [2048, 4096]), ("rw2b", [4096, 4096]),
           ("sb1c", [128, 64]), ("rb1c", [128, 128]), ("sb2c", [128, 16]), ("rb2c", [128, 32])]
    for nm, shp in ext:
        io[nm] = nc.dram_tensor(nm, shp, F32, kind="ExternalInput").ap()
    x1f = nc.dram_tensor("x1f", [1024, SH], F32)
    io["x1f"] = x1f.ap()
    io["x2f"] = nc.dram_tensor("x2f", [1024, SH], F32, kind="ExternalOutput").ap()
    with ExitStack() as stack:
        sems = Sems(nc, stack, "a")
        P = Prog(nc)
        build_p1(nc, P, sems, io)
        P.flush()
        build_p2(nc, P, sems, io)
        P.flush()
    return nc


def build_launchC():
    nc = mk_nc()
    io = {}
    for nm, shp in [("otx", [1024, SH]), ("wo", [1024, 1024]), ("x2own", [1024, SH]),
                    ("ot0", [1024, 1]), ("x20", [1024, 1]), ("wr", [1024, 8]), ("br", [8, 1]),
                    ("w1f", [8192, 1024]), ("w2b", [8192, 1024]), ("b1c", [128, 64]), ("b2c", [128, 64])]:
        io[nm] = nc.dram_tensor(nm, shp, F32, kind="ExternalInput").ap()
    x3f = nc.dram_tensor("x3f", [1024, SH], F32)
    io["x3f"] = x3f.ap()
    io["x30d"] = nc.dram_tensor("x30d", [1024, 1], F32).ap()
    io["x30"] = io["x30d"]
    io["yf"] = nc.dram_tensor("yf", [1024, SH], F32, kind="ExternalOutput").ap()
    with ExitStack() as stack:
        sems = Sems(nc, stack, "c")
        P = Prog(nc)
        build_p4b(nc, P, sems, io)
        P.flush()
        build_p5(nc, P, sems, io)
        P.flush()
    return nc


def build_launchB2():
    nc = mk_nc()
    io = {}
    for nm, shp in [("x2full", [1024, S]), ("x20", [1024, 1]),
                    ("wk", [1024, 1024]), ("wv", [1024, 1024]), ("wq", [1024, 1024]),
                    ("wdown", [1024, 256]), ("wupk", [256, 1024]), ("wupv", [256, 1024]),
                    ("wo", [1024, 1024]),
                    ("akq4", [128, 128]), ("dql", [1, 1024]), ("dec4", [128, 2048]), ("ak0", [128, 32]),
                    ("wr", [1024, 8]), ("br", [8, 1]), ("w1f", [8192, 1024]), ("w2b", [8192, 1024]),
                    ("b1c", [128, 64]), ("b2c", [128, 64])]:
        io[nm] = nc.dram_tensor(nm, shp, F32, kind="ExternalInput").ap()
    x3f = nc.dram_tensor("x3f", [1024, SH], F32)
    io["x3f"] = x3f.ap()
    io["x30d"] = nc.dram_tensor("x30d", [1024, 1], F32).ap()
    io["x30"] = io["x30d"]
    io["yf"] = nc.dram_tensor("yf", [1024, SH], F32, kind="ExternalOutput").ap()
    io["w1f_t"] = io["w1f"].tensor
    io["w2b_t"] = io["w2b"].tensor
    io["b1c_t"] = io["b1c"].tensor
    io["b2c_t"] = io["b2c"].tensor
    with ExitStack() as stack:
        sems = Sems(nc, stack, "b2")
        P = Prog(nc)
        build_p42(nc, P, sems, io)
        P.flush()
        build_p5(nc, P, sems, io)
        P.flush()
    return nc


def prep_l2_consts(ins):
    lam = LAM
    p_ = np.arange(128)
    akq4 = np.zeros((128, 128), np.float32)
    for qc in range(4):
        for kt in range(32):
            if kt < 4 * qc:
                col = np.exp(-lam * (512 * qc - 128 * kt - p_)) / 32.0
            elif kt > 4 * qc + 3:
                col = np.exp(-lam * (128 * kt + p_ - 512 * qc)) / 32.0
            else:
                col = np.full(128, 1.0 / 32.0)
            akq4[:, qc * 32 + kt] = col
    f_ = np.arange(512)
    dql = np.concatenate([np.exp(-lam * f_), np.exp(lam * f_)]).astype(np.float32).reshape(1, 1024)
    dec4 = np.zeros((128, 4 * 512), np.float32)
    for d in range(4):
        dec4[:, d * 512:(d + 1) * 512] = np.exp(-lam * np.abs((128 * d + p_)[:, None] - f_[None, :]))
    w1 = ins["L1_ffn_W1"]
    w2 = ins["L1_ffn_W2"]
    w2b = np.ascontiguousarray(w2.reshape(8, 8, 128, 8, 128).transpose(0, 3, 2, 1, 4).reshape(8192, 1024))
    shared = {
        "wk": np.ascontiguousarray(ins["L1_attn_Wk"]), "wv": np.ascontiguousarray(ins["L1_attn_Wv"]),
        "wq": np.ascontiguousarray(ins["L1_attn_Wq"]), "wdown": np.ascontiguousarray(ins["L1_attn_W_down_kv"]),
        "wupk": np.ascontiguousarray(ins["L1_attn_W_up_k"]), "wupv": np.ascontiguousarray(ins["L1_attn_W_up_v"]),
        "wo": np.ascontiguousarray(ins["L1_attn_Wo"]),
        "akq4": akq4, "dql": dql, "dec4": dec4,
        "wr": np.ascontiguousarray(ins["L1_ffn_Wr"]),
        "br": np.ascontiguousarray(ins["L1_ffn_br"].reshape(8, 1)),
        "w1f": np.ascontiguousarray(w1.reshape(8 * 1024, 1024)), "w2b": w2b,
        "b1c": np.ascontiguousarray(ins["L1_ffn_b1"].reshape(8 * 8, 128).T.reshape(128, 64)),
        "b2c": np.ascontiguousarray(ins["L1_ffn_b2"].reshape(8 * 8, 128).T.reshape(128, 64)),
    }
    ak0 = np.zeros((2, 128, 32), np.float32)
    kk = (128 * np.arange(32)[None, :] + p_[:, None]).astype(np.float64)
    kk = (p_[:, None] + 128 * np.arange(32)[None, :])
    ak0[0] = np.exp(-lam * kk) / 32.0
    ak0[1] = np.exp(-lam * (4095.0 - kk)) / 32.0
    return shared, ak0


def kernel2(ins):
    ins = _to_np(ins)
    cores = list(range(NCORES))
    pc1 = prep_p1(ins)
    shared2 = prep_p2(ins)
    in_a = []
    for c in cores:
        d = {}
        d.update(pc1[c])
        d.update(shared2)
        in_a.append(d)
    ra = run_bass_kernel_spmd(build_launchA(), in_a, core_ids=cores)
    x2f = [ra.results[c]["x2f"] for c in cores]

    shared, ak0 = prep_l2_consts(ins)
    in_b = []
    for c in cores:
        b, p = c // 2, c % 2
        x2T = np.concatenate([x2f[2 * b], x2f[2 * b + 1]], axis=1)
        if p:
            x2v = np.ascontiguousarray(x2T[:, ::-1])
        else:
            x2v = np.ascontiguousarray(x2T)
        d = dict(shared)
        d["x2full"] = x2v
        d["x20"] = np.ascontiguousarray(x2T[:, 0:1])
        d["ak0"] = ak0[p]
        in_b.append(d)
    rb = run_bass_kernel_spmd(build_launchB2(), in_b, core_ids=cores)
    out = np.empty((4, 4096, 1024), np.float32)
    for c in cores:
        b, p = c // 2, c % 2
        y = rb.results[c]["yf"]
        if p:
            out[b, SH:, :] = y[:, ::-1].T
        else:
            out[b, :SH, :] = y.T
    return out


def kernel3(ins):
    ins = _to_np(ins)
    cores = list(range(NCORES))
    pc1 = prep_p1(ins)
    shared2 = prep_p2(ins)
    in_a = []
    for c in cores:
        d = {}
        d.update(pc1[c])
        d.update(shared2)
        in_a.append(d)
    ra = run_bass_kernel_spmd(build_launchA(), in_a, core_ids=cores)
    x2f = [ra.results[c]["x2f"] for c in cores]

    x2full = [np.ascontiguousarray(np.concatenate([x2f[2 * b], x2f[2 * b + 1]], axis=1)) for b in range(4)]
    pc4a = prep_p4a_from_cols(ins, x2full)
    rb = run_bass_kernel_spmd(build_p4a_standalone(), pc4a, core_ids=cores)
    ot = [rb.results[c]["otf"] for c in cores]

    wo = np.ascontiguousarray(ins["L1_attn_Wo"])
    shared5 = prep_p5_shared(ins)
    in_c = []
    for c in cores:
        b, p = c // 2, c % 2
        otx = np.concatenate([ot[2 * b][p * 512:(p + 1) * 512, :],
                              ot[2 * b + 1][p * 512:(p + 1) * 512, :]], axis=0)
        ot0 = np.concatenate([ot[2 * b][0:512, 0:1], ot[2 * b + 1][0:512, 0:1]], axis=0)
        d = {"otx": np.ascontiguousarray(otx), "wo": wo, "x2own": x2f[c],
             "ot0": np.ascontiguousarray(ot0), "x20": np.ascontiguousarray(x2f[2 * b][:, 0:1])}
        d.update(shared5)
        in_c.append(d)
    rc_ = run_bass_kernel_spmd(build_launchC(), in_c, core_ids=cores)
    out = np.empty((4, 4096, 1024), np.float32)
    for c in cores:
        b, p = c // 2, c % 2
        out[b, SH * p:SH * (p + 1), :] = rc_.results[c]["yf"].T
    return out


RG_PAIRS = [[0, 1], [2, 3], [4, 5], [6, 7]]


def build_fused():
    from concourse.aot_env import allow_ici_all_to_all
    import contextlib
    nc = mk_nc()
    io = {}
    ext = [("xT", [1024, S]), ("wqkT", [1024, 3072]), ("wv_own", [1024, 512]),
           ("wpT", [1024, 1024]), ("bproj8", [128, 8]), ("x_res", [1024, SH]),
           ("gate_w", [1024, 4]), ("gate_b", [4, 1]),
           ("sw1f", [2048, 4096]), ("rw1f", [4096, 4096]), ("sw2b", [2048, 4096]), ("rw2b", [4096, 4096]),
           ("sb1c", [128, 64]), ("rb1c", [128, 128]), ("sb2c", [128, 16]), ("rb2c", [128, 32]),
           ("wk", [1024, 1024]), ("wv", [1024, 1024]), ("wqh", [1024, 512]), ("wdown", [1024, 256]),
           ("wupkh", [256, 512]), ("wupvh", [256, 512]),
           ("akq", [128, 256]), ("dql", [1, 1024]), ("dec4", [128, 4 * 512]),
           ("wo", [1024, 1024]),
           ("wr", [1024, 8]), ("br", [8, 1]), ("w1f", [8192, 1024]), ("w2b", [8192, 1024]),
           ("b1c", [128, 64]), ("b2c", [128, 64])]
    for nm, shp in ext:
        io[nm] = nc.dram_tensor(nm, shp, F32, kind="ExternalInput").ap()
    io["ofs"] = nc.dram_tensor("ofs", [1, 1], mybir.dt.int32, kind="ExternalInput").ap()
    x1f = nc.dram_tensor("x1f", [1024, SH], F32)
    x2f = nc.dram_tensor("x2f", [1024, SH], F32)
    x2full = nc.dram_tensor("x2full_i", [2048, SH], F32, addr_space="Local")
    otf = nc.dram_tensor("otf_i", [1024, SH], F32)
    otg = nc.dram_tensor("otg_i", [2048, SH], F32, addr_space="Local")
    x3f = nc.dram_tensor("x3f_i", [1024, SH], F32)
    x30b = nc.dram_tensor("x30b", [1024, 1], F32)
    x30g = nc.dram_tensor("x30g", [2048, 1], F32, addr_space="Local")
    io["x1f"] = x1f.ap()
    io["x2f"] = x2f.ap()
    io["x2full"] = x2full.ap()
    io["otf"] = otf.ap()
    io["yf"] = nc.dram_tensor("yf", [1024, SH], F32, kind="ExternalOutput").ap()

    with ExitStack() as stack:
        stack.enter_context(allow_ici_all_to_all())
        sems = Sems(nc, stack, "f")
        P = Prog(nc)
        build_p1(nc, P, sems, io)
        P.flush()
        build_p2(nc, P, sems, io)
        P.flush()
        cc1 = Sem(stack.enter_context(nc.semaphore("fcc1")), 16)
        P.inc("gpsimd", lambda g: g.collective_compute(
            "AllGather", ALU.bypass, replica_groups=RG_PAIRS,
            ins=[x2f.ap().opt()], outs=[x2full.ap().opt()]), cc1)
        P.wait("gpsimd", cc1, 16)
        P.flush()
        build_p4a(nc, P, sems, io)
        P.flush()
        cc2 = Sem(stack.enter_context(nc.semaphore("fcc2")), 16)
        P.inc("gpsimd", lambda g: g.collective_compute(
            "AllGather", ALU.bypass, replica_groups=RG_PAIRS,
            ins=[otf.ap().opt()], outs=[otg.ap().opt()]), cc2)
        P.wait("gpsimd", cc2, 16)
        P.flush()
        io4b = {"otg_t": otg, "ofs": io["ofs"], "wo": io["wo"], "x2own": x2f.ap(), "x3f": x3f.ap()}
        build_p4b(nc, P, sems, io4b)
        P.flush()
        cc3 = Sem(stack.enter_context(nc.semaphore("fcc3")), 16)
        cps = sems.new16("fx30")
        P.inc("gpsimd", lambda g: g.dma_start(out=x30b.ap(), in_=x3f.ap()[:, 0:1], allow_slow_non_contiguous=True), cps)
        P.wait("gpsimd", cps, cps.n)
        P.inc("gpsimd", lambda g: g.collective_compute(
            "AllGather", ALU.bypass, replica_groups=RG_PAIRS,
            ins=[x30b.ap().opt()], outs=[x30g.ap().opt()]), cc3)
        P.wait("gpsimd", cc3, 16)
        P.flush()
        io5 = {"x3f": x3f.ap(), "x30": x30g.ap()[0:1024, :], "wr": io["wr"], "br": io["br"],
               "w1f": io["w1f"], "w2b": io["w2b"], "b1c": io["b1c"], "b2c": io["b2c"],
               "yf": io["yf"]}
        build_p5(nc, P, sems, io5)
        P.flush()
    return nc


def kernel(**ins):
    return kernel2(ins)


def kernel_fused(**ins):
    ins = _to_np(ins)
    cores = list(range(NCORES))
    pc1 = prep_p1(ins)
    shared2 = prep_p2(ins)
    pc4a = prep_p4a_consts(ins)
    wo = np.ascontiguousarray(ins["L1_attn_Wo"])
    shared5 = prep_p5_shared(ins)
    in_maps = []
    for c in cores:
        d = {}
        d.update(pc1[c])
        d.update(shared2)
        d.update(pc4a[c])
        d["wo"] = wo
        d.update(shared5)
        d["ofs"] = np.array([[(c % 2) * 512 * SH]], dtype=np.int32)
        in_maps.append(d)
    nc = build_fused()
    res = run_bass_kernel_spmd(nc, in_maps, core_ids=cores)
    out = np.empty((4, 4096, 1024), np.float32)
    for c in cores:
        b, p = c // 2, c % 2
        out[b, SH * p:SH * (p + 1), :] = res.results[c]["yf"].T
    return out


def prep_p4a_consts(ins):
    lam = LAM
    akq = np.zeros((128, 256), np.float32)
    p_ = np.arange(128)
    for qc in range(8):
        for kt in range(32):
            if kt < 4 * qc:
                col = np.exp(-lam * (512 * qc - 128 * kt - p_)) / 32.0
            elif kt > 4 * qc + 3:
                col = np.exp(-lam * (128 * kt + p_ - 512 * qc)) / 32.0
            else:
                col = np.full(128, 1.0 / 32.0)
            akq[:, qc * 32 + kt] = col
    f_ = np.arange(512)
    dql = np.concatenate([np.exp(-lam * f_), np.exp(lam * f_)]).astype(np.float32).reshape(1, 1024)
    dec4 = np.zeros((128, 4 * 512), np.float32)
    for d in range(4):
        dec4[:, d * 512:(d + 1) * 512] = np.exp(-lam * np.abs((128 * d + p_)[:, None] - f_[None, :]))
    wk = np.ascontiguousarray(ins["L1_attn_Wk"])
    wv = np.ascontiguousarray(ins["L1_attn_Wv"])
    wq = ins["L1_attn_Wq"]
    wd = np.ascontiguousarray(ins["L1_attn_W_down_kv"])
    wupk = ins["L1_attn_W_up_k"]
    wupv = ins["L1_attn_W_up_v"]
    per_core = []
    for c in range(8):
        b, p = c // 2, c % 2
        per_core.append({
            "wk": wk, "wv": wv, "wdown": wd,
            "wqh": np.ascontiguousarray(wq[:, 512 * p:512 * (p + 1)]),
            "wupkh": np.ascontiguousarray(wupk[:, 512 * p:512 * (p + 1)]),
            "wupvh": np.ascontiguousarray(wupv[:, 512 * p:512 * (p + 1)]),
            "akq": akq, "dql": dql, "dec4": dec4,
        })
    return per_core


def _to_np(ins):
    out = {}
    for k, v in ins.items():
        a = np.asarray(v)
        if a.dtype != np.float32:
            a = np.asarray(a, dtype=np.float32)
        out[k] = a
    return out


def prep_p4a_from_cols(ins, x2fullT):
    lam = LAM
    akq = np.zeros((128, 256), np.float32)
    p_ = np.arange(128)
    for qc in range(8):
        for kt in range(32):
            if kt < 4 * qc:
                col = np.exp(-lam * (512 * qc - 128 * kt - p_)) / 32.0
            elif kt > 4 * qc + 3:
                col = np.exp(-lam * (128 * kt + p_ - 512 * qc)) / 32.0
            else:
                col = np.full(128, 1.0 / 32.0)
            akq[:, qc * 32 + kt] = col
    f_ = np.arange(512)
    dql = np.concatenate([np.exp(-lam * f_), np.exp(lam * f_)]).astype(np.float32).reshape(1, 1024)
    dec4 = np.zeros((128, 4 * 512), np.float32)
    for d in range(4):
        dec4[:, d * 512:(d + 1) * 512] = np.exp(-lam * np.abs((128 * d + p_)[:, None] - f_[None, :]))
    wk = np.ascontiguousarray(ins["L1_attn_Wk"])
    wv = np.ascontiguousarray(ins["L1_attn_Wv"])
    wq = ins["L1_attn_Wq"]
    wd = np.ascontiguousarray(ins["L1_attn_W_down_kv"])
    wupk = ins["L1_attn_W_up_k"]
    wupv = ins["L1_attn_W_up_v"]
    per_core = []
    for c in range(8):
        b, p = c // 2, c % 2
        per_core.append({
            "x2full": np.ascontiguousarray(np.concatenate([x2fullT[b][:, :SH], x2fullT[b][:, SH:]], axis=0)),
            "wk": wk, "wv": wv, "wdown": wd,
            "wqh": np.ascontiguousarray(wq[:, 512 * p:512 * (p + 1)]),
            "wupkh": np.ascontiguousarray(wupk[:, 512 * p:512 * (p + 1)]),
            "wupvh": np.ascontiguousarray(wupv[:, 512 * p:512 * (p + 1)]),
            "akq": akq, "dql": dql, "dec4": dec4,
        })
    return per_core


def prep_p5_shared(ins):
    w1 = ins["L1_ffn_W1"]
    w2 = ins["L1_ffn_W2"]
    w2b = np.ascontiguousarray(w2.reshape(8, 8, 128, 8, 128).transpose(0, 3, 2, 1, 4).reshape(8192, 1024))
    return {
        "wr": np.ascontiguousarray(ins["L1_ffn_Wr"]),
        "br": np.ascontiguousarray(ins["L1_ffn_br"].reshape(8, 1)),
        "w1f": np.ascontiguousarray(w1.reshape(8 * 1024, 1024)),
        "w2b": w2b,
        "b1c": np.ascontiguousarray(ins["L1_ffn_b1"].reshape(8 * 8, 128).T.reshape(128, 64)),
        "b2c": np.ascontiguousarray(ins["L1_ffn_b2"].reshape(8 * 8, 128).T.reshape(128, 64)),
    }


# version 10
# speedup vs baseline: 1.4393x; 1.1258x over previous
import numpy as np
import ml_dtypes
from contextlib import ExitStack

import concourse.bass as bass
import concourse.mybir as mybir
from concourse.bass_utils import run_bass_kernel_spmd

F32 = mybir.dt.float32
BF16 = mybir.dt.bfloat16
AF = mybir.ActivationFunctionType
ALU = mybir.AluOpType
bfdt = ml_dtypes.bfloat16


def mk_nc():
    return bass.Bass("TRN2", target_bir_lowering=False, dynamic_dma_scratch_size=16384)

B, S, D = 4, 4096, 1024
NCORES = 8
SH = S // 2
NSC = SH // 512

L0_SCALE = 1.0 / 8.0
L0_C = 16.0
N_HEADS, HD, DL = 16, 64, 256
L1_SCALE = float(1.0 / (np.sqrt(np.float32(1024.0)) + 1e-8))
L1_C = 4.0
LAM = 1e-4



class Sem:
    def __init__(self, handle, step=1):
        self.h = handle
        self.step = step
        self.n = 0

    def bump(self):
        self.n += self.step
        return self.n


class Prog:
    ENGINES = ("gpsimd", "sync", "scalar", "vector", "tensor")

    def __init__(self, nc):
        self.nc = nc
        self.steps = {e: [] for e in self.ENGINES}

    def on(self, eng, fn):
        self.steps[eng].append(fn)

    def inc(self, eng, fn, sem):
        t = sem.bump()
        self.steps[eng].append(lambda E: fn(E).then_inc(sem.h, sem.step))
        return t

    def wait(self, eng, sem, val):
        self.steps[eng].append(lambda E: E.wait_ge(sem.h, val))

    def flush(self):
        if not any(self.steps[e] for e in self.ENGINES):
            return
        with self.nc.Block() as blk:
            for e in self.ENGINES:
                fns = self.steps[e]
                if fns:
                    getattr(blk, e)(lambda eng, fns=fns: [f(eng) for f in fns])
        self.steps = {e: [] for e in self.ENGINES}


class Sems:

    def __init__(self, nc, stack, tag=""):
        self.nc = nc
        self.stack = stack
        self._n16 = 0
        self.tag = tag
        self.WL = Sem(stack.enter_context(nc.semaphore(f"{tag}wl")), 16)
        self.MM = Sem(stack.enter_context(nc.semaphore(f"{tag}mm")), 1)
        self.SE = Sem(stack.enter_context(nc.semaphore(f"{tag}se")), 1)
        self.VE = Sem(stack.enter_context(nc.semaphore(f"{tag}ve")), 1)
        self.ST = Sem(stack.enter_context(nc.semaphore(f"{tag}st")), 16)
        self.XL = Sem(stack.enter_context(nc.semaphore(f"{tag}xl")), 16)
        self.GP = Sem(stack.enter_context(nc.semaphore(f"{tag}gp")), 1)

    def new16(self, name=""):
        self._n16 += 1
        return Sem(self.stack.enter_context(self.nc.semaphore(f"{self.tag}L{self._n16}{name}")), 16)


def mm_group(P, sems, bank_ap, pairs, bump=True, accum_from=None, accum_to=None):
    n = len(pairs)
    tick = None
    for i, (st, mv) in enumerate(pairs):
        first = (i == 0) if accum_from is None else accum_from
        last = (i == n - 1) if accum_to is None else False
        if i == n - 1 and bump:
            tick = P.inc("tensor",
                         lambda t, st=st, mv=mv, first=first, last=last:
                         t.matmul(bank_ap, st, mv, start=first, stop=last), sems.MM)
        else:
            P.on("tensor",
                 lambda t, st=st, mv=mv, first=first, last=last:
                 t.matmul(bank_ap, st, mv, start=first, stop=last))
    return tick


def emit_proj(P, sems, *, nO, nC, nSc, st_ap, mv_ap, banks, bank_ticks, evac):
    nb = len(banks)
    ti = 0
    for o in range(nO):
        for s in range(nSc):
            bi = ti % nb
            bank = banks[bi]
            prev = bank_ticks.get(id(banks[bi]))
            if prev is not None:
                P.wait("tensor", prev[0], prev[1])
            pairs = [(st_ap(c, o), mv_ap(c, s)) for c in range(nC)]
            mmt = mm_group(P, sems, bank[:, :], pairs)
            bank_ticks[id(banks[bi])] = evac(o, s, bank, mmt)
            ti += 1


def inc_chain(P, eng, fn, sem):
    t = P.inc(eng, fn, sem)
    P.wait(eng, sem, t)
    return t


def scalar_copy_evac(P, sems, dst_ap, bank, mmt):
    P.wait("scalar", sems.MM, mmt)
    t = P.inc("scalar", lambda sc: sc.copy(dst_ap, bank[:, :]), sems.SE)
    return (sems.SE, t)



def build_p1(nc, P, sems, io):
    ph0 = ExitStack()
    with ph0:
        pb = [ph0.enter_context(nc.psum_tensor(f"p1_pb{i}", [128, 512], F32)) for i in range(8)]
        OT2 = ph0.enter_context(nc.sbuf_tensor("p1_ot2", [128, 8 * SH], BF16))
        _build_p1_attn(nc, P, sems, io, pb, OT2)
        _build_p1_proj(nc, P, sems, io, OT2, pb)


def _build_p1_attn(nc, P, sems, io, pb, OT2):
  pha = ExitStack()
  with pha:
    KT = [pha.enter_context(nc.sbuf_tensor(f"p1_kt{i}", [128, S], BF16)) for i in range(8)]
    VK = pha.enter_context(nc.sbuf_tensor("p1_vk", [128, 32 * 512], BF16))

    ph = ExitStack()
    with ph:
        WK = ph.enter_context(nc.sbuf_tensor("p1_wk", [128, 8 * 1024], BF16))
        WV = ph.enter_context(nc.sbuf_tensor("p1_wv", [128, 8 * 512], BF16))
        XS = [ph.enter_context(nc.sbuf_tensor(f"p1_xs{k}", [128, 8 * 512], BF16)) for k in range(2)]

        wA = sems.new16("wa")
        sXS = [sems.new16(f"xs{k}") for k in range(2)]
        for i in range(8):
            P.inc("gpsimd", lambda g, i=i: g.dma_start(out=WK[:, i * 1024:(i + 1) * 1024],
                                                       in_=io["wqkT"][i * 128:(i + 1) * 128, 1024:2048]), wA)
            P.inc("gpsimd", lambda g, i=i: g.dma_start(out=WV[:, i * 512:(i + 1) * 512],
                                                       in_=io["wv_own"][i * 128:(i + 1) * 128, :]), wA)
        P.wait("tensor", wA, wA.n)

        xs_used = {}
        xs_load = {}

        def load_xs(si):
            buf = XS[si % 2]
            sem = sXS[si % 2]
            prev = xs_used.get(si - 2)
            if prev is not None:
                P.wait("gpsimd", sems.MM, prev)
            for e in range(8):
                P.inc("gpsimd",
                      lambda g, e=e, buf=buf, si=si:
                      g.dma_start(out=buf[:, e * 512:(e + 1) * 512],
                                  in_=io["xT"][e * 128:(e + 1) * 128, si * 512:(si + 1) * 512]),
                      sem)
            xs_load[si] = (sem, sem.n)

        load_xs(0)
        bank_ticks = {}
        for si in range(8):
            xbuf = XS[si % 2]
            P.wait("tensor", xs_load[si][0], xs_load[si][1])
            emit_proj(P, sems, nO=8, nC=8, nSc=1,
                      st_ap=lambda c, o: WK[:, c * 1024 + o * 128: c * 1024 + (o + 1) * 128],
                      mv_ap=lambda c, _s, xbuf=xbuf: xbuf[:, c * 512:(c + 1) * 512],
                      banks=pb[0:2], bank_ticks=bank_ticks,
                      evac=lambda o, _s, bank, mmt, si=si: scalar_copy_evac(
                          P, sems, KT[o][:, si * 512:(si + 1) * 512], bank, mmt))
            emit_proj(P, sems, nO=4, nC=8, nSc=1,
                      st_ap=lambda c, j, xbuf=xbuf: xbuf[:, c * 512 + j * 128: c * 512 + (j + 1) * 128],
                      mv_ap=lambda c, _s: WV[:, c * 512:(c + 1) * 512],
                      banks=pb[0:2], bank_ticks=bank_ticks,
                      evac=lambda j, _s, bank, mmt, si=si: scalar_copy_evac(
                          P, sems, VK[:, (4 * si + j) * 512:(4 * si + j + 1) * 512], bank, mmt))
            xs_used[si] = sems.MM.n
            if si < 7:
                load_xs(si + 1)
        P.flush()

    ph = ExitStack()
    with ph:
        WQ = ph.enter_context(nc.sbuf_tensor("p1_wq", [128, 8 * 1024], BF16))
        XQ = [ph.enter_context(nc.sbuf_tensor(f"p1_xq{k}", [128, 8 * 512], BF16)) for k in range(2)]
        QT = [ph.enter_context(nc.sbuf_tensor(f"p1_qt{k}", [128, 8 * 512], BF16)) for k in range(2)]
        Ebuf = [ph.enter_context(nc.sbuf_tensor(f"p1_e{k}", [128, 512], BF16)) for k in range(3)]
        ones_bf = ph.enter_context(nc.sbuf_tensor("p1_ones", [128, 1], BF16))
        cbias = ph.enter_context(nc.sbuf_tensor("p1_cb", [128, 1], F32))
        rcol = [ph.enter_context(nc.sbuf_tensor(f"p1_rc{k}", [128, 1], F32)) for k in range(4)]

        wB = sems.new16("wb")
        sXQ = [sems.new16(f"xq{k}") for k in range(2)]
        for i in range(8):
            P.inc("gpsimd", lambda g, i=i: g.dma_start(out=WQ[:, i * 1024:(i + 1) * 1024],
                                                       in_=io["wqkT"][i * 128:(i + 1) * 128, 0:1024]), wB)
        P.inc("gpsimd", lambda g: g.memset(ones_bf[:, :], 1.0), sems.GP)
        P.inc("gpsimd", lambda g: g.memset(cbias[:, :], -L0_C), sems.GP)
        P.wait("tensor", sems.GP, sems.GP.n)
        P.wait("tensor", wB, wB.n)
        P.wait("scalar", sems.GP, sems.GP.n)

        xq_used = {}
        xq_load = {}

        def load_xq(qci):
            buf = XQ[qci % 2]
            sem = sXQ[qci % 2]
            prev = xq_used.get(qci - 2)
            if prev is not None:
                P.wait("gpsimd", sems.MM, prev)
            for e in range(8):
                P.inc("gpsimd",
                      lambda g, e=e, buf=buf, qci=qci:
                      g.dma_start(out=buf[:, e * 512:(e + 1) * 512],
                                  in_=io["xT"][e * 128:(e + 1) * 128, qci * 512:(qci + 1) * 512]),
                      sem)
            xq_load[qci] = (sem, sem.n)

        load_xq(0)
        prev_chunk_evac = None
        bank_ticks = {}
        for qc in range(8):
            qbuf = QT[qc % 2]
            xbuf = XQ[qc % 2]
            P.wait("tensor", xq_load[qc][0], xq_load[qc][1])
            emit_proj(P, sems, nO=8, nC=8, nSc=1,
                      st_ap=lambda c, o: WQ[:, c * 1024 + o * 128: c * 1024 + (o + 1) * 128],
                      mv_ap=lambda c, _s, xbuf=xbuf: xbuf[:, c * 512:(c + 1) * 512],
                      banks=pb[0:2], bank_ticks=bank_ticks,
                      evac=lambda o, _s, bank, mmt, qbuf=qbuf: scalar_copy_evac(
                          P, sems, qbuf[:, o * 512:(o + 1) * 512], bank, mmt))
            xq_used[qc] = sems.MM.n
            if qc < 7:
                load_xq(qc + 1)
            P.wait("tensor", sems.SE, sems.SE.n)
            if prev_chunk_evac is not None:
                P.wait("tensor", sems.VE, prev_chunk_evac)

            for kt in range(32):
                bi = kt % 2
                bank = pb[bi]
                prev = bank_ticks.get(id(pb[bi]))
                if prev is not None:
                    P.wait("tensor", prev[0], prev[1])
                pairs = [(KT[dd][:, kt * 128:(kt + 1) * 128], qbuf[:, dd * 512:(dd + 1) * 512])
                         for dd in range(8)]
                mmt = mm_group(P, sems, bank[:, :], pairs)
                eb = Ebuf[kt % 3]
                P.wait("scalar", sems.MM, mmt)
                et = P.inc("scalar",
                           lambda sc, eb=eb, bank=bank:
                           sc.activation(eb[:, :], bank[:, :], AF.Exp, bias=cbias[:, :], scale=L0_SCALE),
                           sems.SE)
                bank_ticks[id(pb[bi])] = (sems.SE, et)
                P.wait("tensor", sems.SE, et)
                first, last = kt == 0, kt == 31
                for qs in range(4):
                    stE = eb[:, qs * 128:(qs + 1) * 128]
                    mvV = VK[:, kt * 512:(kt + 1) * 512]
                    avb = pb[2 + qs]
                    P.on("tensor",
                         lambda t, stE=stE, mvV=mvV, avb=avb, first=first, last=last:
                         t.matmul(avb[:, :], stE, mvV, start=first, stop=last))
                    dcol = pb[6][:, qs:qs + 1]
                    dfirst = first and qs == 0
                    dlast = last and qs == 3
                    if dlast:
                        mmt2 = P.inc("tensor",
                                     lambda t, stE=stE, dcol=dcol:
                                     t.matmul(dcol, stE, ones_bf[:, :], start=False, stop=True), sems.MM)
                    else:
                        P.on("tensor",
                             lambda t, stE=stE, dcol=dcol, dfirst=dfirst:
                             t.matmul(dcol, stE, ones_bf[:, :], start=dfirst, stop=False))
            P.wait("vector", sems.MM, mmt2)
            for qs in range(4):
                qt = qc * 4 + qs
                hc, r = qt % 8, qt // 8
                rc = rcol[qs]
                inc_chain(P, "vector", lambda v, rc=rc, qs=qs: v.reciprocal(rc[:, :], pb[6][:, qs:qs + 1]), sems.VE)
                dst = OT2[:, hc * SH + r: hc * SH + r + 4 * 511 + 1: 4]
                t2 = inc_chain(P, "vector",
                               lambda v, dst=dst, qs=qs, rc=rc:
                               v.tensor_scalar_mul(dst, pb[2 + qs][:, :], rc[:, :]), sems.VE)
            prev_chunk_evac = t2
        P.flush()


def _build_p1_proj(nc, P, sems, io, OT2, pb):
    ph2 = ExitStack()
    with ph2:
        RES = [ph2.enter_context(nc.sbuf_tensor(f"p1_res{i}", [128, SH], F32)) for i in range(8)]
        _p1_proj_mm(nc, P, sems, io, OT2, pb, RES)
        emit_ln(nc, P, sems, "p1ln", RES, io["x1f"], pb)


def _p1_proj_mm(nc, P, sems, io, OT2, pb, RES):
    ph2 = ExitStack()
    with ph2:
        WP = ph2.enter_context(nc.sbuf_tensor("p1_wp", [128, 8 * 1024], BF16))
        XR = [ph2.enter_context(nc.sbuf_tensor(f"p1_xr{i}", [128, SH], F32)) for i in range(8)]
        bp = ph2.enter_context(nc.sbuf_tensor("p1_bp", [128, 8], F32))

        wP = sems.new16("wp")
        xR = sems.new16("xr")
        for i in range(8):
            P.inc("gpsimd", lambda g, i=i: g.dma_start(out=WP[:, i * 1024:(i + 1) * 1024],
                                                       in_=io["wpT"][i * 128:(i + 1) * 128, :]), wP)
            P.inc("sync", lambda sy, i=i: sy.dma_start(out=XR[i][:, :],
                                                       in_=io["x_res"][i * 128:(i + 1) * 128, :]), xR)
        P.inc("sync", lambda sy: sy.dma_start(out=bp[:, :], in_=io["bproj8"][:, :]), xR)
        P.wait("vector", wP, wP.n)
        P.wait("vector", xR, xR.n)
        for o in range(8):
            P.inc("vector",
                  lambda v, o=o: v.tensor_scalar_add(XR[o][:, :], XR[o][:, :], bp[:, o:o + 1]), sems.VE)
        P.flush()

        bank_ticks = {}

        def proj_evac(o, s, bank, mmt):
            P.wait("vector", sems.MM, mmt)
            t = P.inc("vector",
                      lambda v, o=o, s=s, bank=bank:
                      v.tensor_add(RES[o][:, s * 512:(s + 1) * 512], bank[:, :],
                                   XR[o][:, s * 512:(s + 1) * 512]), sems.VE)
            return (sems.VE, t)

        emit_proj(P, sems, nO=8, nC=8, nSc=4,
                  st_ap=lambda c, o: WP[:, c * 1024 + o * 128: c * 1024 + (o + 1) * 128],
                  mv_ap=lambda c, s: OT2[:, c * SH + s * 512: c * SH + (s + 1) * 512],
                  banks=pb[0:4], bank_ticks=bank_ticks, evac=proj_evac)
        P.flush()



def emit_ln(nc, P, sems, tag, RES, out_f32_dram, pbs):
    ph = ExitStack()
    with ph:
        R16 = [ph.enter_context(nc.sbuf_tensor(f"{tag}_r16{k}", [128, SH], BF16)) for k in range(2)]
        SQ = [ph.enter_context(nc.sbuf_tensor(f"{tag}_sq{k}", [128, SH], BF16)) for k in range(2)]
        onesc = ph.enter_context(nc.sbuf_tensor(f"{tag}_oc", [128, 1], BF16))
        onesr = ph.enter_context(nc.sbuf_tensor(f"{tag}_or", [1, 128], F32))
        zcol = ph.enter_context(nc.sbuf_tensor(f"{tag}_zc", [1, 1], F32))
        ecol = ph.enter_context(nc.sbuf_tensor(f"{tag}_ec", [1, 1], F32))
        rows = ph.enter_context(nc.sbuf_tensor(f"{tag}_rows", [1, 4 * SH], F32))
        AB = ph.enter_context(nc.sbuf_tensor(f"{tag}_abb", [128, SH], F32))
        CB = ph.enter_context(nc.sbuf_tensor(f"{tag}_cbb", [128, SH], F32))
        XF = [ph.enter_context(nc.sbuf_tensor(f"{tag}_xf{k}", [128, SH], F32)) for k in range(2)]

        P.on("gpsimd", lambda g: g.memset(onesc[:, :], 1.0))
        P.on("gpsimd", lambda g: g.memset(onesr[:, :], 1.0))
        P.on("gpsimd", lambda g: g.memset(zcol[:, :], 0.0))
        P.on("gpsimd", lambda g: g.memset(ecol[:, :], 1e-5))
        P.flush()

        pe_tick = {}
        for o in range(8):
            k = o % 2
            if o >= 2:
                P.wait("vector", sems.MM, pe_tick[o - 2])
                P.wait("scalar", sems.MM, pe_tick[o - 2])
            cv = P.inc("vector", lambda v, o=o, k=k: v.tensor_copy(R16[k][:, :], RES[o][:, :]), sems.VE)
            cs = P.inc("scalar", lambda sc, o=o, k=k: sc.activation(SQ[k][:, :], RES[o][:, :], AF.Square), sems.SE)
            P.wait("tensor", sems.VE, cv)
            P.wait("tensor", sems.SE, cs)
            for s in range(NSC):
                first, last = o == 0, o == 7
                P.on("tensor", lambda t, s=s, k=k, first=first, last=last:
                     t.matmul(pbs[s][0:1, :], onesc[:, :], R16[k][:, s * 512:(s + 1) * 512],
                              start=first, stop=last))
                if s < NSC - 1:
                    P.on("tensor", lambda t, s=s, k=k, first=first, last=last:
                         t.matmul(pbs[4 + s][0:1, :], onesc[:, :], SQ[k][:, s * 512:(s + 1) * 512],
                                  start=first, stop=last))
                else:
                    pe_tick[o] = P.inc("tensor", lambda t, s=s, k=k, first=first, last=last:
                                       t.matmul(pbs[4 + s][0:1, :], onesc[:, :], SQ[k][:, s * 512:(s + 1) * 512],
                                                start=first, stop=last), sems.MM)
        P.wait("vector", sems.MM, pe_tick[7])
        for s in range(NSC):
            P.inc("vector", lambda v, s=s: v.tensor_copy(rows[0:1, s * 512:(s + 1) * 512], pbs[s][0:1, :]), sems.VE)
            P.inc("vector", lambda v, s=s: v.tensor_copy(rows[0:1, SH + s * 512: SH + (s + 1) * 512],
                                                         pbs[4 + s][0:1, :]), sems.VE)
        P.flush()

        s1 = rows[0:1, 0:SH]
        s2 = rows[0:1, SH:2 * SH]
        a = rows[0:1, 2 * SH:3 * SH]
        c = rows[0:1, 3 * SH:4 * SH]
        inc_chain(P, "vector", lambda v: v.tensor_scalar_mul(c, s1, 1.0 / 1024.0), sems.VE)
        inc_chain(P, "vector", lambda v: v.tensor_scalar_mul(s2, s2, 1.0 / 1024.0), sems.VE)
        inc_chain(P, "vector", lambda v: v.tensor_tensor(a, c, c, ALU.mult), sems.VE)
        inc_chain(P, "vector", lambda v: v.tensor_tensor(a, s2, a, ALU.subtract), sems.VE)
        P.flush()
        P.inc("scalar", lambda sc: sc.activation(a, a, AF.Ln, bias=ecol[:, :], scale=1.0), sems.SE)
        P.flush()
        P.inc("scalar", lambda sc: sc.activation(a, a, AF.Exp, bias=zcol[:, :], scale=-0.5), sems.SE)
        P.flush()
        for s in range(NSC):
            P.on("tensor", lambda t, s=s: t.matmul(pbs[s][:, :], onesr[:, :],
                                                   a[0:1, s * 512:(s + 1) * 512], start=True, stop=True))
            if s < NSC - 1:
                P.on("tensor", lambda t, s=s: t.matmul(pbs[4 + s][:, :], onesr[:, :],
                                                       c[0:1, s * 512:(s + 1) * 512], start=True, stop=True))
            else:
                bct = P.inc("tensor", lambda t, s=s: t.matmul(pbs[4 + s][:, :], onesr[:, :],
                                                              c[0:1, s * 512:(s + 1) * 512], start=True, stop=True), sems.MM)
        P.wait("vector", sems.MM, bct)
        for s in range(NSC):
            P.inc("vector", lambda v, s=s: v.tensor_copy(AB[:, s * 512:(s + 1) * 512], pbs[s][:, :]), sems.VE)
            P.inc("vector", lambda v, s=s: v.tensor_copy(CB[:, s * 512:(s + 1) * 512], pbs[4 + s][:, :]), sems.VE)
        P.flush()

        sstore = [sems.new16(f"{tag}st{k}") for k in range(2)]
        store_tick = {}
        for o in range(8):
            k = o % 2
            if o >= 2:
                P.wait("vector", store_tick[o - 2][0], store_tick[o - 2][1])
            inc_chain(P, "vector", lambda v, o=o, k=k: v.tensor_tensor(XF[k][:, :], RES[o][:, :], CB[:, :], ALU.subtract), sems.VE)
            vt = inc_chain(P, "vector", lambda v, k=k: v.tensor_tensor(XF[k][:, :], XF[k][:, :], AB[:, :], ALU.mult), sems.VE)
            P.wait("sync", sems.VE, vt)
            t = P.inc("sync", lambda sy, o=o, k=k: sy.dma_start(out=out_f32_dram[o * 128:(o + 1) * 128, :],
                                                                in_=XF[k][:, :]), sstore[k])
            store_tick[o] = (sstore[k], t)
        P.wait("sync", sstore[0], sstore[0].n)
        P.wait("sync", sstore[1], sstore[1].n)
        P.flush()




def build_p2(nc, P, sems, io):
    ph0 = ExitStack()
    with ph0:
        pb = [ph0.enter_context(nc.psum_tensor(f"p2_pb{i}", [128, 512], F32)) for i in range(8)]
        X1bT = ph0.enter_context(nc.sbuf_tensor("p2_x1b", [128, 8 * SH], BF16))
        X1b = [X1bT[:, i * SH:(i + 1) * SH] for i in range(8)]
        B4 = ph0.enter_context(nc.sbuf_tensor("p2_b4", [128, 4 * SH], BF16))

        lx = sems.new16("p2x")
        for i in range(8):
            P.inc("gpsimd", lambda g, i=i: g.dma_start(out=X1b[i],
                                                       in_=io["x1f"][i * 128:(i + 1) * 128, :]), lx)
        _p2_gate(nc, P, sems, io, pb, X1b, B4, lx)

        REST = ph0.enter_context(nc.sbuf_tensor("p2_res", [128, 8 * SH], F32))
        RES = [REST[:, i * SH:(i + 1) * SH] for i in range(8)]
        _p2_units(nc, P, sems, io, pb, X1b, B4, RES)
        emit_ln(nc, P, sems, "p2ln", RES, io["x2f"], pb)


def _p2_gate(nc, P, sems, io, pb, X1b, B4, lx):
    ph = ExitStack()
    with ph:
        GW = ph.enter_context(nc.sbuf_tensor("p2_gw", [128, 8 * 4], F32))
        GB = ph.enter_context(nc.sbuf_tensor("p2_gb", [4, 1], F32))
        G4 = ph.enter_context(nc.sbuf_tensor("p2_g4", [4, SH], F32))
        G1 = ph.enter_context(nc.sbuf_tensor("p2_g1", [1, 4 * SH], F32))
        MW = ph.enter_context(nc.sbuf_tensor("p2_mw", [1, 4 * SH], F32))
        TR = ph.enter_context(nc.sbuf_tensor("p2_tr", [1, 3 * SH], F32))
        onesr = ph.enter_context(nc.sbuf_tensor("p2_or", [1, 128], F32))

        lw = sems.new16("p2gw")
        XG = [ph.enter_context(nc.sbuf_tensor(f"p2_xg{k}", [128, 8 * 512], F32)) for k in range(2)]
        for c in range(8):
            P.inc("gpsimd", lambda g, c=c: g.dma_start(out=GW[:, c * 4:(c + 1) * 4],
                                                       in_=io["gate_w"][c * 128:(c + 1) * 128, :]), lw)
        P.inc("gpsimd", lambda g: g.dma_start(out=GB[:, :], in_=io["gate_b"][:, :]), lw)
        sXG = [sems.new16(f"p2xg{k}") for k in range(2)]
        xg_used = {}
        xg_ld = {}

        def load_xg(si):
            k = si % 2
            prev = xg_used.get(si - 2)
            if prev is not None:
                P.wait("sync", sems.MM, prev)
            for c in range(8):
                P.inc("sync", lambda sy, c=c, k=k, si=si:
                      sy.dma_start(out=XG[k][:, c * 512:(c + 1) * 512],
                                   in_=io["x1f"][c * 128:(c + 1) * 128, si * 512:(si + 1) * 512]), sXG[k])
            xg_ld[si] = (sXG[k], sXG[k].n)

        load_xg(0)
        P.inc("gpsimd", lambda g: g.memset(onesr[:, :], 1.0), sems.GP)
        gp_t = sems.GP.n
        P.wait("tensor", lw, lw.n)
        P.wait("tensor", lx, lx.n)
        for s4 in range(NSC):
            bank = pb[s4 % 2]
            P.wait("tensor", xg_ld[s4][0], xg_ld[s4][1])
            if s4 < NSC - 1:
                load_xg(s4 + 1)
            pairs = [(GW[:, c * 4:(c + 1) * 4], XG[s4 % 2][:, c * 512:(c + 1) * 512]) for c in range(8)]
            mmt = mm_group(P, sems, bank[0:4, :], pairs)
            xg_used[s4] = sems.MM.n
            P.wait("vector", sems.MM, mmt)
            t = inc_chain(P, "vector", lambda v, s4=s4, bank=bank:
                          v.tensor_scalar_add(G4[:, s4 * 512:(s4 + 1) * 512], bank[0:4, :], GB[:, :]), sems.VE)
            P.wait("tensor", sems.VE, t)
        st0 = sems.new16("p2g1")
        P.wait("sync", sems.VE, sems.VE.n)
        for i in range(4):
            P.inc("sync", lambda sy, i=i: sy.dma_start(out=G1[0:1, i * SH:(i + 1) * SH],
                                                       in_=G4[i:i + 1, :]), st0)
        P.wait("vector", st0, st0.n)
        P.wait("scalar", st0, st0.n)

        def row(t_, i):
            return t_[0:1, i * SH:(i + 1) * SH]

        e_r = [row(G1, i) for i in range(4)]
        w_r = [row(MW, i) for i in range(4)]
        m_r, t1, t2 = row(TR, 0), row(TR, 1), row(TR, 2)
        V, E_ = "vector", sems.VE
        inc_chain(P, V, lambda v: v.tensor_tensor(m_r, e_r[0], e_r[1], ALU.max), E_)
        inc_chain(P, V, lambda v: v.tensor_tensor(t1, e_r[2], e_r[3], ALU.max), E_)
        inc_chain(P, V, lambda v: v.tensor_tensor(m_r, m_r, t1, ALU.max), E_)
        for i in range(4):
            inc_chain(P, V, lambda v, i=i: v.tensor_tensor(e_r[i], e_r[i], m_r, ALU.subtract), E_)
            P.wait("scalar", sems.VE, sems.VE.n)
            t = P.inc("scalar", lambda sc, i=i: sc.activation(e_r[i], e_r[i], AF.Exp), sems.SE)
            P.wait(V, sems.SE, t)
        inc_chain(P, V, lambda v: v.tensor_tensor(t1, e_r[0], e_r[1], ALU.add), E_)
        inc_chain(P, V, lambda v: v.tensor_tensor(t2, e_r[2], e_r[3], ALU.add), E_)
        inc_chain(P, V, lambda v: v.tensor_tensor(t1, t1, t2, ALU.add), E_)
        inc_chain(P, V, lambda v: v.reciprocal(t1, t1), E_)
        for i in range(4):
            js = [j for j in range(4) if j != i]
            inc_chain(P, V, lambda v, i=i, j=js[0]: v.tensor_tensor(w_r[i], e_r[j], e_r[i], ALU.is_gt), E_)
            for j in js[1:]:
                inc_chain(P, V, lambda v, i=i, j=j: v.tensor_tensor(t2, e_r[j], e_r[i], ALU.is_gt), E_)
                inc_chain(P, V, lambda v, i=i: v.tensor_tensor(w_r[i], w_r[i], t2, ALU.add), E_)
            inc_chain(P, V, lambda v, i=i: v.tensor_scalar(w_r[i], w_r[i], 2.0, None, ALU.is_lt), E_)
            inc_chain(P, V, lambda v, i=i: v.tensor_tensor(w_r[i], w_r[i], e_r[i], ALU.mult), E_)
            inc_chain(P, V, lambda v, i=i: v.tensor_tensor(w_r[i], w_r[i], t1, ALU.mult), E_)
        P.wait("tensor", sems.VE, sems.VE.n)
        P.wait("tensor", sems.GP, gp_t)
        for i in range(4):
            for s4 in range(NSC):
                bank = pb[2 + (s4 % 2)]
                mmt = P.inc("tensor", lambda t, i=i, s4=s4, bank=bank:
                            t.matmul(bank[:, :], onesr[:, :], w_r[i][0:1, s4 * 512:(s4 + 1) * 512],
                                     start=True, stop=True), sems.MM)
                P.wait("vector", sems.MM, mmt)
                t = inc_chain(P, V, lambda v, i=i, s4=s4, bank=bank:
                              v.tensor_copy(B4[:, i * SH + s4 * 512: i * SH + (s4 + 1) * 512], bank[:, :]), sems.VE)
                P.wait("tensor", sems.VE, t)
        P.flush()


def _p2_units(nc, P, sems, io, pb, X1b, B4, RES, which=None):
    ph = ExitStack()
    with ph:
        HT = ph.enter_context(nc.sbuf_tensor("p2_ht", [128, 32 * 1024], BF16))
        W1S = [ph.enter_context(nc.sbuf_tensor(f"p2_w1s{k}", [128, 8 * 256], BF16)) for k in range(2)]
        W2S = [ph.enter_context(nc.sbuf_tensor(f"p2_w2s{k}", [128, 32 * 128], BF16)) for k in range(2)]
        X1R = [ph.enter_context(nc.sbuf_tensor("p2_x1r0", [128, 512], F32))]
        B1C = ph.enter_context(nc.sbuf_tensor("p2_b1c", [128, 6 * 32], F32))
        B2C = ph.enter_context(nc.sbuf_tensor("p2_b2c", [128, 6 * 8], F32))
        TMP = ph.enter_context(nc.sbuf_tensor("p2_tmp", [128, 512], F32))
        TMPB = ph.enter_context(nc.sbuf_tensor("p2_tmpb", [128, 512], BF16))

        lb = sems.new16("p2b")
        P.inc("sync", lambda sy: sy.dma_start(out=B1C[:, 0:64], in_=io["sb1c"][:, :]), lb)
        P.inc("sync", lambda sy: sy.dma_start(out=B1C[:, 64:192], in_=io["rb1c"][:, :]), lb)
        P.inc("sync", lambda sy: sy.dma_start(out=B2C[:, 0:16], in_=io["sb2c"][:, :]), lb)
        P.inc("sync", lambda sy: sy.dma_start(out=B2C[:, 16:48], in_=io["rb2c"][:, :]), lb)
        P.wait("scalar", lb, lb.n)
        P.wait("vector", lb, lb.n)

        units = []
        for n in range(2):
            units.append((io["sw1f"], n * 1024, io["sw2b"], n * 1024, n * 32, n * 8, "s", n))
        for e in range(4):
            units.append((io["rw1f"], e * 1024, io["rw2b"], e * 1024, 64 + e * 32, 16 + e * 8, "r", e))

        sW1 = [sems.new16(f"p2w1{k}") for k in range(2)]
        sW2 = [sems.new16(f"p2w2{k}") for k in range(2)]
        sXR = [sems.new16("p2xr0")]
        w1_used = {}
        w2_used = {}
        xr_used = {}
        w1_ld = {}
        w2_ld = {}
        xr_ld = {}

        def load_w1(idx, wdram, wrow, fs):
            k = idx % 2
            prev = w1_used.get(idx - 2)
            if prev is not None:
                P.wait("gpsimd", sems.MM, prev)
            for c in range(8):
                P.inc("gpsimd", lambda g, c=c, k=k, wdram=wdram, wrow=wrow, fs=fs:
                      g.dma_start(out=W1S[k][:, c * 256:(c + 1) * 256],
                                  in_=wdram[wrow + c * 128: wrow + (c + 1) * 128, fs * 256:(fs + 1) * 256]), sW1[k])
            w1_ld[idx] = (sW1[k], sW1[k].n)

        def load_w2(idx, wdram, wrow, o):
            k = idx % 2
            prev = w2_used.get(idx - 2)
            if prev is not None:
                P.wait("gpsimd", sems.MM, prev)
            P.inc("gpsimd", lambda g, k=k, wdram=wdram, wrow=wrow, o=o:
                  g.dma_start(out=W2S[k][:, :],
                              in_=wdram[(wrow + o * 128):(wrow + (o + 1) * 128), :]), sW2[k])
            w2_ld[idx] = (sW2[k], sW2[k].n)

        def load_xr(idx, p2, o, s4):
            prev = xr_used.get(idx - 1)
            if prev is not None:
                P.wait("sync", sems.VE, prev)
            P.inc("sync", lambda sy, o=o, p2=p2, s4=s4:
                  sy.dma_start(out=X1R[0][:, :],
                               in_=io["x1f"][o * 128:(o + 1) * 128,
                                             p2 * 1024 + s4 * 512: p2 * 1024 + (s4 + 1) * 512]), sXR[0])
            xr_ld[idx] = (sXR[0], sXR[0].n)

        bank_ticks = {}
        for p2 in range(2):
            for ui, (w1d, w1r, w2d, w2r, b1b, b2b, kind, eidx) in enumerate(units):
                if which is not None and ui not in which:
                    continue
                gi = p2 * 6 + ui
                w1idx0 = gi * 16
                for fs in range(16):
                    load_w1(w1idx0 + fs, w1d, w1r, fs)
                    P.wait("tensor", w1_ld[w1idx0 + fs][0], w1_ld[w1idx0 + fs][1])
                    for fc2 in range(2):
                        fc = fs * 2 + fc2
                        for s4 in range(2):
                            bi = (fc * 2 + s4) % 2
                            bank = pb[bi]
                            prev = bank_ticks.get(id(bank))
                            if prev is not None:
                                P.wait("tensor", prev[0], prev[1])
                            pairs = [(W1S[(w1idx0 + fs) % 2][:, c * 256 + fc2 * 128: c * 256 + (fc2 + 1) * 128],
                                      X1b[c][:, p2 * 1024 + s4 * 512: p2 * 1024 + (s4 + 1) * 512])
                                     for c in range(8)]
                            mmt = mm_group(P, sems, bank[:, :], pairs)
                            P.wait("scalar", sems.MM, mmt)
                            et = P.inc("scalar",
                                       lambda sc, fc=fc, s4=s4, bank=bank, b1b=b1b:
                                       sc.activation(HT[:, fc * 1024 + s4 * 512: fc * 1024 + (s4 + 1) * 512],
                                                     bank[:, :], AF.Relu, bias=B1C[:, b1b + fc: b1b + fc + 1],
                                                     scale=1.0), sems.SE)
                            bank_ticks[id(bank)] = (sems.SE, et)
                    w1_used[w1idx0 + fs] = sems.MM.n
                P.wait("tensor", sems.SE, sems.SE.n)
                if gi == 0 and "dbg_ht" in io:
                    dbg_s = sems.new16("dbght")
                    P.wait("gpsimd", sems.SE, sems.SE.n)
                    P.inc("gpsimd", lambda g: g.dma_start(out=io["dbg_ht"], in_=HT[:, :]), dbg_s)
                    P.wait("gpsimd", dbg_s, dbg_s.n)
                w2idx0 = gi * 8
                load_w2(w2idx0, w2d, w2r, 0)
                for o in range(8):
                    if o < 7:
                        load_w2(w2idx0 + o + 1, w2d, w2r, o + 1)
                    P.wait("tensor", w2_ld[w2idx0 + o][0], w2_ld[w2idx0 + o][1])
                    for s4 in range(2):
                        if kind == "s" and eidx == 0:
                            load_xr((p2 * 8 + o) * 2 + s4, p2, o, s4)
                        bi = 2 + s4 % 2
                        bank = pb[bi]
                        prev = bank_ticks.get(id(bank))
                        if prev is not None:
                            P.wait("tensor", prev[0], prev[1])
                        pairs = [(W2S[(w2idx0 + o) % 2][:, fc * 128:(fc + 1) * 128],
                                  HT[:, fc * 1024 + s4 * 512: fc * 1024 + (s4 + 1) * 512])
                                 for fc in range(32)]
                        mmt = mm_group(P, sems, bank[:, :], pairs)
                        P.wait("vector", sems.MM, mmt)
                        dres = RES[o][:, p2 * 1024 + s4 * 512: p2 * 1024 + (s4 + 1) * 512]
                        b2col = B2C[:, b2b + o: b2b + o + 1]
                        if kind == "s":
                            inc_chain(P, "vector",
                                      lambda v, bank=bank, b2col=b2col:
                                      v.tensor_scalar(TMP[:, :], bank[:, :], b2col, 0.5, ALU.add, ALU.mult),
                                      sems.VE)
                            if eidx == 0:
                                xi = (p2 * 8 + o) * 2 + s4
                                P.wait("vector", xr_ld[xi][0], xr_ld[xi][1])
                                t = inc_chain(P, "vector",
                                              lambda v, dres=dres:
                                              v.tensor_tensor(dres, TMP[:, :], X1R[0][:, :], ALU.add),
                                              sems.VE)
                                xr_used[xi] = sems.VE.n
                            else:
                                t = inc_chain(P, "vector",
                                              lambda v, dres=dres: v.tensor_tensor(dres, TMP[:, :], dres, ALU.add),
                                              sems.VE)
                        else:
                            inc_chain(P, "vector",
                                      lambda v, bank=bank, b2col=b2col:
                                      v.tensor_scalar_add(TMPB[:, :], bank[:, :], b2col), sems.VE)
                            inc_chain(P, "vector",
                                      lambda v, eidx=eidx, p2=p2, s4=s4:
                                      v.tensor_tensor(TMP[:, :], TMPB[:, :],
                                                      B4[:, eidx * SH + p2 * 1024 + s4 * 512: eidx * SH + p2 * 1024 + (s4 + 1) * 512],
                                                      ALU.mult), sems.VE)
                            t = inc_chain(P, "vector",
                                          lambda v, dres=dres: v.tensor_tensor(dres, TMP[:, :], dres, ALU.add),
                                          sems.VE)
                        bank_ticks[id(bank)] = (sems.VE, t)
                    w2_used[w2idx0 + o] = sems.MM.n
        P.flush()



def build_p4a(nc, P, sems, io):
    ph0 = ExitStack()
    with ph0:
        pb = [ph0.enter_context(nc.psum_tensor(f"p4_pb{i}", [128, 512], F32)) for i in range(8)]
        KR = ph0.enter_context(nc.sbuf_tensor("p4_kr", [128, 4 * S], BF16))
        VA = ph0.enter_context(nc.sbuf_tensor("p4_va", [128, 32 * 520], BF16))
        QT = ph0.enter_context(nc.sbuf_tensor("p4_qt", [128, 4 * S], BF16))

        ph = ExitStack()
        with ph:
            LT = ph.enter_context(nc.sbuf_tensor("p4_lt", [128, 2 * S], BF16))
            ph2 = ExitStack()
            with ph2:
                WKV = ph2.enter_context(nc.sbuf_tensor("p4_wkv", [128, 16 * 1024], BF16))
                WQH = ph2.enter_context(nc.sbuf_tensor("p4_wqh", [128, 8 * 512], BF16))
                WD = ph2.enter_context(nc.sbuf_tensor("p4_wd", [128, 8 * 256], BF16))
                XS = [ph2.enter_context(nc.sbuf_tensor(f"p4_xs{k}", [128, 8 * 512], BF16)) for k in range(2)]
                KVC = [ph2.enter_context(nc.sbuf_tensor(f"p4_kvc{k}", [128, 8 * 512], BF16)) for k in range(2)]

                lw = sems.new16("p4w")
                for i in range(8):
                    P.inc("gpsimd", lambda g, i=i: g.dma_start(out=WKV[:, i * 1024:(i + 1) * 1024],
                                                               in_=io["wk"][i * 128:(i + 1) * 128, :]), lw)
                    P.inc("gpsimd", lambda g, i=i: g.dma_start(out=WKV[:, (8 + i) * 1024:(9 + i) * 1024],
                                                               in_=io["wv"][i * 128:(i + 1) * 128, :]), lw)
                    P.inc("gpsimd", lambda g, i=i: g.dma_start(out=WQH[:, i * 512:(i + 1) * 512],
                                                               in_=io["wqh"][i * 128:(i + 1) * 128, :]), lw)
                    P.inc("gpsimd", lambda g, i=i: g.dma_start(out=WD[:, i * 256:(i + 1) * 256],
                                                               in_=io["wdown"][i * 128:(i + 1) * 128, :]), lw)
                P.wait("tensor", lw, lw.n)

                sXS = [sems.new16(f"p4xs{k}") for k in range(2)]
                xs_used, xs_ld = {}, {}

                def load_xs(si):
                    k = si % 2
                    prev = xs_used.get(si - 2)
                    if prev is not None:
                        P.wait("gpsimd", sems.MM, prev)
                    half, sl = divmod(si, 4)
                    for c in range(8):
                        P.inc("gpsimd", lambda g, c=c, k=k, half=half, sl=sl:
                              g.dma_start(out=XS[k][:, c * 512:(c + 1) * 512],
                                          in_=io["x2full"][half * 1024 + c * 128: half * 1024 + (c + 1) * 128,
                                                           sl * 512:(sl + 1) * 512]), sXS[k])
                    xs_ld[si] = (sXS[k], sXS[k].n)

                load_xs(0)
                bank_ticks = {}
                kvc_t = {}
                for si in range(8):
                    k = si % 2
                    P.wait("tensor", xs_ld[si][0], xs_ld[si][1])
                    if si < 7:
                        load_xs(si + 1)
                    for o in range(8):
                        bi = o % 2
                        bank = pb[bi]
                        prev = bank_ticks.get(id(bank))
                        if prev is not None:
                            P.wait("tensor", prev[0], prev[1])
                        pairs = [(WKV[:, c * 1024 + o * 128: c * 1024 + (o + 1) * 128],
                                  XS[k][:, c * 512:(c + 1) * 512]) for c in range(8)]
                        pairs += [(WKV[:, (8 + c) * 1024 + o * 128: (8 + c) * 1024 + (o + 1) * 128],
                                   XS[k][:, c * 512:(c + 1) * 512]) for c in range(8)]
                        mmt = mm_group(P, sems, bank[:, :], pairs)
                        bank_ticks[id(bank)] = scalar_copy_evac(
                            P, sems, KVC[k][:, o * 512:(o + 1) * 512], bank, mmt)
                    kvc_done = sems.SE.n
                    for o in range(4):
                        bi = o % 2
                        bank = pb[bi]
                        prev = bank_ticks.get(id(bank))
                        if prev is not None:
                            P.wait("tensor", prev[0], prev[1])
                        pairs = [(WQH[:, c * 512 + o * 128: c * 512 + (o + 1) * 128],
                                  XS[k][:, c * 512:(c + 1) * 512]) for c in range(8)]
                        mmt = mm_group(P, sems, bank[:, :], pairs)
                        bank_ticks[id(bank)] = scalar_copy_evac(
                            P, sems, QT[:, o * S + si * 512: o * S + (si + 1) * 512], bank, mmt)
                    xs_used[si] = sems.MM.n
                    P.wait("tensor", sems.SE, kvc_done)
                    prevk = kvc_t.get(k)
                    for o in range(2):
                        bi = o % 2
                        bank = pb[bi]
                        prev = bank_ticks.get(id(bank))
                        if prev is not None:
                            P.wait("tensor", prev[0], prev[1])
                        pairs = [(WD[:, c * 256 + o * 128: c * 256 + (o + 1) * 128],
                                  KVC[k][:, c * 512:(c + 1) * 512]) for c in range(8)]
                        mmt = mm_group(P, sems, bank[:, :], pairs)
                        bank_ticks[id(bank)] = scalar_copy_evac(
                            P, sems, LT[:, o * S + si * 512: o * S + (si + 1) * 512], bank, mmt)
                    kvc_t[k] = sems.MM.n
                P.flush()

            ph3 = ExitStack()
            with ph3:
                WUK = ph3.enter_context(nc.sbuf_tensor("p4_wuk", [128, 2 * 512], BF16))
                WUV = ph3.enter_context(nc.sbuf_tensor("p4_wuv", [128, 2 * 512], BF16))
                lw2 = sems.new16("p4u")
                for i in range(2):
                    P.inc("gpsimd", lambda g, i=i: g.dma_start(out=WUK[:, i * 512:(i + 1) * 512],
                                                               in_=io["wupkh"][i * 128:(i + 1) * 128, :]), lw2)
                    P.inc("gpsimd", lambda g, i=i: g.dma_start(out=WUV[:, i * 512:(i + 1) * 512],
                                                               in_=io["wupvh"][i * 128:(i + 1) * 128, :]), lw2)
                P.wait("tensor", lw2, lw2.n)
                bank_ticks = {}
                for o in range(4):
                    for sc_ in range(8):
                        bi = (o * 8 + sc_) % 2
                        bank = pb[bi]
                        prev = bank_ticks.get(id(bank))
                        if prev is not None:
                            P.wait("tensor", prev[0], prev[1])
                        pairs = [(WUK[:, lc * 512 + o * 128: lc * 512 + (o + 1) * 128],
                                  LT[:, lc * S + sc_ * 512: lc * S + (sc_ + 1) * 512]) for lc in range(2)]
                        mmt = mm_group(P, sems, bank[:, :], pairs)
                        bank_ticks[id(bank)] = scalar_copy_evac(
                            P, sems, KR[:, o * S + sc_ * 512: o * S + (sc_ + 1) * 512], bank, mmt)
                P.inc("gpsimd", lambda g: g.memset(VA[:, 64:32 * 520:65], 1.0), sems.GP)
                P.wait("vector", sems.GP, sems.GP.n)
                for kt in range(32):
                    bi = kt % 2
                    bank = pb[bi]
                    prev = bank_ticks.get(id(bank))
                    if prev is not None:
                        P.wait("tensor", prev[0], prev[1])
                    pairs = [(LT[:, lc * S + kt * 128: lc * S + (kt + 1) * 128],
                              WUV[:, lc * 512:(lc + 1) * 512]) for lc in range(2)]
                    mmt = mm_group(P, sems, bank[:, :], pairs)
                    P.wait("vector", sems.MM, mmt)
                    t = inc_chain(P, "vector",
                                  lambda v, kt=kt, bank=bank:
                                  v.tensor_copy(VA[:, kt * 520: kt * 520 + 520].rearrange("p (h c) -> p h c", h=8)[:, :, 0:64],
                                                bank[:, :].rearrange("p (h c) -> p h c", h=8)), sems.VE)
                    bank_ticks[id(bank)] = (sems.VE, t)
                P.wait("tensor", sems.VE, sems.VE.n)
                P.flush()

        _p4_attn(nc, P, sems, io, pb, KR, VA, QT)


def _p4_attn(nc, P, sems, io, pb, KR, VA, QT):
    ph = ExitStack()
    with ph:
        OT = ph.enter_context(nc.sbuf_tensor("p4_ot", [128, 4 * S], BF16))
        QPM = ph.enter_context(nc.sbuf_tensor("p4_qpm", [128, 2 * 4 * 512], BF16))
        AKQ = ph.enter_context(nc.sbuf_tensor("p4_akq", [128, 256], F32))
        DEC = ph.enter_context(nc.sbuf_tensor("p4_dec", [128, 4 * 512], F32))
        DQL = ph.enter_context(nc.sbuf_tensor("p4_dql", [1, 1024], F32))
        CB = ph.enter_context(nc.sbuf_tensor("p4_cb", [128, 1], F32))
        Ebuf = [ph.enter_context(nc.sbuf_tensor(f"p4_e{k}", [128, 512], BF16)) for k in range(3)]
        TMX = ph.enter_context(nc.sbuf_tensor("p4_tmx", [128, 512], F32))
        RC = ph.enter_context(nc.sbuf_tensor("p4_rc", [1, 512], F32))
        BC = ph.enter_context(nc.sbuf_tensor("p4_bc", [64, 512], F32))
        DQB = ph.enter_context(nc.sbuf_tensor("p4_dqb", [128, 2 * 512], BF16))
        onesr = ph.enter_context(nc.sbuf_tensor("p4_or", [1, 128], F32))

        lw = sems.new16("p4c")
        P.inc("sync", lambda sy: sy.dma_start(out=AKQ[:, :], in_=io["akq"][:, :]), lw)
        P.inc("sync", lambda sy: sy.dma_start(out=DEC[:, :], in_=io["dec4"][:, :]), lw)
        P.inc("sync", lambda sy: sy.dma_start(out=DQL[:, :], in_=io["dql"][:, :]), lw)
        P.inc("gpsimd", lambda g: g.memset(CB[:, :], -L1_C), sems.GP)
        P.inc("gpsimd", lambda g: g.memset(onesr[:, :], 1.0), sems.GP)
        P.wait("tensor", lw, lw.n)
        P.wait("tensor", sems.GP, sems.GP.n)
        P.wait("scalar", sems.GP, sems.GP.n)
        P.wait("vector", lw, lw.n)
        P.flush()

        sc_ticks = {}
        for qc in range(8):
            for sgn in range(2):
                bmt = P.inc("tensor", lambda t, sgn=sgn:
                            t.matmul(pb[6][:, :], onesr[:, :], DQL[0:1, sgn * 512:(sgn + 1) * 512],
                                     start=True, stop=True), sems.MM)
                P.wait("vector", sems.MM, bmt)
                ct = inc_chain(P, "vector", lambda v, sgn=sgn:
                               v.tensor_copy(DQB[:, sgn * 512:(sgn + 1) * 512], pb[6][:, :]), sems.VE)
                P.wait("tensor", sems.VE, ct)
            P.wait("vector", sems.SE, sems.SE.n)
            for ht in range(4):
                for sgn in range(2):
                    inc_chain(P, "vector",
                              lambda v, ht=ht, sgn=sgn, qc=qc:
                              v.tensor_tensor(QPM[:, (sgn * 4 + ht) * 512:(sgn * 4 + ht + 1) * 512],
                                              QT[:, ht * S + qc * 512: ht * S + (qc + 1) * 512],
                                              DQB[:, sgn * 512:(sgn + 1) * 512], ALU.mult), sems.VE)
            qpm_t = sems.VE.n
            P.wait("tensor", sems.VE, qpm_t)

            for grp in range(2):
                for kt in range(32):
                    cls = 0 if kt < 4 * qc else (2 if kt > 4 * qc + 3 else 1)
                    for hh in range(4):
                        h = grp * 4 + hh
                        ht, m = h // 2, h % 2
                        si = (kt * 4 + hh) % 2
                        sbank = pb[si]
                        prev = sc_ticks.get(si)
                        if prev is not None:
                            P.wait("tensor", prev[0], prev[1])
                        st = KR[64 * m:64 * m + 64, ht * S + kt * 128: ht * S + (kt + 1) * 128]
                        if cls == 1:
                            mv = QT[64 * m:64 * m + 64, ht * S + qc * 512: ht * S + (qc + 1) * 512]
                        else:
                            sgn = 0 if cls == 0 else 1
                            mv = QPM[64 * m:64 * m + 64, (sgn * 4 + ht) * 512:(sgn * 4 + ht + 1) * 512]
                        mmt = P.inc("tensor",
                                    lambda t, st=st, mv=mv, sbank=sbank, m=m:
                                    t.matmul(sbank[:, :], st, mv, start=True, stop=True,
                                             tile_position=(64 * m, 0)), sems.MM)
                        eb = Ebuf[(kt * 4 + hh) % 3]
                        acol = AKQ[:, qc * 32 + kt: qc * 32 + kt + 1]
                        if cls == 1:
                            P.wait("vector", sems.MM, mmt)
                            dct = inc_chain(P, "vector",
                                            lambda v, sbank=sbank, kt=kt, qc=qc:
                                            v.tensor_tensor(TMX[:, :], sbank[:, :],
                                                            DEC[:, (kt - 4 * qc) * 512:(kt - 4 * qc + 1) * 512],
                                                            ALU.mult), sems.VE)
                            P.wait("scalar", sems.VE, dct)
                            et = P.inc("scalar",
                                       lambda sc, eb=eb, acol=acol:
                                       sc.activation(eb[:, :], TMX[:, :], AF.Exp, bias=CB[:, :], scale=acol), sems.SE)
                            sc_ticks[si] = (sems.VE, dct)
                        else:
                            P.wait("scalar", sems.MM, mmt)
                            et = P.inc("scalar",
                                       lambda sc, eb=eb, sbank=sbank, acol=acol:
                                       sc.activation(eb[:, :], sbank[:, :], AF.Exp, bias=CB[:, :], scale=acol), sems.SE)
                            sc_ticks[si] = (sems.SE, et)
                        P.wait("tensor", sems.SE, et)
                        avb = pb[2 + hh]
                        stV = VA[:, kt * 520 + 65 * h: kt * 520 + 65 * h + 65]
                        first, last = kt == 0, kt == 31
                        if last:
                            avt = P.inc("tensor",
                                        lambda t, stV=stV, eb=eb, avb=avb, first=first:
                                        t.matmul(avb[0:65, :], stV, eb[:, :], start=first, stop=True), sems.MM)
                        else:
                            P.on("tensor",
                                 lambda t, stV=stV, eb=eb, avb=avb, first=first:
                                 t.matmul(avb[0:65, :], stV, eb[:, :], start=first, stop=False))
                P.wait("vector", sems.MM, avt)
                for hh in range(4):
                    h = grp * 4 + hh
                    avb = pb[2 + hh]
                    rct = inc_chain(P, "vector", lambda v, avb=avb: v.reciprocal(RC[:, :], avb[64:65, :]), sems.VE)
                    P.wait("tensor", sems.VE, rct)
                    bmt = P.inc("tensor", lambda t: t.matmul(pb[6][0:64, :], onesr[0:1, 0:64], RC[0:1, :],
                                                             start=True, stop=True), sems.MM)
                    P.wait("vector", sems.MM, bmt)
                    inc_chain(P, "vector", lambda v: v.tensor_copy(BC[:, :], pb[6][0:64, :]), sems.VE)
                    ht, m = h // 2, h % 2
                    t = inc_chain(P, "vector",
                                  lambda v, avb=avb, ht=ht, m=m, qc=qc:
                                  v.tensor_tensor(OT[64 * m:64 * m + 64, ht * S + qc * 512: ht * S + (qc + 1) * 512],
                                                  avb[0:64, :], BC[:, :], ALU.mult), sems.VE)
                P.wait("tensor", sems.VE, sems.VE.n)
        st0 = sems.new16("p4st")
        P.wait("gpsimd", sems.VE, sems.VE.n)
        for i in range(4):
            for hf in range(2):
                P.inc("gpsimd", lambda g, i=i, hf=hf:
                      g.dma_start(out=io["otf"][hf * 512 + i * 128: hf * 512 + (i + 1) * 128, :],
                                  in_=OT[:, i * S + hf * SH: i * S + (hf + 1) * SH]), st0)
        P.wait("gpsimd", st0, st0.n)
        P.flush()



def build_p42(nc, P, sems, io):
    ph0 = ExitStack()
    with ph0:
        pb = [ph0.enter_context(nc.psum_tensor(f"p42_pb{i}", [128, 512], F32)) for i in range(8)]
        OT = ph0.enter_context(nc.sbuf_tensor("p42_ot", [128, 8 * SH], BF16))
        OT0 = ph0.enter_context(nc.sbuf_tensor("p42_ot0", [128, 8], F32))
        Q0 = ph0.enter_context(nc.sbuf_tensor("p42_q0", [128, 8], BF16))
        phm = ExitStack()
        with phm:
            QT = phm.enter_context(nc.sbuf_tensor("p42_qt", [128, 8 * SH], BF16))
            LT = phm.enter_context(nc.sbuf_tensor("p42_lt", [128, 2 * S], BF16))
            _p42_main(nc, P, sems, io, pb, QT, OT, LT, OT0, Q0)
        _p42_proj(nc, P, sems, io, pb, OT, OT0)


def _p42_main(nc, P, sems, io, pb, QT, OT, LT, OT0, Q0):
        ph = ExitStack()
        with ph:
            WKV = ph.enter_context(nc.sbuf_tensor("p42_wkv", [128, 16 * 1024], BF16))
            WQ = ph.enter_context(nc.sbuf_tensor("p42_wq", [128, 8 * 1024], BF16))
            WD = ph.enter_context(nc.sbuf_tensor("p42_wd", [128, 8 * 256], BF16))
            XS = [ph.enter_context(nc.sbuf_tensor(f"p42_xs{k}", [128, 8 * 512], BF16)) for k in range(2)]
            KVC = [ph.enter_context(nc.sbuf_tensor(f"p42_kvc{k}", [128, 8 * 512], BF16)) for k in range(2)]
            X20B = ph.enter_context(nc.sbuf_tensor("p42_x20", [128, 8], BF16))

            lw = sems.new16("p42w")
            for i in range(8):
                P.inc("gpsimd", lambda g, i=i: g.dma_start(out=WKV[:, i * 1024:(i + 1) * 1024],
                                                           in_=io["wk"][i * 128:(i + 1) * 128, :]), lw)
                P.inc("gpsimd", lambda g, i=i: g.dma_start(out=WKV[:, (8 + i) * 1024:(9 + i) * 1024],
                                                           in_=io["wv"][i * 128:(i + 1) * 128, :]), lw)
                P.inc("gpsimd", lambda g, i=i: g.dma_start(out=WQ[:, i * 1024:(i + 1) * 1024],
                                                           in_=io["wq"][i * 128:(i + 1) * 128, :]), lw)
                P.inc("gpsimd", lambda g, i=i: g.dma_start(out=WD[:, i * 256:(i + 1) * 256],
                                                           in_=io["wdown"][i * 128:(i + 1) * 128, :]), lw)
                P.inc("gpsimd", lambda g, i=i: g.dma_start(out=X20B[:, i:i + 1],
                                                           in_=io["x20"][i * 128:(i + 1) * 128, :]), lw)
            P.wait("tensor", lw, lw.n)

            sXS = [sems.new16(f"p42x{k}") for k in range(2)]
            xs_used, xs_ld = {}, {}

            def load_xs(si):
                k = si % 2
                prev = xs_used.get(si - 2)
                if prev is not None:
                    P.wait("gpsimd", sems.MM, prev)
                for c in range(8):
                    P.inc("gpsimd", lambda g, c=c, k=k, si=si:
                          g.dma_start(out=XS[k][:, c * 512:(c + 1) * 512],
                                      in_=io["x2full"][c * 128:(c + 1) * 128, si * 512:(si + 1) * 512]), sXS[k])
                xs_ld[si] = (sXS[k], sXS[k].n)

            load_xs(0)
            bank_ticks = {}
            for si in range(8):
                k = si % 2
                P.wait("tensor", xs_ld[si][0], xs_ld[si][1])
                if si < 7:
                    load_xs(si + 1)
                for o in range(8):
                    bi = o % 2
                    bank = pb[bi]
                    prev = bank_ticks.get(id(bank))
                    if prev is not None:
                        P.wait("tensor", prev[0], prev[1])
                    pairs = [(WKV[:, c * 1024 + o * 128: c * 1024 + (o + 1) * 128],
                              XS[k][:, c * 512:(c + 1) * 512]) for c in range(8)]
                    pairs += [(WKV[:, (8 + c) * 1024 + o * 128: (8 + c) * 1024 + (o + 1) * 128],
                               XS[k][:, c * 512:(c + 1) * 512]) for c in range(8)]
                    mmt = mm_group(P, sems, bank[:, :], pairs)
                    bank_ticks[id(bank)] = scalar_copy_evac(
                        P, sems, KVC[k][:, o * 512:(o + 1) * 512], bank, mmt)
                kvc_done = sems.SE.n
                if si < 4:
                    for o in range(8):
                        bi = o % 2
                        bank = pb[bi]
                        prev = bank_ticks.get(id(bank))
                        if prev is not None:
                            P.wait("tensor", prev[0], prev[1])
                        pairs = [(WQ[:, c * 1024 + o * 128: c * 1024 + (o + 1) * 128],
                                  XS[k][:, c * 512:(c + 1) * 512]) for c in range(8)]
                        mmt = mm_group(P, sems, bank[:, :], pairs)
                        bank_ticks[id(bank)] = scalar_copy_evac(
                            P, sems, QT[:, o * SH + si * 512: o * SH + (si + 1) * 512], bank, mmt)
                xs_used[si] = sems.MM.n
                P.wait("tensor", sems.SE, kvc_done)
                for o in range(2):
                    bi = o % 2
                    bank = pb[bi]
                    prev = bank_ticks.get(id(bank))
                    if prev is not None:
                        P.wait("tensor", prev[0], prev[1])
                    pairs = [(WD[:, c * 256 + o * 128: c * 256 + (o + 1) * 128],
                              KVC[k][:, c * 512:(c + 1) * 512]) for c in range(8)]
                    mmt = mm_group(P, sems, bank[:, :], pairs)
                    bank_ticks[id(bank)] = scalar_copy_evac(
                        P, sems, LT[:, o * S + si * 512: o * S + (si + 1) * 512], bank, mmt)
            for o in range(8):
                bi = o % 2
                bank = pb[bi]
                prev = bank_ticks.get(id(bank))
                if prev is not None:
                    P.wait("tensor", prev[0], prev[1])
                pairs = [(WQ[:, c * 1024 + o * 128: c * 1024 + (o + 1) * 128],
                          X20B[:, c:c + 1]) for c in range(8)]
                mmt = mm_group(P, sems, bank[:, 0:1], pairs)
                P.wait("scalar", sems.MM, mmt)
                t = P.inc("scalar", lambda sc, o=o, bank=bank: sc.copy(Q0[:, o:o + 1], bank[:, 0:1]), sems.SE)
                bank_ticks[id(bank)] = (sems.SE, t)
            P.flush()

        for hp in range(2):
            _p42_headhalf(nc, P, sems, io, pb, QT, OT, LT, OT0, Q0, hp)


def _p42_headhalf(nc, P, sems, io, pb, QT, OT, LT, OT0, Q0, hp):
    ph = ExitStack()
    with ph:
        KR = ph.enter_context(nc.sbuf_tensor(f"p42_kr{hp}", [128, 4 * S], BF16))
        VA = ph.enter_context(nc.sbuf_tensor(f"p42_va{hp}", [128, 32 * 520], BF16))
        WUK = ph.enter_context(nc.sbuf_tensor(f"p42_wuk{hp}", [128, 2 * 512], BF16))
        WUV = ph.enter_context(nc.sbuf_tensor(f"p42_wuv{hp}", [128, 2 * 512], BF16))
        QPM = ph.enter_context(nc.sbuf_tensor(f"p42_qpm{hp}", [128, 2 * 4 * 512], BF16))
        AKQ = ph.enter_context(nc.sbuf_tensor(f"p42_akq{hp}", [128, 128], F32))
        DEC = ph.enter_context(nc.sbuf_tensor(f"p42_dec{hp}", [128, 4 * 512], F32))
        AK0 = ph.enter_context(nc.sbuf_tensor(f"p42_ak0{hp}", [128, 32], F32))
        DQL = ph.enter_context(nc.sbuf_tensor(f"p42_dql{hp}", [1, 1024], F32))
        CB = ph.enter_context(nc.sbuf_tensor(f"p42_cb{hp}", [128, 1], F32))
        Ebuf = [ph.enter_context(nc.sbuf_tensor(f"p42_e{hp}_{k}", [128, 512], BF16)) for k in range(4)]
        E0 = ph.enter_context(nc.sbuf_tensor(f"p42_e0{hp}", [128, 2], BF16))
        TMX2 = [ph.enter_context(nc.sbuf_tensor(f"p42_tmx{hp}_{k}", [128, 512], F32)) for k in range(2)]
        RC = ph.enter_context(nc.sbuf_tensor(f"p42_rc{hp}", [1, 512], F32))
        BC = ph.enter_context(nc.sbuf_tensor(f"p42_bc{hp}", [128, 512], F32))
        RC4 = ph.enter_context(nc.sbuf_tensor(f"p42_rc4{hp}", [1, 2048], F32))
        RC4X = ph.enter_context(nc.sbuf_tensor(f"p42_rc4x{hp}", [1, 2048], F32))
        ZB4 = ph.enter_context(nc.sbuf_tensor(f"p42_zb4{hp}", [1, 1], F32))
        DQB = ph.enter_context(nc.sbuf_tensor(f"p42_dqb{hp}", [128, 2 * 512], BF16))
        onesr = ph.enter_context(nc.sbuf_tensor(f"p42_or{hp}", [1, 128], F32))

        lw = sems.new16(f"p42h{hp}")
        for i in range(2):
            P.inc("gpsimd", lambda g, i=i, hp=hp: g.dma_start(out=WUK[:, i * 512:(i + 1) * 512],
                                                              in_=io["wupk"][i * 128:(i + 1) * 128, hp * 512:(hp + 1) * 512]), lw)
            P.inc("gpsimd", lambda g, i=i, hp=hp: g.dma_start(out=WUV[:, i * 512:(i + 1) * 512],
                                                              in_=io["wupv"][i * 128:(i + 1) * 128, hp * 512:(hp + 1) * 512]), lw)
        lc = sems.new16(f"p42c{hp}")
        P.inc("sync", lambda sy: sy.dma_start(out=AKQ[:, :], in_=io["akq4"][:, :]), lc)
        P.inc("sync", lambda sy: sy.dma_start(out=DEC[:, :], in_=io["dec4"][:, :]), lc)
        P.inc("sync", lambda sy: sy.dma_start(out=DQL[:, :], in_=io["dql"][:, :]), lc)
        P.inc("sync", lambda sy: sy.dma_start(out=AK0[:, :], in_=io["ak0"][:, :]), lc)
        P.inc("gpsimd", lambda g: g.memset(CB[:, :], -L1_C), sems.GP)
        P.inc("gpsimd", lambda g: g.memset(onesr[:, :], 1.0), sems.GP)
        P.inc("gpsimd", lambda g: g.memset(ZB4[:, :], 0.0), sems.GP)
        P.wait("tensor", lw, lw.n)
        P.wait("tensor", lc, lc.n)
        P.wait("tensor", sems.GP, sems.GP.n)
        P.wait("scalar", sems.GP, sems.GP.n)
        P.wait("vector", lc, lc.n)

        bank_ticks = {}
        for o in range(4):
            for sc_ in range(8):
                bi = (o * 8 + sc_) % 2
                bank = pb[bi]
                prev = bank_ticks.get(id(bank))
                if prev is not None:
                    P.wait("tensor", prev[0], prev[1])
                pairs = [(WUK[:, lc2 * 512 + o * 128: lc2 * 512 + (o + 1) * 128],
                          LT[:, lc2 * S + sc_ * 512: lc2 * S + (sc_ + 1) * 512]) for lc2 in range(2)]
                mmt = mm_group(P, sems, bank[:, :], pairs)
                bank_ticks[id(bank)] = scalar_copy_evac(
                    P, sems, KR[:, o * S + sc_ * 512: o * S + (sc_ + 1) * 512], bank, mmt)
        P.inc("gpsimd", lambda g: g.memset(VA[:, 64:32 * 520:65], 1.0), sems.GP)
        P.wait("vector", sems.GP, sems.GP.n)
        for kt in range(32):
            bi = kt % 2
            bank = pb[bi]
            prev = bank_ticks.get(id(bank))
            if prev is not None:
                P.wait("tensor", prev[0], prev[1])
            pairs = [(LT[:, lc2 * S + kt * 128: lc2 * S + (kt + 1) * 128],
                      WUV[:, lc2 * 512:(lc2 + 1) * 512]) for lc2 in range(2)]
            mmt = mm_group(P, sems, bank[:, :], pairs)
            P.wait("vector", sems.MM, mmt)
            t = inc_chain(P, "vector",
                          lambda v, kt=kt, bank=bank:
                          v.tensor_copy(VA[:, kt * 520: kt * 520 + 520].rearrange("p (h c) -> p h c", h=8)[:, :, 0:64],
                                        bank[:, :].rearrange("p (h c) -> p h c", h=8)), sems.VE)
            bank_ticks[id(bank)] = (sems.VE, t)
        P.flush()

        sbanks = [pb[0], pb[1], pb[6], pb[7]]
        sc_ticks = {}
        prev_group_mults = [None]
        for qc in range(4):
            prev = sc_ticks.get(2)
            if prev is not None:
                P.wait("tensor", prev[0], prev[1])
            for sgn in range(2):
                bmt = P.inc("tensor", lambda t, sgn=sgn:
                            t.matmul(pb[6][:, :], onesr[:, :], DQL[0:1, sgn * 512:(sgn + 1) * 512],
                                     start=True, stop=True), sems.MM)
                P.wait("vector", sems.MM, bmt)
                ct = inc_chain(P, "vector", lambda v, sgn=sgn:
                               v.tensor_copy(DQB[:, sgn * 512:(sgn + 1) * 512], pb[6][:, :]), sems.VE)
                P.wait("tensor", sems.VE, ct)
            sc_ticks[2] = (sems.VE, sems.VE.n)
            for ht in range(4):
                for sgn in range(2):
                    inc_chain(P, "vector",
                              lambda v, ht=ht, sgn=sgn, qc=qc:
                              v.tensor_tensor(QPM[:, (sgn * 4 + ht) * 512:(sgn * 4 + ht + 1) * 512],
                                              QT[:, (4 * hp + ht) * SH + qc * 512: (4 * hp + ht) * SH + (qc + 1) * 512],
                                              DQB[:, sgn * 512:(sgn + 1) * 512], ALU.mult), sems.VE)
            qpm_t = sems.VE.n
            P.wait("tensor", sems.VE, qpm_t)

            for grp in range(2):
                tiles = [(kt, hh) for kt in range(32) for hh in range(4)]
                et_q = {}
                mixed_prev = None
                avt = [None]

                def emit_av_pair(Pj, tiles=tiles, et_q=et_q, avt=avt, grp=grp, pgm=prev_group_mults):
                    u1 = 2 * Pj + 1
                    P.wait("tensor", sems.SE, et_q[u1][1])
                    if tiles[2 * Pj][0] == 0 and pgm[0] is not None:
                        P.wait("tensor", sems.VE, pgm[0])
                    for u in (2 * Pj, u1):
                        kt, hh = tiles[u]
                        eb, _ = et_q.pop(u)
                        h = grp * 4 + hh
                        avb = pb[2 + hh]
                        stV = VA[:, kt * 520 + 65 * h: kt * 520 + 65 * h + 65]
                        first, last = kt == 0, kt == 31
                        if last and hh == 3:
                            avt[0] = P.inc("tensor",
                                           lambda tn, stV=stV, eb=eb, avb=avb, first=first:
                                           tn.matmul(avb[0:65, :], stV, eb[:, :], start=first, stop=True), sems.MM)
                        else:
                            P.on("tensor",
                                 lambda tn, stV=stV, eb=eb, avb=avb, first=first, last=last:
                                 tn.matmul(avb[0:65, :], stV, eb[:, :], start=first, stop=last))

                for Pi in range(64):
                    u0, u1 = 2 * Pi, 2 * Pi + 1
                    for u in (u0, u1):
                        prevk = sc_ticks.get(u % 4)
                        if prevk is not None:
                            P.wait("tensor", prevk[0], prevk[1])
                    mmts = {}
                    for u in (u0, u1):
                        kt, hh = tiles[u]
                        cls = 0 if kt < 4 * qc else (2 if kt > 4 * qc + 3 else 1)
                        h = grp * 4 + hh
                        ht, m = h // 2, h % 2
                        sbank = sbanks[u % 4]
                        st = KR[64 * m:64 * m + 64, ht * S + kt * 128: ht * S + (kt + 1) * 128]
                        if cls == 1:
                            mv = QT[64 * m:64 * m + 64, (4 * hp + ht) * SH + qc * 512: (4 * hp + ht) * SH + (qc + 1) * 512]
                        else:
                            sgn = 0 if cls == 0 else 1
                            mv = QPM[64 * m:64 * m + 64, (sgn * 4 + ht) * 512:(sgn * 4 + ht + 1) * 512]
                        mmts[u] = P.inc("tensor",
                                        lambda tn, st=st, mv=mv, sbank=sbank, m=m:
                                        tn.matmul(sbank[:, :], st, mv, start=True, stop=True,
                                                  tile_position=(64 * m, 0)), sems.MM)
                    for u in (u0, u1):
                        kt, hh = tiles[u]
                        cls = 0 if kt < 4 * qc else (2 if kt > 4 * qc + 3 else 1)
                        sbank = sbanks[u % 4]
                        eb = Ebuf[u % 4]
                        acol = AKQ[:, qc * 32 + kt: qc * 32 + kt + 1]
                        if cls == 1:
                            tmx = TMX2[u % 2]
                            P.wait("vector", sems.MM, mmts[u])
                            if mixed_prev is not None:
                                P.wait("vector", sems.SE, mixed_prev)
                            dct = P.inc("vector",
                                        lambda v, sbank=sbank, kt=kt, qc=qc, tmx=tmx:
                                        v.tensor_tensor(tmx[:, :], sbank[:, :],
                                                        DEC[:, (kt - 4 * qc) * 512:(kt - 4 * qc + 1) * 512],
                                                        ALU.mult), sems.VE)
                            P.wait("scalar", sems.VE, dct)
                            et = P.inc("scalar",
                                       lambda sc, eb=eb, acol=acol, tmx=tmx:
                                       sc.activation(eb[:, :], tmx[:, :], AF.Exp, bias=CB[:, :], scale=acol), sems.SE)
                            mixed_prev = et
                            sc_ticks[u % 4] = (sems.VE, dct)
                        else:
                            P.wait("scalar", sems.MM, mmts[u])
                            et = P.inc("scalar",
                                       lambda sc, eb=eb, sbank=sbank, acol=acol:
                                       sc.activation(eb[:, :], sbank[:, :], AF.Exp, bias=CB[:, :], scale=acol), sems.SE)
                            sc_ticks[u % 4] = (sems.SE, et)
                        et_q[u] = (eb, et)
                    if Pi >= 1:
                        emit_av_pair(Pi - 1)
                emit_av_pair(63)

                P.wait("scalar", sems.MM, avt[0])
                for hh in range(4):
                    P.inc("scalar",
                          lambda sc, hh=hh:
                          sc.activation(RC4[0:1, hh * 512:(hh + 1) * 512], pb[2 + hh][64:65, :], AF.Ln,
                                        bias=ZB4[0:1, :], scale=1.0), sems.SE)
                P.wait("scalar", sems.SE, sems.SE.n)
                ext = P.inc("scalar",
                            lambda sc: sc.activation(RC4X[0:1, :], RC4[0:1, :], AF.Exp,
                                                     bias=ZB4[0:1, :], scale=-1.0), sems.SE)
                prev7 = sc_ticks.get(3)
                if prev7 is not None:
                    P.wait("tensor", prev7[0], prev7[1])
                lastm = None
                for p_ in range(2):
                    P.wait("tensor", sems.SE, ext)
                    if lastm is not None:
                        P.wait("tensor", sems.VE, lastm)
                    P.on("tensor",
                         lambda t, p_=p_: t.matmul(pb[6][0:64, :], onesr[0:1, 0:64],
                                                   RC4X[0:1, (2 * p_) * 512:(2 * p_ + 1) * 512],
                                                   start=True, stop=True))
                    bmt = P.inc("tensor",
                                lambda t, p_=p_: t.matmul(pb[7][64:128, :], onesr[0:1, 0:64],
                                                          RC4X[0:1, (2 * p_ + 1) * 512:(2 * p_ + 2) * 512],
                                                          start=True, stop=True), sems.MM)
                    P.wait("vector", sems.MM, bmt)
                    inc_chain(P, "vector", lambda v: v.tensor_copy(BC[0:64, :], pb[6][0:64, :]), sems.VE)
                    inc_chain(P, "vector", lambda v: v.tensor_copy(BC[64:128, :], pb[7][64:128, :]), sems.VE)
                    for m_ in range(2):
                        hh = 2 * p_ + m_
                        hg = 8 * hp + grp * 4 + hh
                        ht2 = hg // 2
                        inc_chain(P, "vector",
                                  lambda v, hh=hh, m_=m_, ht2=ht2, qc=qc:
                                  v.tensor_tensor(OT[64 * m_:64 * m_ + 64, ht2 * SH + qc * 512: ht2 * SH + (qc + 1) * 512],
                                                  pb[2 + hh][0:64, :], BC[64 * m_:64 * m_ + 64, :], ALU.mult), sems.VE)
                    lastm = sems.VE.n
                sc_ticks[2] = (sems.VE, sems.VE.n)
                sc_ticks[3] = (sems.VE, sems.VE.n)
                prev_group_mults[0] = sems.VE.n
        P.wait("tensor", sems.VE, sems.VE.n)
        P.wait("tensor", sems.SE, sems.SE.n)
        t0_ticks = {}
        for h in range(8):
            hg = 8 * hp + h
            ht, m = h // 2, h % 2
            av0 = pb[4]
            e0q = {}
            av0t = [None]

            def emit_av0(k, h=h, e0q=e0q, av0=av0, av0t=av0t):
                P.wait("tensor", sems.SE, e0q[k])
                stV = VA[:, k * 520 + 65 * h: k * 520 + 65 * h + 65]
                first, last = k == 0, k == 31
                if last:
                    av0t[0] = P.inc("tensor", lambda t, stV=stV, first=first, k=k:
                                    t.matmul(av0[0:65, 0:1], stV, E0[:, k % 2:k % 2 + 1],
                                             start=first, stop=True), sems.MM)
                else:
                    P.on("tensor", lambda t, stV=stV, first=first, k=k:
                         t.matmul(av0[0:65, 0:1], stV, E0[:, k % 2:k % 2 + 1], start=first, stop=False))

            for kt in range(32):
                prevk = t0_ticks.get(kt % 2)
                if prevk is not None:
                    P.wait("tensor", sems.SE, prevk)
                sb0 = pb[kt % 2]
                st = KR[64 * m:64 * m + 64, ht * S + kt * 128: ht * S + (kt + 1) * 128]
                mv = Q0[64 * m:64 * m + 64, (4 * hp + ht):(4 * hp + ht) + 1]
                mmt = P.inc("tensor", lambda t, st=st, mv=mv, m=m, sb0=sb0:
                            t.matmul(sb0[:, 0:1], st, mv, start=True, stop=True,
                                     tile_position=(64 * m, 0)), sems.MM)
                P.wait("scalar", sems.MM, mmt)
                a0 = AK0[:, kt:kt + 1]
                et = P.inc("scalar", lambda sc, a0=a0, sb0=sb0, kt=kt:
                           sc.activation(E0[:, kt % 2:kt % 2 + 1], sb0[:, 0:1], AF.Exp,
                                         bias=CB[:, :], scale=a0), sems.SE)
                t0_ticks[kt % 2] = et
                e0q[kt] = et
                if kt >= 1:
                    emit_av0(kt - 1)
            emit_av0(31)
            P.wait("scalar", sems.MM, av0t[0])
            P.inc("scalar", lambda sc: sc.activation(RC[0:1, 0:1], av0[64:65, 0:1], AF.Ln,
                                                     bias=ZB4[0:1, :], scale=1.0), sems.SE)
            P.wait("scalar", sems.SE, sems.SE.n)
            ext0 = P.inc("scalar", lambda sc: sc.activation(RC[0:1, 1:2], RC[0:1, 0:1], AF.Exp,
                                                            bias=ZB4[0:1, :], scale=-1.0), sems.SE)
            P.wait("tensor", sems.SE, ext0)
            bmt = P.inc("tensor", lambda t: t.matmul(pb[6][0:64, 0:1], onesr[0:1, 0:64], RC[0:1, 1:2],
                                                     start=True, stop=True), sems.MM)
            P.wait("vector", sems.MM, bmt)
            inc_chain(P, "vector", lambda v: v.tensor_copy(BC[0:64, 0:1], pb[6][0:64, 0:1]), sems.VE)
            t = inc_chain(P, "vector",
                          lambda v, m=m, hg=hg:
                          v.tensor_tensor(OT0[64 * m:64 * m + 64, (hg // 2):(hg // 2) + 1],
                                          av0[0:64, 0:1], BC[0:64, 0:1], ALU.mult), sems.VE)
            P.wait("tensor", sems.VE, t)
        P.flush()


def _p42_proj(nc, P, sems, io, pb, OT, OT0):
    ph2 = ExitStack()
    with ph2:
        RES = ph2.enter_context(nc.sbuf_tensor("p42_res", [128, 8 * SH], F32))
        RESL = [RES[:, i * SH:(i + 1) * SH] for i in range(8)]
        ph = ExitStack()
        with ph:
            WO = ph.enter_context(nc.sbuf_tensor("p42_wo", [128, 8 * 1024], BF16))
            XR = ph.enter_context(nc.sbuf_tensor("p42_xr2", [128, 8 * SH], F32))
            lw = sems.new16("p42o")
            lx = sems.new16("p42r")
            for i in range(8):
                P.inc("gpsimd", lambda g, i=i: g.dma_start(out=WO[:, i * 1024:(i + 1) * 1024],
                                                           in_=io["wo"][i * 128:(i + 1) * 128, :]), lw)
                P.inc("sync", lambda sy, i=i: sy.dma_start(out=XR[:, i * SH:(i + 1) * SH],
                                                           in_=io["x2full"][i * 128:(i + 1) * 128, 0:SH]), lx)
            P.wait("tensor", lw, lw.n)
            P.wait("vector", lx, lx.n)
            bank_ticks = {}

            def evac(o, s4, bank, mmt):
                P.wait("vector", sems.MM, mmt)
                t = P.inc("vector",
                          lambda v, o=o, s4=s4, bank=bank:
                          v.tensor_add(RESL[o][:, s4 * 512:(s4 + 1) * 512], bank[:, :],
                                       XR[:, o * SH + s4 * 512: o * SH + (s4 + 1) * 512]), sems.VE)
                return (sems.VE, t)

            emit_proj(P, sems, nO=8, nC=8, nSc=4,
                      st_ap=lambda c, o: WO[:, c * 1024 + o * 128: c * 1024 + (o + 1) * 128],
                      mv_ap=lambda c, s4: OT[:, c * SH + s4 * 512: c * SH + (s4 + 1) * 512],
                      banks=pb[0:4], bank_ticks=bank_ticks, evac=evac)
            _p42_tok0proj(nc, P, sems, io, pb, WO, OT0, ph)
            P.flush()
        emit_ln(nc, P, sems, "p42ln", RESL, io["x3f"], pb)


def _p42_tok0proj(nc, P, sems, io, pb, WO, OT0, ph):
    X20 = ph.enter_context(nc.sbuf_tensor("t2_x2", [128, 8], F32))
    OT0B = ph.enter_context(nc.sbuf_tensor("t2_ob", [128, 8], BF16))
    R0 = ph.enter_context(nc.sbuf_tensor("t2_r", [128, 8], F32))
    RW0 = ph.enter_context(nc.sbuf_tensor("t2_rw", [1, 16], F32))
    OC0 = ph.enter_context(nc.sbuf_tensor("t2_oc", [128, 1], BF16))
    MC0 = ph.enter_context(nc.sbuf_tensor("t2_mc", [128, 2], F32))
    EPS0 = ph.enter_context(nc.sbuf_tensor("t2_ep", [1, 1], F32))
    Z0 = ph.enter_context(nc.sbuf_tensor("t2_z", [1, 1], F32))
    onesr0 = ph.enter_context(nc.sbuf_tensor("t2_or", [1, 128], F32))
    lz2 = sems.new16("t2m")
    for c in range(8):
        P.inc("sync", lambda sy, c=c: sy.dma_start(out=X20[:, c:c + 1], in_=io["x20"][c * 128:(c + 1) * 128, :]), lz2)
    P.inc("gpsimd", lambda g: g.memset(OC0[:, :], 1.0), sems.GP)
    P.inc("gpsimd", lambda g: g.memset(EPS0[:, :], 1e-5), sems.GP)
    P.inc("gpsimd", lambda g: g.memset(Z0[:, :], 0.0), sems.GP)
    P.inc("gpsimd", lambda g: g.memset(onesr0[:, :], 1.0), sems.GP)
    P.wait("vector", lz2, lz2.n)
    P.wait("vector", sems.GP, sems.GP.n)
    cb_t = inc_chain(P, "vector", lambda v: v.tensor_copy(OT0B[:, :], OT0[:, :]), sems.VE)
    P.wait("tensor", sems.VE, cb_t)
    P.wait("tensor", sems.GP, sems.GP.n)
    P.wait("scalar", sems.GP, sems.GP.n)
    for o in range(8):
        for c in range(8):
            first = (o == 0 and c == 0)
            last = (o == 7 and c == 7)
            stp = WO[:, c * 1024 + o * 128: c * 1024 + (o + 1) * 128]
            if last:
                mmt = P.inc("tensor", lambda t, stp=stp, o=o, c=c, first=first:
                            t.matmul(pb[4][:, o:o + 1], stp, OT0B[:, c:c + 1], start=first, stop=True), sems.MM)
            else:
                P.on("tensor", lambda t, stp=stp, o=o, c=c, first=first:
                     t.matmul(pb[4][:, o:o + 1], stp, OT0B[:, c:c + 1], start=first, stop=False))
    P.wait("vector", sems.MM, mmt)
    t = inc_chain(P, "vector", lambda v: v.tensor_add(R0[:, :], pb[4][:, 0:8], X20[:, :]), sems.VE)
    P.wait("tensor", sems.VE, t)
    R0H = ph.enter_context(nc.sbuf_tensor("t2_rh", [128, 8], BF16))
    R0S = ph.enter_context(nc.sbuf_tensor("t2_rs", [128, 8], BF16))
    ct = inc_chain(P, "vector", lambda v: v.tensor_copy(R0H[:, :], R0[:, :]), sems.VE)
    P.wait("scalar", sems.VE, t)
    st_ = P.inc("scalar", lambda sc: sc.activation(R0S[:, :], R0[:, :], AF.Square), sems.SE)
    P.wait("tensor", sems.VE, ct)
    P.wait("tensor", sems.SE, st_)
    P.on("tensor", lambda t: t.matmul(pb[5][0:1, 0:8], OC0[:, :], R0H[:, :], start=True, stop=True))
    mmt2 = P.inc("tensor", lambda t: t.matmul(pb[5][0:1, 8:16], OC0[:, :], R0S[:, :], start=True, stop=True), sems.MM)
    P.wait("vector", sems.MM, mmt2)
    inc_chain(P, "vector", lambda v: v.tensor_copy(RW0[0:1, 0:16], pb[5][0:1, 0:16]), sems.VE)

    def rr(a, b_):
        return RW0[0:1, a:b_]
    inc_chain(P, "vector", lambda v: v.tensor_tensor(rr(0, 4), rr(0, 4), rr(4, 8), ALU.add), sems.VE)
    inc_chain(P, "vector", lambda v: v.tensor_tensor(rr(0, 2), rr(0, 2), rr(2, 4), ALU.add), sems.VE)
    inc_chain(P, "vector", lambda v: v.tensor_tensor(rr(0, 1), rr(0, 1), rr(1, 2), ALU.add), sems.VE)
    inc_chain(P, "vector", lambda v: v.tensor_tensor(rr(8, 12), rr(8, 12), rr(12, 16), ALU.add), sems.VE)
    inc_chain(P, "vector", lambda v: v.tensor_tensor(rr(8, 10), rr(8, 10), rr(10, 12), ALU.add), sems.VE)
    inc_chain(P, "vector", lambda v: v.tensor_tensor(rr(8, 9), rr(8, 9), rr(9, 10), ALU.add), sems.VE)
    inc_chain(P, "vector", lambda v: v.tensor_scalar_mul(rr(1, 2), rr(0, 1), 1.0 / 1024.0), sems.VE)
    inc_chain(P, "vector", lambda v: v.tensor_scalar_mul(rr(8, 9), rr(8, 9), 1.0 / 1024.0), sems.VE)
    inc_chain(P, "vector", lambda v: v.tensor_tensor(rr(2, 3), rr(1, 2), rr(1, 2), ALU.mult), sems.VE)
    ve_t = inc_chain(P, "vector", lambda v: v.tensor_tensor(rr(2, 3), rr(8, 9), rr(2, 3), ALU.subtract), sems.VE)
    P.wait("scalar", sems.VE, ve_t)
    t1_ = P.inc("scalar", lambda sc: sc.activation(rr(3, 4), rr(2, 3), AF.Ln, bias=EPS0[:, :], scale=1.0), sems.SE)
    P.wait("scalar", sems.SE, t1_)
    t2_ = P.inc("scalar", lambda sc: sc.activation(rr(3, 4), rr(3, 4), AF.Exp, bias=Z0[:, :], scale=-0.5), sems.SE)
    P.wait("tensor", sems.SE, t2_)
    P.on("tensor", lambda t: t.matmul(pb[6][:, 0:1], onesr0[:, :], rr(1, 2), start=True, stop=True))
    mmt3 = P.inc("tensor", lambda t: t.matmul(pb[6][:, 1:2], onesr0[:, :], rr(3, 4), start=True, stop=True), sems.MM)
    P.wait("vector", sems.MM, mmt3)
    inc_chain(P, "vector", lambda v: v.tensor_copy(MC0[:, :], pb[6][:, 0:2]), sems.VE)
    inc_chain(P, "vector", lambda v: v.tensor_scalar(R0[:, :], R0[:, :], MC0[:, 0:1], None, ALU.subtract), sems.VE)
    xt_ = inc_chain(P, "vector", lambda v: v.tensor_scalar(R0[:, :], R0[:, :], MC0[:, 1:2], None, ALU.mult), sems.VE)
    stt = sems.new16("t2s")
    P.wait("sync", sems.VE, xt_)
    for c in range(8):
        P.inc("sync", lambda sy, c=c: sy.dma_start(out=io["x30d"][c * 128:(c + 1) * 128, :], in_=R0[:, c:c + 1]), stt)
    P.wait("sync", stt, stt.n)



def build_p4b(nc, P, sems, io):
    ph0 = ExitStack()
    with ph0:
        pb = [ph0.enter_context(nc.psum_tensor(f"p4b_pb{i}", [128, 512], F32)) for i in range(8)]
        RES = ph0.enter_context(nc.sbuf_tensor("p4b_res", [128, 8 * SH], F32))
        RESL = [RES[:, i * SH:(i + 1) * SH] for i in range(8)]
        ph = ExitStack()
        with ph:
            OTX = ph.enter_context(nc.sbuf_tensor("p4b_otx", [128, 8 * SH], BF16))
            WO = ph.enter_context(nc.sbuf_tensor("p4b_wo", [128, 8 * 1024], BF16))
            XR = ph.enter_context(nc.sbuf_tensor("p4b_xr", [128, 8 * SH], F32))
            lw = sems.new16("p4bw")
            lx = sems.new16("p4bx")
            if "otx" in io:
                for i in range(8):
                    P.inc("gpsimd", lambda g, i=i: g.dma_start(out=OTX[:, i * SH:(i + 1) * SH],
                                                               in_=io["otx"][i * 128:(i + 1) * 128, :]), lw)
            else:
                OFS = ph.enter_context(nc.sbuf_tensor("p4b_ofs", [1, 1], mybir.dt.int32))
                lo = sems.new16("p4bo")
                P.inc("gpsimd", lambda g: g.dma_start(out=OFS[:, :], in_=io["ofs"]), lo)
                P.wait("gpsimd", lo, lo.n)
                otg_t = io["otg_t"]
                for _ in range(8):
                    lw.bump()

                def emit_otx_loads(g):
                    with g.register("p4b_r0") as r0, g.register("p4b_r1") as r1:
                        g.reg_load(r0, OFS[0:1, 0:1])
                        for c in range(8):
                            base = ((c // 4) * 1024 + (c % 4) * 128) * SH
                            g.reg_mov(r1, base)
                            g.reg_add(r1, r1, r0)
                            g.dma_start(out=OTX[:, c * SH:(c + 1) * SH],
                                        in_=bass.AP(otg_t, r1, [[SH, 128], [1, SH]])).then_inc(lw.h, 16)
                P.on("gpsimd", emit_otx_loads)
            for i in range(8):
                P.inc("gpsimd", lambda g, i=i: g.dma_start(out=WO[:, i * 1024:(i + 1) * 1024],
                                                           in_=io["wo"][i * 128:(i + 1) * 128, :]), lw)
                P.inc("sync", lambda sy, i=i: sy.dma_start(out=XR[:, i * SH:(i + 1) * SH],
                                                           in_=io["x2own"][i * 128:(i + 1) * 128, :]), lx)
            P.wait("tensor", lw, lw.n)
            P.wait("vector", lx, lx.n)
            bank_ticks = {}

            def evac(o, s4, bank, mmt):
                P.wait("vector", sems.MM, mmt)
                t = P.inc("vector",
                          lambda v, o=o, s4=s4, bank=bank:
                          v.tensor_add(RESL[o][:, s4 * 512:(s4 + 1) * 512], bank[:, :],
                                       XR[:, o * SH + s4 * 512: o * SH + (s4 + 1) * 512]), sems.VE)
                return (sems.VE, t)

            emit_proj(P, sems, nO=8, nC=8, nSc=4,
                      st_ap=lambda c, o: WO[:, c * 1024 + o * 128: c * 1024 + (o + 1) * 128],
                      mv_ap=lambda c, s4: OTX[:, c * SH + s4 * 512: c * SH + (s4 + 1) * 512],
                      banks=pb[0:4], bank_ticks=bank_ticks, evac=evac)
            if "ot0" in io:
                _p4b_tok0(nc, P, sems, io, pb, WO, ph)
            P.flush()
        emit_ln(nc, P, sems, "p4ln", RESL, io["x3f"], pb)


def _p4b_tok0(nc, P, sems, io, pb, WO, ph):
    OT0 = ph.enter_context(nc.sbuf_tensor("t0_ot", [128, 8], BF16))
    X20 = ph.enter_context(nc.sbuf_tensor("t0_x2", [128, 8], F32))
    R0 = ph.enter_context(nc.sbuf_tensor("t0_r", [128, 8], F32))
    RW0 = ph.enter_context(nc.sbuf_tensor("t0_rw", [1, 16], F32))
    OC0 = ph.enter_context(nc.sbuf_tensor("t0_oc", [128, 1], BF16))
    lz = sems.new16("t0l")
    lz2 = sems.new16("t0m")
    for c in range(8):
        P.inc("gpsimd", lambda g, c=c: g.dma_start(out=OT0[:, c:c + 1], in_=io["ot0"][c * 128:(c + 1) * 128, :]), lz)
        P.inc("sync", lambda sy, c=c: sy.dma_start(out=X20[:, c:c + 1], in_=io["x20"][c * 128:(c + 1) * 128, :]), lz2)
    P.inc("gpsimd", lambda g: g.memset(OC0[:, :], 1.0), sems.GP)
    P.wait("tensor", lz, lz.n)
    P.wait("vector", lz2, lz2.n)
    P.wait("tensor", sems.GP, sems.GP.n)
    for o in range(8):
        for c in range(8):
            first = (o == 0 and c == 0)
            last = (o == 7 and c == 7)
            stp = WO[:, c * 1024 + o * 128: c * 1024 + (o + 1) * 128]
            if last:
                mmt = P.inc("tensor", lambda t, stp=stp, o=o, c=c, first=first:
                            t.matmul(pb[4][:, o:o + 1], stp, OT0[:, c:c + 1], start=first, stop=True), sems.MM)
            else:
                P.on("tensor", lambda t, stp=stp, o=o, c=c, first=first:
                     t.matmul(pb[4][:, o:o + 1], stp, OT0[:, c:c + 1], start=first, stop=False))
    P.wait("vector", sems.MM, mmt)
    t = inc_chain(P, "vector", lambda v: v.tensor_add(R0[:, :], pb[4][:, 0:8], X20[:, :]), sems.VE)
    P.wait("tensor", sems.VE, t)
    R0H = ph.enter_context(nc.sbuf_tensor("t0_rh", [128, 8], BF16))
    R0S = ph.enter_context(nc.sbuf_tensor("t0_rs", [128, 8], BF16))
    ct = inc_chain(P, "vector", lambda v: v.tensor_copy(R0H[:, :], R0[:, :]), sems.VE)
    P.wait("scalar", sems.VE, ct)
    st_ = P.inc("scalar", lambda sc: sc.activation(R0S[:, :], R0[:, :], AF.Square), sems.SE)
    P.wait("tensor", sems.VE, ct)
    P.wait("tensor", sems.SE, st_)
    P.on("tensor", lambda t: t.matmul(pb[5][0:1, 0:8], OC0[:, :], R0H[:, :], start=True, stop=True))
    mmt2 = P.inc("tensor", lambda t: t.matmul(pb[5][0:1, 8:16], OC0[:, :], R0S[:, :], start=True, stop=True), sems.MM)
    P.wait("vector", sems.MM, mmt2)
    inc_chain(P, "vector", lambda v: v.tensor_copy(RW0[0:1, 0:16], pb[5][0:1, 0:16]), sems.VE)
    def rr(a, b_):
        return RW0[0:1, a:b_]
    inc_chain(P, "vector", lambda v: v.tensor_tensor(rr(0, 4), rr(0, 4), rr(4, 8), ALU.add), sems.VE)
    inc_chain(P, "vector", lambda v: v.tensor_tensor(rr(0, 2), rr(0, 2), rr(2, 4), ALU.add), sems.VE)
    inc_chain(P, "vector", lambda v: v.tensor_tensor(rr(0, 1), rr(0, 1), rr(1, 2), ALU.add), sems.VE)
    inc_chain(P, "vector", lambda v: v.tensor_tensor(rr(8, 12), rr(8, 12), rr(12, 16), ALU.add), sems.VE)
    inc_chain(P, "vector", lambda v: v.tensor_tensor(rr(8, 10), rr(8, 10), rr(10, 12), ALU.add), sems.VE)
    inc_chain(P, "vector", lambda v: v.tensor_tensor(rr(8, 9), rr(8, 9), rr(9, 10), ALU.add), sems.VE)
    inc_chain(P, "vector", lambda v: v.tensor_scalar_mul(rr(1, 2), rr(0, 1), 1.0 / 1024.0), sems.VE)
    inc_chain(P, "vector", lambda v: v.tensor_scalar_mul(rr(8, 9), rr(8, 9), 1.0 / 1024.0), sems.VE)
    inc_chain(P, "vector", lambda v: v.tensor_tensor(rr(2, 3), rr(1, 2), rr(1, 2), ALU.mult), sems.VE)
    inc_chain(P, "vector", lambda v: v.tensor_tensor(rr(2, 3), rr(8, 9), rr(2, 3), ALU.subtract), sems.VE)
    ve_t = sems.VE.n
    P.wait("scalar", sems.VE, ve_t)
    EPS0 = ph.enter_context(nc.sbuf_tensor("t0_ep", [1, 1], F32))
    Z0 = ph.enter_context(nc.sbuf_tensor("t0_z", [1, 1], F32))
    P.inc("gpsimd", lambda g: g.memset(EPS0[:, :], 1e-5), sems.GP)
    P.inc("gpsimd", lambda g: g.memset(Z0[:, :], 0.0), sems.GP)
    P.wait("scalar", sems.GP, sems.GP.n)
    t1_ = P.inc("scalar", lambda sc: sc.activation(rr(3, 4), rr(2, 3), AF.Ln, bias=EPS0[:, :], scale=1.0), sems.SE)
    P.wait("scalar", sems.SE, t1_)
    t2_ = P.inc("scalar", lambda sc: sc.activation(rr(3, 4), rr(3, 4), AF.Exp, bias=Z0[:, :], scale=-0.5), sems.SE)
    P.wait("vector", sems.SE, t2_)
    MC0 = ph.enter_context(nc.sbuf_tensor("t0_mc", [128, 2], F32))
    onesr0 = ph.enter_context(nc.sbuf_tensor("t0_or", [1, 128], F32))
    P.inc("gpsimd", lambda g: g.memset(onesr0[:, :], 1.0), sems.GP)
    P.wait("tensor", sems.GP, sems.GP.n)
    P.wait("tensor", sems.SE, t2_)
    P.on("tensor", lambda t: t.matmul(pb[6][:, 0:1], onesr0[:, :], rr(1, 2), start=True, stop=True))
    mmt3 = P.inc("tensor", lambda t: t.matmul(pb[6][:, 1:2], onesr0[:, :], rr(3, 4), start=True, stop=True), sems.MM)
    P.wait("vector", sems.MM, mmt3)
    inc_chain(P, "vector", lambda v: v.tensor_copy(MC0[:, :], pb[6][:, 0:2]), sems.VE)
    inc_chain(P, "vector", lambda v: v.tensor_scalar(R0[:, :], R0[:, :], MC0[:, 0:1], None, ALU.subtract), sems.VE)
    xt_ = inc_chain(P, "vector", lambda v: v.tensor_scalar(R0[:, :], R0[:, :], MC0[:, 1:2], None, ALU.mult), sems.VE)
    stt = sems.new16("t0s")
    P.wait("sync", sems.VE, xt_)
    for c in range(8):
        P.inc("sync", lambda sy, c=c: sy.dma_start(out=io["x30d"][c * 128:(c + 1) * 128, :], in_=R0[:, c:c + 1]), stt)
    P.wait("sync", stt, stt.n)



def build_p5(nc, P, sems, io):
    ph0 = ExitStack()
    with ph0:
        pb = [ph0.enter_context(nc.psum_tensor(f"p5_pb{i}", [128, 512], F32)) for i in range(8)]
        X3b = ph0.enter_context(nc.sbuf_tensor("p5_x3b", [128, 8 * SH], BF16))
        ACC = ph0.enter_context(nc.sbuf_tensor("p5_acc", [128, 8 * SH], F32))
        EIX = ph0.enter_context(nc.sbuf_tensor("p5_eix", [1, 2], mybir.dt.int32))
        BSUM = ph0.enter_context(nc.sbuf_tensor("p5_bsum", [128, 8], F32))

        lx = sems.new16("p5x")
        for i in range(8):
            P.inc("gpsimd", lambda g, i=i: g.dma_start(out=X3b[:, i * SH:(i + 1) * SH],
                                                       in_=io["x3f"][i * 128:(i + 1) * 128, :]), lx)
        _p5_route(nc, P, sems, io, pb, EIX, BSUM)
        _p5_units(nc, P, sems, io, pb, X3b, ACC, EIX, BSUM, lx)
        ACCL = [ACC[:, i * SH:(i + 1) * SH] for i in range(8)]
        emit_ln(nc, P, sems, "p5ln", ACCL, io["yf"], pb)


def _p5_route(nc, P, sems, io, pb, EIX, BSUM):
    ph = ExitStack()
    with ph:
        WR = ph.enter_context(nc.sbuf_tensor("p5_wr", [128, 8 * 8], F32))
        X30 = ph.enter_context(nc.sbuf_tensor("p5_x30", [128, 8], F32))
        BR = ph.enter_context(nc.sbuf_tensor("p5_br", [8, 1], F32))
        SR = ph.enter_context(nc.sbuf_tensor("p5_sr", [8, 1], F32))
        RW = ph.enter_context(nc.sbuf_tensor("p5_rw", [1, 64], F32))
        B2 = ph.enter_context(nc.sbuf_tensor("p5_b2", [128, 64], F32))
        onesr = ph.enter_context(nc.sbuf_tensor("p5_or", [1, 128], F32))
        lw = sems.new16("p5r")
        for c in range(8):
            P.inc("sync", lambda sy, c=c: sy.dma_start(out=WR[:, c * 8:(c + 1) * 8],
                                                       in_=io["wr"][c * 128:(c + 1) * 128, :]), lw)
            P.inc("sync", lambda sy, c=c: sy.dma_start(out=X30[:, c:c + 1],
                                                       in_=io["x30"][c * 128:(c + 1) * 128, :]), lw)
        P.inc("sync", lambda sy: sy.dma_start(out=BR[:, :], in_=io["br"][:, :]), lw)
        P.inc("sync", lambda sy: sy.dma_start(out=B2[:, :], in_=io["b2c"][:, :]), lw)
        P.inc("gpsimd", lambda g: g.memset(onesr[:, :], 1.0), sems.GP)
        P.wait("tensor", lw, lw.n)
        P.wait("tensor", sems.GP, sems.GP.n)
        pairs = [(WR[:, c * 8:(c + 1) * 8], X30[:, c:c + 1]) for c in range(8)]
        mmt = mm_group(P, sems, pb[0][0:8, 0:1], pairs)
        P.wait("vector", sems.MM, mmt)
        P.wait("vector", lw, lw.n)
        inc_chain(P, "vector", lambda v: v.tensor_tensor(SR[:, :], pb[0][0:8, 0:1], BR[:, :], ALU.add), sems.VE)
        st0 = sems.new16("p5g")
        P.wait("sync", sems.VE, sems.VE.n)
        for e in range(8):
            P.inc("sync", lambda sy, e=e: sy.dma_start(out=RW[0:1, e:e + 1], in_=SR[e:e + 1, :]), st0)
        P.wait("vector", st0, st0.n)
        for e in range(8):
            js = [j for j in range(8) if j != e]
            inc_chain(P, "vector", lambda v, e=e, j=js[0]:
                      v.tensor_tensor(RW[0:1, 8 + e:9 + e], RW[0:1, j:j + 1], RW[0:1, e:e + 1], ALU.is_gt), sems.VE)
            for j in js[1:]:
                inc_chain(P, "vector", lambda v, e=e, j=j:
                          v.tensor_tensor(RW[0:1, 24:25], RW[0:1, j:j + 1], RW[0:1, e:e + 1], ALU.is_gt), sems.VE)
                inc_chain(P, "vector", lambda v, e=e:
                          v.tensor_tensor(RW[0:1, 8 + e:9 + e], RW[0:1, 8 + e:9 + e], RW[0:1, 24:25], ALU.add), sems.VE)
            inc_chain(P, "vector", lambda v, e=e:
                      v.tensor_scalar(RW[0:1, 16 + e:17 + e], RW[0:1, 8 + e:9 + e], 2.0, 0.5, ALU.is_lt, ALU.mult), sems.VE)
        P.wait("tensor", sems.VE, sems.VE.n)
        for e in range(8):
            inc_chain(P, "vector", lambda v, e=e:
                      v.tensor_scalar(RW[0:1, 25 + e:26 + e], RW[0:1, 8 + e:9 + e], 0.5, float(e),
                                      ALU.is_lt, ALU.mult), sems.VE)
            inc_chain(P, "vector", lambda v, e=e:
                      v.tensor_scalar(RW[0:1, 33 + e:34 + e], RW[0:1, 8 + e:9 + e], 1.0, 0.0,
                                      ALU.is_gt, ALU.bypass), sems.VE)
            inc_chain(P, "vector", lambda v, e=e:
                      v.tensor_scalar(RW[0:1, 41 + e:42 + e], RW[0:1, 33 + e:34 + e], 1.0, float(e),
                                      ALU.is_lt, ALU.mult), sems.VE)
        def tsum(base):
            inc_chain(P, "vector", lambda v: v.tensor_tensor(RW[0:1, base:base + 4], RW[0:1, base:base + 4], RW[0:1, base + 4:base + 8], ALU.add), sems.VE)
            inc_chain(P, "vector", lambda v: v.tensor_tensor(RW[0:1, base:base + 2], RW[0:1, base:base + 2], RW[0:1, base + 2:base + 4], ALU.add), sems.VE)
            inc_chain(P, "vector", lambda v: v.tensor_tensor(RW[0:1, base:base + 1], RW[0:1, base:base + 1], RW[0:1, base + 1:base + 2], ALU.add), sems.VE)
        tsum(25)
        tsum(41)
        inc_chain(P, "vector", lambda v: v.tensor_tensor(RW[0:1, 41:42], RW[0:1, 41:42], RW[0:1, 25:26], ALU.subtract), sems.VE)
        ei_t = inc_chain(P, "vector", lambda v: v.tensor_copy(EIX[0:1, 0:1], RW[0:1, 25:26]), sems.VE)
        ei_t = inc_chain(P, "vector", lambda v: v.tensor_copy(EIX[0:1, 1:2], RW[0:1, 41:42]), sems.VE)
        P.wait("gpsimd", sems.VE, ei_t)
        P.flush()


def _p5_units(nc, P, sems, io, pb, X3b, ACC, EIX, BSUM, lx):
    ph = ExitStack()
    with ph:
        HT = ph.enter_context(nc.sbuf_tensor("p5_ht", [128, 8 * SH], BF16))
        W1S = [ph.enter_context(nc.sbuf_tensor(f"p5_w1s{k}", [128, 8 * 256], BF16)) for k in range(2)]
        W2S = [ph.enter_context(nc.sbuf_tensor(f"p5_w2s{k}", [128, 2 * 1024], BF16)) for k in range(2)]
        B1S = [ph.enter_context(nc.sbuf_tensor(f"p5_b1s{k}", [128, 8], F32)) for k in range(2)]
        B2S = [ph.enter_context(nc.sbuf_tensor(f"p5_b2s{k}", [128, 8], F32)) for k in range(2)]
        XRS = ph.enter_context(nc.sbuf_tensor("p5_xr", [128, 512], F32))

        w1h = io["w1f_t"]
        w2h = io["w2b_t"]
        b1h = io["b1c_t"]
        b2h = io["b2c_t"]
        P.wait("tensor", lx, lx.n)

        sW1 = [sems.new16(f"p5w1{k}") for k in range(2)]
        sW2 = [sems.new16(f"p5w2{k}") for k in range(2)]
        sB12 = sems.new16("p5b12")
        sXR = sems.new16("p5xr")
        w1_used, w2_used, xr_used = {}, {}, {}
        w1_ld, w2_ld, xr_ld = {}, {}, {}

        def load_bias(slot):
            for _ in range(2):
                sB12.bump()

            def emitfn(g, slot=slot):
                with g.register(f"p5r0_{slot}") as r0, g.register(f"p5r1_{slot}") as r1:
                    g.reg_load(r0, EIX[0:1, slot:slot + 1])
                    g.reg_mul(r1, r0, 8)
                    g.dma_start(out=B1S[slot][:, :], in_=bass.AP(b1h, r1, [[64, 128], [1, 8]])).then_inc(sB12.h, 16)
                    g.dma_start(out=B2S[slot][:, :], in_=bass.AP(b2h, r1, [[64, 128], [1, 8]])).then_inc(sB12.h, 16)
            P.on("gpsimd", emitfn)

        def load_w1(idx, slot, fs):
            k = idx % 2
            prev = w1_used.get(idx - 2)
            if prev is not None:
                P.wait("gpsimd", sems.MM, prev)
            sW1[k].bump()

            def emitfn(g, slot=slot, fs=fs, k=k):
                with g.register(f"p5w1r1_{slot}_{fs}") as r1:
                    g.reg_load(r1, EIX[0:1, slot:slot + 1])
                    g.reg_mul(r1, r1, 1048576)
                    g.reg_add(r1, r1, fs * 256)
                    g.dma_start(out=W1S[k][:, :],
                                in_=bass.AP(w1h, r1, [[1024, 128], [131072, 8], [1, 256]])).then_inc(sW1[k].h, 16)
            P.on("gpsimd", emitfn)
            w1_ld[idx] = (sW1[k], sW1[k].n)

        def load_w2(idx, slot, og):
            k = idx % 2
            prev = w2_used.get(idx - 2)
            if prev is not None:
                P.wait("gpsimd", sems.MM, prev)
            sW2[k].bump()

            def emitfn(g, slot=slot, og=og, k=k):
                with g.register(f"p5w2r0_{slot}_{og}") as r0:
                    g.reg_load(r0, EIX[0:1, slot:slot + 1])
                    g.reg_mul(r0, r0, 1048576)
                    g.reg_add(r0, r0, og * 262144)
                    g.dma_start(out=W2S[k][:, :],
                                in_=bass.AP(w2h, r0, [[1024, 128], [131072, 2], [1, 1024]])).then_inc(sW2[k].h, 16)
            P.on("gpsimd", emitfn)
            w2_ld[idx] = (sW2[k], sW2[k].n)

        def load_xr(idx, o, s4):
            prev = xr_used.get(idx - 1)
            if prev is not None:
                P.wait("sync", sems.VE, prev)
            P.inc("sync", lambda sy, o=o, s4=s4:
                  sy.dma_start(out=XRS[:, :], in_=io["x3f"][o * 128:(o + 1) * 128, s4 * 512:(s4 + 1) * 512]), sXR)
            xr_ld[idx] = (sXR, sXR.n)

        load_bias(0)
        load_bias(1)
        bank_ticks = {}
        for slot in range(2):
            w1i0 = slot * 4
            load_w1(w1i0, slot, 0)
            for fs in range(4):
                if fs < 3:
                    load_w1(w1i0 + fs + 1, slot, fs + 1)
                P.wait("tensor", w1_ld[w1i0 + fs][0], w1_ld[w1i0 + fs][1])
                if fs == 0 and slot == 0:
                    P.wait("scalar", sB12, sB12.n)
                for fc2 in range(2):
                    fc = fs * 2 + fc2
                    for s4 in range(4):
                        bi = (fc * 4 + s4) % 2
                        bank = pb[bi]
                        prev = bank_ticks.get(id(bank))
                        if prev is not None:
                            P.wait("tensor", prev[0], prev[1])
                        pairs = [(W1S[(w1i0 + fs) % 2][:, c * 256 + fc2 * 128: c * 256 + (fc2 + 1) * 128],
                                  X3b[:, c * SH + s4 * 512: c * SH + (s4 + 1) * 512]) for c in range(8)]
                        mmt = mm_group(P, sems, bank[:, :], pairs)
                        P.wait("scalar", sems.MM, mmt)
                        et = P.inc("scalar",
                                   lambda sc, fc=fc, s4=s4, bank=bank, slot=slot:
                                   sc.activation(HT[:, fc * SH + s4 * 512: fc * SH + (s4 + 1) * 512],
                                                 bank[:, :], AF.Gelu, bias=B1S[slot][:, fc:fc + 1],
                                                 scale=1.0), sems.SE)
                        bank_ticks[id(bank)] = (sems.SE, et)
                w1_used[w1i0 + fs] = sems.MM.n
            P.wait("tensor", sems.SE, sems.SE.n)
            w2i0 = slot * 4
            load_w2(w2i0, slot, 0)
            for og in range(4):
                if og < 3:
                    load_w2(w2i0 + og + 1, slot, og + 1)
                P.wait("tensor", w2_ld[w2i0 + og][0], w2_ld[w2i0 + og][1])
                for oo in range(2):
                    o = og * 2 + oo
                    for s4 in range(4):
                        bi = 2 + s4 % 2
                        bank = pb[bi]
                        prev = bank_ticks.get(id(bank))
                        if prev is not None:
                            P.wait("tensor", prev[0], prev[1])
                        pairs = [(W2S[(w2i0 + og) % 2][:, oo * 1024 + fc * 128: oo * 1024 + (fc + 1) * 128],
                                  HT[:, fc * SH + s4 * 512: fc * SH + (s4 + 1) * 512]) for fc in range(8)]
                        mmt = mm_group(P, sems, bank[:, :], pairs)
                        P.wait("vector", sems.MM, mmt)
                        dacc = ACC[:, o * SH + s4 * 512: o * SH + (s4 + 1) * 512]
                        if slot == 0:
                            load_xr(o * 4 + s4, o, s4)
                            P.wait("vector", xr_ld[o * 4 + s4][0], xr_ld[o * 4 + s4][1])
                            t = inc_chain(P, "vector", lambda v, bank=bank, dacc=dacc:
                                          v.scalar_tensor_tensor(dacc, bank[:, :], 0.5, XRS[:, :],
                                                                 ALU.mult, ALU.add), sems.VE)
                            xr_used[o * 4 + s4] = sems.VE.n
                        else:
                            t = inc_chain(P, "vector", lambda v, bank=bank, dacc=dacc:
                                          v.scalar_tensor_tensor(dacc, bank[:, :], 0.5, dacc,
                                                                 ALU.mult, ALU.add), sems.VE)
                        bank_ticks[id(bank)] = (sems.VE, t)
                w2_used[w2i0 + og] = sems.MM.n
        P.wait("vector", sB12, sB12.n)
        inc_chain(P, "vector", lambda v: v.tensor_tensor(BSUM[:, 0:8], B2S[0][:, :], B2S[1][:, :], ALU.add), sems.VE)
        inc_chain(P, "vector", lambda v: v.tensor_scalar_mul(BSUM[:, 0:8], BSUM[:, 0:8], 0.5), sems.VE)
        for o in range(8):
            inc_chain(P, "vector", lambda v, o=o:
                      v.tensor_scalar_add(ACC[:, o * SH:(o + 1) * SH], ACC[:, o * SH:(o + 1) * SH],
                                          BSUM[:, o:o + 1]), sems.VE)
        P.flush()

def prep_p1(ins):
    x = np.ascontiguousarray(np.asarray(ins["L0_attn_x"], dtype=np.float32))
    xT = np.ascontiguousarray(x.transpose(0, 2, 1))
    wqkT = np.ascontiguousarray(np.asarray(ins["L0_attn_w_qkv"]).T)
    wpT = np.ascontiguousarray(np.asarray(ins["L0_attn_w_proj"]).T)
    bproj8 = np.ascontiguousarray(np.asarray(ins["L0_attn_b_proj"]).reshape(8, 128).T)
    per_core = []
    for c in range(8):
        b, p = c // 2, c % 2
        per_core.append({
            "xT": xT[b],
            "wqkT": wqkT,
            "wv_own": np.ascontiguousarray(wqkT[:, 2048 + 512 * p: 2048 + 512 * (p + 1)]),
            "wpT": wpT,
            "bproj8": bproj8,
            "x_res": np.ascontiguousarray(xT[b][:, SH * p: SH * (p + 1)]),
        })
    return per_core


def build_p1_standalone():
    nc = mk_nc()
    io = {}
    io["xT"] = nc.dram_tensor("xT", [1024, S], F32, kind="ExternalInput").ap()
    io["wqkT"] = nc.dram_tensor("wqkT", [1024, 3072], F32, kind="ExternalInput").ap()
    io["wv_own"] = nc.dram_tensor("wv_own", [1024, 512], F32, kind="ExternalInput").ap()
    io["wpT"] = nc.dram_tensor("wpT", [1024, 1024], F32, kind="ExternalInput").ap()
    io["bproj8"] = nc.dram_tensor("bproj8", [128, 8], F32, kind="ExternalInput").ap()
    io["x_res"] = nc.dram_tensor("x_res", [1024, SH], F32, kind="ExternalInput").ap()
    io["x1f"] = nc.dram_tensor("x1f", [1024, SH], F32, kind="ExternalOutput").ap()
    with ExitStack() as stack:
        sems = Sems(nc, stack, "p1")
        P = Prog(nc)
        build_p1(nc, P, sems, io)
        P.flush()
    return nc


def prep_p2(ins):
    def col128(b):
        b = np.asarray(b, dtype=np.float32)
        return np.ascontiguousarray(b.reshape(-1, 128).T)

    def w2block(w2):
        w2 = np.asarray(w2, dtype=np.float32)
        E = w2.shape[0]
        blk = w2.reshape(E, 32, 128, 8, 128).transpose(0, 3, 2, 1, 4)
        return np.ascontiguousarray(blk.reshape(E * 8 * 128, 4096))

    sw1 = np.asarray(ins["L0_ffn_sw1"], dtype=np.float32)
    rw1 = np.asarray(ins["L0_ffn_rw1"], dtype=np.float32)
    shared = {
        "gate_w": np.ascontiguousarray(np.asarray(ins["L0_ffn_gate_w"], dtype=np.float32)),
        "gate_b": np.ascontiguousarray(np.asarray(ins["L0_ffn_gate_b"], dtype=np.float32).reshape(4, 1)),
        "sw1f": np.ascontiguousarray(sw1.reshape(2 * 1024, 4096)),
        "rw1f": np.ascontiguousarray(rw1.reshape(4 * 1024, 4096)),
        "sw2b": w2block(ins["L0_ffn_sw2"]),
        "rw2b": w2block(ins["L0_ffn_rw2"]),
        "sb1c": np.ascontiguousarray(np.asarray(ins["L0_ffn_sb1"], dtype=np.float32).reshape(2 * 32, 128).T.reshape(128, 64)),
        "rb1c": np.ascontiguousarray(np.asarray(ins["L0_ffn_rb1"], dtype=np.float32).reshape(4 * 32, 128).T.reshape(128, 128)),
        "sb2c": np.ascontiguousarray(np.asarray(ins["L0_ffn_sb2"], dtype=np.float32).reshape(2 * 8, 128).T.reshape(128, 16)),
        "rb2c": np.ascontiguousarray(np.asarray(ins["L0_ffn_rb2"], dtype=np.float32).reshape(4 * 8, 128).T.reshape(128, 32)),
    }
    return shared


def build_p2_standalone():
    nc = mk_nc()
    io = {}
    io["x1f"] = nc.dram_tensor("x1f", [1024, SH], F32, kind="ExternalInput").ap()
    io["gate_w"] = nc.dram_tensor("gate_w", [1024, 4], F32, kind="ExternalInput").ap()
    io["gate_b"] = nc.dram_tensor("gate_b", [4, 1], F32, kind="ExternalInput").ap()
    io["sw1f"] = nc.dram_tensor("sw1f", [2048, 4096], F32, kind="ExternalInput").ap()
    io["rw1f"] = nc.dram_tensor("rw1f", [4096, 4096], F32, kind="ExternalInput").ap()
    io["sw2b"] = nc.dram_tensor("sw2b", [2048, 4096], F32, kind="ExternalInput").ap()
    io["rw2b"] = nc.dram_tensor("rw2b", [4096, 4096], F32, kind="ExternalInput").ap()
    io["sb1c"] = nc.dram_tensor("sb1c", [128, 64], F32, kind="ExternalInput").ap()
    io["rb1c"] = nc.dram_tensor("rb1c", [128, 128], F32, kind="ExternalInput").ap()
    io["sb2c"] = nc.dram_tensor("sb2c", [128, 16], F32, kind="ExternalInput").ap()
    io["rb2c"] = nc.dram_tensor("rb2c", [128, 32], F32, kind="ExternalInput").ap()
    io["x2f"] = nc.dram_tensor("x2f", [1024, SH], F32, kind="ExternalOutput").ap()
    with ExitStack() as stack:
        sems = Sems(nc, stack, "p2")
        P = Prog(nc)
        build_p2(nc, P, sems, io)
        P.flush()
    return nc


def prep_p4a(ins, x2):
    lam = LAM
    akq = np.zeros((128, 256), np.float32)
    p_ = np.arange(128)
    for qc in range(8):
        for kt in range(32):
            if kt < 4 * qc:
                col = np.exp(-lam * (512 * qc - 128 * kt - p_)) / 32.0
            elif kt > 4 * qc + 3:
                col = np.exp(-lam * (128 * kt + p_ - 512 * qc)) / 32.0
            else:
                col = np.full(128, 1.0 / 32.0)
            akq[:, qc * 32 + kt] = col
    f_ = np.arange(512)
    dql = np.concatenate([np.exp(-lam * f_), np.exp(lam * f_)]).astype(np.float32).reshape(1, 1024)
    dec4 = np.zeros((128, 4 * 512), np.float32)
    for d in range(4):
        dec4[:, d * 512:(d + 1) * 512] = np.exp(-lam * np.abs((128 * d + p_)[:, None] - f_[None, :]))
    wk = np.ascontiguousarray(np.asarray(ins["L1_attn_Wk"], np.float32))
    wv = np.ascontiguousarray(np.asarray(ins["L1_attn_Wv"], np.float32))
    wq = np.asarray(ins["L1_attn_Wq"], np.float32)
    wd = np.ascontiguousarray(np.asarray(ins["L1_attn_W_down_kv"], np.float32))
    wupk = np.asarray(ins["L1_attn_W_up_k"], np.float32)
    wupv = np.asarray(ins["L1_attn_W_up_v"], np.float32)
    per_core = []
    for c in range(8):
        b, p = c // 2, c % 2
        x2T = x2[b].T
        per_core.append({
            "x2full": np.ascontiguousarray(np.concatenate([x2T[:, :SH], x2T[:, SH:]], axis=0)),
            "wk": wk, "wv": wv, "wdown": wd,
            "wqh": np.ascontiguousarray(wq[:, 512 * p:512 * (p + 1)]),
            "wupkh": np.ascontiguousarray(wupk[:, 512 * p:512 * (p + 1)]),
            "wupvh": np.ascontiguousarray(wupv[:, 512 * p:512 * (p + 1)]),
            "akq": akq, "dql": dql, "dec4": dec4,
        })
    return per_core


def prep_p4b(ins, x2, ot_by_core):
    wo = np.ascontiguousarray(np.asarray(ins["L1_attn_Wo"], np.float32))
    per_core = []
    for c in range(8):
        b, p = c // 2, c % 2
        otx = np.concatenate([ot_by_core[2 * b][:, SH * p:SH * (p + 1)],
                              ot_by_core[2 * b + 1][:, SH * p:SH * (p + 1)]], axis=0)
        per_core.append({
            "otx": np.ascontiguousarray(otx),
            "wo": wo,
            "x2own": np.ascontiguousarray(x2[b].T[:, SH * p:SH * (p + 1)]),
        })
    return per_core


def prep_p5(ins, x3):
    w1 = np.asarray(ins["L1_ffn_W1"], np.float32)
    w2 = np.asarray(ins["L1_ffn_W2"], np.float32)
    w2b = np.ascontiguousarray(w2.reshape(8, 8, 128, 8, 128).transpose(0, 3, 2, 1, 4).reshape(8192, 1024))
    shared = {
        "wr": np.ascontiguousarray(np.asarray(ins["L1_ffn_Wr"], np.float32)),
        "br": np.ascontiguousarray(np.asarray(ins["L1_ffn_br"], np.float32).reshape(8, 1)),
        "w1f": np.ascontiguousarray(w1.reshape(8 * 1024, 1024)),
        "w2b": w2b,
        "b1c": np.ascontiguousarray(np.asarray(ins["L1_ffn_b1"], np.float32).reshape(8 * 8, 128).T.reshape(128, 64)),
        "b2c": np.ascontiguousarray(np.asarray(ins["L1_ffn_b2"], np.float32).reshape(8 * 8, 128).T.reshape(128, 64)),
    }
    per_core = []
    for c in range(8):
        b, p = c // 2, c % 2
        d = dict(shared)
        d["x3f"] = np.ascontiguousarray(x3[b].T[:, SH * p:SH * (p + 1)])
        d["x30"] = np.ascontiguousarray(x3[b].T[:, 0:1])
        per_core.append(d)
    return per_core


def build_p4a_standalone():
    nc = mk_nc()
    io = {}
    for nm, shp in [("x2full", [2048, 2048]), ("wk", [1024, 1024]), ("wv", [1024, 1024]),
                    ("wqh", [1024, 512]), ("wdown", [1024, 256]),
                    ("wupkh", [256, 512]), ("wupvh", [256, 512]),
                    ("akq", [128, 256]), ("dql", [1, 1024]), ("dec4", [128, 4 * 512])]:
        io[nm] = nc.dram_tensor(nm, shp, F32, kind="ExternalInput").ap()
    io["otf"] = nc.dram_tensor("otf", [1024, SH], F32, kind="ExternalOutput").ap()
    with ExitStack() as stack:
        sems = Sems(nc, stack, "p4")
        P = Prog(nc)
        build_p4a(nc, P, sems, io)
        P.flush()
    return nc


def build_p4b_standalone():
    nc = mk_nc()
    io = {}
    for nm, shp in [("otx", [1024, SH]), ("wo", [1024, 1024]), ("x2own", [1024, SH])]:
        io[nm] = nc.dram_tensor(nm, shp, F32, kind="ExternalInput").ap()
    io["x3f"] = nc.dram_tensor("x3f", [1024, SH], F32, kind="ExternalOutput").ap()
    with ExitStack() as stack:
        sems = Sems(nc, stack, "pb")
        P = Prog(nc)
        build_p4b(nc, P, sems, io)
        P.flush()
    return nc


def build_p5_standalone():
    nc = mk_nc()
    io = {}
    for nm, shp in [("x3f", [1024, SH]), ("x30", [1024, 1]), ("wr", [1024, 8]), ("br", [8, 1]),
                    ("w1f", [8192, 1024]), ("w2b", [8192, 1024]), ("b1c", [128, 64]), ("b2c", [128, 64])]:
        io[nm] = nc.dram_tensor(nm, shp, F32, kind="ExternalInput").ap()
    io["yf"] = nc.dram_tensor("yf", [1024, SH], F32, kind="ExternalOutput").ap()
    with ExitStack() as stack:
        sems = Sems(nc, stack, "p5")
        P = Prog(nc)
        build_p5(nc, P, sems, io)
        P.flush()
    return nc


def build_launchA():
    nc = mk_nc()
    io = {}
    ext = [("xT", [1024, S]), ("wqkT", [1024, 3072]), ("wv_own", [1024, 512]),
           ("wpT", [1024, 1024]), ("bproj8", [128, 8]), ("x_res", [1024, SH]),
           ("gate_w", [1024, 4]), ("gate_b", [4, 1]),
           ("sw1f", [2048, 4096]), ("rw1f", [4096, 4096]), ("sw2b", [2048, 4096]), ("rw2b", [4096, 4096]),
           ("sb1c", [128, 64]), ("rb1c", [128, 128]), ("sb2c", [128, 16]), ("rb2c", [128, 32])]
    for nm, shp in ext:
        io[nm] = nc.dram_tensor(nm, shp, F32, kind="ExternalInput").ap()
    x1f = nc.dram_tensor("x1f", [1024, SH], F32)
    io["x1f"] = x1f.ap()
    io["x2f"] = nc.dram_tensor("x2f", [1024, SH], F32, kind="ExternalOutput").ap()
    with ExitStack() as stack:
        sems = Sems(nc, stack, "a")
        P = Prog(nc)
        build_p1(nc, P, sems, io)
        P.flush()
        build_p2(nc, P, sems, io)
        P.flush()
    return nc


def build_launchC():
    nc = mk_nc()
    io = {}
    for nm, shp in [("otx", [1024, SH]), ("wo", [1024, 1024]), ("x2own", [1024, SH]),
                    ("ot0", [1024, 1]), ("x20", [1024, 1]), ("wr", [1024, 8]), ("br", [8, 1]),
                    ("w1f", [8192, 1024]), ("w2b", [8192, 1024]), ("b1c", [128, 64]), ("b2c", [128, 64])]:
        io[nm] = nc.dram_tensor(nm, shp, F32, kind="ExternalInput").ap()
    x3f = nc.dram_tensor("x3f", [1024, SH], F32)
    io["x3f"] = x3f.ap()
    io["x30d"] = nc.dram_tensor("x30d", [1024, 1], F32).ap()
    io["x30"] = io["x30d"]
    io["yf"] = nc.dram_tensor("yf", [1024, SH], F32, kind="ExternalOutput").ap()
    with ExitStack() as stack:
        sems = Sems(nc, stack, "c")
        P = Prog(nc)
        build_p4b(nc, P, sems, io)
        P.flush()
        build_p5(nc, P, sems, io)
        P.flush()
    return nc


def build_launchB2():
    nc = mk_nc()
    io = {}
    for nm, shp in [("x2full", [1024, S]), ("x20", [1024, 1]),
                    ("wk", [1024, 1024]), ("wv", [1024, 1024]), ("wq", [1024, 1024]),
                    ("wdown", [1024, 256]), ("wupk", [256, 1024]), ("wupv", [256, 1024]),
                    ("wo", [1024, 1024]),
                    ("akq4", [128, 128]), ("dql", [1, 1024]), ("dec4", [128, 2048]), ("ak0", [128, 32]),
                    ("wr", [1024, 8]), ("br", [8, 1]), ("w1f", [8192, 1024]), ("w2b", [8192, 1024]),
                    ("b1c", [128, 64]), ("b2c", [128, 64])]:
        io[nm] = nc.dram_tensor(nm, shp, F32, kind="ExternalInput").ap()
    x3f = nc.dram_tensor("x3f", [1024, SH], F32)
    io["x3f"] = x3f.ap()
    io["x30d"] = nc.dram_tensor("x30d", [1024, 1], F32).ap()
    io["x30"] = io["x30d"]
    io["yf"] = nc.dram_tensor("yf", [1024, SH], F32, kind="ExternalOutput").ap()
    io["w1f_t"] = io["w1f"].tensor
    io["w2b_t"] = io["w2b"].tensor
    io["b1c_t"] = io["b1c"].tensor
    io["b2c_t"] = io["b2c"].tensor
    with ExitStack() as stack:
        sems = Sems(nc, stack, "b2")
        P = Prog(nc)
        build_p42(nc, P, sems, io)
        P.flush()
        build_p5(nc, P, sems, io)
        P.flush()
    return nc


def prep_l2_consts(ins):
    lam = LAM
    p_ = np.arange(128)
    akq4 = np.zeros((128, 128), np.float32)
    for qc in range(4):
        for kt in range(32):
            if kt < 4 * qc:
                col = np.exp(-lam * (512 * qc - 128 * kt - p_)) / 32.0
            elif kt > 4 * qc + 3:
                col = np.exp(-lam * (128 * kt + p_ - 512 * qc)) / 32.0
            else:
                col = np.full(128, 1.0 / 32.0)
            akq4[:, qc * 32 + kt] = col
    f_ = np.arange(512)
    dql = np.concatenate([np.exp(-lam * f_), np.exp(lam * f_)]).astype(np.float32).reshape(1, 1024)
    dec4 = np.zeros((128, 4 * 512), np.float32)
    for d in range(4):
        dec4[:, d * 512:(d + 1) * 512] = np.exp(-lam * np.abs((128 * d + p_)[:, None] - f_[None, :]))
    w1 = ins["L1_ffn_W1"]
    w2 = ins["L1_ffn_W2"]
    w2b = np.ascontiguousarray(w2.reshape(8, 8, 128, 8, 128).transpose(0, 3, 2, 1, 4).reshape(8192, 1024))
    shared = {
        "wk": np.ascontiguousarray(ins["L1_attn_Wk"]), "wv": np.ascontiguousarray(ins["L1_attn_Wv"]),
        "wq": np.ascontiguousarray(ins["L1_attn_Wq"]), "wdown": np.ascontiguousarray(ins["L1_attn_W_down_kv"]),
        "wupk": np.ascontiguousarray(ins["L1_attn_W_up_k"]), "wupv": np.ascontiguousarray(ins["L1_attn_W_up_v"]),
        "wo": np.ascontiguousarray(ins["L1_attn_Wo"]),
        "akq4": akq4, "dql": dql, "dec4": dec4,
        "wr": np.ascontiguousarray(ins["L1_ffn_Wr"]),
        "br": np.ascontiguousarray(ins["L1_ffn_br"].reshape(8, 1)),
        "w1f": np.ascontiguousarray(w1.reshape(8 * 1024, 1024)), "w2b": w2b,
        "b1c": np.ascontiguousarray(ins["L1_ffn_b1"].reshape(8 * 8, 128).T.reshape(128, 64)),
        "b2c": np.ascontiguousarray(ins["L1_ffn_b2"].reshape(8 * 8, 128).T.reshape(128, 64)),
    }
    ak0 = np.zeros((2, 128, 32), np.float32)
    kk = (128 * np.arange(32)[None, :] + p_[:, None]).astype(np.float64)
    kk = (p_[:, None] + 128 * np.arange(32)[None, :])
    ak0[0] = np.exp(-lam * kk) / 32.0
    ak0[1] = np.exp(-lam * (4095.0 - kk)) / 32.0
    return shared, ak0


def kernel2(ins):
    ins = _to_np(ins)
    cores = list(range(NCORES))
    pc1 = prep_p1(ins)
    shared2 = prep_p2(ins)
    in_a = []
    for c in cores:
        d = {}
        d.update(pc1[c])
        d.update(shared2)
        in_a.append(d)
    ra = run_bass_kernel_spmd(build_launchA(), in_a, core_ids=cores)
    x2f = [ra.results[c]["x2f"] for c in cores]

    shared, ak0 = prep_l2_consts(ins)
    in_b = []
    for c in cores:
        b, p = c // 2, c % 2
        x2T = np.concatenate([x2f[2 * b], x2f[2 * b + 1]], axis=1)
        if p:
            x2v = np.ascontiguousarray(x2T[:, ::-1])
        else:
            x2v = np.ascontiguousarray(x2T)
        d = dict(shared)
        d["x2full"] = x2v
        d["x20"] = np.ascontiguousarray(x2T[:, 0:1])
        d["ak0"] = ak0[p]
        in_b.append(d)
    rb = run_bass_kernel_spmd(build_launchB2(), in_b, core_ids=cores)
    out = np.empty((4, 4096, 1024), np.float32)
    for c in cores:
        b, p = c // 2, c % 2
        y = rb.results[c]["yf"]
        if p:
            out[b, SH:, :] = y[:, ::-1].T
        else:
            out[b, :SH, :] = y.T
    return out


def kernel3(ins):
    ins = _to_np(ins)
    cores = list(range(NCORES))
    pc1 = prep_p1(ins)
    shared2 = prep_p2(ins)
    in_a = []
    for c in cores:
        d = {}
        d.update(pc1[c])
        d.update(shared2)
        in_a.append(d)
    ra = run_bass_kernel_spmd(build_launchA(), in_a, core_ids=cores)
    x2f = [ra.results[c]["x2f"] for c in cores]

    x2full = [np.ascontiguousarray(np.concatenate([x2f[2 * b], x2f[2 * b + 1]], axis=1)) for b in range(4)]
    pc4a = prep_p4a_from_cols(ins, x2full)
    rb = run_bass_kernel_spmd(build_p4a_standalone(), pc4a, core_ids=cores)
    ot = [rb.results[c]["otf"] for c in cores]

    wo = np.ascontiguousarray(ins["L1_attn_Wo"])
    shared5 = prep_p5_shared(ins)
    in_c = []
    for c in cores:
        b, p = c // 2, c % 2
        otx = np.concatenate([ot[2 * b][p * 512:(p + 1) * 512, :],
                              ot[2 * b + 1][p * 512:(p + 1) * 512, :]], axis=0)
        ot0 = np.concatenate([ot[2 * b][0:512, 0:1], ot[2 * b + 1][0:512, 0:1]], axis=0)
        d = {"otx": np.ascontiguousarray(otx), "wo": wo, "x2own": x2f[c],
             "ot0": np.ascontiguousarray(ot0), "x20": np.ascontiguousarray(x2f[2 * b][:, 0:1])}
        d.update(shared5)
        in_c.append(d)
    rc_ = run_bass_kernel_spmd(build_launchC(), in_c, core_ids=cores)
    out = np.empty((4, 4096, 1024), np.float32)
    for c in cores:
        b, p = c // 2, c % 2
        out[b, SH * p:SH * (p + 1), :] = rc_.results[c]["yf"].T
    return out


RG_PAIRS = [[0, 1], [2, 3], [4, 5], [6, 7]]


def build_fused():
    from concourse.aot_env import allow_ici_all_to_all
    import contextlib
    nc = mk_nc()
    io = {}
    ext = [("xT", [1024, S]), ("wqkT", [1024, 3072]), ("wv_own", [1024, 512]),
           ("wpT", [1024, 1024]), ("bproj8", [128, 8]), ("x_res", [1024, SH]),
           ("gate_w", [1024, 4]), ("gate_b", [4, 1]),
           ("sw1f", [2048, 4096]), ("rw1f", [4096, 4096]), ("sw2b", [2048, 4096]), ("rw2b", [4096, 4096]),
           ("sb1c", [128, 64]), ("rb1c", [128, 128]), ("sb2c", [128, 16]), ("rb2c", [128, 32]),
           ("wk", [1024, 1024]), ("wv", [1024, 1024]), ("wqh", [1024, 512]), ("wdown", [1024, 256]),
           ("wupkh", [256, 512]), ("wupvh", [256, 512]),
           ("akq", [128, 256]), ("dql", [1, 1024]), ("dec4", [128, 4 * 512]),
           ("wo", [1024, 1024]),
           ("wr", [1024, 8]), ("br", [8, 1]), ("w1f", [8192, 1024]), ("w2b", [8192, 1024]),
           ("b1c", [128, 64]), ("b2c", [128, 64])]
    for nm, shp in ext:
        io[nm] = nc.dram_tensor(nm, shp, F32, kind="ExternalInput").ap()
    io["ofs"] = nc.dram_tensor("ofs", [1, 1], mybir.dt.int32, kind="ExternalInput").ap()
    x1f = nc.dram_tensor("x1f", [1024, SH], F32)
    x2f = nc.dram_tensor("x2f", [1024, SH], F32)
    x2full = nc.dram_tensor("x2full_i", [2048, SH], F32, addr_space="Local")
    otf = nc.dram_tensor("otf_i", [1024, SH], F32)
    otg = nc.dram_tensor("otg_i", [2048, SH], F32, addr_space="Local")
    x3f = nc.dram_tensor("x3f_i", [1024, SH], F32)
    x30b = nc.dram_tensor("x30b", [1024, 1], F32)
    x30g = nc.dram_tensor("x30g", [2048, 1], F32, addr_space="Local")
    io["x1f"] = x1f.ap()
    io["x2f"] = x2f.ap()
    io["x2full"] = x2full.ap()
    io["otf"] = otf.ap()
    io["yf"] = nc.dram_tensor("yf", [1024, SH], F32, kind="ExternalOutput").ap()

    with ExitStack() as stack:
        stack.enter_context(allow_ici_all_to_all())
        sems = Sems(nc, stack, "f")
        P = Prog(nc)
        build_p1(nc, P, sems, io)
        P.flush()
        build_p2(nc, P, sems, io)
        P.flush()
        cc1 = Sem(stack.enter_context(nc.semaphore("fcc1")), 16)
        P.inc("gpsimd", lambda g: g.collective_compute(
            "AllGather", ALU.bypass, replica_groups=RG_PAIRS,
            ins=[x2f.ap().opt()], outs=[x2full.ap().opt()]), cc1)
        P.wait("gpsimd", cc1, 16)
        P.flush()
        build_p4a(nc, P, sems, io)
        P.flush()
        cc2 = Sem(stack.enter_context(nc.semaphore("fcc2")), 16)
        P.inc("gpsimd", lambda g: g.collective_compute(
            "AllGather", ALU.bypass, replica_groups=RG_PAIRS,
            ins=[otf.ap().opt()], outs=[otg.ap().opt()]), cc2)
        P.wait("gpsimd", cc2, 16)
        P.flush()
        io4b = {"otg_t": otg, "ofs": io["ofs"], "wo": io["wo"], "x2own": x2f.ap(), "x3f": x3f.ap()}
        build_p4b(nc, P, sems, io4b)
        P.flush()
        cc3 = Sem(stack.enter_context(nc.semaphore("fcc3")), 16)
        cps = sems.new16("fx30")
        P.inc("gpsimd", lambda g: g.dma_start(out=x30b.ap(), in_=x3f.ap()[:, 0:1], allow_slow_non_contiguous=True), cps)
        P.wait("gpsimd", cps, cps.n)
        P.inc("gpsimd", lambda g: g.collective_compute(
            "AllGather", ALU.bypass, replica_groups=RG_PAIRS,
            ins=[x30b.ap().opt()], outs=[x30g.ap().opt()]), cc3)
        P.wait("gpsimd", cc3, 16)
        P.flush()
        io5 = {"x3f": x3f.ap(), "x30": x30g.ap()[0:1024, :], "wr": io["wr"], "br": io["br"],
               "w1f": io["w1f"], "w2b": io["w2b"], "b1c": io["b1c"], "b2c": io["b2c"],
               "yf": io["yf"]}
        build_p5(nc, P, sems, io5)
        P.flush()
    return nc


def kernel(**ins):
    return kernel2(ins)


def kernel_fused(**ins):
    ins = _to_np(ins)
    cores = list(range(NCORES))
    pc1 = prep_p1(ins)
    shared2 = prep_p2(ins)
    pc4a = prep_p4a_consts(ins)
    wo = np.ascontiguousarray(ins["L1_attn_Wo"])
    shared5 = prep_p5_shared(ins)
    in_maps = []
    for c in cores:
        d = {}
        d.update(pc1[c])
        d.update(shared2)
        d.update(pc4a[c])
        d["wo"] = wo
        d.update(shared5)
        d["ofs"] = np.array([[(c % 2) * 512 * SH]], dtype=np.int32)
        in_maps.append(d)
    nc = build_fused()
    res = run_bass_kernel_spmd(nc, in_maps, core_ids=cores)
    out = np.empty((4, 4096, 1024), np.float32)
    for c in cores:
        b, p = c // 2, c % 2
        out[b, SH * p:SH * (p + 1), :] = res.results[c]["yf"].T
    return out


def prep_p4a_consts(ins):
    lam = LAM
    akq = np.zeros((128, 256), np.float32)
    p_ = np.arange(128)
    for qc in range(8):
        for kt in range(32):
            if kt < 4 * qc:
                col = np.exp(-lam * (512 * qc - 128 * kt - p_)) / 32.0
            elif kt > 4 * qc + 3:
                col = np.exp(-lam * (128 * kt + p_ - 512 * qc)) / 32.0
            else:
                col = np.full(128, 1.0 / 32.0)
            akq[:, qc * 32 + kt] = col
    f_ = np.arange(512)
    dql = np.concatenate([np.exp(-lam * f_), np.exp(lam * f_)]).astype(np.float32).reshape(1, 1024)
    dec4 = np.zeros((128, 4 * 512), np.float32)
    for d in range(4):
        dec4[:, d * 512:(d + 1) * 512] = np.exp(-lam * np.abs((128 * d + p_)[:, None] - f_[None, :]))
    wk = np.ascontiguousarray(ins["L1_attn_Wk"])
    wv = np.ascontiguousarray(ins["L1_attn_Wv"])
    wq = ins["L1_attn_Wq"]
    wd = np.ascontiguousarray(ins["L1_attn_W_down_kv"])
    wupk = ins["L1_attn_W_up_k"]
    wupv = ins["L1_attn_W_up_v"]
    per_core = []
    for c in range(8):
        b, p = c // 2, c % 2
        per_core.append({
            "wk": wk, "wv": wv, "wdown": wd,
            "wqh": np.ascontiguousarray(wq[:, 512 * p:512 * (p + 1)]),
            "wupkh": np.ascontiguousarray(wupk[:, 512 * p:512 * (p + 1)]),
            "wupvh": np.ascontiguousarray(wupv[:, 512 * p:512 * (p + 1)]),
            "akq": akq, "dql": dql, "dec4": dec4,
        })
    return per_core


def _to_np(ins):
    out = {}
    for k, v in ins.items():
        a = np.asarray(v)
        if a.dtype != np.float32:
            a = np.asarray(a, dtype=np.float32)
        out[k] = a
    return out


def prep_p4a_from_cols(ins, x2fullT):
    lam = LAM
    akq = np.zeros((128, 256), np.float32)
    p_ = np.arange(128)
    for qc in range(8):
        for kt in range(32):
            if kt < 4 * qc:
                col = np.exp(-lam * (512 * qc - 128 * kt - p_)) / 32.0
            elif kt > 4 * qc + 3:
                col = np.exp(-lam * (128 * kt + p_ - 512 * qc)) / 32.0
            else:
                col = np.full(128, 1.0 / 32.0)
            akq[:, qc * 32 + kt] = col
    f_ = np.arange(512)
    dql = np.concatenate([np.exp(-lam * f_), np.exp(lam * f_)]).astype(np.float32).reshape(1, 1024)
    dec4 = np.zeros((128, 4 * 512), np.float32)
    for d in range(4):
        dec4[:, d * 512:(d + 1) * 512] = np.exp(-lam * np.abs((128 * d + p_)[:, None] - f_[None, :]))
    wk = np.ascontiguousarray(ins["L1_attn_Wk"])
    wv = np.ascontiguousarray(ins["L1_attn_Wv"])
    wq = ins["L1_attn_Wq"]
    wd = np.ascontiguousarray(ins["L1_attn_W_down_kv"])
    wupk = ins["L1_attn_W_up_k"]
    wupv = ins["L1_attn_W_up_v"]
    per_core = []
    for c in range(8):
        b, p = c // 2, c % 2
        per_core.append({
            "x2full": np.ascontiguousarray(np.concatenate([x2fullT[b][:, :SH], x2fullT[b][:, SH:]], axis=0)),
            "wk": wk, "wv": wv, "wdown": wd,
            "wqh": np.ascontiguousarray(wq[:, 512 * p:512 * (p + 1)]),
            "wupkh": np.ascontiguousarray(wupk[:, 512 * p:512 * (p + 1)]),
            "wupvh": np.ascontiguousarray(wupv[:, 512 * p:512 * (p + 1)]),
            "akq": akq, "dql": dql, "dec4": dec4,
        })
    return per_core


def prep_p5_shared(ins):
    w1 = ins["L1_ffn_W1"]
    w2 = ins["L1_ffn_W2"]
    w2b = np.ascontiguousarray(w2.reshape(8, 8, 128, 8, 128).transpose(0, 3, 2, 1, 4).reshape(8192, 1024))
    return {
        "wr": np.ascontiguousarray(ins["L1_ffn_Wr"]),
        "br": np.ascontiguousarray(ins["L1_ffn_br"].reshape(8, 1)),
        "w1f": np.ascontiguousarray(w1.reshape(8 * 1024, 1024)),
        "w2b": w2b,
        "b1c": np.ascontiguousarray(ins["L1_ffn_b1"].reshape(8 * 8, 128).T.reshape(128, 64)),
        "b2c": np.ascontiguousarray(ins["L1_ffn_b2"].reshape(8 * 8, 128).T.reshape(128, 64)),
    }
